# version 01
# baseline (speedup 1.0000x reference)
import contextlib
import numpy as np

import concourse.bass as bass
import concourse.mybir as mybir
from concourse.bass_utils import run_bass_kernel_spmd
from concourse.alu_op_type import AluOpType as OP

F32 = mybir.dt.float32
BF16 = mybir.dt.bfloat16
I32 = mybir.dt.int32
AF = mybir.ActivationFunctionType
AX = mybir.AxisListType

T = 2048
D = 1024
NCORE = 8
NT = T // 128
LN2 = float(np.log(2.0))


class PB:
    def __init__(self, tag):
        self.tag = tag
        self.nc = bass.Bass("TRN2", target_bir_lowering=False)
        self.es = contextlib.ExitStack()
        self.ops = []
        self.n_sem = 8
        self.sems = [self.es.enter_context(self.nc.semaphore(f"ch{i}"))
                     for i in range(self.n_sem)]
        self._sb_names = {}
        self._cms = []
        self._finalized = False

    def _uniq(self, name):
        k = self._sb_names.get(name, 0)
        self._sb_names[name] = k + 1
        return name if k == 0 else f"{name}_{k}"

    def ccol(self, value):
        key = float(value)
        if not hasattr(self, "_ccols"):
            self._ccols = {}
        if key not in self._ccols:
            t = self.sb(f"cc_{len(self._ccols)}", (128, 1), F32)
            self.G(lambda e: e.memset(t[:, :1], key))
            self._ccols[key] = t
        return self._ccols[key][:, :1]

    def sb(self, name, shape, dt=F32):
        cm = self.nc.sbuf_tensor(self._uniq(name), list(shape), dt)
        t = cm.__enter__()
        self._cms.append((id(t), cm))
        return t

    def sbfree(self, *tiles):
        ids = {id(t) for t in tiles}
        keep = []
        for tid, cm in reversed(self._cms):
            if tid in ids:
                cm.__exit__(None, None, None)
                ids.discard(tid)
            else:
                keep.append((tid, cm))
        self._cms = list(reversed(keep))

    def ps(self, name, shape, dt=F32):
        return self.es.enter_context(
            self.nc.psum_tensor(self._uniq(name), list(shape), dt))

    def din(self, name, shape, dt=F32):
        return self.nc.dram_tensor(name, list(shape), dt, kind="ExternalInput").ap()

    def dout(self, name, shape, dt=F32):
        return self.nc.dram_tensor(name, list(shape), dt, kind="ExternalOutput").ap()

    def op(self, eng, fn, inc=1, kind=None):
        if kind is None:
            kind = "c" if inc == 1 else ("g" if eng == "gpsimd" else "h")
        self.ops.append((eng, fn, inc, kind))

    def V(self, fn):
        self.op("vector", fn, 1)

    def S(self, fn):
        self.op("scalar", fn, 1)

    def TE(self, fn):
        self.op("tensor", fn, 1)

    def G(self, fn):
        self.op("gpsimd", fn, 1)

    def dma(self, dst, src, q="sync", **kw):
        self.op(q, lambda e: e.dma_start(out=dst, in_=src, **kw), 16)

    def dmaT(self, dst, src, q="scalar"):
        self.op(q, lambda e: e.dma_start(out=dst, in_=src, transpose=True), 16)

    def finalize(self):
        assert not self._finalized
        self._finalized = True
        blk = self.es.enter_context(self.nc.Block())
        ops = self.ops
        n = len(ops)
        sem_idx = []
        rot = [0]
        for eng, fn, inc, kind in ops:
            if kind == "g":
                sem_idx.append(7)
            elif kind == "h":
                sem_idx.append(6)
            else:
                sem_idx.append(rot[0] % 6)
                rot[0] += 1
        sem_of = [self.sems[j] for j in sem_idx]
        tot = [0] * self.n_sem
        val_after = []
        for i in range(n):
            tot[sem_idx[i]] += ops[i][2]
            val_after.append(tot[sem_idx[i]])
        by_eng = {}
        for i, (eng, fn, inc, kind) in enumerate(ops):
            by_eng.setdefault(eng, []).append(i)
        for eng, idxs in by_eng.items():
            def body(e, idxs=idxs):
                for i in idxs:
                    if i > 0:
                        e.wait_ge(sem_of[i - 1], val_after[i - 1])
                    ops[i][1](e).then_inc(sem_of[i], ops[i][2])
            getattr(blk, eng)(body)
        last = n - 1

        @blk.vector
        def _(e):
            e.wait_ge(sem_of[last], val_after[last])
        for _, cm in reversed(self._cms):
            cm.__exit__(None, None, None)
        self._cms = []
        self.es.close()
        return self.nc

    def run(self, in_maps, core_ids=None):
        nc = self.finalize()
        return run_bass_kernel_spmd(nc, in_maps,
                                    core_ids=core_ids or list(range(len(in_maps))))


def e_matmul_acc(pb, out_ps, lhsT_tiles, rhs_tiles, out_sl=None):
    K = len(lhsT_tiles)
    for k in range(K):
        lt, rt = lhsT_tiles[k], rhs_tiles[k]
        pb.TE(lambda e, lt=lt, rt=rt, s=(k == 0), p=(k == K - 1):
              e.matmul(out_ps if out_sl is None else out_ps, lt, rt,
                       start=s, stop=p))


def e_round_rne(pb, out, inp, itile):
    pb.V(lambda e: e.tensor_copy(itile, inp))
    pb.V(lambda e: e.tensor_copy(out, itile))


def make_constants():
    cons = {}
    cons["c_ident_bf16"] = np.eye(128, dtype=np.float32)
    cons["c_ones_row"] = np.ones((1, 128), np.float32)
    return cons


def host_prep(inp):
    pre = {}
    f = np.float32
    pre["x0"] = np.ascontiguousarray(inp["L0_attn_x"], dtype=f)
    pre["l1f_rhs"] = np.ascontiguousarray(inp["L1_ffn_rhs"], dtype=f)
    return pre


def prog_layernorm():
    pb = PB("ln")
    d_x = pb.din("x", [T, D], F32)
    d_xn = pb.dout("xn", [T, D], F32)
    d_xnb = pb.dout("xnb", [T, D], BF16)
    xt = pb.sb("xt", (128, D))
    sq = pb.sb("sq", (128, D))
    mcol = pb.sb("mcol", (128, 1))
    vcol = pb.sb("vcol", (128, 1))
    rcol = pb.sb("rcol", (128, 1))
    ot = pb.sb("ot", (128, D))
    ob = pb.sb("ob", (128, D), BF16)
    for i in range(NT):
        sl = slice(i * 128, (i + 1) * 128)
        pb.dma(xt[:, :], d_x[sl, :])
        pb.V(lambda e: e.reduce_sum(mcol[:, :1], xt[:, :], axis=AX.X))
        pb.V(lambda e: e.tensor_scalar(mcol[:, :1], mcol[:, :1], 1.0 / D, None, OP.mult))
        pb.V(lambda e: e.tensor_scalar(ot[:, :], xt[:, :], mcol[:, :1], None, OP.subtract))
        pb.S(lambda e: e.activation(sq[:, :], ot[:, :], AF.Square))
        pb.V(lambda e: e.reduce_sum(vcol[:, :1], sq[:, :], axis=AX.X))
        pb.V(lambda e: e.tensor_scalar(vcol[:, :1], vcol[:, :1], 1.0 / D, 1e-5, OP.mult, OP.add))
        pb.S(lambda e: e.activation(rcol[:, :1], vcol[:, :1], AF.Ln))
        pb.S(lambda e: e.activation(rcol[:, :1], rcol[:, :1], AF.Exp, scale=-0.5))
        pb.V(lambda e: e.tensor_scalar(ot[:, :], ot[:, :], rcol[:, :1], None, OP.mult))
        pb.V(lambda e: e.tensor_copy(ob[:, :], ot[:, :]))
        pb.dma(d_xn[sl, :], ot[:, :], q="gpsimd")
        pb.dma(d_xnb[sl, :], ob[:, :], q="gpsimd")
    return pb


def prog_l1_ffn():
    pb = PB("l1f")
    d_xn = pb.din("xn", [T, D], F32)
    d_rhs = pb.din("rhs", [D, D], F32)
    d_out = pb.dout("out", [T, D], F32)

    rt = [pb.sb(f"rt{i}", (128, D)) for i in range(8)]
    for i in range(8):
        pb.dma(rt[i][:, :], d_rhs[i * 128:(i + 1) * 128, :])
    am = pb.sb("am", (128, D))
    abst = pb.sb("abst", (128, D))
    pb.S(lambda e: e.activation(am[:, :], rt[0][:, :], AF.Abs))
    for i in range(1, 8):
        pb.S(lambda e, i=i: e.activation(abst[:, :], rt[i][:, :], AF.Abs))
        pb.V(lambda e: e.tensor_tensor(am[:, :], am[:, :], abst[:, :], OP.max))
    ident = pb.sb("ident", (128, 128))
    d_id = pb.din("c_ident", [128, 128], F32)
    pb.dma(ident[:, :], d_id)
    ptr = pb.ps("ptr", (128, 128))
    colmax = pb.sb("colmax", (128, 8))
    for j in range(8):
        pb.TE(lambda e, j=j: e.transpose(ptr[:, :], am[:, j * 128:(j + 1) * 128], ident[:, :]))
        pb.V(lambda e, j=j: e.tensor_reduce(colmax[:, j:j + 1], ptr[:, :], axis=AX.X, op=OP.max))
    rsc = pb.sb("rsc", (128, 8))
    rin = pb.sb("rin", (128, 8))
    pb.V(lambda e: e.tensor_scalar(rsc[:, :8], colmax[:, :8], 1e-12, 1.0 / 127.0, OP.max, OP.mult))
    pb.V(lambda e: e.reciprocal(rin[:, :8], rsc[:, :8]))
    ones1 = pb.sb("ones1", (1, 128))
    pb.G(lambda e: e.memset(ones1[:, :], 1.0))
    rinT = pb.ps("rinT", (128, 128))
    pb.TE(lambda e: e.transpose(rinT[:8, :], rin[:, :8], ident[:, :]))
    rinTs = pb.sb("rinTs", (8, 128))
    pb.V(lambda e: e.tensor_copy(rinTs[:, :], rinT[:8, :]))
    rowflat = pb.sb("rowflat", (1, D))
    for j in range(8):
        pb.dma(rowflat[0:1, j * 128:(j + 1) * 128], rinTs[j:j + 1, :], q="gpsimd")
    binst = pb.ps("binst", (128, 512))
    rinv_full = pb.sb("rinv_full", (128, D))
    for j in range(2):
        pb.TE(lambda e, j=j: e.matmul(binst[:, :], ones1[:1, :], rowflat[0:1, j * 512:(j + 1) * 512],
                                      start=True, stop=True))
        pb.V(lambda e, j=j: e.tensor_copy(rinv_full[:, j * 512:(j + 1) * 512], binst[:, :]))
    qr = [pb.sb(f"qr{i}", (128, D), BF16) for i in range(8)]
    it = pb.sb("it", (128, D), I32)
    tmp = pb.sb("tmp", (128, D))
    for i in range(8):
        pb.V(lambda e, i=i: e.tensor_tensor(tmp[:, :], rt[i][:, :], rinv_full[:, :], OP.mult))
        pb.V(lambda e: e.tensor_copy(it[:, :], tmp[:, :]))
        pb.V(lambda e, i=i: e.tensor_copy(qr[i][:, :], it[:, :]))
    rscTs = pb.sb("rscTs", (8, 128))
    rscT = pb.ps("rscT", (128, 128))
    pb.TE(lambda e: e.transpose(rscT[:8, :], rsc[:, :8], ident[:, :]))
    pb.V(lambda e: e.tensor_copy(rscTs[:, :], rscT[:8, :]))
    rowflat2 = pb.sb("rowflat2", (1, D))
    for j in range(8):
        pb.dma(rowflat2[0:1, j * 128:(j + 1) * 128], rscTs[j:j + 1, :], q="gpsimd")
    rsc_full = pb.sb("rsc_full", (128, D))
    for j in range(2):
        pb.TE(lambda e, j=j: e.matmul(binst[:, :], ones1[:1, :], rowflat2[0:1, j * 512:(j + 1) * 512],
                                      start=True, stop=True))
        pb.V(lambda e, j=j: e.tensor_copy(rsc_full[:, j * 512:(j + 1) * 512], binst[:, :]))

    xt = pb.sb("xt", (128, D))
    rowm = pb.sb("rowm", (128, 1))
    lsc = pb.sb("lsc", (128, 1))
    linv = pb.sb("linv", (128, 1))
    ql = pb.sb("ql", (128, D), BF16)
    qlT = pb.sb("qlT", (128, D), BF16)
    idb = pb.sb("idb", (128, 128), BF16)
    pb.V(lambda e: e.tensor_copy(idb[:, :], ident[:, :]))
    ptb = pb.ps("ptb", (128, 128), BF16)
    acc = pb.ps("acc", (128, 512))
    ot = pb.sb("ot", (128, D))
    for i in range(NT):
        sl = slice(i * 128, (i + 1) * 128)
        pb.dma(xt[:, :], d_xn[sl, :])
        pb.S(lambda e: e.activation(tmp[:, :], xt[:, :], AF.Abs))
        pb.V(lambda e: e.tensor_reduce(rowm[:, :1], tmp[:, :], axis=AX.X, op=OP.max))
        pb.V(lambda e: e.tensor_scalar(lsc[:, :1], rowm[:, :1], 1e-12, 1.0 / 127.0, OP.max, OP.mult))
        pb.V(lambda e: e.reciprocal(linv[:, :1], lsc[:, :1]))
        pb.V(lambda e: e.tensor_scalar(tmp[:, :], xt[:, :], linv[:, :1], None, OP.mult))
        pb.V(lambda e: e.tensor_copy(it[:, :], tmp[:, :]))
        pb.V(lambda e: e.tensor_copy(ql[:, :], it[:, :]))
        for j in range(8):
            pb.TE(lambda e, j=j: e.transpose(ptb[:, :], ql[:, j * 128:(j + 1) * 128], idb[:, :]))
            pb.V(lambda e, j=j: e.tensor_copy(qlT[:, j * 128:(j + 1) * 128], ptb[:, :]))
        for half in range(2):
            for k in range(8):
                pb.TE(lambda e, k=k, half=half, s=(k == 0), p=(k == 7):
                      e.matmul(acc[:, :], qlT[:, k * 128:(k + 1) * 128],
                               qr[k][:, half * 512:(half + 1) * 512], start=s, stop=p))
            pb.V(lambda e, half=half: e.scalar_tensor_tensor(
                ot[:, half * 512:(half + 1) * 512], acc[:, :], lsc[:, :1],
                rsc_full[:, half * 512:(half + 1) * 512], OP.mult, OP.mult))
        pb.dma(d_out[sl, :], ot[:, :], q="gpsimd")
    return pb


def kernel(**inputs):
    raise NotImplementedError("under construction")


C_L2 = float(0.5 / np.log(2.0))


def e_fq52(pb, outb, x, sc, ebias=2.0, eclamp=-14.0, keep_m=False, relu=False, npart=128):
    B, I, INV, S, V = sc["B"], sc["I"], sc["INV"], sc["S"], sc["V"]
    pb.S(lambda e: e.activation(B, x, AF.Square))
    pb.S(lambda e: e.activation(B, B, AF.Ln))
    pb.V(lambda e: e.tensor_scalar(B, B, C_L2, 0.5, OP.mult, OP.subtract))
    pb.V(lambda e: e.tensor_copy(I, B))
    pb.V(lambda e: e.tensor_copy(B, I))
    pb.V(lambda e: e.tensor_scalar(B, B, eclamp, None, OP.max))
    bp = pb.ccol(ebias * LN2)[:npart, :1]
    bn = pb.ccol(-ebias * LN2)[:npart, :1]
    pb.S(lambda e: e.activation(INV, B, AF.Exp, scale=-LN2, bias=bp))
    pb.S(lambda e: e.activation(S, B, AF.Exp, scale=LN2, bias=bn))
    pb.V(lambda e: e.tensor_tensor(V, x, INV, OP.mult))
    pb.V(lambda e: e.tensor_copy(I, V))
    pb.V(lambda e: e.tensor_copy(V, I))
    if relu:
        pb.V(lambda e: e.tensor_scalar(V, V, 0.0, None, OP.max))
    if keep_m:
        return
    pb.V(lambda e: e.tensor_tensor(outb, V, S, OP.mult))


def e_fq52_of_grid(pb, outb, sc):
    B, I, S, V = sc["B"], sc["I"], sc["S"], sc["V"]
    pb.V(lambda e: e.tensor_scalar(B, V, 1.0 / 32.0, None, OP.mult))
    pb.V(lambda e: e.tensor_copy(I, B))
    pb.V(lambda e: e.tensor_copy(B, I))
    pb.V(lambda e: e.tensor_scalar(S, S, 32.0, None, OP.mult))
    pb.V(lambda e: e.tensor_tensor(outb, B, S, OP.mult))


def prog_l3_ffn(dbg=False, upto=99):
    pb = PB("l3f")
    d_xn = pb.din("xn", [T, D], F32)
    d_w1t = pb.din("w1t", [D, 4096], F32)
    d_b1 = pb.din("b1", [1, 4096], F32)
    d_w2t = pb.din("w2t", [4096, D], F32)
    d_b2 = pb.din("b2", [1, D], F32)
    d_mask = pb.din("mask", [T, D], BF16)
    d_out = pb.dout("out", [T, D], F32)
    nc = pb.nc
    ikind = "ExternalOutput" if dbg else "Internal"
    d_qx = nc.dram_tensor("qx", [T, D], BF16, kind=ikind).ap()
    d_qw1 = nc.dram_tensor("qw1", [D, 4096], BF16, kind=ikind).ap()
    d_qw2 = nc.dram_tensor("qw2", [4096, D], BF16, kind=ikind).ap()
    d_hq = nc.dram_tensor("hq", [T, 4096], BF16, kind=ikind).ap()
    d_y1 = nc.dram_tensor("y1dbg", [T, 4096], F32, kind=ikind).ap() if dbg else None

    FW = 2048
    xt = pb.sb("xt", (128, FW))
    scr = {k: pb.sb("scr" + k, (128, FW)) for k in ["B", "INV", "S", "V"]}
    scr["I"] = pb.sb("scrI", (128, FW), I32)
    ob = pb.sb("ob", (128, FW), BF16)

    def scrv(F):
        return {k: (v[:, :F] if k != "I" else v[:, :F]) for k, v in scr.items()}

    for v in (2.0 * LN2, -2.0 * LN2, 7.0 * LN2, -7.0 * LN2):
        pb.ccol(v)
    for i in range(NT):
        sl = slice(i * 128, (i + 1) * 128)
        pb.dma(xt[:, :D], d_xn[sl, :])
        e_fq52(pb, ob[:, :D], xt[:, :D], scrv(D))
        pb.dma(d_qx[sl, :], ob[:, :D], q="gpsimd")
    if upto < 1:
        d_zzz = pb.dout("zzz", [1, 1], F32)
        pb.dma(d_zzz, ob[:1, :1], q="gpsimd")
        return pb
    for i in range(8):
        sl = slice(i * 128, (i + 1) * 128)
        for hh in range(2):
            cs = slice(hh * 2048, (hh + 1) * 2048)
            pb.dma(xt[:, :], d_w1t[sl, cs])
            e_fq52(pb, ob[:, :], xt[:, :], scrv(2048))
            pb.dma(d_qw1[sl, cs], ob[:, :], q="gpsimd")
    for i in range(32):
        sl = slice(i * 128, (i + 1) * 128)
        pb.dma(xt[:, :D], d_w2t[sl, :])
        e_fq52(pb, ob[:, :D], xt[:, :D], scrv(D))
        pb.dma(d_qw2[sl, :], ob[:, :D], q="gpsimd")
    b1r = pb.sb("b1r", (1, 4096))
    qb1 = pb.sb("qb1", (1, 4096), BF16)
    pb.dma(b1r[:1, :], d_b1)
    for hh in range(2):
        cs = slice(hh * 2048, (hh + 1) * 2048)
        sc1 = {k: v[:1, :2048] for k, v in scr.items()}
        e_fq52(pb, qb1[:1, cs], b1r[:1, cs], sc1, npart=1)
    b2r = pb.sb("b2r", (1, D))
    qb2 = pb.sb("qb2", (1, D), BF16)
    pb.dma(b2r[:1, :], d_b2)
    sc2 = {k: v[:1, :D] for k, v in scr.items()}
    e_fq52(pb, qb2[:1, :], b2r[:1, :], sc2, npart=1)
    if upto < 2:
        d_zzz = pb.dout("zzz", [1, 1], F32)
        pb.dma(d_zzz, ob[:1, :1], q="gpsimd")
        return pb
    ones_b = pb.sb("ones_b", (1, 128), BF16)
    pb.G(lambda e: e.memset(ones_b[:, :], 1.0))
    acc = pb.ps("acc", (128, 512))
    hqt = pb.sb("hqt", (128, 4096), BF16)
    qxT = [pb.sb(f"qxT{k}", (128, T), BF16) for k in range(8)]
    for k in range(8):
        pb.dmaT(qxT[k][:, :], d_qx[:, k * 128:(k + 1) * 128])
    qw1s = [pb.sb(f"qw1s{k}", (128, 4096), BF16) for k in range(8)]
    for k in range(8):
        pb.dma(qw1s[k][:, :], d_qw1[k * 128:(k + 1) * 128, :])
    for i in range(NT):
        tsl = slice(i * 128, (i + 1) * 128)
        for nchunk in range(8):
            nsl = slice(nchunk * 512, (nchunk + 1) * 512)
            for k in range(8):
                pb.TE(lambda e, k=k, tsl=tsl, nsl=nsl, s=(k == 0):
                      e.matmul(acc[:, :], qxT[k][:, tsl], qw1s[k][:, nsl],
                               start=s, stop=False))
            pb.TE(lambda e, nsl=nsl: e.matmul(acc[:, :], ones_b[:1, :], qb1[:1, nsl],
                                              start=False, stop=True))
            s5 = scrv(512)
            if dbg:
                pb.V(lambda e, nsl=nsl: e.tensor_copy(scr["B"][:, :512], acc[:, :]))
                pb.dma(d_y1[tsl, nsl], scr["B"][:, :512], q="gpsimd")
            e_fq52(pb, None, acc[:, :], s5, ebias=7.0, eclamp=-30.0,
                   keep_m=True, relu=True)
            e_fq52_of_grid(pb, hqt[:, nsl], s5)
        pb.dma(d_hq[tsl, :], hqt[:, :], q="gpsimd")
    pb.sbfree(*qw1s)
    pb.sbfree(*qxT)
    hqT = [pb.sb(f"hqT{k}", (128, 1024), BF16) for k in range(32)]
    w2s = pb.sb("w2s", (128, D), BF16)
    mt = pb.sb("mt", (128, D), BF16)
    otile = pb.sb("otile", (128, D))
    acc2 = pb.ps("acc2", (128, 512))
    for half in range(2):
        hsl = slice(half * 1024, (half + 1) * 1024)
        for k in range(32):
            pb.dmaT(hqT[k][:, :], d_hq[hsl, k * 128:(k + 1) * 128])
        for it in range(8):
            i = half * 8 + it
            tsl = slice(i * 128, (i + 1) * 128)
            ttsl = slice(it * 128, (it + 1) * 128)
            for nchunk in range(2):
                nsl = slice(nchunk * 512, (nchunk + 1) * 512)
                for k in range(32):
                    pb.dma(w2s[:, :], d_qw2[k * 128:(k + 1) * 128, :])
                    pb.TE(lambda e, k=k, ttsl=ttsl, nsl=nsl, s=(k == 0):
                          e.matmul(acc2[:, :], hqT[k][:, ttsl], w2s[:, nsl],
                                   start=s, stop=False))
                pb.TE(lambda e, nsl=nsl: e.matmul(acc2[:, :], ones_b[:1, :], qb2[:1, nsl],
                                                  start=False, stop=True))
                s5 = scrv(512)
                e_fq52(pb, None, acc2[:, :], s5, ebias=7.0, eclamp=-30.0, keep_m=True)
                pb.V(lambda e, nsl=nsl, s5=s5: e.tensor_tensor(otile[:, nsl], s5["V"], s5["S"], OP.mult))
            pb.dma(mt[:, :], d_mask[tsl, :])
            pb.V(lambda e: e.tensor_tensor(otile[:, :], otile[:, :], mt[:, :], OP.mult))
            pb.dma(d_out[tsl, :], otile[:, :], q="gpsimd")
    return pb


def prog_l0_ffn():
    pb = PB("l0f")
    d_xnb = pb.din("xnb", [T, D], BF16)
    d_rwt = pb.din("rwt", [D, 8], F32)
    d_rb = pb.din("rb", [1, 8], F32)
    d_ewt = pb.din("ewt", [8, D, D], F32)
    d_eb = pb.din("eb", [8, D], F32)
    d_out = pb.dout("out", [T, D], F32)

    xnT = [pb.sb(f"xnT{k}", (128, T), BF16) for k in range(8)]
    for k in range(8):
        pb.dmaT(xnT[k][:, :], d_xnb[:, k * 128:(k + 1) * 128])
    rwts = pb.sb("rwts", (128, 64), BF16)
    for k in range(8):
        pb.op("gpsimd", lambda e, k=k: e.dma_start(
            out=rwts[:, k * 8:(k + 1) * 8], in_=d_rwt[k * 128:(k + 1) * 128, :]), 16)
    ones1 = pb.sb("ones1", (1, 128))
    pb.G(lambda e: e.memset(ones1[:, :], 1.0))
    rbrow = pb.sb("rbrow", (1, 8))
    pb.dma(rbrow[:1, :], d_rb)
    psb = pb.ps("psb", (128, 128))
    rbb = pb.sb("rbb", (128, 8))
    pb.TE(lambda e: e.matmul(psb[:, :8], ones1[:1, :], rbrow[:1, :8], start=True, stop=True))
    pb.V(lambda e: e.tensor_copy(rbb[:, :8], psb[:, :8]))
    ebrow = pb.sb("ebrow", (8, D))
    pb.dma(ebrow[:8, :], d_eb)
    ebr0 = pb.sb("ebr0", (1, D), BF16)
    onesbf = pb.sb("onesbf", (1, 128), BF16)
    pb.G(lambda e: e.memset(onesbf[:, :], 1.0))

    accp = pb.ps("accp", (128, 512))
    lg = pb.ps("lg", (128, 8))
    lgt = pb.sb("lgt", (128, 8))
    mx = pb.sb("mx", (128, 1))
    w8 = pb.sb("w8", (128, 8))
    den = pb.sb("den", (128, 1))
    pmax = pb.sb("pmax", (128, 1))
    selw = pb.sb("selw", (128, 8))
    running = pb.sb("running", (128, 1))
    mcol = pb.sb("mcol2", (128, 1))
    ew = pb.sb("ew", (128, D), BF16)
    out_t = pb.sb("out_t", (128, D))
    for i in range(NT):
        tsl = slice(i * 128, (i + 1) * 128)
        for k in range(8):
            pb.TE(lambda e, k=k, tsl=tsl, s=(k == 0), p=(k == 7):
                  e.matmul(lg[:, :8], xnT[k][:, tsl], rwts[:, k * 8:(k + 1) * 8],
                           start=s, stop=p))
        pb.V(lambda e: e.tensor_tensor(lgt[:, :8], lg[:, :8], rbb[:, :8], OP.add))
        pb.V(lambda e: e.tensor_reduce(mx[:, :1], lgt[:, :8], axis=AX.X, op=OP.max))
        pb.V(lambda e: e.tensor_scalar(mx[:, :1], mx[:, :1], -1.0, None, OP.mult))
        pb.S(lambda e: e.activation(w8[:, :8], lgt[:, :8], AF.Exp, bias=mx[:, :1],
                                    accum_out=den[:, :1]))
        pb.V(lambda e: e.reciprocal(den[:, :1], den[:, :1]))
        pb.V(lambda e: e.tensor_scalar(w8[:, :8], w8[:, :8], den[:, :1], None, OP.mult))
        pb.V(lambda e: e.tensor_reduce(pmax[:, :1], w8[:, :8], axis=AX.X, op=OP.max))
        pb.G(lambda e: e.memset(running[:, :1], 0.0))
        for ex in range(8):
            pb.V(lambda e, ex=ex: e.tensor_scalar(mcol[:, :1], w8[:, ex:ex + 1],
                                                  pmax[:, :1], None, OP.is_ge))
            pb.V(lambda e, ex=ex: e.scalar_tensor_tensor(
                selw[:, ex:ex + 1], running[:, :1], 1.0, mcol[:, :1], OP.subtract, OP.mult))
            pb.V(lambda e, ex=ex: e.tensor_scalar(selw[:, ex:ex + 1], selw[:, ex:ex + 1],
                                                  -1.0, None, OP.mult))
            pb.V(lambda e, ex=ex: e.tensor_tensor(running[:, :1], running[:, :1],
                                                  selw[:, ex:ex + 1], OP.add))
            pb.V(lambda e, ex=ex: e.tensor_tensor(selw[:, ex:ex + 1], selw[:, ex:ex + 1],
                                                  pmax[:, :1], OP.mult))
        pb.G(lambda e: e.memset(out_t[:, :], 0.0))
        for ex in range(8):
            pb.op("gpsimd", lambda e, ex=ex: e.dma_start(
                out=ebr0[:1, :], in_=ebrow[ex:ex + 1, :]), 16)
            for half in range(2):
                nsl = slice(half * 512, (half + 1) * 512)
                for k in range(8):
                    pb.op("gpsimd", lambda e, ex=ex, k=k: e.dma_start(
                        out=ew[:, :], in_=d_ewt[ex, k * 128:(k + 1) * 128, :]), 16)
                    pb.TE(lambda e, k=k, tsl=tsl, nsl=nsl, s=(k == 0):
                          e.matmul(accp[:, :], xnT[k][:, tsl], ew[:, nsl],
                                   start=s, stop=False))
                pb.TE(lambda e, nsl=nsl: e.matmul(accp[:, :], onesbf[:1, :], ebr0[:1, nsl], start=False, stop=True))
                pb.V(lambda e, ex=ex, nsl=nsl: e.scalar_tensor_tensor(
                    out_t[:, nsl], accp[:, :], selw[:, ex:ex + 1], out_t[:, nsl],
                    OP.mult, OP.add))
        pb.dma(d_out[tsl, :], out_t[:, :], q="gpsimd")
    return pb



def prog_l2_ffn(upto=99):
    pb = PB("l2f")
    d_xnb = pb.din("xnb", [T, D], BF16)
    d_ctrl = pb.din("ctrl", [D, 64], F32)
    d_f1 = pb.din("f1", [D, 4096], F32)
    d_f2 = pb.din("f2", [4096, D], F32)
    d_biasT = pb.din("biasT", [4096, 1], F32)
    d_tie = pb.din("tiecol", [128, 1], F32)
    d_ident = pb.din("c_ident", [128, 128], F32)
    d_out = pb.dout("out", [T, D], F32)
    nc = pb.nc
    d_innerM = nc.dram_tensor("innerM", [4096, T], BF16, kind="Internal").ap()

    xnT = [pb.sb(f"xnT{k}", (128, T), BF16) for k in range(8)]
    for k in range(8):
        pb.dmaT(xnT[k][:, :], d_xnb[:, k * 128:(k + 1) * 128])
    ident = pb.sb("ident", (128, 128))
    pb.dma(ident[:, :], d_ident)
    tie = pb.sb("tie", (128, 1))
    pb.dma(tie[:, :], d_tie)
    ctrls = pb.sb("ctrls", (128, 8 * 64), BF16)
    for k in range(8):
        pb.op("gpsimd", lambda e, k=k: e.dma_start(
            out=ctrls[:, k * 64:(k + 1) * 64], in_=d_ctrl[k * 128:(k + 1) * 128, :]), 16)

    lgp = pb.ps("lgp", (128, 64))
    lgt = pb.sb("lgt", (128, 64))
    ptp = pb.ps("ptp", (128, 128))
    lgT = pb.sb("lgT", (64, 128))
    mx8 = pb.sb("mx8", (64, 8))
    permT = pb.sb("permT", (64, T))
    for i in range(NT):
        tsl = slice(i * 128, (i + 1) * 128)
        for k in range(8):
            pb.TE(lambda e, k=k, tsl=tsl, s=(k == 0), p=(k == 7):
                  e.matmul(lgp[:, :64], xnT[k][:, tsl], ctrls[:, k * 64:(k + 1) * 64],
                           start=s, stop=p))
        pb.V(lambda e: e.tensor_scalar(lgt[:, :64], lgp[:, :64], tie[:, :1], None, OP.add))
        pb.TE(lambda e: e.transpose(ptp[:64, :], lgt[:, :64], ident[:, :]))
        pb.V(lambda e: e.tensor_copy(lgT[:, :], ptp[:64, :128]))
        pb.V(lambda e: e.reduce_max(mx8[:, :8], lgT[:, :].rearrange("p (a b) -> p a b", a=8), axis=AX.X))
        pb.V(lambda e, tsl=tsl: e.tensor_tensor(
            permT[:, tsl].rearrange("p (a b) -> p a b", a=8),
            lgT[:, :].rearrange("p (a b) -> p a b", a=8),
            mx8[:, :8].broadcast_to([64, 8, 16]),
            OP.is_equal))

    if upto < 1:
        d_pt = pb.dout("permTdbg", [64, T], F32)
        pb.dma(d_pt, permT[:, :], q="gpsimd")
        return pb
    d_sel = pb.din("sel", [64, 4096], F32)
    sel64 = pb.sb("sel64", (64, 4096), BF16)
    pb.op("gpsimd", lambda e: e.dma_start(out=sel64[:, :], in_=d_sel), 16)
    permTb = pb.sb("permTb", (64, T), BF16)
    pb.V(lambda e: e.tensor_copy(permTb[:, :], permT[:, :]))
    f1t = pb.sb("f1t", (128, 4096), BF16)
    bcol = pb.sb("bcol", (128, 1))
    pin = pb.ps("pin", (128, 512))
    pbm = pb.ps("pbm", (128, 512))
    inn = pb.sb("inn", (128, T))
    pmb = pb.sb("pmb", (128, T))
    imb = pb.sb("imb", (128, T), BF16)
    for c in range(32):
        csl = slice(c * 128, (c + 1) * 128)
        for k in range(8):
            pb.op("gpsimd", lambda e, k=k, c=c: e.dma_start(
                out=f1t[:, k * 128:(k + 1) * 128],
                in_=d_f1[k * 128:(k + 1) * 128, c * 128:(c + 1) * 128]), 16)
        pb.dma(bcol[:, :], d_biasT[csl, :])
        for th in range(4):
            tfl = slice(th * 512, (th + 1) * 512)
            for k in range(8):
                pb.TE(lambda e, k=k, tfl=tfl, s=(k == 0), p=(k == 7):
                      e.matmul(pin[:, :], f1t[:, k * 128:(k + 1) * 128], xnT[k][:, tfl],
                               start=s, stop=p))
            pb.S(lambda e, tfl=tfl: e.activation(inn[:, tfl], pin[:, :], AF.Relu,
                                                 bias=bcol[:, :1]))
            pb.TE(lambda e, tfl=tfl, csl=csl: e.matmul(
                pbm[:, :], sel64[:64, csl], permTb[:64, tfl], start=True, stop=True))
            pb.V(lambda e, tfl=tfl: e.tensor_copy(pmb[:, tfl], pbm[:, :]))
        pb.V(lambda e: e.tensor_tensor(imb[:, :], inn[:, :], pmb[:, :], OP.mult))
        pb.dma(d_innerM[csl, :], imb[:, :], q="gpsimd")

    if upto < 2:
        d_zz = pb.dout("zz2", [1, 1], F32)
        pb.dma(d_zz, permT[:1, :1], q="gpsimd")
        return pb
    f2t = pb.sb("f2t", (128, D), BF16)
    po = [pb.ps(f"po{h}", (128, 512)) for h in range(2)]
    ot = pb.sb("ot2", (128, D))
    imt = pb.sb("imt", (128, 128), BF16)
    for i in range(NT if upto >= 3 else 1):
        tsl = slice(i * 128, (i + 1) * 128)
        for c in range(32):
            pb.op("gpsimd", lambda e, c=c: e.dma_start(
                out=f2t[:, :], in_=d_f2[c * 128:(c + 1) * 128, :]), 16)
            pb.dma(imt[:, :], d_innerM[c * 128:(c + 1) * 128, tsl])
            for h in range(2):
                pb.TE(lambda e, c=c, h=h, s=(c == 0), p=(c == 31):
                      e.matmul(po[h][:, :], imt[:, :], f2t[:, h * 512:(h + 1) * 512],
                               start=s, stop=p))
        for h in range(2):
            pb.V(lambda e, h=h: e.tensor_copy(ot[:, h * 512:(h + 1) * 512], po[h][:, :]))
        pb.dma(d_out[tsl, :], ot[:, :], q="gpsimd")
    return pb


def prog_l2_attn(dbg=False):
    pb = PB("l2a")
    d_xn = pb.din("xn", [T, D], F32)
    d_wqkvt = pb.din("wqkvt", [D, 3072], F32)
    d_wo_t = pb.din("wot", [D, D], F32)
    d_g = pb.din("gvec", [1, D], F32)
    d_cos = pb.din("cosb", [T, D], F32)
    d_sin = pb.din("sinb", [T, D], F32)
    d_mask = pb.din("maskc", [512, 512], F32)
    d_pmk = pb.din("pmkT", [128, 256], F32)
    d_pmv = pb.din("pmvT", [16, 1024], F32)
    d_out = pb.dout("out", [T, D], F32)
    nc = pb.nc
    ik = "ExternalOutput" if dbg else "Internal"
    d_x2b = nc.dram_tensor("x2b", [T, D], BF16, kind=ik).ap()
    d_q = nc.dram_tensor("qro", [T, D], BF16, kind=ik).ap()
    d_k = nc.dram_tensor("kro", [T, D], BF16, kind=ik).ap()
    d_v = nc.dram_tensor("vt", [T, D], BF16, kind=ik).ap()
    d_ao = nc.dram_tensor("ao", [T, D], BF16, kind=ik).ap()
    d_sc0 = nc.dram_tensor("sc0", [128, 528], F32, kind=ik).ap() if dbg else None
    d_pr0 = nc.dram_tensor("pr0", [128, 528], BF16, kind=ik).ap() if dbg else None

    gb = pb.sb("gb", (1, D))
    pb.dma(gb[:1, :], d_g)
    ones1 = pb.sb("ones1", (1, 128))
    pb.G(lambda e: e.memset(ones1[:, :], 1.0))
    gfull = pb.sb("gfull", (128, D))
    psb = pb.ps("psb", (128, 512))
    for j in range(2):
        pb.TE(lambda e, j=j: e.matmul(psb[:, :], ones1[:1, :], gb[:1, j * 512:(j + 1) * 512],
                                      start=True, stop=True))
        pb.V(lambda e, j=j: e.tensor_copy(gfull[:, j * 512:(j + 1) * 512], psb[:, :]))
    xt = pb.sb("xt", (128, D))
    sq = pb.sb("sq", (128, D))
    ms = pb.sb("ms", (128, 1))
    x2 = pb.sb("x2", (128, D))
    x2bt = pb.sb("x2bt", (128, D), BF16)
    for i in range(NT):
        tsl = slice(i * 128, (i + 1) * 128)
        pb.dma(xt[:, :], d_xn[tsl, :])
        pb.S(lambda e: e.activation(sq[:, :], xt[:, :], AF.Square))
        pb.V(lambda e: e.reduce_sum(ms[:, :1], sq[:, :], axis=AX.X))
        pb.V(lambda e: e.tensor_scalar(ms[:, :1], ms[:, :1], 1.0 / D, 1.1920929e-07, OP.mult, OP.add))
        pb.S(lambda e: e.activation(ms[:, :1], ms[:, :1], AF.Ln))
        pb.S(lambda e: e.activation(ms[:, :1], ms[:, :1], AF.Exp, scale=-0.5))
        pb.V(lambda e: e.tensor_scalar(x2[:, :], xt[:, :], ms[:, :1], None, OP.mult))
        pb.V(lambda e: e.tensor_tensor(x2[:, :], x2[:, :], gfull[:, :], OP.mult))
        pb.V(lambda e: e.tensor_copy(x2bt[:, :], x2[:, :]))
        pb.dma(d_x2b[tsl, :], x2bt[:, :], q="gpsimd")
    x2T = [pb.sb(f"x2T{k}", (128, T), BF16) for k in range(8)]
    for k in range(8):
        pb.dmaT(x2T[k][:, :], d_x2b[:, k * 128:(k + 1) * 128])
    wq = pb.sb("wq", (128, 512), BF16)
    pq = pb.ps("pq", (128, 512))
    qkv = pb.sb("qkv", (128, 3072))
    cost = pb.sb("cost", (128, D))
    sint = pb.sb("sint", (128, D))
    rot = pb.sb("rot", (128, D))
    ro = pb.sb("ro", (128, D))
    rob = pb.sb("rob", (128, D), BF16)
    for i in range(NT):
        tsl = slice(i * 128, (i + 1) * 128)
        for nch in range(6):
            for k in range(8):
                pb.op("gpsimd", lambda e, k=k, nch=nch: e.dma_start(
                    out=wq[:, :], in_=d_wqkvt[k * 128:(k + 1) * 128,
                                              nch * 512:(nch + 1) * 512]), 16)
                pb.TE(lambda e, k=k, tsl=tsl, s=(k == 0), p=(k == 7):
                      e.matmul(pq[:, :], x2T[k][:, tsl], wq[:, :], start=s, stop=p))
            pb.V(lambda e, nch=nch: e.tensor_copy(qkv[:, nch * 512:(nch + 1) * 512], pq[:, :]))
        pb.dma(cost[:, :], d_cos[tsl, :])
        pb.dma(sint[:, :], d_sin[tsl, :])
        for part, dst in ((0, d_q), (1, d_k)):
            qsl = slice(part * D, (part + 1) * D)
            pb.V(lambda e, qsl=qsl: e.tensor_copy(
                rot[:, 0:D:2], qkv[:, qsl][:, 1:D:2]))
            pb.V(lambda e, qsl=qsl: e.tensor_copy(
                rot[:, 1:D:2], qkv[:, qsl][:, 0:D:2]))
            pb.V(lambda e, qsl=qsl: e.tensor_tensor(ro[:, :], qkv[:, qsl], cost[:, :], OP.mult))
            pb.V(lambda e: e.tensor_tensor(rot[:, :], rot[:, :], sint[:, :], OP.mult))
            pb.V(lambda e: e.tensor_tensor(ro[:, :], ro[:, :], rot[:, :], OP.add))
            pb.V(lambda e: e.tensor_copy(rob[:, :], ro[:, :]))
            pb.dma(dst[tsl, :], rob[:, :], q="gpsimd")
        pb.V(lambda e: e.tensor_copy(rob[:, :], qkv[:, 2 * D:3 * D]))
        pb.dma(d_v[tsl, :], rob[:, :], q="gpsimd")


    pmk = pb.sb("pmk", (128, 256), BF16)
    pb.op("gpsimd", lambda e: e.dma_start(out=pmk[:, :], in_=d_pmk), 16)
    pmv = pb.sb("pmv", (16, 1024), BF16)
    pb.op("gpsimd", lambda e: e.dma_start(out=pmv[:, :], in_=d_pmv), 16)
    idb = pb.sb("idb", (128, 128), BF16)
    pb.G(lambda e: e.memset(idb[:, :], 0.0))
    pb.V(lambda e: e.iota(idb[:, :], None)) if False else None
    d_ident = pb.din("c_ident", [128, 128], F32)
    identf = pb.sb("identf", (128, 128))
    pb.dma(identf[:, :], d_ident)
    pb.V(lambda e: e.tensor_copy(idb[:, :], identf[:, :]))
    qT2 = pb.sb("qT2", (128, T), BF16)
    kT2 = pb.sb("kT2", (128, T), BF16)
    vseg = [pb.sb(f"vseg{j}", (128, D), BF16) for j in range(4)]
    mskt = pb.sb("mskt", (128, 512))
    ps_k = pb.ps("ps_k", (128, 512))
    ps_pm = pb.ps("ps_pm", (128, 16))
    sct = pb.sb("sct", (128, 528))
    mxc = pb.sb("mxc", (128, 1))
    den = pb.sb("den", (128, 1))
    prb = pb.sb("prb", (128, 528), BF16)
    ptb = pb.ps("ptb", (128, 128), BF16)
    lhsT = pb.sb("lhsTt", (128, 128 * 5), BF16)
    pov = pb.ps("pov", (128, 64))
    oht = pb.sb("oht", (128, D))
    for seg in range(4):
        ssl = slice(seg * 512, (seg + 1) * 512)
        for j in range(4):
            pb.dma(vseg[j][:, :], d_v[seg * 512 + j * 128:seg * 512 + (j + 1) * 128, :])
        for hp in range(8):
            pb.dmaT(qT2[:, :], d_q[:, hp * 128:(hp + 1) * 128])
            pb.dmaT(kT2[:, :], d_k[:, hp * 128:(hp + 1) * 128])
            for hh in range(2):
                h = hp * 2 + hh
                qsl = slice(hh * 64, (hh + 1) * 64)
                for ti in range(4):
                    trow = seg * 512 + ti * 128
                    pb.TE(lambda e, qsl=qsl, trow=trow, ssl=ssl:
                          e.matmul(ps_k[:, :], qT2[qsl, trow:trow + 128],
                                   kT2[qsl, ssl], start=True, stop=True))
                    pb.TE(lambda e, qsl=qsl, trow=trow, h=h:
                          e.matmul(ps_pm[:, :], qT2[qsl, trow:trow + 128],
                                   pmk[qsl, h * 16:(h + 1) * 16], start=True, stop=True))
                    pb.dma(mskt[:, :], d_mask[ti * 128:(ti + 1) * 128, :])
                    pb.V(lambda e: e.scalar_tensor_tensor(
                        sct[:, 0:512], ps_k[:, :], 0.125, mskt[:, :], OP.mult, OP.add))
                    pb.V(lambda e: e.tensor_scalar(sct[:, 512:528], ps_pm[:, :], 0.125, None, OP.mult))
                    if dbg and seg == 0 and h == 0 and ti == 0:
                        pb.dma(d_sc0, sct[:, :528], q="gpsimd")
                    pb.V(lambda e: e.reduce_max(mxc[:, :1], sct[:, :528], axis=AX.X))
                    pb.V(lambda e: e.tensor_scalar(mxc[:, :1], mxc[:, :1], -1.0, None, OP.mult))
                    pb.S(lambda e: e.activation(prb[:, :528], sct[:, :528], AF.Exp,
                                                bias=mxc[:, :1], accum_out=den[:, :1]))
                    pb.V(lambda e: e.reciprocal(den[:, :1], den[:, :1]))
                    if dbg and seg == 0 and h == 0 and ti == 0:
                        pb.dma(d_pr0, prb[:, :528], q="gpsimd")
                    for c in range(4):
                        pb.TE(lambda e, c=c: e.transpose(
                            ptb[:, :], prb[:, c * 128:(c + 1) * 128], idb[:, :]))
                        pb.V(lambda e, c=c: e.tensor_copy(
                            lhsT[:, c * 128:(c + 1) * 128], ptb[:, :]))
                    pb.TE(lambda e: e.transpose(ptb[:16, :], prb[:, 512:528], idb[:, :]))
                    pb.V(lambda e: e.tensor_copy(lhsT[:16, 512:640], ptb[:16, :128]))
                    for c in range(4):
                        pb.TE(lambda e, c=c, h=h, s=(c == 0):
                              e.matmul(pov[:, :], lhsT[:, c * 128:(c + 1) * 128],
                                       vseg[c][:, h * 64:(h + 1) * 64], start=s, stop=False))
                    pb.TE(lambda e, h=h: e.matmul(pov[:, :], lhsT[:16, 512:640],
                                                  pmv[:16, h * 64:(h + 1) * 64],
                                                  start=False, stop=True))
                    pb.V(lambda e, h=h: e.tensor_scalar(
                        oht[:, h * 64:(h + 1) * 64], pov[:, :], den[:, :1], None, OP.mult))
                    pb.V(lambda e, h=h: e.tensor_copy(rob[:, h * 64:(h + 1) * 64],
                                                      oht[:, h * 64:(h + 1) * 64]))
                    pb.op("gpsimd", lambda e, h=h, trow=trow: e.dma_start(
                        out=d_ao[trow:trow + 128, h * 64:(h + 1) * 64],
                        in_=rob[:, h * 64:(h + 1) * 64]), 16)
    aoT = [pb.sb(f"aoT{k}", (128, T), BF16) for k in range(8)]
    for k in range(8):
        pb.dmaT(aoT[k][:, :], d_ao[:, k * 128:(k + 1) * 128])
    wo = pb.sb("wo", (128, D), BF16)
    ot = pb.sb("otl2", (128, D))
    po = pb.ps("po_l2", (128, 512))
    for i in range(NT):
        tsl = slice(i * 128, (i + 1) * 128)
        for half in range(2):
            nsl = slice(half * 512, (half + 1) * 512)
            for k in range(8):
                pb.op("gpsimd", lambda e, k=k: e.dma_start(
                    out=wo[:, :], in_=d_wo_t[k * 128:(k + 1) * 128, :]), 16)
                pb.TE(lambda e, k=k, tsl=tsl, nsl=nsl, s=(k == 0), p=(k == 7):
                      e.matmul(po[:, :], aoT[k][:, tsl], wo[:, nsl], start=s, stop=p))
            pb.V(lambda e, nsl=nsl: e.tensor_copy(ot[:, nsl], po[:, :]))
        pb.dma(d_out[tsl, :], ot[:, :], q="gpsimd")
    return pb


def prog_l1_attn(dbg=False):
    pb = PB("l1a")
    d_xn = pb.din("xn", [T, D], F32)
    d_win = pb.din("win", [D, 3328], F32)
    d_gb = pb.din("gbias", [D, 1], F32)
    d_rmsw = pb.din("rmsw", [1, D], F32)
    d_wout = pb.din("wout", [D, D], F32)
    d_mask = pb.din("l1mask", [16 * 128, 32], F32)
    d_ident = pb.din("c_ident", [128, 128], F32)
    d_out = pb.dout("out", [T, D], F32)
    nc = pb.nc
    ik = "ExternalOutput" if dbg else "Internal"
    d_n1b = nc.dram_tensor("n1b", [T, D], BF16, kind=ik).ap()
    d_z = nc.dram_tensor("zsig", [D, T], BF16, kind=ik).ap()
    d_xkx = nc.dram_tensor("xkx", [D, T], F32, kind=ik).ap() if dbg else None
    d_aoT = nc.dram_tensor("aoT", [D, T], BF16, kind=ik).ap()

    xt2 = [pb.sb(f"l1x{j}", (128, T), BF16) for j in range(8)]
    gt = [pb.sb(f"l1g{j}", (128, T), BF16) for j in range(8)]
    om = [pb.sb(f"l1om{j}", (128, T), BF16) for j in range(8)]
    qk0 = pb.sb("l1q", (128, T), BF16)
    kk0 = pb.sb("l1k", (128, T), BF16)
    ident = pb.sb("identl1", (128, 128))
    pb.dma(ident[:, :], d_ident)
    idb = pb.sb("idbl1", (128, 128), BF16)
    pb.V(lambda e: e.tensor_copy(idb[:, :], ident[:, :]))
    ones1 = pb.sb("ones1l1", (1, 128))
    pb.G(lambda e: e.memset(ones1[:, :], 1.0))
    onesb = pb.sb("onesbl1", (1, 128), BF16)
    pb.G(lambda e: e.memset(onesb[:, :], 1.0))
    onecol = pb.sb("onecol", (64, 1), BF16)
    pb.G(lambda e: e.memset(onecol[:, :], 1.0))

    rw = pb.sb("rwl1", (1, D))
    pb.dma(rw[:1, :], d_rmsw)
    rwf = pb.sb("rwfl1", (128, D))
    psb = pb.ps("psbl1", (128, 512))
    for j in range(2):
        pb.TE(lambda e, j=j: e.matmul(psb[:, :], ones1[:1, :], rw[:1, j * 512:(j + 1) * 512],
                                      start=True, stop=True))
        pb.V(lambda e, j=j: e.tensor_copy(rwf[:, j * 512:(j + 1) * 512], psb[:, :]))
    xt = pb.sb("xtl1", (128, D))
    sq = pb.sb("sql1", (128, D))
    ms = pb.sb("msl1", (128, 1))
    n1 = pb.sb("n1l1", (128, D))
    n1b = pb.sb("n1bl1", (128, D), BF16)
    for i in range(NT):
        tsl = slice(i * 128, (i + 1) * 128)
        pb.dma(xt[:, :], d_xn[tsl, :])
        pb.S(lambda e: e.activation(sq[:, :], xt[:, :], AF.Square))
        pb.V(lambda e: e.reduce_sum(ms[:, :1], sq[:, :], axis=AX.X))
        pb.V(lambda e: e.tensor_scalar(ms[:, :1], ms[:, :1], 1.0 / D, 1e-6, OP.mult, OP.add))
        pb.S(lambda e: e.activation(ms[:, :1], ms[:, :1], AF.Ln))
        pb.S(lambda e: e.activation(ms[:, :1], ms[:, :1], AF.Exp, scale=-0.5))
        pb.V(lambda e: e.tensor_scalar(n1[:, :], xt[:, :], ms[:, :1], None, OP.mult))
        pb.V(lambda e: e.tensor_tensor(n1[:, :], n1[:, :], rwf[:, :], OP.mult))
        pb.V(lambda e: e.tensor_copy(n1b[:, :], n1[:, :]))
        pb.dma(d_n1b[tsl, :], n1b[:, :], q="gpsimd")

    wt = pb.sb("wtl1", (128, 128), BF16)
    gbc = pb.sb("gbc", (128, 1))
    pp = pb.ps("ppl1", (128, 512))
    ztile = pb.sb("ztileb", (128, T), BF16)
    n1T = [pb.sb(f"n1T{k}", (128, T), BF16) for k in range(8)]
    for k in range(8):
        pb.dmaT(n1T[k][:, :], d_n1b[:, k * 128:(k + 1) * 128])
    for c in range(26):
        if 16 <= c < 24:
            pb.dma(gbc[:, :], d_gb[(c - 16) * 128:(c - 15) * 128, :])
        for tc in range(4):
            tfl = slice(tc * 512, (tc + 1) * 512)
            for k in range(8):
                pb.op("gpsimd", lambda e, k=k, c=c: e.dma_start(
                    out=wt[:, :], in_=d_win[k * 128:(k + 1) * 128, c * 128:(c + 1) * 128]), 16)
                pb.TE(lambda e, k=k, tfl=tfl, s=(k == 0), p=(k == 7):
                      e.matmul(pp[:, :], wt[:, :], n1T[k][:, tfl], start=s, stop=p))
            if c < 8:
                pb.S(lambda e, tfl=tfl: e.activation(ztile[:, tfl], pp[:, :], AF.Sigmoid))
            elif c < 16:
                pb.V(lambda e, c=c, tfl=tfl: e.tensor_copy(xt2[c - 8][:, tfl], pp[:, :]))
            elif c < 24:
                pb.S(lambda e, c=c, tfl=tfl: e.activation(gt[c - 16][:, tfl], pp[:, :],
                                                          AF.Sigmoid, bias=gbc[:, :1]))
                pb.V(lambda e, c=c, tfl=tfl: e.tensor_scalar(
                    om[c - 16][:, tfl], gt[c - 16][:, tfl], -1.0, 1.0, OP.mult, OP.add))
            elif c == 24:
                pb.V(lambda e, tfl=tfl: e.tensor_copy(qk0[:, tfl], pp[:, :]))
            else:
                pb.V(lambda e, tfl=tfl: e.tensor_copy(kk0[:, tfl], pp[:, :]))
        if c < 8:
            pb.dma(d_z[c * 128:(c + 1) * 128, :], ztile[:, :], q="gpsimd")
    pb.sbfree(*n1T)

    at = pb.sb("atl1", (128, T), BF16)
    vt = pb.sb("vtl1", (128, T), BF16)
    sco = pb.sb("scol1", (128, T))
    xkx, xkk, aot = xt2, om, gt
    for j in range(8):
        pb.V(lambda e, j=j: e.tensor_copy(at[:, :], gt[j][:, :]))
        pb.G(lambda e: e.memset(at[:, 0:T:64], 0.0))
        pb.V(lambda e, j=j: e.tensor_tensor(vt[:, :], om[j][:, :], xt2[j][:, :], OP.mult))
        pb.V(lambda e: e.tensor_tensor_scan(sco[:, :], at[:, :], vt[:, :], 0.0, OP.mult, OP.add))
        pb.V(lambda e, j=j: e.tensor_copy(xkx[j][:, :], sco[:, :]))
        pb.V(lambda e, j=j: e.tensor_tensor(vt[:, :], om[j][:, :], kk0[:, :], OP.mult))
        pb.V(lambda e: e.tensor_tensor_scan(sco[:, :], at[:, :], vt[:, :], 0.0, OP.mult, OP.add))
        pb.V(lambda e, j=j: e.tensor_copy(xkk[j][:, :], sco[:, :]))
        if dbg and j == 0:
            pb.dma(d_xkx[0:128, :], xkx[0][:, :], q="gpsimd")
    chk = [pb.sb(f"chk{j}", (128, 32), BF16) for j in range(8)]
    chx = [pb.sb(f"chx{j}", (128, 32), BF16) for j in range(8)]
    for j in range(8):
        pb.V(lambda e, j=j: e.tensor_copy(chk[j][:, :], xkk[j][:, 0:T:64]))
        pb.V(lambda e, j=j: e.tensor_copy(chx[j][:, :], xkx[j][:, 0:T:64]))
    ptc = pb.ps("ptcl1", (128, 64), BF16)
    chxT = [pb.sb(f"chxT{h}", (32, 64), BF16) for h in range(16)]
    chxh = pb.sb("chxh", (64, 32), BF16)
    for h in range(16):
        pb.V(lambda e, h=h: e.tensor_copy(chxh[:, :], chx[h // 2][(h % 2) * 64:(h % 2) * 64 + 64, :]))
        pb.TE(lambda e: e.transpose(ptc[:32, :64], chxh[:, :], idb[:64, :64]))
        pb.V(lambda e, h=h: e.tensor_copy(chxT[h][:, :], ptc[:32, :64]))

    msk = pb.sb("mskl1", (128, 32))
    pgenA = pb.ps("pgenA", (128, 128))
    psc = pgenA[:, 0:32]
    plse = pgenA[:, 32:33]
    prodb = pb.sb("prodb", (64, 128), BF16)
    sct = pb.sb("sctl1", (128, 32))
    lse = pb.sb("lsel1", (128, 1))
    mx = pb.sb("mxl1", (128, 1))
    wst = pb.sb("wstl1", (128, 32), BF16)
    iw = pb.sb("iwl1", (128, 1))
    den = pb.sb("denl1", (128, 1))
    pwT = pb.ps("pwTl1", (128, 128), BF16)
    wT = pb.sb("wTl1", (32, 128), BF16)
    pot = pb.ps("potl1", (64, 128))
    piw = pb.ps("piwl1", (64, 128))
    o1 = pb.sb("o1l1", (64, 128))
    iwTp = pgenA
    iwT = pb.sb("iwTl1", (1, 128))
    denT = pb.sb("denTl1", (1, 128))
    for h in range(16):
        j = h // 2
        rsl = slice((h % 2) * 64, (h % 2) * 64 + 64)
        for i in range(NT):
            tsl = slice(i * 128, (i + 1) * 128)
            pb.TE(lambda e, j=j, rsl=rsl, tsl=tsl: e.matmul(
                psc, qk0[rsl, tsl], chk[j][rsl, :], start=True, stop=True))
            pb.dma(msk[:, :], d_mask[i * 128:(i + 1) * 128, :])
            pb.V(lambda e: e.scalar_tensor_tensor(sct[:, :], psc, 0.125,
                                                  msk[:, :], OP.mult, OP.add))
            pb.V(lambda e, j=j, rsl=rsl, tsl=tsl: e.tensor_tensor(
                prodb[:, :], qk0[rsl, tsl], xkk[j][rsl, tsl], OP.mult))
            pb.TE(lambda e: e.matmul(plse, prodb[:, :], onecol[:, :1],
                                     start=True, stop=True))
            pb.V(lambda e: e.tensor_scalar(lse[:, :1], plse, 0.125, None, OP.mult))
            pb.V(lambda e: e.reduce_max(mx[:, :1], sct[:, :], axis=AX.X))
            pb.V(lambda e: e.tensor_tensor(mx[:, :1], mx[:, :1], lse[:, :1], OP.max))
            pb.V(lambda e: e.tensor_scalar(mx[:, :1], mx[:, :1], -1.0, None, OP.mult))
            pb.S(lambda e: e.activation(wst[:, :], sct[:, :], AF.Exp, bias=mx[:, :1],
                                        accum_out=den[:, :1]))
            pb.S(lambda e: e.activation(iw[:, :1], lse[:, :1], AF.Exp, bias=mx[:, :1]))
            pb.V(lambda e: e.tensor_tensor(den[:, :1], den[:, :1], iw[:, :1], OP.add))
            pb.TE(lambda e: e.transpose(pwT[:32, :], wst[:, :], idb[:, :]))
            pb.V(lambda e: e.tensor_copy(wT[:, :], pwT[:32, :128]))
            pb.TE(lambda e, h=h: e.matmul(pot[:, :], chxT[h][:, :], wT[:, :],
                                          start=True, stop=True))
            pb.V(lambda e: e.tensor_copy(o1[:, :], pot[:, :]))
            pb.TE(lambda e: e.transpose(iwTp[:1, :], iw[:, :1], ident[:, :]))
            pb.V(lambda e: e.tensor_copy(iwT[:1, :], iwTp[:1, :128]))
            pb.V(lambda e: e.reciprocal(denT[:1, :] if False else den[:, :1], den[:, :1])) if False else None
            pb.TE(lambda e: e.matmul(piw[:, :], ones1[:1, :64], iwT[:1, :],
                                     start=True, stop=True))
            pb.V(lambda e, j=j, rsl=rsl, tsl=tsl: e.scalar_tensor_tensor(
                o1[:, :], piw[:, :], 1.0, xkx[j][rsl, tsl], OP.mult, OP.mult))
            pb.V(lambda e: e.tensor_tensor(o1[:, :], pot[:, :], o1[:, :], OP.add))
            pb.V(lambda e: e.reciprocal(den[:, :1], den[:, :1]))
            pb.TE(lambda e: e.transpose(iwTp[:1, :], den[:, :1], ident[:, :]))
            pb.V(lambda e: e.tensor_copy(denT[:1, :], iwTp[:1, :128]))
            pb.TE(lambda e: e.matmul(piw[:, :], ones1[:1, :64], denT[:1, :],
                                     start=True, stop=True))
            pb.V(lambda e: e.scalar_tensor_tensor(o1[:, :], piw[:, :], 1.0, o1[:, :],
                                                  OP.mult, OP.mult))
            pb.V(lambda e, j=j, rsl=rsl, tsl=tsl: e.tensor_copy(aot[j][rsl, tsl], o1[:, :]))
    for j in range(8):
        pb.op("gpsimd", lambda e, j=j: e.dma_start(out=ztile[:, :], in_=d_z[j * 128:(j + 1) * 128, :]), 16)
        pb.V(lambda e, j=j: e.tensor_tensor(aot[j][:, :], aot[j][:, :], ztile[:, :], OP.mult))
        pb.dma(d_aoT[j * 128:(j + 1) * 128, :], aot[j][:, :], q="gpsimd")
    wo = pb.sb("wol1", (128, D), BF16)
    po = pb.ps("pol1", (128, 512))
    ot = pb.sb("otl1", (128, D))
    for i in range(NT):
        tsl = slice(i * 128, (i + 1) * 128)
        for half in range(2):
            nsl = slice(half * 512, (half + 1) * 512)
            for k in range(8):
                pb.op("gpsimd", lambda e, k=k: e.dma_start(
                    out=wo[:, :], in_=d_wout[k * 128:(k + 1) * 128, :]), 16)
                pb.TE(lambda e, k=k, tsl=tsl, nsl=nsl, s=(k == 0), p=(k == 7):
                      e.matmul(po[:, :], aot[k][:, tsl], wo[:, nsl], start=s, stop=p))
            pb.V(lambda e, nsl=nsl: e.tensor_copy(ot[:, nsl], po[:, :]))
        pb.dma(xt[:, :], d_xn[tsl, :])
        pb.V(lambda e: e.tensor_tensor(ot[:, :], ot[:, :], xt[:, :], OP.add))
        pb.dma(d_out[tsl, :], ot[:, :], q="gpsimd")
    return pb


def e_global_scalar_max(pb, dstcol, srctile, F, scr, ident, ones1, pgen):
    pb.V(lambda e: e.reduce_max(scr, srctile, axis=AX.X))
    pb.TE(lambda e: e.transpose(pgen[:1, :], scr, ident))
    pb.V(lambda e: e.tensor_copy(dstcol, pgen[:1, 0:1])) if False else None
    pb.V(lambda e: e.reduce_max(pgen[:1, 0:1] if False else scr[:1, :1], pgen[:1, :128].rearrange("p f -> p f"), axis=AX.X))
    pb.TE(lambda e: e.matmul(pgen[:, 0:1], ones1[:1, :], scr[:1, :1], start=True, stop=True))
    pb.V(lambda e: e.tensor_copy(dstcol, pgen[:, 0:1]))


def prog_l3_attn(ncore=NCORE, dbg=False):
    pb = PB("l3a")
    d_xn = pb.din("xn", [T, D], F32)
    d_wt = {w: pb.din(f"w{w}t", [D, D], F32) for w in "qkvo"}
    d_ident = pb.din("c_ident", [128, 128], F32)
    d_out = pb.dout("out", [T, D], F32)
    nc = pb.nc
    ik = "ExternalOutput" if dbg else "Internal"
    d_qx = nc.dram_tensor("qx", [T, D], BF16, kind=ik).ap()
    d_qwt = {w: nc.dram_tensor(f"qw{w}", [D, D], BF16, kind="Internal").ap() for w in "qkvo"}
    d_y = {w: nc.dram_tensor(f"y{w}", [T, D], F32, kind=ik).ap() for w in "qkv"}
    d_q3 = {w: nc.dram_tensor(f"q3{w}", [T, D], BF16, kind=ik).ap() for w in "qkv"}
    d_sc = nc.dram_tensor("scl3", [16 * T, T], F32, kind="Internal").ap()
    d_w6 = nc.dram_tensor("w6l3", [16 * T, T], F32, kind="Internal").ap()
    d_q6 = nc.dram_tensor("q6l3", [16 * T, T], BF16, kind="Internal").ap()
    d_yo = nc.dram_tensor("yol3", [T, D], F32, kind=ik).ap()
    d_at = nc.dram_tensor("atl3", [T, D], BF16, kind="Internal").ap()
    d_yf = nc.dram_tensor("yfl3", [T, D], F32, kind="Internal").ap()
    cc_in = nc.dram_tensor("cc_in", [128, 8], F32, kind="Internal")
    cc_out = nc.dram_tensor("cc_out", [128, 8], F32, kind="Internal")

    ident = pb.sb("identl3", (128, 128))
    pb.dma(ident[:, :], d_ident)
    ones1 = pb.sb("ones1l3", (1, 128))
    pb.G(lambda e: e.memset(ones1[:, :], 1.0))
    pgen = pb.ps("pgenl3", (128, 128))
    scr1 = pb.sb("scr1l3", (128, 1))
    stats = pb.sb("stats", (128, 8))
    pb.G(lambda e: e.memset(stats[:, :], 0.0))
    xt = pb.sb("xtl3", (128, D))
    ab = pb.sb("abl3", (128, 2 * D))
    itl = pb.sb("itl3", (128, D), I32)
    vb = pb.sb("vbl3", (128, D))
    ob = pb.sb("obl3", (128, D), BF16)
    run = pb.sb("runl3", (128, 1))

    def allreduce(cols, op=OP.max):
        pb.dma(cc_in.ap(), stats[:, :8], q="gpsimd")
        pb.op("gpsimd", lambda e: e.collective_compute(
            "AllReduce", op, replica_groups=[list(range(ncore))],
            ins=[cc_in.ap().opt()], outs=[cc_out.ap().opt()]), 1, kind="x")
        pb.op("gpsimd", lambda e: e.dma_start(out=stats[:, :8], in_=cc_out.ap()), 16)

    def fqscale(dst, src):
        pb.V(lambda e: e.tensor_scalar(dst, src, 1.0 / 127.0, 1e-8, OP.mult, OP.max))

    swcol = pb.sb("swcol", (128, 4))
    winv = pb.sb("winv", (128, 1))
    for wi, w in enumerate("qkvo"):
        pb.G(lambda e: e.memset(run[:, :1], 0.0))
        for i in range(8):
            pb.dma(xt[:, :], d_wt[w][i * 128:(i + 1) * 128, :])
            pb.S(lambda e: e.activation(ab[:, :D], xt[:, :], AF.Abs))
            pb.V(lambda e: e.reduce_max(scr1[:, :1], ab[:, :D], axis=AX.X))
            pb.V(lambda e: e.tensor_tensor(run[:, :1], run[:, :1], scr1[:, :1], OP.max))
        pb.TE(lambda e: e.transpose(pgen[:1, :], run[:, :1], ident[:, :]))
        pb.V(lambda e: e.tensor_copy(ab[:1, :128], pgen[:1, :128]))
        pb.V(lambda e: e.reduce_max(scr1[:1, :1], ab[:1, :128], axis=AX.X))
        pb.TE(lambda e: e.matmul(pgen[:, 0:1], ones1[:1, :], scr1[:1, :1], start=True, stop=True))
        pb.V(lambda e, wi=wi: e.tensor_scalar(swcol[:, wi:wi + 1], pgen[:, 0:1],
                                              1.0 / 127.0, 1e-8, OP.mult, OP.max))
        pb.V(lambda e, wi=wi: e.reciprocal(winv[:, :1], swcol[:, wi:wi + 1]))
        for i in range(8):
            pb.dma(xt[:, :], d_wt[w][i * 128:(i + 1) * 128, :])
            pb.V(lambda e: e.tensor_scalar(vb[:, :], xt[:, :], winv[:, :1], None, OP.mult))
            pb.V(lambda e: e.tensor_copy(itl[:, :], vb[:, :]))
            pb.V(lambda e: e.tensor_copy(ob[:, :], itl[:, :]))
            pb.dma(d_qwt[w][i * 128:(i + 1) * 128, :], ob[:, :], q="gpsimd")

    pb.G(lambda e: e.memset(run[:, :1], 0.0))
    for i in range(NT):
        pb.dma(xt[:, :], d_xn[i * 128:(i + 1) * 128, :])
        pb.S(lambda e: e.activation(ab[:, :D], xt[:, :], AF.Abs))
        pb.V(lambda e: e.reduce_max(scr1[:, :1], ab[:, :D], axis=AX.X))
        pb.V(lambda e: e.tensor_tensor(run[:, :1], run[:, :1], scr1[:, :1], OP.max))
    pb.TE(lambda e: e.transpose(pgen[:1, :], run[:, :1], ident[:, :]))
    pb.V(lambda e: e.tensor_copy(ab[:1, :128], pgen[:1, :128]))
    pb.V(lambda e: e.reduce_max(scr1[:1, :1], ab[:1, :128], axis=AX.X))
    pb.TE(lambda e: e.matmul(pgen[:, 0:1], ones1[:1, :], scr1[:1, :1], start=True, stop=True))
    pb.V(lambda e: e.tensor_copy(stats[:, 0:1], pgen[:, 0:1]))
    allreduce(1)
    sx = pb.sb("sxl3", (128, 1))
    sxi = pb.sb("sxil3", (128, 1))
    fqscale(sx[:, :1], stats[:, 0:1])
    pb.V(lambda e: e.reciprocal(sxi[:, :1], sx[:, :1]))
    for i in range(NT):
        pb.dma(xt[:, :], d_xn[i * 128:(i + 1) * 128, :])
        pb.V(lambda e: e.tensor_scalar(vb[:, :], xt[:, :], sxi[:, :1], None, OP.mult))
        pb.V(lambda e: e.tensor_copy(itl[:, :], vb[:, :]))
        pb.V(lambda e: e.tensor_copy(ob[:, :], itl[:, :]))
        pb.dma(d_qx[i * 128:(i + 1) * 128, :], ob[:, :], q="gpsimd")

    wq = pb.sb("wql3", (128, D), BF16)
    pq = pb.ps("pql3", (128, 512))
    sxw = pb.sb("sxwl3", (128, 1))
    ymax = pb.sb("ymaxl3", (128, 3))
    s2c = pb.sb("s2c", (128, 3))
    s3c = pb.sb("s3c", (128, 3))
    r23 = pb.sb("r23", (128, 3))
    i1 = pb.sb("i1l3", (128, 1), I32)
    t1c = pb.sb("t1c", (128, 1))
    s2i = pb.sb("s2i", (128, 3))
    qxT = [pb.sb(f"qxTl3{k}", (128, T), BF16) for k in range(8)]
    for k in range(8):
        pb.dmaT(qxT[k][:, :], d_qx[:, k * 128:(k + 1) * 128])
    for wi, w in enumerate("qkv"):
        pb.V(lambda e, wi=wi: e.tensor_tensor(sxw[:, :1], sx[:, :1], swcol[:, wi:wi + 1], OP.mult))
        pb.G(lambda e: e.memset(run[:, :1], 0.0))
        for i in range(NT):
            tsl = slice(i * 128, (i + 1) * 128)
            for half in range(2):
                nsl = slice(half * 512, (half + 1) * 512)
                for k in range(8):
                    pb.op("gpsimd", lambda e, k=k, w=w: e.dma_start(
                        out=wq[:, :], in_=d_qwt[w][k * 128:(k + 1) * 128, :]), 16)
                    pb.TE(lambda e, k=k, tsl=tsl, nsl=nsl, s=(k == 0), p=(k == 7):
                          e.matmul(pq[:, :], qxT[k][:, tsl], wq[:, nsl], start=s, stop=p))
                pb.V(lambda e, nsl=nsl: e.tensor_scalar(vb[:, nsl], pq[:, :], sxw[:, :1], None, OP.mult))
            pb.S(lambda e: e.activation(ab[:, :D], vb[:, :], AF.Abs))
            pb.V(lambda e: e.reduce_max(scr1[:, :1], ab[:, :D], axis=AX.X))
            pb.V(lambda e: e.tensor_tensor(run[:, :1], run[:, :1], scr1[:, :1], OP.max))
            pb.dma(d_y[w][tsl, :], vb[:, :], q="gpsimd")
        pb.TE(lambda e: e.transpose(pgen[:1, :], run[:, :1], ident[:, :]))
        pb.V(lambda e: e.tensor_copy(ab[:1, :128], pgen[:1, :128]))
        pb.V(lambda e: e.reduce_max(scr1[:1, :1], ab[:1, :128], axis=AX.X))
        pb.TE(lambda e: e.matmul(pgen[:, 0:1], ones1[:1, :], scr1[:1, :1], start=True, stop=True))
        pb.V(lambda e, wi=wi: e.tensor_copy(stats[:, wi:wi + 1], pgen[:, 0:1]))
    allreduce(3)

    for wi in range(3):
        fqscale(s2c[:, wi:wi + 1], stats[:, wi:wi + 1])
        pb.V(lambda e, wi=wi: e.reciprocal(t1c[:, :1], s2c[:, wi:wi + 1]))
        pb.V(lambda e, wi=wi: e.tensor_tensor(t1c[:, :1], stats[:, wi:wi + 1], t1c[:, :1], OP.mult))
        pb.V(lambda e: e.tensor_copy(i1[:, :1], t1c[:, :1]))
        pb.V(lambda e: e.tensor_copy(t1c[:, :1], i1[:, :1]))
        pb.V(lambda e, wi=wi: e.tensor_tensor(t1c[:, :1], t1c[:, :1], s2c[:, wi:wi + 1], OP.mult))
        fqscale(s3c[:, wi:wi + 1], t1c[:, :1])
        pb.V(lambda e, wi=wi: e.reciprocal(r23[:, wi:wi + 1], s3c[:, wi:wi + 1]))
        pb.V(lambda e, wi=wi: e.tensor_tensor(r23[:, wi:wi + 1], s2c[:, wi:wi + 1],
                                              r23[:, wi:wi + 1], OP.mult))
    for wi in range(3):
        pb.V(lambda e, wi=wi: e.reciprocal(s2i[:, wi:wi + 1], s2c[:, wi:wi + 1]))
    for wi, w in enumerate("qkv"):
        for i in range(NT):
            tsl = slice(i * 128, (i + 1) * 128)
            pb.dma(xt[:, :], d_y[w][tsl, :])
            pb.V(lambda e, wi=wi: e.tensor_scalar(vb[:, :], xt[:, :], s2i[:, wi:wi + 1], None, OP.mult))
            pb.V(lambda e: e.tensor_copy(itl[:, :], vb[:, :]))
            pb.V(lambda e: e.tensor_copy(vb[:, :], itl[:, :]))
            pb.V(lambda e, wi=wi: e.tensor_scalar(vb[:, :], vb[:, :], r23[:, wi:wi + 1], None, OP.mult))
            pb.V(lambda e: e.tensor_copy(itl[:, :], vb[:, :]))
            pb.V(lambda e: e.tensor_copy(ob[:, :], itl[:, :]))
            pb.dma(d_q3[w][tsl, :], ob[:, :], q="gpsimd")
    pb.sbfree(*qxT)

    qT2 = pb.sb("qT2l3", (128, T), BF16)
    kT2 = pb.sb("kT2l3", (128, T), BF16)
    sct = pb.sb("sctl3", (128, T))
    rowmx = pb.sb("rowmx", (128, 256))
    denst = pb.sb("denst", (128, 256))
    psck = pb.ps("psckl3", (128, 512))
    pb.G(lambda e: e.memset(run[:, :1], 0.0))
    for hp in range(8):
        pb.dmaT(qT2[:, :], d_q3["q"][:, hp * 128:(hp + 1) * 128])
        pb.dmaT(kT2[:, :], d_q3["k"][:, hp * 128:(hp + 1) * 128])
        for hh in range(2):
            h = hp * 2 + hh
            rsl = slice(hh * 64, hh * 64 + 64)
            for i in range(NT):
                tsl = slice(i * 128, (i + 1) * 128)
                for sc4 in range(4):
                    ssl = slice(sc4 * 512, (sc4 + 1) * 512)
                    pb.TE(lambda e, rsl=rsl, tsl=tsl, ssl=ssl: e.matmul(
                        psck[:, :], qT2[rsl, tsl], kT2[rsl, ssl], start=True, stop=True))
                    pb.V(lambda e, ssl=ssl: e.tensor_copy(sct[:, ssl], psck[:, :]))
                pb.V(lambda e: e.reduce_max(scr1[:, :1], sct[:, :], axis=AX.X))
                pb.V(lambda e, h=h, i=i: e.tensor_copy(rowmx[:, h * 16 + i:h * 16 + i + 1], scr1[:, :1]))
                pb.S(lambda e: e.activation(ab[:1, :1], scr1[:1, :1], AF.Abs)) if False else None
                pb.S(lambda e: e.activation(ab[:, :1024], sct[:, :1024], AF.Abs))
                pb.S(lambda e: e.activation(ab[:, 1024:2048], sct[:, 1024:2048], AF.Abs))
                pb.V(lambda e: e.reduce_max(t1c[:, :1], ab[:, :T], axis=AX.X))
                pb.V(lambda e: e.tensor_tensor(run[:, :1], run[:, :1], t1c[:, :1], OP.max))
                pb.dma(d_sc[h * T + i * 128:h * T + (i + 1) * 128, :], sct[:, :], q="gpsimd")
    pb.TE(lambda e: e.transpose(pgen[:1, :], run[:, :1], ident[:, :]))
    pb.V(lambda e: e.tensor_copy(ab[:1, :128], pgen[:1, :128]))
    pb.V(lambda e: e.reduce_max(scr1[:1, :1], ab[:1, :128], axis=AX.X))
    pb.TE(lambda e: e.matmul(pgen[:, 0:1], ones1[:1, :], scr1[:1, :1], start=True, stop=True))
    pb.V(lambda e: e.tensor_copy(stats[:, 0:1], pgen[:, 0:1]))
    pb.V(lambda e: e.tensor_tensor(stats[:, 0:1], stats[:, 0:1], s3c[:, 0:1], OP.mult))
    pb.V(lambda e: e.tensor_tensor(stats[:, 0:1], stats[:, 0:1], s3c[:, 1:2], OP.mult))
    allreduce(1)
    s5 = pb.sb("s5l3", (128, 1))
    gam = pb.sb("gaml3", (128, 1))
    alp = pb.sb("alpl3", (128, 1))
    fqscale(s5[:, :1], stats[:, 0:1])
    pb.V(lambda e: e.reciprocal(gam[:, :1], s5[:, :1]))
    pb.V(lambda e: e.tensor_tensor(gam[:, :1], gam[:, :1], s3c[:, 0:1], OP.mult))
    pb.V(lambda e: e.tensor_tensor(gam[:, :1], gam[:, :1], s3c[:, 1:2], OP.mult))
    pb.V(lambda e: e.tensor_scalar(alp[:, :1], s5[:, :1], 0.125, None, OP.mult))

    q5mx = pb.sb("q5mx", (128, 1))
    negb = pb.sb("negbl3", (128, 1))
    wtile = pb.sb("wtilel3", (128, T))
    runmin = pb.sb("runminl3", (128, 1))
    pb.G(lambda e: e.memset(runmin[:, :1], 1e30))
    for h in range(16):
        for i in range(NT):
            rr = slice(h * T + i * 128, h * T + (i + 1) * 128)
            pb.dma(sct[:, :], d_sc[rr, :])
            pb.V(lambda e: e.tensor_scalar(sct[:, :], sct[:, :], gam[:, :1], None, OP.mult))
            pb.V(lambda e: e.tensor_copy(itl[:, :], sct[:, :1024]))
            pb.V(lambda e: e.tensor_copy(sct[:, :1024], itl[:, :]))
            pb.V(lambda e: e.tensor_copy(itl[:, :], sct[:, 1024:2048]))
            pb.V(lambda e: e.tensor_copy(sct[:, 1024:2048], itl[:, :]))
            pb.V(lambda e, h=h, i=i: e.tensor_tensor(q5mx[:, :1], rowmx[:, h * 16 + i:h * 16 + i + 1], gam[:, :1], OP.mult))
            pb.V(lambda e: e.tensor_copy(i1[:, :1], q5mx[:, :1]))
            pb.V(lambda e: e.tensor_copy(q5mx[:, :1], i1[:, :1]))
            pb.V(lambda e: e.tensor_tensor(negb[:, :1], q5mx[:, :1], alp[:, :1], OP.mult))
            pb.V(lambda e: e.tensor_scalar(negb[:, :1], negb[:, :1], -1.0, None, OP.mult))
            pb.S(lambda e: e.activation(wtile[:, :], sct[:, :], AF.Exp, bias=negb[:, :1],
                                        scale=alp[:, :1], accum_out=scr1[:, :1]))
            pb.V(lambda e, h=h, i=i: e.tensor_copy(denst[:, h * 16 + i:h * 16 + i + 1], scr1[:, :1]))
            pb.V(lambda e: e.tensor_tensor(runmin[:, :1], runmin[:, :1], scr1[:, :1], OP.min))
            pb.dma(d_w6[rr, :], wtile[:, :], q="gpsimd")
    pb.V(lambda e: e.tensor_scalar(runmin[:, :1], runmin[:, :1], -1.0, None, OP.mult))
    pb.TE(lambda e: e.transpose(pgen[:1, :], runmin[:, :1], ident[:, :]))
    pb.V(lambda e: e.tensor_copy(ab[:1, :128], pgen[:1, :128]))
    pb.V(lambda e: e.reduce_max(scr1[:1, :1], ab[:1, :128], axis=AX.X))
    pb.TE(lambda e: e.matmul(pgen[:, 0:1], ones1[:1, :], scr1[:1, :1], start=True, stop=True))
    pb.V(lambda e: e.tensor_copy(stats[:, 0:1], pgen[:, 0:1]))
    allreduce(1)
    s6 = pb.sb("s6l3", (128, 1))
    pb.V(lambda e: e.tensor_scalar(t1c[:, :1], stats[:, 0:1], -1.0, None, OP.mult))
    pb.V(lambda e: e.reciprocal(t1c[:, :1], t1c[:, :1]))
    fqscale(s6[:, :1], t1c[:, :1])

    q3vt = [pb.sb(f"q3vt{j}", (128, D), BF16) for j in range(NT)]
    for j in range(NT):
        pb.dma(q3vt[j][:, :], d_q3["v"][j * 128:(j + 1) * 128, :])
    beta = pb.sb("betal3", (128, 1))
    for h in range(16):
        for i in range(NT):
            rr = slice(h * T + i * 128, h * T + (i + 1) * 128)
            pb.dma(sct[:, :], d_w6[rr, :])
            pb.V(lambda e, h=h, i=i: e.tensor_tensor(beta[:, :1], denst[:, h * 16 + i:h * 16 + i + 1], s6[:, :1], OP.mult))
            pb.V(lambda e: e.reciprocal(beta[:, :1], beta[:, :1]))
            pb.V(lambda e: e.tensor_scalar(sct[:, :], sct[:, :], beta[:, :1], None, OP.mult))
            pb.V(lambda e: e.tensor_copy(itl[:, :], sct[:, :1024]))
            pb.V(lambda e: e.tensor_copy(sct[:, :1024], itl[:, :]))
            pb.V(lambda e: e.tensor_copy(itl[:, :], sct[:, 1024:2048]))
            pb.V(lambda e: e.tensor_copy(sct[:, 1024:2048], itl[:, :]))
            pb.V(lambda e: e.tensor_copy(wtile[:, :], sct[:, :]))
            pb.dma(d_q6[rr, :], wtile[:, :], q="gpsimd")
    s6v = pb.sb("s6vl3", (128, 1))
    pb.V(lambda e: e.tensor_tensor(s6v[:, :1], s6[:, :1], s3c[:, 2:3], OP.mult))
    q6T = pb.sb("q6Tl3", (128, T), BF16)
    pov = pb.ps("povl3", (128, 64))
    yot = pb.sb("yotl3", (128, D))
    pb.G(lambda e: e.memset(run[:, :1], 0.0))
    for i in range(NT):
        tsl = slice(i * 128, (i + 1) * 128)
        for h in range(16):
            for c in range(NT):
                if i == 0 or True:
                    pass
            pass
        pass
    for h in range(16):
        for c in range(NT):
            pb.dmaT(q6T[:, :] if False else q6T[:, :],
                    d_w6[0:1, 0:128]) if False else None
        pass
    for i in range(NT):
        tsl = slice(i * 128, (i + 1) * 128)
        for h in range(16):
            for c in range(NT):
                pb.op("scalar", lambda e, h=h, c=c, tsl=tsl: e.dma_start(
                    out=q6T[:, tsl.start:tsl.start + 128] if False else q6T[:, 0:128],
                    in_=d_q6[h * T + tsl.start:h * T + tsl.stop, c * 128:(c + 1) * 128],
                    transpose=True), 16)
                pb.TE(lambda e, c=c, h=h, s=(c == 0), p=(c == NT - 1):
                      e.matmul(pov[:, :], q6T[:, 0:128], q3vt[c][:, h * 64:(h + 1) * 64],
                               start=s, stop=p))
            pb.V(lambda e, h=h: e.tensor_scalar(yot[:, h * 64:(h + 1) * 64], pov[:, :],
                                                s6v[:, :1], None, OP.mult))
        pb.S(lambda e: e.activation(ab[:, :D], yot[:, :], AF.Abs))
        pb.V(lambda e: e.reduce_max(scr1[:, :1], ab[:, :D], axis=AX.X))
        pb.V(lambda e: e.tensor_tensor(run[:, :1], run[:, :1], scr1[:, :1], OP.max))
        pb.dma(d_yo[tsl, :], yot[:, :], q="gpsimd")
    pb.TE(lambda e: e.transpose(pgen[:1, :], run[:, :1], ident[:, :]))
    pb.V(lambda e: e.tensor_copy(ab[:1, :128], pgen[:1, :128]))
    pb.V(lambda e: e.reduce_max(scr1[:1, :1], ab[:1, :128], axis=AX.X))
    pb.TE(lambda e: e.matmul(pgen[:, 0:1], ones1[:1, :], scr1[:1, :1], start=True, stop=True))
    pb.V(lambda e: e.tensor_copy(stats[:, 0:1], pgen[:, 0:1]))
    allreduce(1)

    s7 = pb.sb("s7l3", (128, 1))
    s7i = pb.sb("s7il3", (128, 1))
    r78 = pb.sb("r78l3", (128, 1))
    fqscale(s7[:, :1], stats[:, 0:1])
    pb.V(lambda e: e.reciprocal(s7i[:, :1], s7[:, :1]))
    pb.V(lambda e: e.tensor_tensor(t1c[:, :1], stats[:, 0:1], s7i[:, :1], OP.mult))
    pb.V(lambda e: e.tensor_copy(i1[:, :1], t1c[:, :1]))
    pb.V(lambda e: e.tensor_copy(t1c[:, :1], i1[:, :1]))
    pb.V(lambda e: e.tensor_tensor(t1c[:, :1], t1c[:, :1], s7[:, :1], OP.mult))
    fqscale(t1c[:, :1], t1c[:, :1])
    pb.V(lambda e: e.reciprocal(r78[:, :1], t1c[:, :1]))
    pb.V(lambda e: e.tensor_tensor(r78[:, :1], s7[:, :1], r78[:, :1], OP.mult))
    s8w = pb.sb("s8wl3", (128, 1))
    pb.V(lambda e: e.tensor_tensor(s8w[:, :1], t1c[:, :1], swcol[:, 3:4], OP.mult))
    for i in range(NT):
        tsl = slice(i * 128, (i + 1) * 128)
        pb.dma(xt[:, :], d_yo[tsl, :])
        pb.V(lambda e: e.tensor_scalar(vb[:, :], xt[:, :], s7i[:, :1], None, OP.mult))
        pb.V(lambda e: e.tensor_copy(itl[:, :], vb[:, :]))
        pb.V(lambda e: e.tensor_copy(vb[:, :], itl[:, :]))
        pb.V(lambda e: e.tensor_scalar(vb[:, :], vb[:, :], r78[:, :1], None, OP.mult))
        pb.V(lambda e: e.tensor_copy(itl[:, :], vb[:, :]))
        pb.V(lambda e: e.tensor_copy(ob[:, :], itl[:, :]))
        pb.dma(d_at[tsl, :], ob[:, :], q="gpsimd")
    atT = [pb.sb(f"atTl3{k}", (128, T), BF16) for k in range(8)]
    for k in range(8):
        pb.dmaT(atT[k][:, :], d_at[:, k * 128:(k + 1) * 128])
    pb.G(lambda e: e.memset(run[:, :1], 0.0))
    for i in range(NT):
        tsl = slice(i * 128, (i + 1) * 128)
        for half in range(2):
            nsl = slice(half * 512, (half + 1) * 512)
            for k in range(8):
                pb.op("gpsimd", lambda e, k=k: e.dma_start(
                    out=wq[:, :], in_=d_qwt["o"][k * 128:(k + 1) * 128, :]), 16)
                pb.TE(lambda e, k=k, tsl=tsl, nsl=nsl, s=(k == 0), p=(k == 7):
                      e.matmul(pq[:, :], atT[k][:, tsl], wq[:, nsl], start=s, stop=p))
            pb.V(lambda e, nsl=nsl: e.tensor_scalar(vb[:, nsl], pq[:, :], s8w[:, :1], None, OP.mult))
        pb.S(lambda e: e.activation(ab[:, :D], vb[:, :], AF.Abs))
        pb.V(lambda e: e.reduce_max(scr1[:, :1], ab[:, :D], axis=AX.X))
        pb.V(lambda e: e.tensor_tensor(run[:, :1], run[:, :1], scr1[:, :1], OP.max))
        pb.dma(d_yf[tsl, :], vb[:, :], q="gpsimd")
    pb.TE(lambda e: e.transpose(pgen[:1, :], run[:, :1], ident[:, :]))
    pb.V(lambda e: e.tensor_copy(ab[:1, :128], pgen[:1, :128]))
    pb.V(lambda e: e.reduce_max(scr1[:1, :1], ab[:1, :128], axis=AX.X))
    pb.TE(lambda e: e.matmul(pgen[:, 0:1], ones1[:1, :], scr1[:1, :1], start=True, stop=True))
    pb.V(lambda e: e.tensor_copy(stats[:, 0:1], pgen[:, 0:1]))
    allreduce(1)
    sf = pb.sb("sfl3", (128, 1))
    sfi = pb.sb("sfil3", (128, 1))
    fqscale(sf[:, :1], stats[:, 0:1])
    pb.V(lambda e: e.reciprocal(sfi[:, :1], sf[:, :1]))
    for i in range(NT):
        tsl = slice(i * 128, (i + 1) * 128)
        pb.dma(xt[:, :], d_yf[tsl, :])
        pb.V(lambda e: e.tensor_scalar(vb[:, :], xt[:, :], sfi[:, :1], None, OP.mult))
        pb.V(lambda e: e.tensor_copy(itl[:, :], vb[:, :]))
        pb.V(lambda e: e.tensor_copy(vb[:, :], itl[:, :]))
        pb.V(lambda e: e.tensor_scalar(vb[:, :], vb[:, :], sf[:, :1], None, OP.mult))
        pb.dma(d_out[tsl, :], vb[:, :], q="gpsimd")
    return pb


def prog_l0_attn(dbg=False):
    E = 2048
    pb = PB("l0a")
    d_xn = pb.din("xn", [T, D], F32)
    d_inw = pb.din("inw", [D, 2 * E], F32)
    d_convw = pb.din("convw", [E, 4], F32)
    d_convb = pb.din("convb", [E, 1], F32)
    d_dtw = pb.din("dtw", [E, E], F32)
    d_dtb = pb.din("dtb", [E, 1], F32)
    d_bw = pb.din("bw", [E, 64], F32)
    d_cw = pb.din("cw", [E, 64], F32)
    d_alog = pb.din("alog", [1, 64], F32)
    d_outw = pb.din("outw", [E, D], F32)
    d_gatew = pb.din("gatew", [D, E], F32)
    d_upw = pb.din("upw", [D, E], F32)
    d_downw = pb.din("downw", [E, D], F32)
    d_inormw = pb.din("inormw", [1, D], F32)
    d_fnormw = pb.din("fnormw", [1, D], F32)
    d_out = pb.dout("out", [T, D], F32)
    nc = pb.nc
    ik = "ExternalOutput" if dbg else "Internal"
    d_n1b = nc.dram_tensor("n1b0", [T, D], BF16, kind=ik).ap()
    d_xz = nc.dram_tensor("xz0", [E, T], F32, kind=ik).ap()
    d_zs = nc.dram_tensor("zs0", [E, T], BF16, kind=ik).ap()
    d_xc = nc.dram_tensor("xc0", [E, T], BF16, kind=ik).ap()
    d_dt = nc.dram_tensor("dt0", [E, T], BF16, kind=ik).ap()
    d_u = nc.dram_tensor("u0", [E, T], BF16, kind=ik).ap()
    d_y = nc.dram_tensor("y0", [E, T], BF16, kind=ik).ap()
    d_xo = nc.dram_tensor("xo0", [T, D], F32, kind=ik).ap()
    d_n2b = nc.dram_tensor("n2b0", [T, D], BF16, kind=ik).ap()
    d_h = nc.dram_tensor("h0", [E, T], BF16, kind=ik).ap()

    ones1 = pb.sb("ones1l0", (1, 128))
    pb.G(lambda e: e.memset(ones1[:, :], 1.0))
    pbig = pb.ps("pbigl0", (128, 2048))
    psA = pb.ps("psAl0", (128, 512))
    psB = pb.ps("psBl0", (128, 512))
    psb = psA
    rw = pb.sb("rwl0", (1, D))
    pb.dma(rw[:1, :], d_inormw)
    rwf = pb.sb("rwfl0", (128, D))
    for j in range(2):
        pb.TE(lambda e, j=j: e.matmul(psb[:, :], ones1[:1, :], rw[:1, j * 512:(j + 1) * 512],
                                      start=True, stop=True))
        pb.V(lambda e, j=j: e.tensor_copy(rwf[:, j * 512:(j + 1) * 512], psb[:, :]))
    xt = pb.sb("xtl0", (128, D))
    sq = pb.sb("sql0", (128, D))
    ms = pb.sb("msl0", (128, 1))
    n1 = pb.sb("n1l0", (128, D))
    n1b = pb.sb("n1bl0", (128, D), BF16)
    for i in range(NT):
        tsl = slice(i * 128, (i + 1) * 128)
        pb.dma(xt[:, :], d_xn[tsl, :])
        pb.S(lambda e: e.activation(sq[:, :], xt[:, :], AF.Square))
        pb.V(lambda e: e.reduce_sum(ms[:, :1], sq[:, :], axis=AX.X))
        pb.V(lambda e: e.tensor_scalar(ms[:, :1], ms[:, :1], 1.0 / D, 1e-5, OP.mult, OP.add))
        pb.S(lambda e: e.activation(ms[:, :1], ms[:, :1], AF.Ln))
        pb.S(lambda e: e.activation(ms[:, :1], ms[:, :1], AF.Exp, scale=-0.5))
        pb.V(lambda e: e.tensor_scalar(n1[:, :], xt[:, :], ms[:, :1], None, OP.mult))
        pb.V(lambda e: e.tensor_tensor(n1[:, :], n1[:, :], rwf[:, :], OP.mult))
        pb.V(lambda e: e.tensor_copy(n1b[:, :], n1[:, :]))
        pb.dma(d_n1b[tsl, :], n1b[:, :], q="gpsimd")
    wt = pb.sb("wtl0", (128, 128), BF16)
    stile = pb.sb("stileL0", (128, 512))
    sbt = pb.sb("sbtL0", (128, T), BF16)
    n1T = [pb.sb(f"n1T0{k}", (128, T), BF16) for k in range(8)]
    for k in range(8):
        pb.dmaT(n1T[k][:, :], d_n1b[:, k * 128:(k + 1) * 128])
    for c in range(32):
        for tc in range(4):
            tfl = slice(tc * 512, (tc + 1) * 512)
            for k in range(8):
                pb.op("gpsimd", lambda e, k=k, c=c: e.dma_start(
                    out=wt[:, :], in_=d_inw[k * 128:(k + 1) * 128, c * 128:(c + 1) * 128]), 16)
                pb.TE(lambda e, k=k, tfl=tfl, s=(k == 0), p=(k == 7):
                      e.matmul(psb[:, :], wt[:, :], n1T[k][:, tfl], start=s, stop=p))
            if c < 16:
                pb.V(lambda e: e.tensor_copy(stile[:, :], psb[:, :]))
                pb.dma(d_xz[c * 128:(c + 1) * 128, tfl], stile[:, :], q="gpsimd")
            else:
                pb.S(lambda e, tfl=tfl: e.activation(sbt[:, tfl], psb[:, :], AF.Silu))
        if c >= 16:
            pb.dma(d_zs[(c - 16) * 128:(c - 15) * 128, :], sbt[:, :], q="gpsimd")
    pb.sbfree(*n1T)
    BT = pb.sb("BTl0", (64, T), BF16)
    CT = pb.sb("CTl0", (64, T), BF16)
    arow = pb.sb("arowl0", (1, 64))
    afull = pb.sb("afulll0", (128, 64))
    browk = pb.sb("browk", (1, T), BF16)
    crowk = pb.sb("crowk", (1, T), BF16)
    onesbl0 = pb.sb("onesbl0", (1, 128), BF16)
    pb.G(lambda e: e.memset(onesbl0[:, :], 1.0))
    xck = pb.sb("xckl0", (128, T), BF16)
    xzp = pb.sb("xzp", (128, T + 3))
    wcv = pb.sb("wcv", (128, 4))
    cbc = pb.sb("cbc", (128, 1))
    acc = pb.sb("accl0", (128, T))
    for j in range(16):
        esl = slice(j * 128, (j + 1) * 128)
        pb.G(lambda e: e.memset(xzp[:, 0:1], 0.0))
        pb.G(lambda e: e.memset(xzp[:, T + 1:T + 3], 0.0))
        pb.dma(xzp[:, 1:T + 1], d_xz[esl, :])
        pb.dma(wcv[:, :], d_convw[esl, :])
        pb.dma(cbc[:, :], d_convb[esl, :])
        pb.V(lambda e: e.tensor_scalar(acc[:, :], xzp[:, 0:T], wcv[:, 0:1], None, OP.mult))
        for kk in range(1, 4):
            pb.V(lambda e, kk=kk: e.scalar_tensor_tensor(
                acc[:, :], xzp[:, kk:kk + T], wcv[:, kk:kk + 1], acc[:, :], OP.mult, OP.add))
        pb.V(lambda e: e.tensor_scalar(acc[:, :], acc[:, :], cbc[:, :1], None, OP.add))
        pb.S(lambda e: e.activation(acc[:, :], acc[:, :], AF.Silu))
        pb.V(lambda e: e.tensor_copy(sbt[:, :], acc[:, :]))
        pb.dma(d_xc[esl, :], sbt[:, :], q="gpsimd")
    pd = [pbig[:, t_ * 512:(t_ + 1) * 512] for t_ in range(4)]
    dtb = pb.sb("dtbl0", (128, 1))
    dtt = pb.sb("dttl0", (128, T))
    au = pb.sb("aul0", (128, T))
    for j in range(16):
        esl = slice(j * 128, (j + 1) * 128)
        for k in range(16):
            pb.op("gpsimd", lambda e, k=k, j=j: e.dma_start(
                out=wt[:, :], in_=d_dtw[k * 128:(k + 1) * 128, j * 128:(j + 1) * 128]), 16)
            pb.op("gpsimd", lambda e, k=k: e.dma_start(
                out=xck[:, :], in_=d_xc[k * 128:(k + 1) * 128, :]), 16)
            for tc in range(4):
                pb.TE(lambda e, k=k, tc=tc, s=(k == 0), p=(k == 15):
                      e.matmul(pd[tc], wt[:, :], xck[:, tc * 512:(tc + 1) * 512],
                               start=s, stop=p))
        pb.dma(dtb[:, :], d_dtb[esl, :])
        for tc in range(4):
            pb.V(lambda e, tc=tc: e.tensor_scalar(dtt[:, tc * 512:(tc + 1) * 512],
                                                  pd[tc], dtb[:, :1], None, OP.add))
        pb.S(lambda e: e.activation(au[:, :], dtt[:, :], AF.Abs))
        pb.S(lambda e: e.activation(au[:, :], au[:, :], AF.Exp, scale=-1.0))
        pb.S(lambda e: e.activation(au[:, :], au[:, :], AF.Ln, bias=1.0))
        pb.V(lambda e: e.tensor_scalar(dtt[:, :], dtt[:, :], 0.0, None, OP.max))
        pb.V(lambda e: e.tensor_tensor(dtt[:, :], dtt[:, :], au[:, :], OP.add))
        pb.V(lambda e: e.tensor_scalar(dtt[:, :], dtt[:, :], 0.001, 0.1, OP.max, OP.min))
        pb.V(lambda e: e.tensor_copy(sbt[:, :], dtt[:, :]))
        pb.dma(d_dt[esl, :], sbt[:, :], q="gpsimd")
        pb.op("gpsimd", lambda e, esl=esl: e.dma_start(out=xck[:, :], in_=d_xc[esl, :]), 16)
        pb.V(lambda e: e.tensor_tensor(dtt[:, :], dtt[:, :], xck[:, :], OP.mult))
        pb.V(lambda e: e.tensor_copy(sbt[:, :], dtt[:, :]))
        pb.dma(d_u[esl, :], sbt[:, :], q="gpsimd")
    bwt = pb.sb("bwtl0", (128, 128), BF16)
    for (dst, wsrc) in ((BT, d_bw), (CT, d_cw)):
        for tc in range(4):
            for k in range(16):
                pb.op("gpsimd", lambda e, k=k, wsrc=wsrc: e.dma_start(
                    out=bwt[:, 0:64], in_=wsrc[k * 128:(k + 1) * 128, :]), 16)
                pb.op("gpsimd", lambda e, k=k: e.dma_start(
                    out=xck[:, :], in_=d_xc[k * 128:(k + 1) * 128, :]), 16)
                pb.TE(lambda e, k=k, tc=tc, s=(k == 0), p=(k == 15):
                      e.matmul(psb[:64, :], bwt[:, 0:64], xck[:, tc * 512:(tc + 1) * 512],
                               start=s, stop=p))
            pb.V(lambda e, dst=dst, tc=tc: e.tensor_copy(dst[:, tc * 512:(tc + 1) * 512], psb[:64, :]))
    pb.dma(arow[:1, :], d_alog)
    pb.S(lambda e: e.activation(arow[:1, :], arow[:1, :], AF.Exp))
    pb.V(lambda e: e.tensor_scalar(arow[:1, :], arow[:1, :], -1.0, None, OP.mult))
    pab = psA[:, 0:64]
    pb.TE(lambda e: e.matmul(pab, ones1[:1, :], arow[:1, :], start=True, stop=True))
    pb.V(lambda e: e.tensor_copy(afull[:, :], pab))
    pb.sbfree(bwt, au, dtt, dtb, acc, cbc, wcv, xzp)
    pbb = pbig
    Bb = pb.sb("Bbl0", (128, T), BF16)
    Cb = pb.sb("Cbl0", (128, T), BF16)
    dth = [pb.sb(f"dth{j}", (128, T), BF16) for j in range(8)]
    uh = [pb.sb(f"uh{j}", (128, T), BF16) for j in range(8)]
    ya = [pb.sb(f"ya{j}", (128, T)) for j in range(8)]
    sco = pb.sb("scol0", (128, T))
    bt2 = pb.sb("bt2l0", (128, T))
    for half in range(2):
        for j in range(8):
            je = half * 8 + j
            pb.op("gpsimd", lambda e, j=j, je=je: e.dma_start(
                out=dth[j][:, :], in_=d_dt[je * 128:(je + 1) * 128, :]), 16)
            pb.op("gpsimd", lambda e, j=j, je=je: e.dma_start(
                out=uh[j][:, :], in_=d_u[je * 128:(je + 1) * 128, :]), 16)
            pb.G(lambda e, j=j: e.memset(ya[j][:, :], 0.0))
        for n in range(64):
            pb.op("gpsimd", lambda e, n=n: e.dma_start(out=browk[:1, :], in_=BT[n:n + 1, :]), 16)
            pb.op("gpsimd", lambda e, n=n: e.dma_start(out=crowk[:1, :], in_=CT[n:n + 1, :]), 16)
            for tc in range(4):
                pb.TE(lambda e, tc=tc: e.matmul(pbb[:, tc * 512:(tc + 1) * 512], onesbl0[:1, :],
                                                browk[:1, tc * 512:(tc + 1) * 512],
                                                start=True, stop=True))
            pb.V(lambda e: e.tensor_copy(Bb[:, :], pbb[:, :]))
            for tc in range(4):
                pb.TE(lambda e, tc=tc: e.matmul(pbb[:, tc * 512:(tc + 1) * 512], onesbl0[:1, :],
                                                crowk[:1, tc * 512:(tc + 1) * 512],
                                                start=True, stop=True))
            pb.V(lambda e: e.tensor_copy(Cb[:, :], pbb[:, :]))
            for j in range(8):
                pb.V(lambda e, j=j, n=n: e.tensor_scalar(
                    sco[:, :], dth[j][:, :], afull[:, n:n + 1], None, OP.mult))
                pb.S(lambda e: e.activation(sco[:, :], sco[:, :], AF.Exp))
                pb.V(lambda e, j=j: e.tensor_tensor(bt2[:, :], uh[j][:, :], Bb[:, :], OP.mult))
                pb.V(lambda e: e.tensor_tensor_scan(bt2[:, :], sco[:, :], bt2[:, :], 0.0,
                                                    OP.mult, OP.add))
                pb.V(lambda e: e.tensor_tensor(bt2[:, :], bt2[:, :], Cb[:, :], OP.mult))
                pb.V(lambda e, j=j: e.tensor_tensor(ya[j][:, :], ya[j][:, :], bt2[:, :], OP.add))
        for j in range(8):
            je = half * 8 + j
            pb.op("gpsimd", lambda e, je=je: e.dma_start(
                out=xck[:, :], in_=d_zs[je * 128:(je + 1) * 128, :]), 16)
            pb.V(lambda e, j=j: e.tensor_tensor(ya[j][:, :], ya[j][:, :], xck[:, :], OP.mult))
            pb.V(lambda e, j=j: e.tensor_copy(sbt[:, :], ya[j][:, :]))
            pb.dma(d_y[je * 128:(je + 1) * 128, :], sbt[:, :], q="gpsimd")
    pb.sbfree(bt2, sco)
    pb.sbfree(*(list(reversed(ya)) + list(reversed(uh)) + list(reversed(dth))))
    pb.sbfree(Cb, Bb)
    yk = pb.sb("ykl0", (128, 128), BF16)
    wo = pb.sb("wol0", (128, D), BF16)
    po2 = psA
    ot = pb.sb("otl0", (128, D))
    rw2 = pb.sb("rw2l0", (1, D))
    pb.dma(rw2[:1, :], d_fnormw)
    rwf2 = pb.sb("rwf2l0", (128, D))
    for j in range(2):
        pb.TE(lambda e, j=j: e.matmul(psb[:, :], ones1[:1, :], rw2[:1, j * 512:(j + 1) * 512],
                                      start=True, stop=True))
        pb.V(lambda e, j=j: e.tensor_copy(rwf2[:, j * 512:(j + 1) * 512], psb[:, :]))
    for i in range(NT):
        tsl = slice(i * 128, (i + 1) * 128)
        for half in range(2):
            nsl = slice(half * 512, (half + 1) * 512)
            for k in range(16):
                pb.op("gpsimd", lambda e, k=k, tsl=tsl: e.dma_start(
                    out=yk[:, :], in_=d_y[k * 128:(k + 1) * 128, tsl]), 16)
                pb.op("gpsimd", lambda e, k=k: e.dma_start(
                    out=wo[:, :], in_=d_outw[k * 128:(k + 1) * 128, :]), 16)
                pb.TE(lambda e, k=k, nsl=nsl, s=(k == 0), p=(k == 15):
                      e.matmul(po2[:, :], yk[:, :], wo[:, nsl], start=s, stop=p))
            pb.V(lambda e, nsl=nsl: e.tensor_copy(ot[:, nsl], po2[:, :]))
        pb.dma(xt[:, :], d_xn[tsl, :])
        pb.V(lambda e: e.tensor_tensor(ot[:, :], ot[:, :], xt[:, :], OP.add))
        pb.dma(d_xo[tsl, :], ot[:, :], q="gpsimd")
        pb.S(lambda e: e.activation(sq[:, :], ot[:, :], AF.Square))
        pb.V(lambda e: e.reduce_sum(ms[:, :1], sq[:, :], axis=AX.X))
        pb.V(lambda e: e.tensor_scalar(ms[:, :1], ms[:, :1], 1.0 / D, 1e-5, OP.mult, OP.add))
        pb.S(lambda e: e.activation(ms[:, :1], ms[:, :1], AF.Ln))
        pb.S(lambda e: e.activation(ms[:, :1], ms[:, :1], AF.Exp, scale=-0.5))
        pb.V(lambda e: e.tensor_scalar(n1[:, :], ot[:, :], ms[:, :1], None, OP.mult))
        pb.V(lambda e: e.tensor_tensor(n1[:, :], n1[:, :], rwf2[:, :], OP.mult))
        pb.V(lambda e: e.tensor_copy(n1b[:, :], n1[:, :]))
        pb.dma(d_n2b[tsl, :], n1b[:, :], q="gpsimd")
    n2T = [pb.sb(f"n2T0{k}", (128, T), BF16) for k in range(8)]
    for k in range(8):
        pb.dmaT(n2T[k][:, :], d_n2b[:, k * 128:(k + 1) * 128])
    pg = [psA, psB]
    gs = pb.sb("gsl0", (128, T))
    for j in range(16):
        esl = slice(j * 128, (j + 1) * 128)
        for th in range(2):
            tfl = slice(th * 1024, (th + 1) * 1024)
            for k in range(8):
                pb.op("gpsimd", lambda e, k=k, j=j: e.dma_start(
                    out=wt[:, :], in_=d_gatew[k * 128:(k + 1) * 128, j * 128:(j + 1) * 128]), 16)
                for t2 in range(2):
                    pb.TE(lambda e, k=k, th=th, t2=t2, s=(k == 0), p=(k == 7):
                          e.matmul(pg[t2][:, :], wt[:, :],
                                   n1T[0][:, 0:512] if False else n2T[k][:, th * 1024 + t2 * 512:th * 1024 + (t2 + 1) * 512],
                                   start=s, stop=p))
            for t2 in range(2):
                pb.S(lambda e, th=th, t2=t2: e.activation(
                    gs[:, th * 1024 + t2 * 512:th * 1024 + (t2 + 1) * 512], pg[t2][:, :], AF.Silu))
        for th in range(2):
            for k in range(8):
                pb.op("gpsimd", lambda e, k=k, j=j: e.dma_start(
                    out=wt[:, :], in_=d_upw[k * 128:(k + 1) * 128, j * 128:(j + 1) * 128]), 16)
                for t2 in range(2):
                    pb.TE(lambda e, k=k, th=th, t2=t2, s=(k == 0), p=(k == 7):
                          e.matmul(pg[t2][:, :], wt[:, :],
                                   n2T[k][:, th * 1024 + t2 * 512:th * 1024 + (t2 + 1) * 512],
                                   start=s, stop=p))
            for t2 in range(2):
                pb.V(lambda e, th=th, t2=t2: e.scalar_tensor_tensor(
                    sbt[:, th * 1024 + t2 * 512:th * 1024 + (t2 + 1) * 512],
                    pg[t2][:, :], 1.0,
                    gs[:, th * 1024 + t2 * 512:th * 1024 + (t2 + 1) * 512], OP.mult, OP.mult))
        pb.dma(d_h[esl, :], sbt[:, :], q="gpsimd")
    for i in range(NT):
        tsl = slice(i * 128, (i + 1) * 128)
        for half in range(2):
            nsl = slice(half * 512, (half + 1) * 512)
            for k in range(16):
                pb.op("gpsimd", lambda e, k=k, tsl=tsl: e.dma_start(
                    out=yk[:, :], in_=d_h[k * 128:(k + 1) * 128, tsl]), 16)
                pb.op("gpsimd", lambda e, k=k: e.dma_start(
                    out=wo[:, :], in_=d_downw[k * 128:(k + 1) * 128, :]), 16)
                pb.TE(lambda e, k=k, nsl=nsl, s=(k == 0), p=(k == 15):
                      e.matmul(po2[:, :], yk[:, :], wo[:, nsl], start=s, stop=p))
            pb.V(lambda e, nsl=nsl: e.tensor_copy(ot[:, nsl], po2[:, :]))
        pb.dma(xt[:, :], d_xo[tsl, :])
        pb.V(lambda e: e.tensor_tensor(ot[:, :], ot[:, :], xt[:, :], OP.add))
        pb.dma(d_out[tsl, :], ot[:, :], q="gpsimd")
    return pb


def prog_combine():
    pb = PB("comb")
    d_x = pb.din("x", [T, D], F32)
    d_a = pb.din("a", [T, D], F32)
    d_f = pb.din("f", [T, D], F32)
    d_o = pb.dout("out", [T, D], F32)
    t1 = pb.sb("t1c2", (128, D))
    t2 = pb.sb("t2c2", (128, D))
    t3 = pb.sb("t3c2", (128, D))
    for i in range(NT):
        tsl = slice(i * 128, (i + 1) * 128)
        pb.dma(t1[:, :], d_x[tsl, :])
        pb.dma(t2[:, :], d_a[tsl, :])
        pb.dma(t3[:, :], d_f[tsl, :])
        pb.V(lambda e: e.tensor_tensor(t1[:, :], t1[:, :], t2[:, :], OP.add))
        pb.V(lambda e: e.tensor_tensor(t1[:, :], t1[:, :], t3[:, :], OP.add))
        pb.dma(d_o[tsl, :], t1[:, :], q="gpsimd")
    return pb


_PROGS = {}


def _get_prog(name, builder):
    if name not in _PROGS:
        pb = builder()
        _PROGS[name] = pb.finalize()
    return _PROGS[name]


def _consts():
    c = {}
    c["ident"] = np.eye(128, dtype=np.float32)
    tt_ = np.arange(T)[:, None] // 64
    cc_ = np.arange(32)[None, :]
    c["l1mask"] = np.where(tt_ > cc_, 0.0, -1e30).astype(np.float32)
    dh = 64
    inv = 1.0 / 10000.0 ** (np.arange(0, dh, 2) / dh)
    fr = np.arange(T)[:, None] * inv[None, :]
    fr = np.repeat(fr, 2, axis=-1)
    cos = np.cos(fr).astype(np.float32)
    sin = np.sin(fr).astype(np.float32)
    sgn = np.tile(np.array([-1.0, 1.0], np.float32), dh // 2)
    c["cosb"] = np.tile(cos, (1, 16))
    c["sinb"] = np.tile((sin * sgn).astype(np.float32), (1, 16))
    ii = np.arange(512)[:, None]
    jj = np.arange(512)[None, :]
    c["maskc"] = np.where(jj <= ii, 0.0, -1e30).astype(np.float32)
    c["tiecol"] = np.tile(np.linspace(0, 1e-6, 16).astype(np.float32), 8)[:, None].copy()
    selc = np.zeros((64, 4096), np.float32)
    for p in range(4096):
        selc[p // 64, p] = 1.0
    c["selc"] = selc
    return c


def _dropout_mask():
    import jax
    with jax.default_device(jax.devices("cpu")[0]):
        keep = np.asarray(jax.random.bernoulli(jax.random.key(42), 0.8, (8, T, D)))
    import ml_dtypes
    return (keep.astype(np.float32) * 1.25).astype(ml_dtypes.bfloat16)


def kernel(**inp):
    f = np.float32
    ct = lambda a: np.ascontiguousarray(a, dtype=f)
    C = _consts()
    B = inp["L0_attn_x"].shape[0]
    x = [ct(inp["L0_attn_x"][b]) for b in range(B)]
    cores = list(range(B))

    def run(name, builder, maps):
        nc = _get_prog(name, builder)
        return run_bass_kernel_spmd(nc, maps, core_ids=cores).results

    maskscale = _dropout_mask()

    p0 = {k[8:]: inp[k] for k in inp if k.startswith("L0_attn_") and k != "L0_attn_x"}
    p0f = {k[7:]: inp[k] for k in inp if k.startswith("L0_ffn_")}
    p1 = {k[8:]: inp[k] for k in inp if k.startswith("L1_attn_")}
    p2 = {k[8:]: inp[k] for k in inp if k.startswith("L2_attn_")}
    p2f = {k[7:]: inp[k] for k in inp if k.startswith("L2_ffn_")}
    p3 = {k[8:]: inp[k] for k in inp if k.startswith("L3_attn_")}
    p3f = {k[7:]: inp[k] for k in inp if k.startswith("L3_ffn_")}

    for li in range(4):
        r = run("ln", prog_layernorm, [{"x": x[b]} for b in range(B)])
        xn = [np.asarray(r[b]["xn"]) for b in range(B)]
        xnb = [np.asarray(r[b]["xnb"]) for b in range(B)]
        if li == 0:
            ra = run("l0a", prog_l0_attn, [{
                "xn": xn[b], "inw": ct(p0["in_proj_W"]),
                "convw": ct(p0["conv_W"][:, 0, :].T), "convb": ct(p0["conv_b"][:, None]),
                "dtw": ct(p0["dt_W"]), "dtb": ct(p0["dt_b"][:, None]),
                "bw": ct(p0["B_W"]), "cw": ct(p0["C_W"]), "alog": ct(p0["A_log"][None, :]),
                "outw": ct(p0["out_proj_W"]), "gatew": ct(p0["gate_W"]),
                "upw": ct(p0["up_W"]), "downw": ct(p0["down_W"]),
                "inormw": ct(p0["input_norm_w"][None, :]),
                "fnormw": ct(p0["ffn_norm_w"][None, :])} for b in range(B)])
            rf = run("l0f", prog_l0_ffn, [{
                "xnb": xnb[b], "rwt": ct(p0f["router_w"].T),
                "rb": ct(p0f["router_b"][None, :]),
                "ewt": ct(p0f["expert_w"].transpose(0, 2, 1)),
                "eb": ct(p0f["expert_b"])} for b in range(B)])
        elif li == 1:
            ra = run("l1a", prog_l1_attn, [{
                "xn": xn[b], "win": ct(p1["W_in"]), "gbias": ct(p1["gate_bias"][:, None]),
                "rmsw": ct(p1["rms_weight"][None, :]), "wout": ct(p1["W_out"]),
                "l1mask": C["l1mask"], "c_ident": C["ident"]} for b in range(B)])
            rf = run("l1f", prog_l1_ffn, [{
                "xn": xn[b], "rhs": ct(inp["L1_ffn_rhs"]),
                "c_ident": C["ident"]} for b in range(B)])
        elif li == 2:
            pmkT = ct(np.concatenate([np.asarray(p2["pm"])[0].transpose(2, 0, 1).reshape(64, 256)] * 2, axis=0))
            pmvT = ct(np.asarray(p2["pm"])[1].transpose(1, 0, 2).reshape(16, 1024))
            ra = run("l2a", prog_l2_attn, [{
                "xn": xn[b], "wqkvt": ct(np.asarray(p2["w_qkv"]).T),
                "wot": ct(np.asarray(p2["w_out"]).T), "gvec": ct(p2["g"][None, :]),
                "cosb": C["cosb"], "sinb": C["sinb"], "maskc": C["maskc"],
                "pmkT": pmkT, "pmvT": pmvT, "c_ident": C["ident"]} for b in range(B)])
            rf = run("l2f", prog_l2_ffn, [{
                "xnb": xnb[b], "ctrl": ct(np.asarray(p2f["controller"]).reshape(1024, 64)),
                "f1": ct(np.asarray(p2f["f1"]).reshape(1024, 4096)),
                "f2": ct(np.asarray(p2f["f2"]).reshape(4096, 1024)),
                "biasT": ct(np.asarray(p2f["bias"]).reshape(4096, 1)),
                "tiecol": C["tiecol"], "c_ident": C["ident"],
                "sel": C["selc"]} for b in range(B)])
        else:
            ra = run("l3a", lambda: prog_l3_attn(ncore=B), [{
                "xn": xn[b], "wqt": ct(np.asarray(p3["Wq"]).T), "wkt": ct(np.asarray(p3["Wk"]).T),
                "wvt": ct(np.asarray(p3["Wv"]).T), "wot": ct(np.asarray(p3["Wo"]).T),
                "c_ident": C["ident"]} for b in range(B)])
            rf = run("l3f", prog_l3_ffn, [{
                "xn": xn[b], "w1t": ct(np.asarray(p3f["W1"]).T),
                "b1": ct(p3f["b1"][None, :]), "w2t": ct(np.asarray(p3f["W2"]).T),
                "b2": ct(p3f["b2"][None, :]), "mask": np.asarray(maskscale[b])} for b in range(B)])
        rc = run("comb", prog_combine, [{
            "x": x[b], "a": np.asarray(ra[b]["out"]), "f": np.asarray(rf[b]["out"])}
            for b in range(B)])
        x = [np.asarray(rc[b]["out"]) for b in range(B)]
    return np.stack(x)


# version 02
# speedup vs baseline: 1.0032x; 1.0032x over previous
import contextlib
import numpy as np

import concourse.bass as bass
import concourse.mybir as mybir
from concourse.bass_utils import run_bass_kernel_spmd
from concourse.alu_op_type import AluOpType as OP

F32 = mybir.dt.float32
BF16 = mybir.dt.bfloat16
I32 = mybir.dt.int32
AF = mybir.ActivationFunctionType
AX = mybir.AxisListType

T = 2048
D = 1024
NCORE = 8
NT = T // 128
LN2 = float(np.log(2.0))


class PB:
    def __init__(self, tag):
        self.tag = tag
        self.nc = bass.Bass("TRN2", target_bir_lowering=False)
        self.es = contextlib.ExitStack()
        self.ops = []
        self.n_sem = 16
        self.sems = [self.es.enter_context(self.nc.semaphore(f"ch{i}"))
                     for i in range(self.n_sem)]
        self._sb_names = {}
        self._cms = []
        self._finalized = False

    def _uniq(self, name):
        k = self._sb_names.get(name, 0)
        self._sb_names[name] = k + 1
        return name if k == 0 else f"{name}_{k}"

    def ccol(self, value):
        key = float(value)
        if not hasattr(self, "_ccols"):
            self._ccols = {}
        if key not in self._ccols:
            t = self.sb(f"cc_{len(self._ccols)}", (128, 1), F32)
            self.G(lambda e: e.memset(t[:, :1], key))
            self._ccols[key] = t
        return self._ccols[key][:, :1]

    def sb(self, name, shape, dt=F32):
        cm = self.nc.sbuf_tensor(self._uniq(name), list(shape), dt)
        t = cm.__enter__()
        self._cms.append((id(t), cm))
        return t

    def sbfree(self, *tiles):
        ids = {id(t) for t in tiles}
        keep = []
        for tid, cm in reversed(self._cms):
            if tid in ids:
                cm.__exit__(None, None, None)
                ids.discard(tid)
            else:
                keep.append((tid, cm))
        self._cms = list(reversed(keep))

    def ps(self, name, shape, dt=F32):
        cm = self.nc.psum_tensor(self._uniq(name), list(shape), dt)
        t = cm.__enter__()
        self._cms.append((id(t), cm))
        return t

    import contextlib as _ctx

    @_ctx.contextmanager
    def scope(self):
        start = len(self._cms)
        ccols_before = dict(getattr(self, "_ccols", {}))
        yield
        for tid, cm in reversed(self._cms[start:]):
            cm.__exit__(None, None, None)
        self._cms = self._cms[:start]
        self._ccols = ccols_before

    def din(self, name, shape, dt=F32):
        return self.nc.dram_tensor(name, list(shape), dt, kind="ExternalInput").ap()

    def dout(self, name, shape, dt=F32):
        return self.nc.dram_tensor(name, list(shape), dt, kind="ExternalOutput").ap()

    def op(self, eng, fn, inc=1, kind=None):
        if kind is None:
            kind = "c" if inc == 1 else ("g" if eng == "gpsimd" else "h")
        self.ops.append((eng, fn, inc, kind))

    def V(self, fn):
        self.op("vector", fn, 1)

    def S(self, fn):
        self.op("scalar", fn, 1)

    def TE(self, fn):
        self.op("tensor", fn, 1)

    def G(self, fn):
        self.op("gpsimd", fn, 1)

    def dma(self, dst, src, q="sync", **kw):
        self.op(q, lambda e: e.dma_start(out=dst, in_=src, **kw), 16)

    def dmaT(self, dst, src, q="scalar"):
        self.op(q, lambda e: e.dma_start(out=dst, in_=src, transpose=True), 16)

    def finalize(self):
        assert not self._finalized
        self._finalized = True
        blk = self.es.enter_context(self.nc.Block())
        ops = self.ops
        n = len(ops)
        sem_idx = []
        rot = [0, 0, 0]
        for eng, fn, inc, kind in ops:
            if kind == "g":
                sem_idx.append(9 + rot[1] % 7)
                rot[1] += 1
            elif kind == "h":
                sem_idx.append(6 + rot[2] % 3)
                rot[2] += 1
            else:
                sem_idx.append(rot[0] % 6)
                rot[0] += 1
        sem_of = [self.sems[j] for j in sem_idx]
        tot = [0] * self.n_sem
        val_after = []
        for i in range(n):
            tot[sem_idx[i]] += ops[i][2]
            val_after.append(tot[sem_idx[i]])
        by_eng = {}
        for i, (eng, fn, inc, kind) in enumerate(ops):
            by_eng.setdefault(eng, []).append(i)
        for eng, idxs in by_eng.items():
            def body(e, idxs=idxs):
                for i in idxs:
                    if i > 0:
                        e.wait_ge(sem_of[i - 1], val_after[i - 1])
                    ops[i][1](e).then_inc(sem_of[i], ops[i][2])
            getattr(blk, eng)(body)
        last = n - 1

        @blk.vector
        def _(e):
            e.wait_ge(sem_of[last], val_after[last])
        for _, cm in reversed(self._cms):
            cm.__exit__(None, None, None)
        self._cms = []
        self.es.close()
        return self.nc

    def run(self, in_maps, core_ids=None):
        nc = self.finalize()
        return run_bass_kernel_spmd(nc, in_maps,
                                    core_ids=core_ids or list(range(len(in_maps))))


def e_matmul_acc(pb, out_ps, lhsT_tiles, rhs_tiles, out_sl=None):
    K = len(lhsT_tiles)
    for k in range(K):
        lt, rt = lhsT_tiles[k], rhs_tiles[k]
        pb.TE(lambda e, lt=lt, rt=rt, s=(k == 0), p=(k == K - 1):
              e.matmul(out_ps if out_sl is None else out_ps, lt, rt,
                       start=s, stop=p))


def e_round_rne(pb, out, inp, itile):
    pb.V(lambda e: e.tensor_copy(itile, inp))
    pb.V(lambda e: e.tensor_copy(out, itile))


def make_constants():
    cons = {}
    cons["c_ident_bf16"] = np.eye(128, dtype=np.float32)
    cons["c_ones_row"] = np.ones((1, 128), np.float32)
    return cons


def host_prep(inp):
    pre = {}
    f = np.float32
    pre["x0"] = np.ascontiguousarray(inp["L0_attn_x"], dtype=f)
    pre["l1f_rhs"] = np.ascontiguousarray(inp["L1_ffn_rhs"], dtype=f)
    return pre


def prog_layernorm(pb=None, px="", d_x=None, d_xn=None, d_xnb=None):
    pb = pb or PB("ln")
    d_x = d_x if d_x is not None else pb.din(px + "x", [T, D], F32)
    d_xn = d_xn if d_xn is not None else pb.dout(px + "xn", [T, D], F32)
    d_xnb = d_xnb if d_xnb is not None else pb.dout(px + "xnb", [T, D], BF16)
    xt = pb.sb("xt", (128, D))
    sq = pb.sb("sq", (128, D))
    mcol = pb.sb("mcol", (128, 1))
    vcol = pb.sb("vcol", (128, 1))
    rcol = pb.sb("rcol", (128, 1))
    ot = pb.sb("ot", (128, D))
    ob = pb.sb("ob", (128, D), BF16)
    for i in range(NT):
        sl = slice(i * 128, (i + 1) * 128)
        pb.dma(xt[:, :], d_x[sl, :])
        pb.V(lambda e: e.reduce_sum(mcol[:, :1], xt[:, :], axis=AX.X))
        pb.V(lambda e: e.tensor_scalar(mcol[:, :1], mcol[:, :1], 1.0 / D, None, OP.mult))
        pb.V(lambda e: e.tensor_scalar(ot[:, :], xt[:, :], mcol[:, :1], None, OP.subtract))
        pb.S(lambda e: e.activation(sq[:, :], ot[:, :], AF.Square))
        pb.V(lambda e: e.reduce_sum(vcol[:, :1], sq[:, :], axis=AX.X))
        pb.V(lambda e: e.tensor_scalar(vcol[:, :1], vcol[:, :1], 1.0 / D, 1e-5, OP.mult, OP.add))
        pb.S(lambda e: e.activation(rcol[:, :1], vcol[:, :1], AF.Ln))
        pb.S(lambda e: e.activation(rcol[:, :1], rcol[:, :1], AF.Exp, scale=-0.5))
        pb.V(lambda e: e.tensor_scalar(ot[:, :], ot[:, :], rcol[:, :1], None, OP.mult))
        pb.V(lambda e: e.tensor_copy(ob[:, :], ot[:, :]))
        pb.dma(d_xn[sl, :], ot[:, :], q="gpsimd")
        pb.dma(d_xnb[sl, :], ob[:, :], q="gpsimd")
    return pb


def prog_l1_ffn(pb=None, px="", d_xn=None, d_out=None):
    pb = pb or PB("l1f")
    d_xn = d_xn if d_xn is not None else pb.din(px + "xn", [T, D], F32)
    d_rhs = pb.din(px + "rhs", [D, D], F32)
    d_out = d_out if d_out is not None else pb.dout(px + "out", [T, D], F32)

    rt = [pb.sb(f"rt{i}", (128, D)) for i in range(8)]
    for i in range(8):
        pb.dma(rt[i][:, :], d_rhs[i * 128:(i + 1) * 128, :])
    am = pb.sb("am", (128, D))
    abst = pb.sb("abst", (128, D))
    pb.S(lambda e: e.activation(am[:, :], rt[0][:, :], AF.Abs))
    for i in range(1, 8):
        pb.S(lambda e, i=i: e.activation(abst[:, :], rt[i][:, :], AF.Abs))
        pb.V(lambda e: e.tensor_tensor(am[:, :], am[:, :], abst[:, :], OP.max))
    ident = pb.sb("ident", (128, 128))
    d_id = pb.din(px + "c_ident", [128, 128], F32)
    pb.dma(ident[:, :], d_id)
    ptr = pb.ps("ptr", (128, 128))
    colmax = pb.sb("colmax", (128, 8))
    for j in range(8):
        pb.TE(lambda e, j=j: e.transpose(ptr[:, :], am[:, j * 128:(j + 1) * 128], ident[:, :]))
        pb.V(lambda e, j=j: e.tensor_reduce(colmax[:, j:j + 1], ptr[:, :], axis=AX.X, op=OP.max))
    rsc = pb.sb("rsc", (128, 8))
    rin = pb.sb("rin", (128, 8))
    pb.V(lambda e: e.tensor_scalar(rsc[:, :8], colmax[:, :8], 1e-12, 1.0 / 127.0, OP.max, OP.mult))
    pb.V(lambda e: e.reciprocal(rin[:, :8], rsc[:, :8]))
    ones1 = pb.sb("ones1", (1, 128))
    pb.G(lambda e: e.memset(ones1[:, :], 1.0))
    rinT = pb.ps("rinT", (128, 128))
    pb.TE(lambda e: e.transpose(rinT[:8, :], rin[:, :8], ident[:, :]))
    rinTs = pb.sb("rinTs", (8, 128))
    pb.V(lambda e: e.tensor_copy(rinTs[:, :], rinT[:8, :]))
    rowflat = pb.sb("rowflat", (1, D))
    for j in range(8):
        pb.dma(rowflat[0:1, j * 128:(j + 1) * 128], rinTs[j:j + 1, :], q="gpsimd")
    binst = pb.ps("binst", (128, 512))
    rinv_full = pb.sb("rinv_full", (128, D))
    for j in range(2):
        pb.TE(lambda e, j=j: e.matmul(binst[:, :], ones1[:1, :], rowflat[0:1, j * 512:(j + 1) * 512],
                                      start=True, stop=True))
        pb.V(lambda e, j=j: e.tensor_copy(rinv_full[:, j * 512:(j + 1) * 512], binst[:, :]))
    qr = [pb.sb(f"qr{i}", (128, D), BF16) for i in range(8)]
    it = pb.sb("it", (128, D), I32)
    tmp = pb.sb("tmp", (128, D))
    for i in range(8):
        pb.V(lambda e, i=i: e.tensor_tensor(tmp[:, :], rt[i][:, :], rinv_full[:, :], OP.mult))
        pb.V(lambda e: e.tensor_copy(it[:, :], tmp[:, :]))
        pb.V(lambda e, i=i: e.tensor_copy(qr[i][:, :], it[:, :]))
    rscTs = pb.sb("rscTs", (8, 128))
    rscT = pb.ps("rscT", (128, 128))
    pb.TE(lambda e: e.transpose(rscT[:8, :], rsc[:, :8], ident[:, :]))
    pb.V(lambda e: e.tensor_copy(rscTs[:, :], rscT[:8, :]))
    rowflat2 = pb.sb("rowflat2", (1, D))
    for j in range(8):
        pb.dma(rowflat2[0:1, j * 128:(j + 1) * 128], rscTs[j:j + 1, :], q="gpsimd")
    rsc_full = pb.sb("rsc_full", (128, D))
    for j in range(2):
        pb.TE(lambda e, j=j: e.matmul(binst[:, :], ones1[:1, :], rowflat2[0:1, j * 512:(j + 1) * 512],
                                      start=True, stop=True))
        pb.V(lambda e, j=j: e.tensor_copy(rsc_full[:, j * 512:(j + 1) * 512], binst[:, :]))

    xt = pb.sb("xt", (128, D))
    rowm = pb.sb("rowm", (128, 1))
    lsc = pb.sb("lsc", (128, 1))
    linv = pb.sb("linv", (128, 1))
    ql = pb.sb("ql", (128, D), BF16)
    qlT = pb.sb("qlT", (128, D), BF16)
    idb = pb.sb("idb", (128, 128), BF16)
    pb.V(lambda e: e.tensor_copy(idb[:, :], ident[:, :]))
    ptb = pb.ps("ptb", (128, 128), BF16)
    acc = pb.ps("acc", (128, 512))
    ot = pb.sb("ot", (128, D))
    for i in range(NT):
        sl = slice(i * 128, (i + 1) * 128)
        pb.dma(xt[:, :], d_xn[sl, :])
        pb.S(lambda e: e.activation(tmp[:, :], xt[:, :], AF.Abs))
        pb.V(lambda e: e.tensor_reduce(rowm[:, :1], tmp[:, :], axis=AX.X, op=OP.max))
        pb.V(lambda e: e.tensor_scalar(lsc[:, :1], rowm[:, :1], 1e-12, 1.0 / 127.0, OP.max, OP.mult))
        pb.V(lambda e: e.reciprocal(linv[:, :1], lsc[:, :1]))
        pb.V(lambda e: e.tensor_scalar(tmp[:, :], xt[:, :], linv[:, :1], None, OP.mult))
        pb.V(lambda e: e.tensor_copy(it[:, :], tmp[:, :]))
        pb.V(lambda e: e.tensor_copy(ql[:, :], it[:, :]))
        for j in range(8):
            pb.TE(lambda e, j=j: e.transpose(ptb[:, :], ql[:, j * 128:(j + 1) * 128], idb[:, :]))
            pb.V(lambda e, j=j: e.tensor_copy(qlT[:, j * 128:(j + 1) * 128], ptb[:, :]))
        for half in range(2):
            for k in range(8):
                pb.TE(lambda e, k=k, half=half, s=(k == 0), p=(k == 7):
                      e.matmul(acc[:, :], qlT[:, k * 128:(k + 1) * 128],
                               qr[k][:, half * 512:(half + 1) * 512], start=s, stop=p))
            pb.V(lambda e, half=half: e.scalar_tensor_tensor(
                ot[:, half * 512:(half + 1) * 512], acc[:, :], lsc[:, :1],
                rsc_full[:, half * 512:(half + 1) * 512], OP.mult, OP.mult))
        pb.dma(d_out[sl, :], ot[:, :], q="gpsimd")
    return pb





C_L2 = float(0.5 / np.log(2.0))


def e_fq52(pb, outb, x, sc, ebias=2.0, eclamp=-14.0, keep_m=False, relu=False, npart=128):
    B, I, INV, S, V = sc["B"], sc["I"], sc["INV"], sc["S"], sc["V"]
    pb.S(lambda e: e.activation(B, x, AF.Square))
    pb.S(lambda e: e.activation(B, B, AF.Ln))
    pb.V(lambda e: e.tensor_scalar(B, B, C_L2, 0.5, OP.mult, OP.subtract))
    pb.V(lambda e: e.tensor_copy(I, B))
    pb.V(lambda e: e.tensor_copy(B, I))
    pb.V(lambda e: e.tensor_scalar(B, B, eclamp, None, OP.max))
    bp = pb.ccol(ebias * LN2)[:npart, :1]
    bn = pb.ccol(-ebias * LN2)[:npart, :1]
    pb.S(lambda e: e.activation(INV, B, AF.Exp, scale=-LN2, bias=bp))
    pb.S(lambda e: e.activation(S, B, AF.Exp, scale=LN2, bias=bn))
    pb.V(lambda e: e.tensor_tensor(V, x, INV, OP.mult))
    pb.V(lambda e: e.tensor_copy(I, V))
    pb.V(lambda e: e.tensor_copy(V, I))
    if relu:
        pb.V(lambda e: e.tensor_scalar(V, V, 0.0, None, OP.max))
    if keep_m:
        return
    pb.V(lambda e: e.tensor_tensor(outb, V, S, OP.mult))


def e_fq52_of_grid(pb, outb, sc):
    B, I, S, V = sc["B"], sc["I"], sc["S"], sc["V"]
    pb.V(lambda e: e.tensor_scalar(B, V, 1.0 / 32.0, None, OP.mult))
    pb.V(lambda e: e.tensor_copy(I, B))
    pb.V(lambda e: e.tensor_copy(B, I))
    pb.V(lambda e: e.tensor_scalar(S, S, 32.0, None, OP.mult))
    pb.V(lambda e: e.tensor_tensor(outb, B, S, OP.mult))


def prog_l3_ffn(dbg=False, upto=99, pb=None, px="", d_xn=None, d_out=None):
    pb = pb or PB("l3f")
    d_xn = d_xn if d_xn is not None else pb.din(px + "xn", [T, D], F32)
    d_w1t = pb.din(px + "w1t", [D, 4096], F32)
    d_b1 = pb.din(px + "b1", [1, 4096], F32)
    d_w2t = pb.din(px + "w2t", [4096, D], F32)
    d_b2 = pb.din(px + "b2", [1, D], F32)
    d_mask = pb.din(px + "mask", [T, D], BF16)
    d_out = d_out if d_out is not None else pb.dout(px + "out", [T, D], F32)
    nc = pb.nc
    ikind = "ExternalOutput" if dbg else "Internal"
    d_qx = nc.dram_tensor(px + "qx", [T, D], BF16, kind=ikind).ap()
    d_qw1 = nc.dram_tensor(px + "qw1", [D, 4096], BF16, kind=ikind).ap()
    d_qw2 = nc.dram_tensor(px + "qw2", [4096, D], BF16, kind=ikind).ap()
    d_hq = nc.dram_tensor(px + "hq", [T, 4096], BF16, kind=ikind).ap()
    d_y1 = nc.dram_tensor(px + "y1dbg", [T, 4096], F32, kind=ikind).ap() if dbg else None

    FW = 2048
    xt = pb.sb("xt", (128, FW))
    scr = {k: pb.sb("scr" + k, (128, FW)) for k in ["B", "INV", "S", "V"]}
    scr["I"] = pb.sb("scrI", (128, FW), I32)
    ob = pb.sb("ob", (128, FW), BF16)

    def scrv(F):
        return {k: (v[:, :F] if k != "I" else v[:, :F]) for k, v in scr.items()}

    for v in (2.0 * LN2, -2.0 * LN2, 7.0 * LN2, -7.0 * LN2):
        pb.ccol(v)
    for i in range(NT):
        sl = slice(i * 128, (i + 1) * 128)
        pb.dma(xt[:, :D], d_xn[sl, :])
        e_fq52(pb, ob[:, :D], xt[:, :D], scrv(D))
        pb.dma(d_qx[sl, :], ob[:, :D], q="gpsimd")
    if upto < 1:
        d_zzz = pb.dout(px + "zzz", [1, 1], F32)
        pb.dma(d_zzz, ob[:1, :1], q="gpsimd")
        return pb
    for i in range(8):
        sl = slice(i * 128, (i + 1) * 128)
        for hh in range(2):
            cs = slice(hh * 2048, (hh + 1) * 2048)
            pb.dma(xt[:, :], d_w1t[sl, cs])
            e_fq52(pb, ob[:, :], xt[:, :], scrv(2048))
            pb.dma(d_qw1[sl, cs], ob[:, :], q="gpsimd")
    for i in range(32):
        sl = slice(i * 128, (i + 1) * 128)
        pb.dma(xt[:, :D], d_w2t[sl, :])
        e_fq52(pb, ob[:, :D], xt[:, :D], scrv(D))
        pb.dma(d_qw2[sl, :], ob[:, :D], q="gpsimd")
    b1r = pb.sb("b1r", (1, 4096))
    qb1 = pb.sb("qb1", (1, 4096), BF16)
    pb.dma(b1r[:1, :], d_b1)
    for hh in range(2):
        cs = slice(hh * 2048, (hh + 1) * 2048)
        sc1 = {k: v[:1, :2048] for k, v in scr.items()}
        e_fq52(pb, qb1[:1, cs], b1r[:1, cs], sc1, npart=1)
    b2r = pb.sb("b2r", (1, D))
    qb2 = pb.sb("qb2", (1, D), BF16)
    pb.dma(b2r[:1, :], d_b2)
    sc2 = {k: v[:1, :D] for k, v in scr.items()}
    e_fq52(pb, qb2[:1, :], b2r[:1, :], sc2, npart=1)
    if upto < 2:
        d_zzz = pb.dout(px + "zzz", [1, 1], F32)
        pb.dma(d_zzz, ob[:1, :1], q="gpsimd")
        return pb
    ones_b = pb.sb("ones_b", (1, 128), BF16)
    pb.G(lambda e: e.memset(ones_b[:, :], 1.0))
    acc = pb.ps("acc", (128, 512))
    hqt = pb.sb("hqt", (128, 4096), BF16)
    qxT = [pb.sb(f"qxT{k}", (128, T), BF16) for k in range(8)]
    for k in range(8):
        pb.dmaT(qxT[k][:, :], d_qx[:, k * 128:(k + 1) * 128])
    qw1s = [pb.sb(f"qw1s{k}", (128, 4096), BF16) for k in range(8)]
    for k in range(8):
        pb.dma(qw1s[k][:, :], d_qw1[k * 128:(k + 1) * 128, :])
    for i in range(NT):
        tsl = slice(i * 128, (i + 1) * 128)
        for nchunk in range(8):
            nsl = slice(nchunk * 512, (nchunk + 1) * 512)
            for k in range(8):
                pb.TE(lambda e, k=k, tsl=tsl, nsl=nsl, s=(k == 0):
                      e.matmul(acc[:, :], qxT[k][:, tsl], qw1s[k][:, nsl],
                               start=s, stop=False))
            pb.TE(lambda e, nsl=nsl: e.matmul(acc[:, :], ones_b[:1, :], qb1[:1, nsl],
                                              start=False, stop=True))
            s5 = scrv(512)
            if dbg:
                pb.V(lambda e, nsl=nsl: e.tensor_copy(scr["B"][:, :512], acc[:, :]))
                pb.dma(d_y1[tsl, nsl], scr["B"][:, :512], q="gpsimd")
            e_fq52(pb, None, acc[:, :], s5, ebias=7.0, eclamp=-30.0,
                   keep_m=True, relu=True)
            e_fq52_of_grid(pb, hqt[:, nsl], s5)
        pb.dma(d_hq[tsl, :], hqt[:, :], q="gpsimd")
    pb.sbfree(*qw1s)
    pb.sbfree(*qxT)
    hqT = [pb.sb(f"hqT{k}", (128, 1024), BF16) for k in range(32)]
    w2s = pb.sb("w2s", (128, D), BF16)
    mt = pb.sb("mt", (128, D), BF16)
    otile = pb.sb("otile", (128, D))
    acc2 = pb.ps("acc2", (128, 512))
    for half in range(2):
        hsl = slice(half * 1024, (half + 1) * 1024)
        for k in range(32):
            pb.dmaT(hqT[k][:, :], d_hq[hsl, k * 128:(k + 1) * 128])
        for it in range(8):
            i = half * 8 + it
            tsl = slice(i * 128, (i + 1) * 128)
            ttsl = slice(it * 128, (it + 1) * 128)
            for nchunk in range(2):
                nsl = slice(nchunk * 512, (nchunk + 1) * 512)
                for k in range(32):
                    pb.dma(w2s[:, :], d_qw2[k * 128:(k + 1) * 128, :])
                    pb.TE(lambda e, k=k, ttsl=ttsl, nsl=nsl, s=(k == 0):
                          e.matmul(acc2[:, :], hqT[k][:, ttsl], w2s[:, nsl],
                                   start=s, stop=False))
                pb.TE(lambda e, nsl=nsl: e.matmul(acc2[:, :], ones_b[:1, :], qb2[:1, nsl],
                                                  start=False, stop=True))
                s5 = scrv(512)
                e_fq52(pb, None, acc2[:, :], s5, ebias=7.0, eclamp=-30.0, keep_m=True)
                pb.V(lambda e, nsl=nsl, s5=s5: e.tensor_tensor(otile[:, nsl], s5["V"], s5["S"], OP.mult))
            pb.dma(mt[:, :], d_mask[tsl, :])
            pb.V(lambda e: e.tensor_tensor(otile[:, :], otile[:, :], mt[:, :], OP.mult))
            pb.dma(d_out[tsl, :], otile[:, :], q="gpsimd")
    return pb


def prog_l0_ffn(pb=None, px="", d_xnb=None, d_out=None):
    pb = pb or PB("l0f")
    d_xnb = d_xnb if d_xnb is not None else pb.din(px + "xnb", [T, D], BF16)
    d_rwt = pb.din(px + "rwt", [D, 8], F32)
    d_rb = pb.din(px + "rb", [1, 8], F32)
    d_ewt = pb.din(px + "ewt", [8, D, D], F32)
    d_eb = pb.din(px + "eb", [8, D], F32)
    d_out = d_out if d_out is not None else pb.dout(px + "out", [T, D], F32)

    xnT = [pb.sb(f"xnT{k}", (128, T), BF16) for k in range(8)]
    for k in range(8):
        pb.dmaT(xnT[k][:, :], d_xnb[:, k * 128:(k + 1) * 128])
    rwts = pb.sb("rwts", (128, 64), BF16)
    for k in range(8):
        pb.op("gpsimd", lambda e, k=k: e.dma_start(
            out=rwts[:, k * 8:(k + 1) * 8], in_=d_rwt[k * 128:(k + 1) * 128, :]), 16)
    ones1 = pb.sb("ones1", (1, 128))
    pb.G(lambda e: e.memset(ones1[:, :], 1.0))
    rbrow = pb.sb("rbrow", (1, 8))
    pb.dma(rbrow[:1, :], d_rb)
    psb = pb.ps("psb", (128, 128))
    rbb = pb.sb("rbb", (128, 8))
    pb.TE(lambda e: e.matmul(psb[:, :8], ones1[:1, :], rbrow[:1, :8], start=True, stop=True))
    pb.V(lambda e: e.tensor_copy(rbb[:, :8], psb[:, :8]))
    ebrow = pb.sb("ebrow", (8, D))
    pb.dma(ebrow[:8, :], d_eb)
    ebr0 = pb.sb("ebr0", (1, D), BF16)
    onesbf = pb.sb("onesbf", (1, 128), BF16)
    pb.G(lambda e: e.memset(onesbf[:, :], 1.0))

    accp = pb.ps("accp", (128, 512))
    lg = pb.ps("lg", (128, 8))
    lgt = pb.sb("lgt", (128, 8))
    mx = pb.sb("mx", (128, 1))
    w8 = pb.sb("w8", (128, 8))
    den = pb.sb("den", (128, 1))
    pmax = pb.sb("pmax", (128, 1))
    selw = pb.sb("selw", (128, 8))
    running = pb.sb("running", (128, 1))
    mcol = pb.sb("mcol2", (128, 1))
    ew = pb.sb("ew", (128, D), BF16)
    out_t = pb.sb("out_t", (128, D))
    for i in range(NT):
        tsl = slice(i * 128, (i + 1) * 128)
        for k in range(8):
            pb.TE(lambda e, k=k, tsl=tsl, s=(k == 0), p=(k == 7):
                  e.matmul(lg[:, :8], xnT[k][:, tsl], rwts[:, k * 8:(k + 1) * 8],
                           start=s, stop=p))
        pb.V(lambda e: e.tensor_tensor(lgt[:, :8], lg[:, :8], rbb[:, :8], OP.add))
        pb.V(lambda e: e.tensor_reduce(mx[:, :1], lgt[:, :8], axis=AX.X, op=OP.max))
        pb.V(lambda e: e.tensor_scalar(mx[:, :1], mx[:, :1], -1.0, None, OP.mult))
        pb.S(lambda e: e.activation(w8[:, :8], lgt[:, :8], AF.Exp, bias=mx[:, :1],
                                    accum_out=den[:, :1]))
        pb.V(lambda e: e.reciprocal(den[:, :1], den[:, :1]))
        pb.V(lambda e: e.tensor_scalar(w8[:, :8], w8[:, :8], den[:, :1], None, OP.mult))
        pb.V(lambda e: e.tensor_reduce(pmax[:, :1], w8[:, :8], axis=AX.X, op=OP.max))
        pb.G(lambda e: e.memset(running[:, :1], 0.0))
        for ex in range(8):
            pb.V(lambda e, ex=ex: e.tensor_scalar(mcol[:, :1], w8[:, ex:ex + 1],
                                                  pmax[:, :1], None, OP.is_ge))
            pb.V(lambda e, ex=ex: e.scalar_tensor_tensor(
                selw[:, ex:ex + 1], running[:, :1], 1.0, mcol[:, :1], OP.subtract, OP.mult))
            pb.V(lambda e, ex=ex: e.tensor_scalar(selw[:, ex:ex + 1], selw[:, ex:ex + 1],
                                                  -1.0, None, OP.mult))
            pb.V(lambda e, ex=ex: e.tensor_tensor(running[:, :1], running[:, :1],
                                                  selw[:, ex:ex + 1], OP.add))
            pb.V(lambda e, ex=ex: e.tensor_tensor(selw[:, ex:ex + 1], selw[:, ex:ex + 1],
                                                  pmax[:, :1], OP.mult))
        pb.G(lambda e: e.memset(out_t[:, :], 0.0))
        for ex in range(8):
            pb.op("gpsimd", lambda e, ex=ex: e.dma_start(
                out=ebr0[:1, :], in_=ebrow[ex:ex + 1, :]), 16)
            for half in range(2):
                nsl = slice(half * 512, (half + 1) * 512)
                for k in range(8):
                    pb.op("gpsimd", lambda e, ex=ex, k=k: e.dma_start(
                        out=ew[:, :], in_=d_ewt[ex, k * 128:(k + 1) * 128, :]), 16)
                    pb.TE(lambda e, k=k, tsl=tsl, nsl=nsl, s=(k == 0):
                          e.matmul(accp[:, :], xnT[k][:, tsl], ew[:, nsl],
                                   start=s, stop=False))
                pb.TE(lambda e, nsl=nsl: e.matmul(accp[:, :], onesbf[:1, :], ebr0[:1, nsl], start=False, stop=True))
                pb.V(lambda e, ex=ex, nsl=nsl: e.scalar_tensor_tensor(
                    out_t[:, nsl], accp[:, :], selw[:, ex:ex + 1], out_t[:, nsl],
                    OP.mult, OP.add))
        pb.dma(d_out[tsl, :], out_t[:, :], q="gpsimd")
    return pb



def prog_l2_ffn(upto=99, pb=None, px="", d_xnb=None, d_out=None):
    pb = pb or PB("l2f")
    d_xnb = d_xnb if d_xnb is not None else pb.din(px + "xnb", [T, D], BF16)
    d_ctrl = pb.din(px + "ctrl", [D, 64], F32)
    d_f1 = pb.din(px + "f1", [D, 4096], F32)
    d_f2 = pb.din(px + "f2", [4096, D], F32)
    d_biasT = pb.din(px + "biasT", [4096, 1], F32)
    d_tie = pb.din(px + "tiecol", [128, 1], F32)
    d_ident = pb.din(px + "c_ident", [128, 128], F32)
    d_out = d_out if d_out is not None else pb.dout(px + "out", [T, D], F32)
    nc = pb.nc
    d_innerM = nc.dram_tensor(px + "innerM", [4096, T], BF16, kind="Internal").ap()

    xnT = [pb.sb(f"xnT{k}", (128, T), BF16) for k in range(8)]
    for k in range(8):
        pb.dmaT(xnT[k][:, :], d_xnb[:, k * 128:(k + 1) * 128])
    ident = pb.sb("ident", (128, 128))
    pb.dma(ident[:, :], d_ident)
    tie = pb.sb("tie", (128, 1))
    pb.dma(tie[:, :], d_tie)
    ctrls = pb.sb("ctrls", (128, 8 * 64), BF16)
    for k in range(8):
        pb.op("gpsimd", lambda e, k=k: e.dma_start(
            out=ctrls[:, k * 64:(k + 1) * 64], in_=d_ctrl[k * 128:(k + 1) * 128, :]), 16)

    lgp = pb.ps("lgp", (128, 64))
    lgt = pb.sb("lgt", (128, 64))
    ptp = pb.ps("ptp", (128, 128))
    lgT = pb.sb("lgT", (64, 128))
    mx8 = pb.sb("mx8", (64, 8))
    permT = pb.sb("permT", (64, T))
    for i in range(NT):
        tsl = slice(i * 128, (i + 1) * 128)
        for k in range(8):
            pb.TE(lambda e, k=k, tsl=tsl, s=(k == 0), p=(k == 7):
                  e.matmul(lgp[:, :64], xnT[k][:, tsl], ctrls[:, k * 64:(k + 1) * 64],
                           start=s, stop=p))
        pb.V(lambda e: e.tensor_scalar(lgt[:, :64], lgp[:, :64], tie[:, :1], None, OP.add))
        pb.TE(lambda e: e.transpose(ptp[:64, :], lgt[:, :64], ident[:, :]))
        pb.V(lambda e: e.tensor_copy(lgT[:, :], ptp[:64, :128]))
        pb.V(lambda e: e.reduce_max(mx8[:, :8], lgT[:, :].rearrange("p (a b) -> p a b", a=8), axis=AX.X))
        pb.V(lambda e, tsl=tsl: e.tensor_tensor(
            permT[:, tsl].rearrange("p (a b) -> p a b", a=8),
            lgT[:, :].rearrange("p (a b) -> p a b", a=8),
            mx8[:, :8].broadcast_to([64, 8, 16]),
            OP.is_equal))

    if upto < 1:
        d_pt = pb.dout(px + "permTdbg", [64, T], F32)
        pb.dma(d_pt, permT[:, :], q="gpsimd")
        return pb
    d_sel = pb.din(px + "sel", [64, 4096], F32)
    sel64 = pb.sb("sel64", (64, 4096), BF16)
    pb.op("gpsimd", lambda e: e.dma_start(out=sel64[:, :], in_=d_sel), 16)
    permTb = pb.sb("permTb", (64, T), BF16)
    pb.V(lambda e: e.tensor_copy(permTb[:, :], permT[:, :]))
    f1t = pb.sb("f1t", (128, 4096), BF16)
    bcol = pb.sb("bcol", (128, 1))
    pin = pb.ps("pin", (128, 512))
    pbm = pb.ps("pbm", (128, 512))
    inn = pb.sb("inn", (128, T))
    pmb = pb.sb("pmb", (128, T))
    imb = pb.sb("imb", (128, T), BF16)
    for c in range(32):
        csl = slice(c * 128, (c + 1) * 128)
        for k in range(8):
            pb.op("gpsimd", lambda e, k=k, c=c: e.dma_start(
                out=f1t[:, k * 128:(k + 1) * 128],
                in_=d_f1[k * 128:(k + 1) * 128, c * 128:(c + 1) * 128]), 16)
        pb.dma(bcol[:, :], d_biasT[csl, :])
        for th in range(4):
            tfl = slice(th * 512, (th + 1) * 512)
            for k in range(8):
                pb.TE(lambda e, k=k, tfl=tfl, s=(k == 0), p=(k == 7):
                      e.matmul(pin[:, :], f1t[:, k * 128:(k + 1) * 128], xnT[k][:, tfl],
                               start=s, stop=p))
            pb.S(lambda e, tfl=tfl: e.activation(inn[:, tfl], pin[:, :], AF.Relu,
                                                 bias=bcol[:, :1]))
            pb.TE(lambda e, tfl=tfl, csl=csl: e.matmul(
                pbm[:, :], sel64[:64, csl], permTb[:64, tfl], start=True, stop=True))
            pb.V(lambda e, tfl=tfl: e.tensor_copy(pmb[:, tfl], pbm[:, :]))
        pb.V(lambda e: e.tensor_tensor(imb[:, :], inn[:, :], pmb[:, :], OP.mult))
        pb.dma(d_innerM[csl, :], imb[:, :], q="gpsimd")

    if upto < 2:
        d_zz = pb.dout(px + "zz2", [1, 1], F32)
        pb.dma(d_zz, permT[:1, :1], q="gpsimd")
        return pb
    f2t = pb.sb("f2t", (128, D), BF16)
    po = [pb.ps(f"po{h}", (128, 512)) for h in range(2)]
    ot = pb.sb("ot2", (128, D))
    imt = pb.sb("imt", (128, 128), BF16)
    for i in range(NT if upto >= 3 else 1):
        tsl = slice(i * 128, (i + 1) * 128)
        for c in range(32):
            pb.op("gpsimd", lambda e, c=c: e.dma_start(
                out=f2t[:, :], in_=d_f2[c * 128:(c + 1) * 128, :]), 16)
            pb.dma(imt[:, :], d_innerM[c * 128:(c + 1) * 128, tsl])
            for h in range(2):
                pb.TE(lambda e, c=c, h=h, s=(c == 0), p=(c == 31):
                      e.matmul(po[h][:, :], imt[:, :], f2t[:, h * 512:(h + 1) * 512],
                               start=s, stop=p))
        for h in range(2):
            pb.V(lambda e, h=h: e.tensor_copy(ot[:, h * 512:(h + 1) * 512], po[h][:, :]))
        pb.dma(d_out[tsl, :], ot[:, :], q="gpsimd")
    return pb


def prog_l2_attn(dbg=False, pb=None, px="", d_xn=None, d_out=None):
    pb = pb or PB("l2a")
    d_xn = d_xn if d_xn is not None else pb.din(px + "xn", [T, D], F32)
    d_wqkvt = pb.din(px + "wqkvt", [D, 3072], F32)
    d_wo_t = pb.din(px + "wot", [D, D], F32)
    d_g = pb.din(px + "gvec", [1, D], F32)
    d_cos = pb.din(px + "cosb", [T, D], F32)
    d_sin = pb.din(px + "sinb", [T, D], F32)
    d_mask = pb.din(px + "maskc", [512, 512], F32)
    d_pmk = pb.din(px + "pmkT", [128, 256], F32)
    d_pmv = pb.din(px + "pmvT", [16, 1024], F32)
    d_out = d_out if d_out is not None else pb.dout(px + "out", [T, D], F32)
    nc = pb.nc
    ik = "ExternalOutput" if dbg else "Internal"
    d_x2b = nc.dram_tensor(px + "x2b", [T, D], BF16, kind=ik).ap()
    d_q = nc.dram_tensor(px + "qro", [T, D], BF16, kind=ik).ap()
    d_k = nc.dram_tensor(px + "kro", [T, D], BF16, kind=ik).ap()
    d_v = nc.dram_tensor(px + "vt", [T, D], BF16, kind=ik).ap()
    d_ao = nc.dram_tensor(px + "ao", [T, D], BF16, kind=ik).ap()
    d_sc0 = nc.dram_tensor(px + "sc0", [128, 528], F32, kind=ik).ap() if dbg else None
    d_pr0 = nc.dram_tensor(px + "pr0", [128, 528], BF16, kind=ik).ap() if dbg else None

    gb = pb.sb("gb", (1, D))
    pb.dma(gb[:1, :], d_g)
    ones1 = pb.sb("ones1", (1, 128))
    pb.G(lambda e: e.memset(ones1[:, :], 1.0))
    gfull = pb.sb("gfull", (128, D))
    psb = pb.ps("psb", (128, 512))
    for j in range(2):
        pb.TE(lambda e, j=j: e.matmul(psb[:, :], ones1[:1, :], gb[:1, j * 512:(j + 1) * 512],
                                      start=True, stop=True))
        pb.V(lambda e, j=j: e.tensor_copy(gfull[:, j * 512:(j + 1) * 512], psb[:, :]))
    xt = pb.sb("xt", (128, D))
    sq = pb.sb("sq", (128, D))
    ms = pb.sb("ms", (128, 1))
    x2 = pb.sb("x2", (128, D))
    x2bt = pb.sb("x2bt", (128, D), BF16)
    for i in range(NT):
        tsl = slice(i * 128, (i + 1) * 128)
        pb.dma(xt[:, :], d_xn[tsl, :])
        pb.S(lambda e: e.activation(sq[:, :], xt[:, :], AF.Square))
        pb.V(lambda e: e.reduce_sum(ms[:, :1], sq[:, :], axis=AX.X))
        pb.V(lambda e: e.tensor_scalar(ms[:, :1], ms[:, :1], 1.0 / D, 1.1920929e-07, OP.mult, OP.add))
        pb.S(lambda e: e.activation(ms[:, :1], ms[:, :1], AF.Ln))
        pb.S(lambda e: e.activation(ms[:, :1], ms[:, :1], AF.Exp, scale=-0.5))
        pb.V(lambda e: e.tensor_scalar(x2[:, :], xt[:, :], ms[:, :1], None, OP.mult))
        pb.V(lambda e: e.tensor_tensor(x2[:, :], x2[:, :], gfull[:, :], OP.mult))
        pb.V(lambda e: e.tensor_copy(x2bt[:, :], x2[:, :]))
        pb.dma(d_x2b[tsl, :], x2bt[:, :], q="gpsimd")
    x2T = [pb.sb(f"x2T{k}", (128, T), BF16) for k in range(8)]
    for k in range(8):
        pb.dmaT(x2T[k][:, :], d_x2b[:, k * 128:(k + 1) * 128])
    wq = pb.sb("wq", (128, 512), BF16)
    pq = pb.ps("pq", (128, 512))
    qkv = pb.sb("qkv", (128, 3072))
    cost = pb.sb("cost", (128, D))
    sint = pb.sb("sint", (128, D))
    rot = pb.sb("rot", (128, D))
    ro = pb.sb("ro", (128, D))
    rob = pb.sb("rob", (128, D), BF16)
    for i in range(NT):
        tsl = slice(i * 128, (i + 1) * 128)
        for nch in range(6):
            for k in range(8):
                pb.op("gpsimd", lambda e, k=k, nch=nch: e.dma_start(
                    out=wq[:, :], in_=d_wqkvt[k * 128:(k + 1) * 128,
                                              nch * 512:(nch + 1) * 512]), 16)
                pb.TE(lambda e, k=k, tsl=tsl, s=(k == 0), p=(k == 7):
                      e.matmul(pq[:, :], x2T[k][:, tsl], wq[:, :], start=s, stop=p))
            pb.V(lambda e, nch=nch: e.tensor_copy(qkv[:, nch * 512:(nch + 1) * 512], pq[:, :]))
        pb.dma(cost[:, :], d_cos[tsl, :])
        pb.dma(sint[:, :], d_sin[tsl, :])
        for part, dst in ((0, d_q), (1, d_k)):
            qsl = slice(part * D, (part + 1) * D)
            pb.V(lambda e, qsl=qsl: e.tensor_copy(
                rot[:, 0:D:2], qkv[:, qsl][:, 1:D:2]))
            pb.V(lambda e, qsl=qsl: e.tensor_copy(
                rot[:, 1:D:2], qkv[:, qsl][:, 0:D:2]))
            pb.V(lambda e, qsl=qsl: e.tensor_tensor(ro[:, :], qkv[:, qsl], cost[:, :], OP.mult))
            pb.V(lambda e: e.tensor_tensor(rot[:, :], rot[:, :], sint[:, :], OP.mult))
            pb.V(lambda e: e.tensor_tensor(ro[:, :], ro[:, :], rot[:, :], OP.add))
            pb.V(lambda e: e.tensor_copy(rob[:, :], ro[:, :]))
            pb.dma(dst[tsl, :], rob[:, :], q="gpsimd")
        pb.V(lambda e: e.tensor_copy(rob[:, :], qkv[:, 2 * D:3 * D]))
        pb.dma(d_v[tsl, :], rob[:, :], q="gpsimd")


    pmk = pb.sb("pmk", (128, 256), BF16)
    pb.op("gpsimd", lambda e: e.dma_start(out=pmk[:, :], in_=d_pmk), 16)
    pmv = pb.sb("pmv", (16, 1024), BF16)
    pb.op("gpsimd", lambda e: e.dma_start(out=pmv[:, :], in_=d_pmv), 16)
    idb = pb.sb("idb", (128, 128), BF16)
    pb.G(lambda e: e.memset(idb[:, :], 0.0))
    pb.V(lambda e: e.iota(idb[:, :], None)) if False else None
    d_ident = pb.din(px + "c_ident", [128, 128], F32)
    identf = pb.sb("identf", (128, 128))
    pb.dma(identf[:, :], d_ident)
    pb.V(lambda e: e.tensor_copy(idb[:, :], identf[:, :]))
    qT2 = pb.sb("qT2", (128, T), BF16)
    kT2 = pb.sb("kT2", (128, T), BF16)
    vseg = [pb.sb(f"vseg{j}", (128, D), BF16) for j in range(4)]
    mskt = pb.sb("mskt", (128, 512))
    ps_k = pb.ps("ps_k", (128, 512))
    ps_pm = pb.ps("ps_pm", (128, 16))
    sct = pb.sb("sct", (128, 528))
    mxc = pb.sb("mxc", (128, 1))
    den = pb.sb("den", (128, 1))
    prb = pb.sb("prb", (128, 528), BF16)
    ptb = pb.ps("ptb", (128, 128), BF16)
    lhsT = pb.sb("lhsTt", (128, 128 * 5), BF16)
    pov = pb.ps("pov", (128, 64))
    oht = pb.sb("oht", (128, D))
    for seg in range(4):
        ssl = slice(seg * 512, (seg + 1) * 512)
        for j in range(4):
            pb.dma(vseg[j][:, :], d_v[seg * 512 + j * 128:seg * 512 + (j + 1) * 128, :])
        for hp in range(8):
            pb.dmaT(qT2[:, :], d_q[:, hp * 128:(hp + 1) * 128])
            pb.dmaT(kT2[:, :], d_k[:, hp * 128:(hp + 1) * 128])
            for hh in range(2):
                h = hp * 2 + hh
                qsl = slice(hh * 64, (hh + 1) * 64)
                for ti in range(4):
                    trow = seg * 512 + ti * 128
                    pb.TE(lambda e, qsl=qsl, trow=trow, ssl=ssl:
                          e.matmul(ps_k[:, :], qT2[qsl, trow:trow + 128],
                                   kT2[qsl, ssl], start=True, stop=True))
                    pb.TE(lambda e, qsl=qsl, trow=trow, h=h:
                          e.matmul(ps_pm[:, :], qT2[qsl, trow:trow + 128],
                                   pmk[qsl, h * 16:(h + 1) * 16], start=True, stop=True))
                    pb.dma(mskt[:, :], d_mask[ti * 128:(ti + 1) * 128, :])
                    pb.V(lambda e: e.scalar_tensor_tensor(
                        sct[:, 0:512], ps_k[:, :], 0.125, mskt[:, :], OP.mult, OP.add))
                    pb.V(lambda e: e.tensor_scalar(sct[:, 512:528], ps_pm[:, :], 0.125, None, OP.mult))
                    if dbg and seg == 0 and h == 0 and ti == 0:
                        pb.dma(d_sc0, sct[:, :528], q="gpsimd")
                    pb.V(lambda e: e.reduce_max(mxc[:, :1], sct[:, :528], axis=AX.X))
                    pb.V(lambda e: e.tensor_scalar(mxc[:, :1], mxc[:, :1], -1.0, None, OP.mult))
                    pb.S(lambda e: e.activation(prb[:, :528], sct[:, :528], AF.Exp,
                                                bias=mxc[:, :1], accum_out=den[:, :1]))
                    pb.V(lambda e: e.reciprocal(den[:, :1], den[:, :1]))
                    if dbg and seg == 0 and h == 0 and ti == 0:
                        pb.dma(d_pr0, prb[:, :528], q="gpsimd")
                    for c in range(4):
                        pb.TE(lambda e, c=c: e.transpose(
                            ptb[:, :], prb[:, c * 128:(c + 1) * 128], idb[:, :]))
                        pb.V(lambda e, c=c: e.tensor_copy(
                            lhsT[:, c * 128:(c + 1) * 128], ptb[:, :]))
                    pb.TE(lambda e: e.transpose(ptb[:16, :], prb[:, 512:528], idb[:, :]))
                    pb.V(lambda e: e.tensor_copy(lhsT[:16, 512:640], ptb[:16, :128]))
                    for c in range(4):
                        pb.TE(lambda e, c=c, h=h, s=(c == 0):
                              e.matmul(pov[:, :], lhsT[:, c * 128:(c + 1) * 128],
                                       vseg[c][:, h * 64:(h + 1) * 64], start=s, stop=False))
                    pb.TE(lambda e, h=h: e.matmul(pov[:, :], lhsT[:16, 512:640],
                                                  pmv[:16, h * 64:(h + 1) * 64],
                                                  start=False, stop=True))
                    pb.V(lambda e, h=h: e.tensor_scalar(
                        oht[:, h * 64:(h + 1) * 64], pov[:, :], den[:, :1], None, OP.mult))
                    pb.V(lambda e, h=h: e.tensor_copy(rob[:, h * 64:(h + 1) * 64],
                                                      oht[:, h * 64:(h + 1) * 64]))
                    pb.op("gpsimd", lambda e, h=h, trow=trow: e.dma_start(
                        out=d_ao[trow:trow + 128, h * 64:(h + 1) * 64],
                        in_=rob[:, h * 64:(h + 1) * 64]), 16)
    aoT = [pb.sb(f"aoT{k}", (128, T), BF16) for k in range(8)]
    for k in range(8):
        pb.dmaT(aoT[k][:, :], d_ao[:, k * 128:(k + 1) * 128])
    wo = pb.sb("wo", (128, D), BF16)
    ot = pb.sb("otl2", (128, D))
    po = pb.ps("po_l2", (128, 512))
    for i in range(NT):
        tsl = slice(i * 128, (i + 1) * 128)
        for half in range(2):
            nsl = slice(half * 512, (half + 1) * 512)
            for k in range(8):
                pb.op("gpsimd", lambda e, k=k: e.dma_start(
                    out=wo[:, :], in_=d_wo_t[k * 128:(k + 1) * 128, :]), 16)
                pb.TE(lambda e, k=k, tsl=tsl, nsl=nsl, s=(k == 0), p=(k == 7):
                      e.matmul(po[:, :], aoT[k][:, tsl], wo[:, nsl], start=s, stop=p))
            pb.V(lambda e, nsl=nsl: e.tensor_copy(ot[:, nsl], po[:, :]))
        pb.dma(d_out[tsl, :], ot[:, :], q="gpsimd")
    return pb


def prog_l1_attn(dbg=False, pb=None, px="", d_xn=None, d_out=None):
    pb = pb or PB("l1a")
    d_xn = d_xn if d_xn is not None else pb.din(px + "xn", [T, D], F32)
    d_win = pb.din(px + "win", [D, 3328], F32)
    d_gb = pb.din(px + "gbias", [D, 1], F32)
    d_rmsw = pb.din(px + "rmsw", [1, D], F32)
    d_wout = pb.din(px + "wout", [D, D], F32)
    d_mask = pb.din(px + "l1mask", [16 * 128, 32], F32)
    d_ident = pb.din(px + "c_ident", [128, 128], F32)
    d_out = d_out if d_out is not None else pb.dout(px + "out", [T, D], F32)
    nc = pb.nc
    ik = "ExternalOutput" if dbg else "Internal"
    d_n1b = nc.dram_tensor(px + "n1b", [T, D], BF16, kind=ik).ap()
    d_z = nc.dram_tensor(px + "zsig", [D, T], BF16, kind=ik).ap()
    d_xkx = nc.dram_tensor(px + "xkx", [D, T], F32, kind=ik).ap() if dbg else None
    d_aoT = nc.dram_tensor(px + "aoT", [D, T], BF16, kind=ik).ap()

    xt2 = [pb.sb(f"l1x{j}", (128, T), BF16) for j in range(8)]
    gt = [pb.sb(f"l1g{j}", (128, T), BF16) for j in range(8)]
    om = [pb.sb(f"l1om{j}", (128, T), BF16) for j in range(8)]
    qk0 = pb.sb("l1q", (128, T), BF16)
    kk0 = pb.sb("l1k", (128, T), BF16)
    ident = pb.sb("identl1", (128, 128))
    pb.dma(ident[:, :], d_ident)
    idb = pb.sb("idbl1", (128, 128), BF16)
    pb.V(lambda e: e.tensor_copy(idb[:, :], ident[:, :]))
    ones1 = pb.sb("ones1l1", (1, 128))
    pb.G(lambda e: e.memset(ones1[:, :], 1.0))
    onesb = pb.sb("onesbl1", (1, 128), BF16)
    pb.G(lambda e: e.memset(onesb[:, :], 1.0))
    onecol = pb.sb("onecol", (64, 1), BF16)
    pb.G(lambda e: e.memset(onecol[:, :], 1.0))

    rw = pb.sb("rwl1", (1, D))
    pb.dma(rw[:1, :], d_rmsw)
    rwf = pb.sb("rwfl1", (128, D))
    psb = pb.ps("psbl1", (128, 512))
    for j in range(2):
        pb.TE(lambda e, j=j: e.matmul(psb[:, :], ones1[:1, :], rw[:1, j * 512:(j + 1) * 512],
                                      start=True, stop=True))
        pb.V(lambda e, j=j: e.tensor_copy(rwf[:, j * 512:(j + 1) * 512], psb[:, :]))
    xt = pb.sb("xtl1", (128, D))
    sq = pb.sb("sql1", (128, D))
    ms = pb.sb("msl1", (128, 1))
    n1 = pb.sb("n1l1", (128, D))
    n1b = pb.sb("n1bl1", (128, D), BF16)
    for i in range(NT):
        tsl = slice(i * 128, (i + 1) * 128)
        pb.dma(xt[:, :], d_xn[tsl, :])
        pb.S(lambda e: e.activation(sq[:, :], xt[:, :], AF.Square))
        pb.V(lambda e: e.reduce_sum(ms[:, :1], sq[:, :], axis=AX.X))
        pb.V(lambda e: e.tensor_scalar(ms[:, :1], ms[:, :1], 1.0 / D, 1e-6, OP.mult, OP.add))
        pb.S(lambda e: e.activation(ms[:, :1], ms[:, :1], AF.Ln))
        pb.S(lambda e: e.activation(ms[:, :1], ms[:, :1], AF.Exp, scale=-0.5))
        pb.V(lambda e: e.tensor_scalar(n1[:, :], xt[:, :], ms[:, :1], None, OP.mult))
        pb.V(lambda e: e.tensor_tensor(n1[:, :], n1[:, :], rwf[:, :], OP.mult))
        pb.V(lambda e: e.tensor_copy(n1b[:, :], n1[:, :]))
        pb.dma(d_n1b[tsl, :], n1b[:, :], q="gpsimd")

    wt = pb.sb("wtl1", (128, 128), BF16)
    gbc = pb.sb("gbc", (128, 1))
    pp = pb.ps("ppl1", (128, 512))
    ztile = pb.sb("ztileb", (128, T), BF16)
    n1T = [pb.sb(f"n1T{k}", (128, T), BF16) for k in range(8)]
    for k in range(8):
        pb.dmaT(n1T[k][:, :], d_n1b[:, k * 128:(k + 1) * 128])
    for c in range(26):
        if 16 <= c < 24:
            pb.dma(gbc[:, :], d_gb[(c - 16) * 128:(c - 15) * 128, :])
        for tc in range(4):
            tfl = slice(tc * 512, (tc + 1) * 512)
            for k in range(8):
                pb.op("gpsimd", lambda e, k=k, c=c: e.dma_start(
                    out=wt[:, :], in_=d_win[k * 128:(k + 1) * 128, c * 128:(c + 1) * 128]), 16)
                pb.TE(lambda e, k=k, tfl=tfl, s=(k == 0), p=(k == 7):
                      e.matmul(pp[:, :], wt[:, :], n1T[k][:, tfl], start=s, stop=p))
            if c < 8:
                pb.S(lambda e, tfl=tfl: e.activation(ztile[:, tfl], pp[:, :], AF.Sigmoid))
            elif c < 16:
                pb.V(lambda e, c=c, tfl=tfl: e.tensor_copy(xt2[c - 8][:, tfl], pp[:, :]))
            elif c < 24:
                pb.S(lambda e, c=c, tfl=tfl: e.activation(gt[c - 16][:, tfl], pp[:, :],
                                                          AF.Sigmoid, bias=gbc[:, :1]))
                pb.V(lambda e, c=c, tfl=tfl: e.tensor_scalar(
                    om[c - 16][:, tfl], gt[c - 16][:, tfl], -1.0, 1.0, OP.mult, OP.add))
            elif c == 24:
                pb.V(lambda e, tfl=tfl: e.tensor_copy(qk0[:, tfl], pp[:, :]))
            else:
                pb.V(lambda e, tfl=tfl: e.tensor_copy(kk0[:, tfl], pp[:, :]))
        if c < 8:
            pb.dma(d_z[c * 128:(c + 1) * 128, :], ztile[:, :], q="gpsimd")
    pb.sbfree(*n1T)

    at = pb.sb("atl1", (128, T), BF16)
    vt = pb.sb("vtl1", (128, T), BF16)
    sco = pb.sb("scol1", (128, T))
    xkx, xkk, aot = xt2, om, gt
    for j in range(8):
        pb.V(lambda e, j=j: e.tensor_copy(at[:, :], gt[j][:, :]))
        pb.G(lambda e: e.memset(at[:, 0:T:64], 0.0))
        pb.V(lambda e, j=j: e.tensor_tensor(vt[:, :], om[j][:, :], xt2[j][:, :], OP.mult))
        pb.V(lambda e: e.tensor_tensor_scan(sco[:, :], at[:, :], vt[:, :], 0.0, OP.mult, OP.add))
        pb.V(lambda e, j=j: e.tensor_copy(xkx[j][:, :], sco[:, :]))
        pb.V(lambda e, j=j: e.tensor_tensor(vt[:, :], om[j][:, :], kk0[:, :], OP.mult))
        pb.V(lambda e: e.tensor_tensor_scan(sco[:, :], at[:, :], vt[:, :], 0.0, OP.mult, OP.add))
        pb.V(lambda e, j=j: e.tensor_copy(xkk[j][:, :], sco[:, :]))
        if dbg and j == 0:
            pb.dma(d_xkx[0:128, :], xkx[0][:, :], q="gpsimd")
    chk = [pb.sb(f"chk{j}", (128, 32), BF16) for j in range(8)]
    chx = [pb.sb(f"chx{j}", (128, 32), BF16) for j in range(8)]
    for j in range(8):
        pb.V(lambda e, j=j: e.tensor_copy(chk[j][:, :], xkk[j][:, 0:T:64]))
        pb.V(lambda e, j=j: e.tensor_copy(chx[j][:, :], xkx[j][:, 0:T:64]))
    ptc = pb.ps("ptcl1", (128, 64), BF16)
    chxT = [pb.sb(f"chxT{h}", (32, 64), BF16) for h in range(16)]
    chxh = pb.sb("chxh", (64, 32), BF16)
    for h in range(16):
        pb.V(lambda e, h=h: e.tensor_copy(chxh[:, :], chx[h // 2][(h % 2) * 64:(h % 2) * 64 + 64, :]))
        pb.TE(lambda e: e.transpose(ptc[:32, :64], chxh[:, :], idb[:64, :64]))
        pb.V(lambda e, h=h: e.tensor_copy(chxT[h][:, :], ptc[:32, :64]))

    msk = pb.sb("mskl1", (128, 32))
    pgenA = pb.ps("pgenA", (128, 128))
    psc = pgenA[:, 0:32]
    plse = pgenA[:, 32:33]
    prodb = pb.sb("prodb", (64, 128), BF16)
    sct = pb.sb("sctl1", (128, 32))
    lse = pb.sb("lsel1", (128, 1))
    mx = pb.sb("mxl1", (128, 1))
    wst = pb.sb("wstl1", (128, 32), BF16)
    iw = pb.sb("iwl1", (128, 1))
    den = pb.sb("denl1", (128, 1))
    pwT = pb.ps("pwTl1", (128, 128), BF16)
    wT = pb.sb("wTl1", (32, 128), BF16)
    pot = pb.ps("potl1", (64, 128))
    piw = pb.ps("piwl1", (64, 128))
    o1 = pb.sb("o1l1", (64, 128))
    iwTp = pgenA
    iwT = pb.sb("iwTl1", (1, 128))
    denT = pb.sb("denTl1", (1, 128))
    for h in range(16):
        j = h // 2
        rsl = slice((h % 2) * 64, (h % 2) * 64 + 64)
        for i in range(NT):
            tsl = slice(i * 128, (i + 1) * 128)
            pb.TE(lambda e, j=j, rsl=rsl, tsl=tsl: e.matmul(
                psc, qk0[rsl, tsl], chk[j][rsl, :], start=True, stop=True))
            pb.dma(msk[:, :], d_mask[i * 128:(i + 1) * 128, :])
            pb.V(lambda e: e.scalar_tensor_tensor(sct[:, :], psc, 0.125,
                                                  msk[:, :], OP.mult, OP.add))
            pb.V(lambda e, j=j, rsl=rsl, tsl=tsl: e.tensor_tensor(
                prodb[:, :], qk0[rsl, tsl], xkk[j][rsl, tsl], OP.mult))
            pb.TE(lambda e: e.matmul(plse, prodb[:, :], onecol[:, :1],
                                     start=True, stop=True))
            pb.V(lambda e: e.tensor_scalar(lse[:, :1], plse, 0.125, None, OP.mult))
            pb.V(lambda e: e.reduce_max(mx[:, :1], sct[:, :], axis=AX.X))
            pb.V(lambda e: e.tensor_tensor(mx[:, :1], mx[:, :1], lse[:, :1], OP.max))
            pb.V(lambda e: e.tensor_scalar(mx[:, :1], mx[:, :1], -1.0, None, OP.mult))
            pb.S(lambda e: e.activation(wst[:, :], sct[:, :], AF.Exp, bias=mx[:, :1],
                                        accum_out=den[:, :1]))
            pb.S(lambda e: e.activation(iw[:, :1], lse[:, :1], AF.Exp, bias=mx[:, :1]))
            pb.V(lambda e: e.tensor_tensor(den[:, :1], den[:, :1], iw[:, :1], OP.add))
            pb.TE(lambda e: e.transpose(pwT[:32, :], wst[:, :], idb[:, :]))
            pb.V(lambda e: e.tensor_copy(wT[:, :], pwT[:32, :128]))
            pb.TE(lambda e, h=h: e.matmul(pot[:, :], chxT[h][:, :], wT[:, :],
                                          start=True, stop=True))
            pb.V(lambda e: e.tensor_copy(o1[:, :], pot[:, :]))
            pb.TE(lambda e: e.transpose(iwTp[:1, :], iw[:, :1], ident[:, :]))
            pb.V(lambda e: e.tensor_copy(iwT[:1, :], iwTp[:1, :128]))
            pb.V(lambda e: e.reciprocal(denT[:1, :] if False else den[:, :1], den[:, :1])) if False else None
            pb.TE(lambda e: e.matmul(piw[:, :], ones1[:1, :64], iwT[:1, :],
                                     start=True, stop=True))
            pb.V(lambda e, j=j, rsl=rsl, tsl=tsl: e.scalar_tensor_tensor(
                o1[:, :], piw[:, :], 1.0, xkx[j][rsl, tsl], OP.mult, OP.mult))
            pb.V(lambda e: e.tensor_tensor(o1[:, :], pot[:, :], o1[:, :], OP.add))
            pb.V(lambda e: e.reciprocal(den[:, :1], den[:, :1]))
            pb.TE(lambda e: e.transpose(iwTp[:1, :], den[:, :1], ident[:, :]))
            pb.V(lambda e: e.tensor_copy(denT[:1, :], iwTp[:1, :128]))
            pb.TE(lambda e: e.matmul(piw[:, :], ones1[:1, :64], denT[:1, :],
                                     start=True, stop=True))
            pb.V(lambda e: e.scalar_tensor_tensor(o1[:, :], piw[:, :], 1.0, o1[:, :],
                                                  OP.mult, OP.mult))
            pb.V(lambda e, j=j, rsl=rsl, tsl=tsl: e.tensor_copy(aot[j][rsl, tsl], o1[:, :]))
    for j in range(8):
        pb.op("gpsimd", lambda e, j=j: e.dma_start(out=ztile[:, :], in_=d_z[j * 128:(j + 1) * 128, :]), 16)
        pb.V(lambda e, j=j: e.tensor_tensor(aot[j][:, :], aot[j][:, :], ztile[:, :], OP.mult))
        pb.dma(d_aoT[j * 128:(j + 1) * 128, :], aot[j][:, :], q="gpsimd")
    wo = pb.sb("wol1", (128, D), BF16)
    po = pb.ps("pol1", (128, 512))
    ot = pb.sb("otl1", (128, D))
    for i in range(NT):
        tsl = slice(i * 128, (i + 1) * 128)
        for half in range(2):
            nsl = slice(half * 512, (half + 1) * 512)
            for k in range(8):
                pb.op("gpsimd", lambda e, k=k: e.dma_start(
                    out=wo[:, :], in_=d_wout[k * 128:(k + 1) * 128, :]), 16)
                pb.TE(lambda e, k=k, tsl=tsl, nsl=nsl, s=(k == 0), p=(k == 7):
                      e.matmul(po[:, :], aot[k][:, tsl], wo[:, nsl], start=s, stop=p))
            pb.V(lambda e, nsl=nsl: e.tensor_copy(ot[:, nsl], po[:, :]))
        pb.dma(xt[:, :], d_xn[tsl, :])
        pb.V(lambda e: e.tensor_tensor(ot[:, :], ot[:, :], xt[:, :], OP.add))
        pb.dma(d_out[tsl, :], ot[:, :], q="gpsimd")
    return pb


def e_global_scalar_max(pb, dstcol, srctile, F, scr, ident, ones1, pgen):
    pb.V(lambda e: e.reduce_max(scr, srctile, axis=AX.X))
    pb.TE(lambda e: e.transpose(pgen[:1, :], scr, ident))
    pb.V(lambda e: e.tensor_copy(dstcol, pgen[:1, 0:1])) if False else None
    pb.V(lambda e: e.reduce_max(pgen[:1, 0:1] if False else scr[:1, :1], pgen[:1, :128].rearrange("p f -> p f"), axis=AX.X))
    pb.TE(lambda e: e.matmul(pgen[:, 0:1], ones1[:1, :], scr[:1, :1], start=True, stop=True))
    pb.V(lambda e: e.tensor_copy(dstcol, pgen[:, 0:1]))


def prog_l3_attn(ncore=NCORE, dbg=False, pb=None, px="", d_xn=None, d_out=None):
    pb = pb or PB("l3a")
    d_xn = d_xn if d_xn is not None else pb.din(px + "xn", [T, D], F32)
    d_wt = {w: pb.din(px + f"w{w}t", [D, D], F32) for w in "qkvo"}
    d_ident = pb.din(px + "c_ident", [128, 128], F32)
    d_out = d_out if d_out is not None else pb.dout(px + "out", [T, D], F32)
    nc = pb.nc
    ik = "ExternalOutput" if dbg else "Internal"
    d_qx = nc.dram_tensor(px + "qx", [T, D], BF16, kind=ik).ap()
    d_qwt = {w: nc.dram_tensor(px + f"qw{w}", [D, D], BF16, kind="Internal").ap() for w in "qkvo"}
    d_y = {w: nc.dram_tensor(px + f"y{w}", [T, D], F32, kind=ik).ap() for w in "qkv"}
    d_q3 = {w: nc.dram_tensor(px + f"q3{w}", [T, D], BF16, kind=ik).ap() for w in "qkv"}
    d_sc = nc.dram_tensor(px + "scl3", [16 * T, T], F32, kind="Internal").ap()
    d_w6 = nc.dram_tensor(px + "w6l3", [16 * T, T], F32, kind="Internal").ap()
    d_q6 = nc.dram_tensor(px + "q6l3", [16 * T, T], BF16, kind="Internal").ap()
    d_yo = nc.dram_tensor(px + "yol3", [T, D], F32, kind=ik).ap()
    d_at = nc.dram_tensor(px + "atl3", [T, D], BF16, kind="Internal").ap()
    d_yf = nc.dram_tensor(px + "yfl3", [T, D], F32, kind="Internal").ap()
    cc_in = nc.dram_tensor(px + "cc_in", [128, 8], F32, kind="Internal")
    cc_out = nc.dram_tensor(px + "cc_out", [128, 8], F32, kind="Internal")

    ident = pb.sb("identl3", (128, 128))
    pb.dma(ident[:, :], d_ident)
    ones1 = pb.sb("ones1l3", (1, 128))
    pb.G(lambda e: e.memset(ones1[:, :], 1.0))
    pgen = pb.ps("pgenl3", (128, 128))
    scr1 = pb.sb("scr1l3", (128, 1))
    stats = pb.sb("stats", (128, 8))
    pb.G(lambda e: e.memset(stats[:, :], 0.0))
    xt = pb.sb("xtl3", (128, D))
    ab = pb.sb("abl3", (128, 2 * D))
    itl = pb.sb("itl3", (128, D), I32)
    vb = pb.sb("vbl3", (128, D))
    ob = pb.sb("obl3", (128, D), BF16)
    run = pb.sb("runl3", (128, 1))

    def allreduce(cols, op=OP.max):
        pb.dma(cc_in.ap(), stats[:, :8], q="gpsimd")
        pb.op("gpsimd", lambda e: e.collective_compute(
            "AllReduce", op, replica_groups=[list(range(ncore))],
            ins=[cc_in.ap().opt()], outs=[cc_out.ap().opt()]), 1, kind="x")
        pb.op("gpsimd", lambda e: e.dma_start(out=stats[:, :8], in_=cc_out.ap()), 16)

    def fqscale(dst, src):
        pb.V(lambda e: e.tensor_scalar(dst, src, 1.0 / 127.0, 1e-8, OP.mult, OP.max))

    swcol = pb.sb("swcol", (128, 4))
    winv = pb.sb("winv", (128, 1))
    for wi, w in enumerate("qkvo"):
        pb.G(lambda e: e.memset(run[:, :1], 0.0))
        for i in range(8):
            pb.dma(xt[:, :], d_wt[w][i * 128:(i + 1) * 128, :])
            pb.S(lambda e: e.activation(ab[:, :D], xt[:, :], AF.Abs))
            pb.V(lambda e: e.reduce_max(scr1[:, :1], ab[:, :D], axis=AX.X))
            pb.V(lambda e: e.tensor_tensor(run[:, :1], run[:, :1], scr1[:, :1], OP.max))
        pb.TE(lambda e: e.transpose(pgen[:1, :], run[:, :1], ident[:, :]))
        pb.V(lambda e: e.tensor_copy(ab[:1, :128], pgen[:1, :128]))
        pb.V(lambda e: e.reduce_max(scr1[:1, :1], ab[:1, :128], axis=AX.X))
        pb.TE(lambda e: e.matmul(pgen[:, 0:1], ones1[:1, :], scr1[:1, :1], start=True, stop=True))
        pb.V(lambda e, wi=wi: e.tensor_scalar(swcol[:, wi:wi + 1], pgen[:, 0:1],
                                              1.0 / 127.0, 1e-8, OP.mult, OP.max))
        pb.V(lambda e, wi=wi: e.reciprocal(winv[:, :1], swcol[:, wi:wi + 1]))
        for i in range(8):
            pb.dma(xt[:, :], d_wt[w][i * 128:(i + 1) * 128, :])
            pb.V(lambda e: e.tensor_scalar(vb[:, :], xt[:, :], winv[:, :1], None, OP.mult))
            pb.V(lambda e: e.tensor_copy(itl[:, :], vb[:, :]))
            pb.V(lambda e: e.tensor_copy(ob[:, :], itl[:, :]))
            pb.dma(d_qwt[w][i * 128:(i + 1) * 128, :], ob[:, :], q="gpsimd")

    pb.G(lambda e: e.memset(run[:, :1], 0.0))
    for i in range(NT):
        pb.dma(xt[:, :], d_xn[i * 128:(i + 1) * 128, :])
        pb.S(lambda e: e.activation(ab[:, :D], xt[:, :], AF.Abs))
        pb.V(lambda e: e.reduce_max(scr1[:, :1], ab[:, :D], axis=AX.X))
        pb.V(lambda e: e.tensor_tensor(run[:, :1], run[:, :1], scr1[:, :1], OP.max))
    pb.TE(lambda e: e.transpose(pgen[:1, :], run[:, :1], ident[:, :]))
    pb.V(lambda e: e.tensor_copy(ab[:1, :128], pgen[:1, :128]))
    pb.V(lambda e: e.reduce_max(scr1[:1, :1], ab[:1, :128], axis=AX.X))
    pb.TE(lambda e: e.matmul(pgen[:, 0:1], ones1[:1, :], scr1[:1, :1], start=True, stop=True))
    pb.V(lambda e: e.tensor_copy(stats[:, 0:1], pgen[:, 0:1]))
    allreduce(1)
    sx = pb.sb("sxl3", (128, 1))
    sxi = pb.sb("sxil3", (128, 1))
    fqscale(sx[:, :1], stats[:, 0:1])
    pb.V(lambda e: e.reciprocal(sxi[:, :1], sx[:, :1]))
    for i in range(NT):
        pb.dma(xt[:, :], d_xn[i * 128:(i + 1) * 128, :])
        pb.V(lambda e: e.tensor_scalar(vb[:, :], xt[:, :], sxi[:, :1], None, OP.mult))
        pb.V(lambda e: e.tensor_copy(itl[:, :], vb[:, :]))
        pb.V(lambda e: e.tensor_copy(ob[:, :], itl[:, :]))
        pb.dma(d_qx[i * 128:(i + 1) * 128, :], ob[:, :], q="gpsimd")

    wq = pb.sb("wql3", (128, D), BF16)
    pq = pb.ps("pql3", (128, 512))
    sxw = pb.sb("sxwl3", (128, 1))
    ymax = pb.sb("ymaxl3", (128, 3))
    s2c = pb.sb("s2c", (128, 3))
    s3c = pb.sb("s3c", (128, 3))
    r23 = pb.sb("r23", (128, 3))
    i1 = pb.sb("i1l3", (128, 1), I32)
    t1c = pb.sb("t1c", (128, 1))
    s2i = pb.sb("s2i", (128, 3))
    qxT = [pb.sb(f"qxTl3{k}", (128, T), BF16) for k in range(8)]
    for k in range(8):
        pb.dmaT(qxT[k][:, :], d_qx[:, k * 128:(k + 1) * 128])
    for wi, w in enumerate("qkv"):
        pb.V(lambda e, wi=wi: e.tensor_tensor(sxw[:, :1], sx[:, :1], swcol[:, wi:wi + 1], OP.mult))
        pb.G(lambda e: e.memset(run[:, :1], 0.0))
        for i in range(NT):
            tsl = slice(i * 128, (i + 1) * 128)
            for half in range(2):
                nsl = slice(half * 512, (half + 1) * 512)
                for k in range(8):
                    pb.op("gpsimd", lambda e, k=k, w=w: e.dma_start(
                        out=wq[:, :], in_=d_qwt[w][k * 128:(k + 1) * 128, :]), 16)
                    pb.TE(lambda e, k=k, tsl=tsl, nsl=nsl, s=(k == 0), p=(k == 7):
                          e.matmul(pq[:, :], qxT[k][:, tsl], wq[:, nsl], start=s, stop=p))
                pb.V(lambda e, nsl=nsl: e.tensor_scalar(vb[:, nsl], pq[:, :], sxw[:, :1], None, OP.mult))
            pb.S(lambda e: e.activation(ab[:, :D], vb[:, :], AF.Abs))
            pb.V(lambda e: e.reduce_max(scr1[:, :1], ab[:, :D], axis=AX.X))
            pb.V(lambda e: e.tensor_tensor(run[:, :1], run[:, :1], scr1[:, :1], OP.max))
            pb.dma(d_y[w][tsl, :], vb[:, :], q="gpsimd")
        pb.TE(lambda e: e.transpose(pgen[:1, :], run[:, :1], ident[:, :]))
        pb.V(lambda e: e.tensor_copy(ab[:1, :128], pgen[:1, :128]))
        pb.V(lambda e: e.reduce_max(scr1[:1, :1], ab[:1, :128], axis=AX.X))
        pb.TE(lambda e: e.matmul(pgen[:, 0:1], ones1[:1, :], scr1[:1, :1], start=True, stop=True))
        pb.V(lambda e, wi=wi: e.tensor_copy(stats[:, wi:wi + 1], pgen[:, 0:1]))
    allreduce(3)

    for wi in range(3):
        fqscale(s2c[:, wi:wi + 1], stats[:, wi:wi + 1])
        pb.V(lambda e, wi=wi: e.reciprocal(t1c[:, :1], s2c[:, wi:wi + 1]))
        pb.V(lambda e, wi=wi: e.tensor_tensor(t1c[:, :1], stats[:, wi:wi + 1], t1c[:, :1], OP.mult))
        pb.V(lambda e: e.tensor_copy(i1[:, :1], t1c[:, :1]))
        pb.V(lambda e: e.tensor_copy(t1c[:, :1], i1[:, :1]))
        pb.V(lambda e, wi=wi: e.tensor_tensor(t1c[:, :1], t1c[:, :1], s2c[:, wi:wi + 1], OP.mult))
        fqscale(s3c[:, wi:wi + 1], t1c[:, :1])
        pb.V(lambda e, wi=wi: e.reciprocal(r23[:, wi:wi + 1], s3c[:, wi:wi + 1]))
        pb.V(lambda e, wi=wi: e.tensor_tensor(r23[:, wi:wi + 1], s2c[:, wi:wi + 1],
                                              r23[:, wi:wi + 1], OP.mult))
    for wi in range(3):
        pb.V(lambda e, wi=wi: e.reciprocal(s2i[:, wi:wi + 1], s2c[:, wi:wi + 1]))
    for wi, w in enumerate("qkv"):
        for i in range(NT):
            tsl = slice(i * 128, (i + 1) * 128)
            pb.dma(xt[:, :], d_y[w][tsl, :])
            pb.V(lambda e, wi=wi: e.tensor_scalar(vb[:, :], xt[:, :], s2i[:, wi:wi + 1], None, OP.mult))
            pb.V(lambda e: e.tensor_copy(itl[:, :], vb[:, :]))
            pb.V(lambda e: e.tensor_copy(vb[:, :], itl[:, :]))
            pb.V(lambda e, wi=wi: e.tensor_scalar(vb[:, :], vb[:, :], r23[:, wi:wi + 1], None, OP.mult))
            pb.V(lambda e: e.tensor_copy(itl[:, :], vb[:, :]))
            pb.V(lambda e: e.tensor_copy(ob[:, :], itl[:, :]))
            pb.dma(d_q3[w][tsl, :], ob[:, :], q="gpsimd")
    pb.sbfree(*qxT)

    qT2 = pb.sb("qT2l3", (128, T), BF16)
    kT2 = pb.sb("kT2l3", (128, T), BF16)
    sct = pb.sb("sctl3", (128, T))
    rowmx = pb.sb("rowmx", (128, 256))
    denst = pb.sb("denst", (128, 256))
    psck = pb.ps("psckl3", (128, 512))
    pb.G(lambda e: e.memset(run[:, :1], 0.0))
    for hp in range(8):
        pb.dmaT(qT2[:, :], d_q3["q"][:, hp * 128:(hp + 1) * 128])
        pb.dmaT(kT2[:, :], d_q3["k"][:, hp * 128:(hp + 1) * 128])
        for hh in range(2):
            h = hp * 2 + hh
            rsl = slice(hh * 64, hh * 64 + 64)
            for i in range(NT):
                tsl = slice(i * 128, (i + 1) * 128)
                for sc4 in range(4):
                    ssl = slice(sc4 * 512, (sc4 + 1) * 512)
                    pb.TE(lambda e, rsl=rsl, tsl=tsl, ssl=ssl: e.matmul(
                        psck[:, :], qT2[rsl, tsl], kT2[rsl, ssl], start=True, stop=True))
                    pb.V(lambda e, ssl=ssl: e.tensor_copy(sct[:, ssl], psck[:, :]))
                pb.V(lambda e: e.reduce_max(scr1[:, :1], sct[:, :], axis=AX.X))
                pb.V(lambda e, h=h, i=i: e.tensor_copy(rowmx[:, h * 16 + i:h * 16 + i + 1], scr1[:, :1]))
                pb.S(lambda e: e.activation(ab[:1, :1], scr1[:1, :1], AF.Abs)) if False else None
                pb.S(lambda e: e.activation(ab[:, :1024], sct[:, :1024], AF.Abs))
                pb.S(lambda e: e.activation(ab[:, 1024:2048], sct[:, 1024:2048], AF.Abs))
                pb.V(lambda e: e.reduce_max(t1c[:, :1], ab[:, :T], axis=AX.X))
                pb.V(lambda e: e.tensor_tensor(run[:, :1], run[:, :1], t1c[:, :1], OP.max))
                pb.dma(d_sc[h * T + i * 128:h * T + (i + 1) * 128, :], sct[:, :], q="gpsimd")
    pb.TE(lambda e: e.transpose(pgen[:1, :], run[:, :1], ident[:, :]))
    pb.V(lambda e: e.tensor_copy(ab[:1, :128], pgen[:1, :128]))
    pb.V(lambda e: e.reduce_max(scr1[:1, :1], ab[:1, :128], axis=AX.X))
    pb.TE(lambda e: e.matmul(pgen[:, 0:1], ones1[:1, :], scr1[:1, :1], start=True, stop=True))
    pb.V(lambda e: e.tensor_copy(stats[:, 0:1], pgen[:, 0:1]))
    pb.V(lambda e: e.tensor_tensor(stats[:, 0:1], stats[:, 0:1], s3c[:, 0:1], OP.mult))
    pb.V(lambda e: e.tensor_tensor(stats[:, 0:1], stats[:, 0:1], s3c[:, 1:2], OP.mult))
    allreduce(1)
    s5 = pb.sb("s5l3", (128, 1))
    gam = pb.sb("gaml3", (128, 1))
    alp = pb.sb("alpl3", (128, 1))
    fqscale(s5[:, :1], stats[:, 0:1])
    pb.V(lambda e: e.reciprocal(gam[:, :1], s5[:, :1]))
    pb.V(lambda e: e.tensor_tensor(gam[:, :1], gam[:, :1], s3c[:, 0:1], OP.mult))
    pb.V(lambda e: e.tensor_tensor(gam[:, :1], gam[:, :1], s3c[:, 1:2], OP.mult))
    pb.V(lambda e: e.tensor_scalar(alp[:, :1], s5[:, :1], 0.125, None, OP.mult))

    q5mx = pb.sb("q5mx", (128, 1))
    negb = pb.sb("negbl3", (128, 1))
    wtile = pb.sb("wtilel3", (128, T))
    runmin = pb.sb("runminl3", (128, 1))
    pb.G(lambda e: e.memset(runmin[:, :1], 1e30))
    for h in range(16):
        for i in range(NT):
            rr = slice(h * T + i * 128, h * T + (i + 1) * 128)
            pb.dma(sct[:, :], d_sc[rr, :])
            pb.V(lambda e: e.tensor_scalar(sct[:, :], sct[:, :], gam[:, :1], None, OP.mult))
            pb.V(lambda e: e.tensor_copy(itl[:, :], sct[:, :1024]))
            pb.V(lambda e: e.tensor_copy(sct[:, :1024], itl[:, :]))
            pb.V(lambda e: e.tensor_copy(itl[:, :], sct[:, 1024:2048]))
            pb.V(lambda e: e.tensor_copy(sct[:, 1024:2048], itl[:, :]))
            pb.V(lambda e, h=h, i=i: e.tensor_tensor(q5mx[:, :1], rowmx[:, h * 16 + i:h * 16 + i + 1], gam[:, :1], OP.mult))
            pb.V(lambda e: e.tensor_copy(i1[:, :1], q5mx[:, :1]))
            pb.V(lambda e: e.tensor_copy(q5mx[:, :1], i1[:, :1]))
            pb.V(lambda e: e.tensor_tensor(negb[:, :1], q5mx[:, :1], alp[:, :1], OP.mult))
            pb.V(lambda e: e.tensor_scalar(negb[:, :1], negb[:, :1], -1.0, None, OP.mult))
            pb.S(lambda e: e.activation(wtile[:, :], sct[:, :], AF.Exp, bias=negb[:, :1],
                                        scale=alp[:, :1], accum_out=scr1[:, :1]))
            pb.V(lambda e, h=h, i=i: e.tensor_copy(denst[:, h * 16 + i:h * 16 + i + 1], scr1[:, :1]))
            pb.V(lambda e: e.tensor_tensor(runmin[:, :1], runmin[:, :1], scr1[:, :1], OP.min))
            pb.dma(d_w6[rr, :], wtile[:, :], q="gpsimd")
    pb.V(lambda e: e.tensor_scalar(runmin[:, :1], runmin[:, :1], -1.0, None, OP.mult))
    pb.TE(lambda e: e.transpose(pgen[:1, :], runmin[:, :1], ident[:, :]))
    pb.V(lambda e: e.tensor_copy(ab[:1, :128], pgen[:1, :128]))
    pb.V(lambda e: e.reduce_max(scr1[:1, :1], ab[:1, :128], axis=AX.X))
    pb.TE(lambda e: e.matmul(pgen[:, 0:1], ones1[:1, :], scr1[:1, :1], start=True, stop=True))
    pb.V(lambda e: e.tensor_copy(stats[:, 0:1], pgen[:, 0:1]))
    allreduce(1)
    s6 = pb.sb("s6l3", (128, 1))
    pb.V(lambda e: e.tensor_scalar(t1c[:, :1], stats[:, 0:1], -1.0, None, OP.mult))
    pb.V(lambda e: e.reciprocal(t1c[:, :1], t1c[:, :1]))
    fqscale(s6[:, :1], t1c[:, :1])

    q3vt = [pb.sb(f"q3vt{j}", (128, D), BF16) for j in range(NT)]
    for j in range(NT):
        pb.dma(q3vt[j][:, :], d_q3["v"][j * 128:(j + 1) * 128, :])
    beta = pb.sb("betal3", (128, 1))
    for h in range(16):
        for i in range(NT):
            rr = slice(h * T + i * 128, h * T + (i + 1) * 128)
            pb.dma(sct[:, :], d_w6[rr, :])
            pb.V(lambda e, h=h, i=i: e.tensor_tensor(beta[:, :1], denst[:, h * 16 + i:h * 16 + i + 1], s6[:, :1], OP.mult))
            pb.V(lambda e: e.reciprocal(beta[:, :1], beta[:, :1]))
            pb.V(lambda e: e.tensor_scalar(sct[:, :], sct[:, :], beta[:, :1], None, OP.mult))
            pb.V(lambda e: e.tensor_copy(itl[:, :], sct[:, :1024]))
            pb.V(lambda e: e.tensor_copy(sct[:, :1024], itl[:, :]))
            pb.V(lambda e: e.tensor_copy(itl[:, :], sct[:, 1024:2048]))
            pb.V(lambda e: e.tensor_copy(sct[:, 1024:2048], itl[:, :]))
            pb.V(lambda e: e.tensor_copy(wtile[:, :], sct[:, :]))
            pb.dma(d_q6[rr, :], wtile[:, :], q="gpsimd")
    s6v = pb.sb("s6vl3", (128, 1))
    pb.V(lambda e: e.tensor_tensor(s6v[:, :1], s6[:, :1], s3c[:, 2:3], OP.mult))
    q6T = pb.sb("q6Tl3", (128, T), BF16)
    pov = pb.ps("povl3", (128, 64))
    yot = pb.sb("yotl3", (128, D))
    pb.G(lambda e: e.memset(run[:, :1], 0.0))
    for i in range(NT):
        tsl = slice(i * 128, (i + 1) * 128)
        for h in range(16):
            for c in range(NT):
                if i == 0 or True:
                    pass
            pass
        pass
    for h in range(16):
        for c in range(NT):
            pb.dmaT(q6T[:, :] if False else q6T[:, :],
                    d_w6[0:1, 0:128]) if False else None
        pass
    for i in range(NT):
        tsl = slice(i * 128, (i + 1) * 128)
        for h in range(16):
            for c in range(NT):
                pb.op("scalar", lambda e, h=h, c=c, tsl=tsl: e.dma_start(
                    out=q6T[:, tsl.start:tsl.start + 128] if False else q6T[:, 0:128],
                    in_=d_q6[h * T + tsl.start:h * T + tsl.stop, c * 128:(c + 1) * 128],
                    transpose=True), 16)
                pb.TE(lambda e, c=c, h=h, s=(c == 0), p=(c == NT - 1):
                      e.matmul(pov[:, :], q6T[:, 0:128], q3vt[c][:, h * 64:(h + 1) * 64],
                               start=s, stop=p))
            pb.V(lambda e, h=h: e.tensor_scalar(yot[:, h * 64:(h + 1) * 64], pov[:, :],
                                                s6v[:, :1], None, OP.mult))
        pb.S(lambda e: e.activation(ab[:, :D], yot[:, :], AF.Abs))
        pb.V(lambda e: e.reduce_max(scr1[:, :1], ab[:, :D], axis=AX.X))
        pb.V(lambda e: e.tensor_tensor(run[:, :1], run[:, :1], scr1[:, :1], OP.max))
        pb.dma(d_yo[tsl, :], yot[:, :], q="gpsimd")
    pb.TE(lambda e: e.transpose(pgen[:1, :], run[:, :1], ident[:, :]))
    pb.V(lambda e: e.tensor_copy(ab[:1, :128], pgen[:1, :128]))
    pb.V(lambda e: e.reduce_max(scr1[:1, :1], ab[:1, :128], axis=AX.X))
    pb.TE(lambda e: e.matmul(pgen[:, 0:1], ones1[:1, :], scr1[:1, :1], start=True, stop=True))
    pb.V(lambda e: e.tensor_copy(stats[:, 0:1], pgen[:, 0:1]))
    allreduce(1)

    s7 = pb.sb("s7l3", (128, 1))
    s7i = pb.sb("s7il3", (128, 1))
    r78 = pb.sb("r78l3", (128, 1))
    fqscale(s7[:, :1], stats[:, 0:1])
    pb.V(lambda e: e.reciprocal(s7i[:, :1], s7[:, :1]))
    pb.V(lambda e: e.tensor_tensor(t1c[:, :1], stats[:, 0:1], s7i[:, :1], OP.mult))
    pb.V(lambda e: e.tensor_copy(i1[:, :1], t1c[:, :1]))
    pb.V(lambda e: e.tensor_copy(t1c[:, :1], i1[:, :1]))
    pb.V(lambda e: e.tensor_tensor(t1c[:, :1], t1c[:, :1], s7[:, :1], OP.mult))
    fqscale(t1c[:, :1], t1c[:, :1])
    pb.V(lambda e: e.reciprocal(r78[:, :1], t1c[:, :1]))
    pb.V(lambda e: e.tensor_tensor(r78[:, :1], s7[:, :1], r78[:, :1], OP.mult))
    s8w = pb.sb("s8wl3", (128, 1))
    pb.V(lambda e: e.tensor_tensor(s8w[:, :1], t1c[:, :1], swcol[:, 3:4], OP.mult))
    for i in range(NT):
        tsl = slice(i * 128, (i + 1) * 128)
        pb.dma(xt[:, :], d_yo[tsl, :])
        pb.V(lambda e: e.tensor_scalar(vb[:, :], xt[:, :], s7i[:, :1], None, OP.mult))
        pb.V(lambda e: e.tensor_copy(itl[:, :], vb[:, :]))
        pb.V(lambda e: e.tensor_copy(vb[:, :], itl[:, :]))
        pb.V(lambda e: e.tensor_scalar(vb[:, :], vb[:, :], r78[:, :1], None, OP.mult))
        pb.V(lambda e: e.tensor_copy(itl[:, :], vb[:, :]))
        pb.V(lambda e: e.tensor_copy(ob[:, :], itl[:, :]))
        pb.dma(d_at[tsl, :], ob[:, :], q="gpsimd")
    atT = [pb.sb(f"atTl3{k}", (128, T), BF16) for k in range(8)]
    for k in range(8):
        pb.dmaT(atT[k][:, :], d_at[:, k * 128:(k + 1) * 128])
    pb.G(lambda e: e.memset(run[:, :1], 0.0))
    for i in range(NT):
        tsl = slice(i * 128, (i + 1) * 128)
        for half in range(2):
            nsl = slice(half * 512, (half + 1) * 512)
            for k in range(8):
                pb.op("gpsimd", lambda e, k=k: e.dma_start(
                    out=wq[:, :], in_=d_qwt["o"][k * 128:(k + 1) * 128, :]), 16)
                pb.TE(lambda e, k=k, tsl=tsl, nsl=nsl, s=(k == 0), p=(k == 7):
                      e.matmul(pq[:, :], atT[k][:, tsl], wq[:, nsl], start=s, stop=p))
            pb.V(lambda e, nsl=nsl: e.tensor_scalar(vb[:, nsl], pq[:, :], s8w[:, :1], None, OP.mult))
        pb.S(lambda e: e.activation(ab[:, :D], vb[:, :], AF.Abs))
        pb.V(lambda e: e.reduce_max(scr1[:, :1], ab[:, :D], axis=AX.X))
        pb.V(lambda e: e.tensor_tensor(run[:, :1], run[:, :1], scr1[:, :1], OP.max))
        pb.dma(d_yf[tsl, :], vb[:, :], q="gpsimd")
    pb.TE(lambda e: e.transpose(pgen[:1, :], run[:, :1], ident[:, :]))
    pb.V(lambda e: e.tensor_copy(ab[:1, :128], pgen[:1, :128]))
    pb.V(lambda e: e.reduce_max(scr1[:1, :1], ab[:1, :128], axis=AX.X))
    pb.TE(lambda e: e.matmul(pgen[:, 0:1], ones1[:1, :], scr1[:1, :1], start=True, stop=True))
    pb.V(lambda e: e.tensor_copy(stats[:, 0:1], pgen[:, 0:1]))
    allreduce(1)
    sf = pb.sb("sfl3", (128, 1))
    sfi = pb.sb("sfil3", (128, 1))
    fqscale(sf[:, :1], stats[:, 0:1])
    pb.V(lambda e: e.reciprocal(sfi[:, :1], sf[:, :1]))
    for i in range(NT):
        tsl = slice(i * 128, (i + 1) * 128)
        pb.dma(xt[:, :], d_yf[tsl, :])
        pb.V(lambda e: e.tensor_scalar(vb[:, :], xt[:, :], sfi[:, :1], None, OP.mult))
        pb.V(lambda e: e.tensor_copy(itl[:, :], vb[:, :]))
        pb.V(lambda e: e.tensor_copy(vb[:, :], itl[:, :]))
        pb.V(lambda e: e.tensor_scalar(vb[:, :], vb[:, :], sf[:, :1], None, OP.mult))
        pb.dma(d_out[tsl, :], vb[:, :], q="gpsimd")
    return pb


def prog_l0_attn(dbg=False, pb=None, px="", d_xn=None, d_out=None):
    E = 2048
    pb = pb or PB("l0a")
    d_xn = d_xn if d_xn is not None else pb.din(px + "xn", [T, D], F32)
    d_inw = pb.din(px + "inw", [D, 2 * E], F32)
    d_convw = pb.din(px + "convw", [E, 4], F32)
    d_convb = pb.din(px + "convb", [E, 1], F32)
    d_dtw = pb.din(px + "dtw", [E, E], F32)
    d_dtb = pb.din(px + "dtb", [E, 1], F32)
    d_bw = pb.din(px + "bw", [E, 64], F32)
    d_cw = pb.din(px + "cw", [E, 64], F32)
    d_alog = pb.din(px + "alog", [1, 64], F32)
    d_outw = pb.din(px + "outw", [E, D], F32)
    d_gatew = pb.din(px + "gatew", [D, E], F32)
    d_upw = pb.din(px + "upw", [D, E], F32)
    d_downw = pb.din(px + "downw", [E, D], F32)
    d_inormw = pb.din(px + "inormw", [1, D], F32)
    d_fnormw = pb.din(px + "fnormw", [1, D], F32)
    d_out = d_out if d_out is not None else pb.dout(px + "out", [T, D], F32)
    nc = pb.nc
    ik = "ExternalOutput" if dbg else "Internal"
    d_n1b = nc.dram_tensor(px + "n1b0", [T, D], BF16, kind=ik).ap()
    d_xz = nc.dram_tensor(px + "xz0", [E, T], F32, kind=ik).ap()
    d_zs = nc.dram_tensor(px + "zs0", [E, T], BF16, kind=ik).ap()
    d_xc = nc.dram_tensor(px + "xc0", [E, T], BF16, kind=ik).ap()
    d_dt = nc.dram_tensor(px + "dt0", [E, T], BF16, kind=ik).ap()
    d_u = nc.dram_tensor(px + "u0", [E, T], BF16, kind=ik).ap()
    d_y = nc.dram_tensor(px + "y0", [E, T], BF16, kind=ik).ap()
    d_xo = nc.dram_tensor(px + "xo0", [T, D], F32, kind=ik).ap()
    d_n2b = nc.dram_tensor(px + "n2b0", [T, D], BF16, kind=ik).ap()
    d_h = nc.dram_tensor(px + "h0", [E, T], BF16, kind=ik).ap()

    ones1 = pb.sb("ones1l0", (1, 128))
    pb.G(lambda e: e.memset(ones1[:, :], 1.0))
    pbig = pb.ps("pbigl0", (128, 2048))
    psA = pb.ps("psAl0", (128, 512))
    psB = pb.ps("psBl0", (128, 512))
    psb = psA
    rw = pb.sb("rwl0", (1, D))
    pb.dma(rw[:1, :], d_inormw)
    rwf = pb.sb("rwfl0", (128, D))
    for j in range(2):
        pb.TE(lambda e, j=j: e.matmul(psb[:, :], ones1[:1, :], rw[:1, j * 512:(j + 1) * 512],
                                      start=True, stop=True))
        pb.V(lambda e, j=j: e.tensor_copy(rwf[:, j * 512:(j + 1) * 512], psb[:, :]))
    xt = pb.sb("xtl0", (128, D))
    sq = pb.sb("sql0", (128, D))
    ms = pb.sb("msl0", (128, 1))
    n1 = pb.sb("n1l0", (128, D))
    n1b = pb.sb("n1bl0", (128, D), BF16)
    for i in range(NT):
        tsl = slice(i * 128, (i + 1) * 128)
        pb.dma(xt[:, :], d_xn[tsl, :])
        pb.S(lambda e: e.activation(sq[:, :], xt[:, :], AF.Square))
        pb.V(lambda e: e.reduce_sum(ms[:, :1], sq[:, :], axis=AX.X))
        pb.V(lambda e: e.tensor_scalar(ms[:, :1], ms[:, :1], 1.0 / D, 1e-5, OP.mult, OP.add))
        pb.S(lambda e: e.activation(ms[:, :1], ms[:, :1], AF.Ln))
        pb.S(lambda e: e.activation(ms[:, :1], ms[:, :1], AF.Exp, scale=-0.5))
        pb.V(lambda e: e.tensor_scalar(n1[:, :], xt[:, :], ms[:, :1], None, OP.mult))
        pb.V(lambda e: e.tensor_tensor(n1[:, :], n1[:, :], rwf[:, :], OP.mult))
        pb.V(lambda e: e.tensor_copy(n1b[:, :], n1[:, :]))
        pb.dma(d_n1b[tsl, :], n1b[:, :], q="gpsimd")
    wt = pb.sb("wtl0", (128, 128), BF16)
    stile = pb.sb("stileL0", (128, 512))
    sbt = pb.sb("sbtL0", (128, T), BF16)
    n1T = [pb.sb(f"n1T0{k}", (128, T), BF16) for k in range(8)]
    for k in range(8):
        pb.dmaT(n1T[k][:, :], d_n1b[:, k * 128:(k + 1) * 128])
    for c in range(32):
        for tc in range(4):
            tfl = slice(tc * 512, (tc + 1) * 512)
            for k in range(8):
                pb.op("gpsimd", lambda e, k=k, c=c: e.dma_start(
                    out=wt[:, :], in_=d_inw[k * 128:(k + 1) * 128, c * 128:(c + 1) * 128]), 16)
                pb.TE(lambda e, k=k, tfl=tfl, s=(k == 0), p=(k == 7):
                      e.matmul(psb[:, :], wt[:, :], n1T[k][:, tfl], start=s, stop=p))
            if c < 16:
                pb.V(lambda e: e.tensor_copy(stile[:, :], psb[:, :]))
                pb.dma(d_xz[c * 128:(c + 1) * 128, tfl], stile[:, :], q="gpsimd")
            else:
                pb.S(lambda e, tfl=tfl: e.activation(sbt[:, tfl], psb[:, :], AF.Silu))
        if c >= 16:
            pb.dma(d_zs[(c - 16) * 128:(c - 15) * 128, :], sbt[:, :], q="gpsimd")
    pb.sbfree(*n1T)
    BT = pb.sb("BTl0", (64, T), BF16)
    CT = pb.sb("CTl0", (64, T), BF16)
    arow = pb.sb("arowl0", (1, 64))
    afull = pb.sb("afulll0", (128, 64))
    browk = pb.sb("browk", (1, T), BF16)
    crowk = pb.sb("crowk", (1, T), BF16)
    onesbl0 = pb.sb("onesbl0", (1, 128), BF16)
    pb.G(lambda e: e.memset(onesbl0[:, :], 1.0))
    xck = pb.sb("xckl0", (128, T), BF16)
    xzp = pb.sb("xzp", (128, T + 3))
    wcv = pb.sb("wcv", (128, 4))
    cbc = pb.sb("cbc", (128, 1))
    acc = pb.sb("accl0", (128, T))
    for j in range(16):
        esl = slice(j * 128, (j + 1) * 128)
        pb.G(lambda e: e.memset(xzp[:, 0:1], 0.0))
        pb.G(lambda e: e.memset(xzp[:, T + 1:T + 3], 0.0))
        pb.dma(xzp[:, 1:T + 1], d_xz[esl, :])
        pb.dma(wcv[:, :], d_convw[esl, :])
        pb.dma(cbc[:, :], d_convb[esl, :])
        pb.V(lambda e: e.tensor_scalar(acc[:, :], xzp[:, 0:T], wcv[:, 0:1], None, OP.mult))
        for kk in range(1, 4):
            pb.V(lambda e, kk=kk: e.scalar_tensor_tensor(
                acc[:, :], xzp[:, kk:kk + T], wcv[:, kk:kk + 1], acc[:, :], OP.mult, OP.add))
        pb.V(lambda e: e.tensor_scalar(acc[:, :], acc[:, :], cbc[:, :1], None, OP.add))
        pb.S(lambda e: e.activation(acc[:, :], acc[:, :], AF.Silu))
        pb.V(lambda e: e.tensor_copy(sbt[:, :], acc[:, :]))
        pb.dma(d_xc[esl, :], sbt[:, :], q="gpsimd")
    pd = [pbig[:, t_ * 512:(t_ + 1) * 512] for t_ in range(4)]
    dtb = pb.sb("dtbl0", (128, 1))
    dtt = pb.sb("dttl0", (128, T))
    au = pb.sb("aul0", (128, T))
    for j in range(16):
        esl = slice(j * 128, (j + 1) * 128)
        for k in range(16):
            pb.op("gpsimd", lambda e, k=k, j=j: e.dma_start(
                out=wt[:, :], in_=d_dtw[k * 128:(k + 1) * 128, j * 128:(j + 1) * 128]), 16)
            pb.op("gpsimd", lambda e, k=k: e.dma_start(
                out=xck[:, :], in_=d_xc[k * 128:(k + 1) * 128, :]), 16)
            for tc in range(4):
                pb.TE(lambda e, k=k, tc=tc, s=(k == 0), p=(k == 15):
                      e.matmul(pd[tc], wt[:, :], xck[:, tc * 512:(tc + 1) * 512],
                               start=s, stop=p))
        pb.dma(dtb[:, :], d_dtb[esl, :])
        for tc in range(4):
            pb.V(lambda e, tc=tc: e.tensor_scalar(dtt[:, tc * 512:(tc + 1) * 512],
                                                  pd[tc], dtb[:, :1], None, OP.add))
        pb.S(lambda e: e.activation(au[:, :], dtt[:, :], AF.Abs))
        pb.S(lambda e: e.activation(au[:, :], au[:, :], AF.Exp, scale=-1.0))
        pb.S(lambda e: e.activation(au[:, :], au[:, :], AF.Ln, bias=1.0))
        pb.V(lambda e: e.tensor_scalar(dtt[:, :], dtt[:, :], 0.0, None, OP.max))
        pb.V(lambda e: e.tensor_tensor(dtt[:, :], dtt[:, :], au[:, :], OP.add))
        pb.V(lambda e: e.tensor_scalar(dtt[:, :], dtt[:, :], 0.001, 0.1, OP.max, OP.min))
        pb.V(lambda e: e.tensor_copy(sbt[:, :], dtt[:, :]))
        pb.dma(d_dt[esl, :], sbt[:, :], q="gpsimd")
        pb.op("gpsimd", lambda e, esl=esl: e.dma_start(out=xck[:, :], in_=d_xc[esl, :]), 16)
        pb.V(lambda e: e.tensor_tensor(dtt[:, :], dtt[:, :], xck[:, :], OP.mult))
        pb.V(lambda e: e.tensor_copy(sbt[:, :], dtt[:, :]))
        pb.dma(d_u[esl, :], sbt[:, :], q="gpsimd")
    bwt = pb.sb("bwtl0", (128, 128), BF16)
    for (dst, wsrc) in ((BT, d_bw), (CT, d_cw)):
        for tc in range(4):
            for k in range(16):
                pb.op("gpsimd", lambda e, k=k, wsrc=wsrc: e.dma_start(
                    out=bwt[:, 0:64], in_=wsrc[k * 128:(k + 1) * 128, :]), 16)
                pb.op("gpsimd", lambda e, k=k: e.dma_start(
                    out=xck[:, :], in_=d_xc[k * 128:(k + 1) * 128, :]), 16)
                pb.TE(lambda e, k=k, tc=tc, s=(k == 0), p=(k == 15):
                      e.matmul(psb[:64, :], bwt[:, 0:64], xck[:, tc * 512:(tc + 1) * 512],
                               start=s, stop=p))
            pb.V(lambda e, dst=dst, tc=tc: e.tensor_copy(dst[:, tc * 512:(tc + 1) * 512], psb[:64, :]))
    pb.dma(arow[:1, :], d_alog)
    pb.S(lambda e: e.activation(arow[:1, :], arow[:1, :], AF.Exp))
    pb.V(lambda e: e.tensor_scalar(arow[:1, :], arow[:1, :], -1.0, None, OP.mult))
    pab = psA[:, 0:64]
    pb.TE(lambda e: e.matmul(pab, ones1[:1, :], arow[:1, :], start=True, stop=True))
    pb.V(lambda e: e.tensor_copy(afull[:, :], pab))
    pb.sbfree(bwt, au, dtt, dtb, acc, cbc, wcv, xzp)
    pbb = pbig
    Bb = pb.sb("Bbl0", (128, T), BF16)
    Cb = pb.sb("Cbl0", (128, T), BF16)
    dth = [pb.sb(f"dth{j}", (128, T), BF16) for j in range(8)]
    uh = [pb.sb(f"uh{j}", (128, T), BF16) for j in range(8)]
    ya = [pb.sb(f"ya{j}", (128, T)) for j in range(8)]
    sco = pb.sb("scol0", (128, T))
    bt2 = pb.sb("bt2l0", (128, T))
    for half in range(2):
        for j in range(8):
            je = half * 8 + j
            pb.op("gpsimd", lambda e, j=j, je=je: e.dma_start(
                out=dth[j][:, :], in_=d_dt[je * 128:(je + 1) * 128, :]), 16)
            pb.op("gpsimd", lambda e, j=j, je=je: e.dma_start(
                out=uh[j][:, :], in_=d_u[je * 128:(je + 1) * 128, :]), 16)
            pb.G(lambda e, j=j: e.memset(ya[j][:, :], 0.0))
        for n in range(64):
            pb.op("gpsimd", lambda e, n=n: e.dma_start(out=browk[:1, :], in_=BT[n:n + 1, :]), 16)
            pb.op("gpsimd", lambda e, n=n: e.dma_start(out=crowk[:1, :], in_=CT[n:n + 1, :]), 16)
            for tc in range(4):
                pb.TE(lambda e, tc=tc: e.matmul(pbb[:, tc * 512:(tc + 1) * 512], onesbl0[:1, :],
                                                browk[:1, tc * 512:(tc + 1) * 512],
                                                start=True, stop=True))
            pb.V(lambda e: e.tensor_copy(Bb[:, :], pbb[:, :]))
            for tc in range(4):
                pb.TE(lambda e, tc=tc: e.matmul(pbb[:, tc * 512:(tc + 1) * 512], onesbl0[:1, :],
                                                crowk[:1, tc * 512:(tc + 1) * 512],
                                                start=True, stop=True))
            pb.V(lambda e: e.tensor_copy(Cb[:, :], pbb[:, :]))
            for j in range(8):
                pb.V(lambda e, j=j, n=n: e.tensor_scalar(
                    sco[:, :], dth[j][:, :], afull[:, n:n + 1], None, OP.mult))
                pb.S(lambda e: e.activation(sco[:, :], sco[:, :], AF.Exp))
                pb.V(lambda e, j=j: e.tensor_tensor(bt2[:, :], uh[j][:, :], Bb[:, :], OP.mult))
                pb.V(lambda e: e.tensor_tensor_scan(bt2[:, :], sco[:, :], bt2[:, :], 0.0,
                                                    OP.mult, OP.add))
                pb.V(lambda e: e.tensor_tensor(bt2[:, :], bt2[:, :], Cb[:, :], OP.mult))
                pb.V(lambda e, j=j: e.tensor_tensor(ya[j][:, :], ya[j][:, :], bt2[:, :], OP.add))
        for j in range(8):
            je = half * 8 + j
            pb.op("gpsimd", lambda e, je=je: e.dma_start(
                out=xck[:, :], in_=d_zs[je * 128:(je + 1) * 128, :]), 16)
            pb.V(lambda e, j=j: e.tensor_tensor(ya[j][:, :], ya[j][:, :], xck[:, :], OP.mult))
            pb.V(lambda e, j=j: e.tensor_copy(sbt[:, :], ya[j][:, :]))
            pb.dma(d_y[je * 128:(je + 1) * 128, :], sbt[:, :], q="gpsimd")
    pb.sbfree(bt2, sco)
    pb.sbfree(*(list(reversed(ya)) + list(reversed(uh)) + list(reversed(dth))))
    pb.sbfree(Cb, Bb)
    yk = pb.sb("ykl0", (128, 128), BF16)
    wo = pb.sb("wol0", (128, D), BF16)
    po2 = psA
    ot = pb.sb("otl0", (128, D))
    rw2 = pb.sb("rw2l0", (1, D))
    pb.dma(rw2[:1, :], d_fnormw)
    rwf2 = pb.sb("rwf2l0", (128, D))
    for j in range(2):
        pb.TE(lambda e, j=j: e.matmul(psb[:, :], ones1[:1, :], rw2[:1, j * 512:(j + 1) * 512],
                                      start=True, stop=True))
        pb.V(lambda e, j=j: e.tensor_copy(rwf2[:, j * 512:(j + 1) * 512], psb[:, :]))
    for i in range(NT):
        tsl = slice(i * 128, (i + 1) * 128)
        for half in range(2):
            nsl = slice(half * 512, (half + 1) * 512)
            for k in range(16):
                pb.op("gpsimd", lambda e, k=k, tsl=tsl: e.dma_start(
                    out=yk[:, :], in_=d_y[k * 128:(k + 1) * 128, tsl]), 16)
                pb.op("gpsimd", lambda e, k=k: e.dma_start(
                    out=wo[:, :], in_=d_outw[k * 128:(k + 1) * 128, :]), 16)
                pb.TE(lambda e, k=k, nsl=nsl, s=(k == 0), p=(k == 15):
                      e.matmul(po2[:, :], yk[:, :], wo[:, nsl], start=s, stop=p))
            pb.V(lambda e, nsl=nsl: e.tensor_copy(ot[:, nsl], po2[:, :]))
        pb.dma(xt[:, :], d_xn[tsl, :])
        pb.V(lambda e: e.tensor_tensor(ot[:, :], ot[:, :], xt[:, :], OP.add))
        pb.dma(d_xo[tsl, :], ot[:, :], q="gpsimd")
        pb.S(lambda e: e.activation(sq[:, :], ot[:, :], AF.Square))
        pb.V(lambda e: e.reduce_sum(ms[:, :1], sq[:, :], axis=AX.X))
        pb.V(lambda e: e.tensor_scalar(ms[:, :1], ms[:, :1], 1.0 / D, 1e-5, OP.mult, OP.add))
        pb.S(lambda e: e.activation(ms[:, :1], ms[:, :1], AF.Ln))
        pb.S(lambda e: e.activation(ms[:, :1], ms[:, :1], AF.Exp, scale=-0.5))
        pb.V(lambda e: e.tensor_scalar(n1[:, :], ot[:, :], ms[:, :1], None, OP.mult))
        pb.V(lambda e: e.tensor_tensor(n1[:, :], n1[:, :], rwf2[:, :], OP.mult))
        pb.V(lambda e: e.tensor_copy(n1b[:, :], n1[:, :]))
        pb.dma(d_n2b[tsl, :], n1b[:, :], q="gpsimd")
    n2T = [pb.sb(f"n2T0{k}", (128, T), BF16) for k in range(8)]
    for k in range(8):
        pb.dmaT(n2T[k][:, :], d_n2b[:, k * 128:(k + 1) * 128])
    pg = [psA, psB]
    gs = pb.sb("gsl0", (128, T))
    for j in range(16):
        esl = slice(j * 128, (j + 1) * 128)
        for th in range(2):
            tfl = slice(th * 1024, (th + 1) * 1024)
            for k in range(8):
                pb.op("gpsimd", lambda e, k=k, j=j: e.dma_start(
                    out=wt[:, :], in_=d_gatew[k * 128:(k + 1) * 128, j * 128:(j + 1) * 128]), 16)
                for t2 in range(2):
                    pb.TE(lambda e, k=k, th=th, t2=t2, s=(k == 0), p=(k == 7):
                          e.matmul(pg[t2][:, :], wt[:, :],
                                   n1T[0][:, 0:512] if False else n2T[k][:, th * 1024 + t2 * 512:th * 1024 + (t2 + 1) * 512],
                                   start=s, stop=p))
            for t2 in range(2):
                pb.S(lambda e, th=th, t2=t2: e.activation(
                    gs[:, th * 1024 + t2 * 512:th * 1024 + (t2 + 1) * 512], pg[t2][:, :], AF.Silu))
        for th in range(2):
            for k in range(8):
                pb.op("gpsimd", lambda e, k=k, j=j: e.dma_start(
                    out=wt[:, :], in_=d_upw[k * 128:(k + 1) * 128, j * 128:(j + 1) * 128]), 16)
                for t2 in range(2):
                    pb.TE(lambda e, k=k, th=th, t2=t2, s=(k == 0), p=(k == 7):
                          e.matmul(pg[t2][:, :], wt[:, :],
                                   n2T[k][:, th * 1024 + t2 * 512:th * 1024 + (t2 + 1) * 512],
                                   start=s, stop=p))
            for t2 in range(2):
                pb.V(lambda e, th=th, t2=t2: e.scalar_tensor_tensor(
                    sbt[:, th * 1024 + t2 * 512:th * 1024 + (t2 + 1) * 512],
                    pg[t2][:, :], 1.0,
                    gs[:, th * 1024 + t2 * 512:th * 1024 + (t2 + 1) * 512], OP.mult, OP.mult))
        pb.dma(d_h[esl, :], sbt[:, :], q="gpsimd")
    for i in range(NT):
        tsl = slice(i * 128, (i + 1) * 128)
        for half in range(2):
            nsl = slice(half * 512, (half + 1) * 512)
            for k in range(16):
                pb.op("gpsimd", lambda e, k=k, tsl=tsl: e.dma_start(
                    out=yk[:, :], in_=d_h[k * 128:(k + 1) * 128, tsl]), 16)
                pb.op("gpsimd", lambda e, k=k: e.dma_start(
                    out=wo[:, :], in_=d_downw[k * 128:(k + 1) * 128, :]), 16)
                pb.TE(lambda e, k=k, nsl=nsl, s=(k == 0), p=(k == 15):
                      e.matmul(po2[:, :], yk[:, :], wo[:, nsl], start=s, stop=p))
            pb.V(lambda e, nsl=nsl: e.tensor_copy(ot[:, nsl], po2[:, :]))
        pb.dma(xt[:, :], d_xo[tsl, :])
        pb.V(lambda e: e.tensor_tensor(ot[:, :], ot[:, :], xt[:, :], OP.add))
        pb.dma(d_out[tsl, :], ot[:, :], q="gpsimd")
    return pb


def prog_combine(pb=None, px="", d_x=None, d_a=None, d_f=None, d_o=None):
    pb = pb or PB("comb")
    d_x = d_x if d_x is not None else pb.din(px + "x", [T, D], F32)
    d_a = d_a if d_a is not None else pb.din(px + "a", [T, D], F32)
    d_f = d_f if d_f is not None else pb.din(px + "f", [T, D], F32)
    d_o = d_o if d_o is not None else pb.dout(px + "out", [T, D], F32)
    t1 = pb.sb("t1c2", (128, D))
    t2 = pb.sb("t2c2", (128, D))
    t3 = pb.sb("t3c2", (128, D))
    for i in range(NT):
        tsl = slice(i * 128, (i + 1) * 128)
        pb.dma(t1[:, :], d_x[tsl, :])
        pb.dma(t2[:, :], d_a[tsl, :])
        pb.dma(t3[:, :], d_f[tsl, :])
        pb.V(lambda e: e.tensor_tensor(t1[:, :], t1[:, :], t2[:, :], OP.add))
        pb.V(lambda e: e.tensor_tensor(t1[:, :], t1[:, :], t3[:, :], OP.add))
        pb.dma(d_o[tsl, :], t1[:, :], q="gpsimd")
    return pb


_PROGS = {}


def _get_prog(name, builder):
    if name not in _PROGS:
        pb = builder()
        _PROGS[name] = pb.finalize()
    return _PROGS[name]


def _consts():
    c = {}
    c["ident"] = np.eye(128, dtype=np.float32)
    tt_ = np.arange(T)[:, None] // 64
    cc_ = np.arange(32)[None, :]
    c["l1mask"] = np.where(tt_ > cc_, 0.0, -1e30).astype(np.float32)
    dh = 64
    inv = 1.0 / 10000.0 ** (np.arange(0, dh, 2) / dh)
    fr = np.arange(T)[:, None] * inv[None, :]
    fr = np.repeat(fr, 2, axis=-1)
    cos = np.cos(fr).astype(np.float32)
    sin = np.sin(fr).astype(np.float32)
    sgn = np.tile(np.array([-1.0, 1.0], np.float32), dh // 2)
    c["cosb"] = np.tile(cos, (1, 16))
    c["sinb"] = np.tile((sin * sgn).astype(np.float32), (1, 16))
    ii = np.arange(512)[:, None]
    jj = np.arange(512)[None, :]
    c["maskc"] = np.where(jj <= ii, 0.0, -1e30).astype(np.float32)
    c["tiecol"] = np.tile(np.linspace(0, 1e-6, 16).astype(np.float32), 8)[:, None].copy()
    selc = np.zeros((64, 4096), np.float32)
    for p in range(4096):
        selc[p // 64, p] = 1.0
    c["selc"] = selc
    return c


def _dropout_mask():
    import jax
    with jax.default_device(jax.devices("cpu")[0]):
        keep = np.asarray(jax.random.bernoulli(jax.random.key(42), 0.8, (8, T, D)))
    import ml_dtypes
    return (keep.astype(np.float32) * 1.25).astype(ml_dtypes.bfloat16)


def kernel(**inp):
    return kernel_fused(**inp)


def kernel_multilaunch(**inp):
    f = np.float32
    ct = lambda a: np.ascontiguousarray(a, dtype=f)
    C = _consts()
    B = inp["L0_attn_x"].shape[0]
    x = [ct(inp["L0_attn_x"][b]) for b in range(B)]
    cores = list(range(B))

    def run(name, builder, maps):
        nc = _get_prog(name, builder)
        return run_bass_kernel_spmd(nc, maps, core_ids=cores).results

    maskscale = _dropout_mask()

    p0 = {k[8:]: inp[k] for k in inp if k.startswith("L0_attn_") and k != "L0_attn_x"}
    p0f = {k[7:]: inp[k] for k in inp if k.startswith("L0_ffn_")}
    p1 = {k[8:]: inp[k] for k in inp if k.startswith("L1_attn_")}
    p2 = {k[8:]: inp[k] for k in inp if k.startswith("L2_attn_")}
    p2f = {k[7:]: inp[k] for k in inp if k.startswith("L2_ffn_")}
    p3 = {k[8:]: inp[k] for k in inp if k.startswith("L3_attn_")}
    p3f = {k[7:]: inp[k] for k in inp if k.startswith("L3_ffn_")}

    for li in range(4):
        r = run("ln", prog_layernorm, [{"x": x[b]} for b in range(B)])
        xn = [np.asarray(r[b]["xn"]) for b in range(B)]
        xnb = [np.asarray(r[b]["xnb"]) for b in range(B)]
        if li == 0:
            ra = run("l0a", prog_l0_attn, [{
                "xn": xn[b], "inw": ct(p0["in_proj_W"]),
                "convw": ct(p0["conv_W"][:, 0, :].T), "convb": ct(p0["conv_b"][:, None]),
                "dtw": ct(p0["dt_W"]), "dtb": ct(p0["dt_b"][:, None]),
                "bw": ct(p0["B_W"]), "cw": ct(p0["C_W"]), "alog": ct(p0["A_log"][None, :]),
                "outw": ct(p0["out_proj_W"]), "gatew": ct(p0["gate_W"]),
                "upw": ct(p0["up_W"]), "downw": ct(p0["down_W"]),
                "inormw": ct(p0["input_norm_w"][None, :]),
                "fnormw": ct(p0["ffn_norm_w"][None, :])} for b in range(B)])
            rf = run("l0f", prog_l0_ffn, [{
                "xnb": xnb[b], "rwt": ct(p0f["router_w"].T),
                "rb": ct(p0f["router_b"][None, :]),
                "ewt": ct(p0f["expert_w"].transpose(0, 2, 1)),
                "eb": ct(p0f["expert_b"])} for b in range(B)])
        elif li == 1:
            ra = run("l1a", prog_l1_attn, [{
                "xn": xn[b], "win": ct(p1["W_in"]), "gbias": ct(p1["gate_bias"][:, None]),
                "rmsw": ct(p1["rms_weight"][None, :]), "wout": ct(p1["W_out"]),
                "l1mask": C["l1mask"], "c_ident": C["ident"]} for b in range(B)])
            rf = run("l1f", prog_l1_ffn, [{
                "xn": xn[b], "rhs": ct(inp["L1_ffn_rhs"]),
                "c_ident": C["ident"]} for b in range(B)])
        elif li == 2:
            pmkT = ct(np.concatenate([np.asarray(p2["pm"])[0].transpose(2, 0, 1).reshape(64, 256)] * 2, axis=0))
            pmvT = ct(np.asarray(p2["pm"])[1].transpose(1, 0, 2).reshape(16, 1024))
            ra = run("l2a", prog_l2_attn, [{
                "xn": xn[b], "wqkvt": ct(np.asarray(p2["w_qkv"]).T),
                "wot": ct(np.asarray(p2["w_out"]).T), "gvec": ct(p2["g"][None, :]),
                "cosb": C["cosb"], "sinb": C["sinb"], "maskc": C["maskc"],
                "pmkT": pmkT, "pmvT": pmvT, "c_ident": C["ident"]} for b in range(B)])
            rf = run("l2f", prog_l2_ffn, [{
                "xnb": xnb[b], "ctrl": ct(np.asarray(p2f["controller"]).reshape(1024, 64)),
                "f1": ct(np.asarray(p2f["f1"]).reshape(1024, 4096)),
                "f2": ct(np.asarray(p2f["f2"]).reshape(4096, 1024)),
                "biasT": ct(np.asarray(p2f["bias"]).reshape(4096, 1)),
                "tiecol": C["tiecol"], "c_ident": C["ident"],
                "sel": C["selc"]} for b in range(B)])
        else:
            ra = run("l3a", lambda: prog_l3_attn(ncore=B), [{
                "xn": xn[b], "wqt": ct(np.asarray(p3["Wq"]).T), "wkt": ct(np.asarray(p3["Wk"]).T),
                "wvt": ct(np.asarray(p3["Wv"]).T), "wot": ct(np.asarray(p3["Wo"]).T),
                "c_ident": C["ident"]} for b in range(B)])
            rf = run("l3f", prog_l3_ffn, [{
                "xn": xn[b], "w1t": ct(np.asarray(p3f["W1"]).T),
                "b1": ct(p3f["b1"][None, :]), "w2t": ct(np.asarray(p3f["W2"]).T),
                "b2": ct(p3f["b2"][None, :]), "mask": np.asarray(maskscale[b])} for b in range(B)])
        rc = run("comb", prog_combine, [{
            "x": x[b], "a": np.asarray(ra[b]["out"]), "f": np.asarray(rf[b]["out"])}
            for b in range(B)])
        x = [np.asarray(rc[b]["out"]) for b in range(B)]
    return np.stack(x)


def build_fused(B=NCORE):
    pb = PB("mega")
    nc = pb.nc
    d_x0 = pb.din("x0", [T, D], F32)
    cur = d_x0
    d_final = pb.dout("xfinal", [T, D], F32)
    for li in range(4):
        d_xn = nc.dram_tensor(f"xn{li}", [T, D], F32, kind="Internal").ap()
        d_xnb = nc.dram_tensor(f"xnb{li}", [T, D], BF16, kind="Internal").ap()
        with pb.scope():
            prog_layernorm(pb=pb, px=f"ln{li}_", d_x=cur, d_xn=d_xn, d_xnb=d_xnb)
        d_a = nc.dram_tensor(f"abr{li}", [T, D], F32, kind="Internal").ap()
        d_f = nc.dram_tensor(f"fbr{li}", [T, D], F32, kind="Internal").ap()
        if li == 0:
            with pb.scope():
                prog_l0_attn(pb=pb, px="l0a_", d_xn=d_xn, d_out=d_a)
            with pb.scope():
                prog_l0_ffn(pb=pb, px="l0f_", d_xnb=d_xnb, d_out=d_f)
        elif li == 1:
            with pb.scope():
                prog_l1_attn(pb=pb, px="l1a_", d_xn=d_xn, d_out=d_a)
            with pb.scope():
                prog_l1_ffn(pb=pb, px="l1f_", d_xn=d_xn, d_out=d_f)
        elif li == 2:
            with pb.scope():
                prog_l2_attn(pb=pb, px="l2a_", d_xn=d_xn, d_out=d_a)
            with pb.scope():
                prog_l2_ffn(pb=pb, px="l2f_", d_xnb=d_xnb, d_out=d_f)
        else:
            with pb.scope():
                prog_l3_attn(ncore=B, pb=pb, px="l3a_", d_xn=d_xn, d_out=d_a)
            with pb.scope():
                prog_l3_ffn(pb=pb, px="l3f_", d_xn=d_xn, d_out=d_f)
        d_nx = d_final if li == 3 else nc.dram_tensor(f"xres{li + 1}", [T, D], F32, kind="Internal").ap()
        with pb.scope():
            prog_combine(pb=pb, px=f"cb{li}_", d_x=cur, d_a=d_a, d_f=d_f, d_o=d_nx)
        cur = d_nx
    return pb


def fused_in_map(inp, C, maskscale, b):
    f = np.float32
    ct = lambda a: np.ascontiguousarray(a, dtype=f)
    p0 = {k[8:]: inp[k] for k in inp if k.startswith("L0_attn_") and k != "L0_attn_x"}
    p0f = {k[7:]: inp[k] for k in inp if k.startswith("L0_ffn_")}
    p1 = {k[8:]: inp[k] for k in inp if k.startswith("L1_attn_")}
    p2 = {k[8:]: inp[k] for k in inp if k.startswith("L2_attn_")}
    p2f = {k[7:]: inp[k] for k in inp if k.startswith("L2_ffn_")}
    p3 = {k[8:]: inp[k] for k in inp if k.startswith("L3_attn_")}
    p3f = {k[7:]: inp[k] for k in inp if k.startswith("L3_ffn_")}
    pmkT = ct(np.concatenate([np.asarray(p2["pm"])[0].transpose(2, 0, 1).reshape(64, 256)] * 2, axis=0))
    pmvT = ct(np.asarray(p2["pm"])[1].transpose(1, 0, 2).reshape(16, 1024))
    m = {"x0": ct(inp["L0_attn_x"][b])}
    m.update({
        "l0a_inw": ct(p0["in_proj_W"]), "l0a_convw": ct(p0["conv_W"][:, 0, :].T),
        "l0a_convb": ct(p0["conv_b"][:, None]), "l0a_dtw": ct(p0["dt_W"]),
        "l0a_dtb": ct(p0["dt_b"][:, None]), "l0a_bw": ct(p0["B_W"]), "l0a_cw": ct(p0["C_W"]),
        "l0a_alog": ct(p0["A_log"][None, :]), "l0a_outw": ct(p0["out_proj_W"]),
        "l0a_gatew": ct(p0["gate_W"]), "l0a_upw": ct(p0["up_W"]), "l0a_downw": ct(p0["down_W"]),
        "l0a_inormw": ct(p0["input_norm_w"][None, :]), "l0a_fnormw": ct(p0["ffn_norm_w"][None, :]),
        "l0f_rwt": ct(np.asarray(p0f["router_w"]).T), "l0f_rb": ct(p0f["router_b"][None, :]),
        "l0f_ewt": ct(np.asarray(p0f["expert_w"]).transpose(0, 2, 1)), "l0f_eb": ct(p0f["expert_b"]),
        "l1a_win": ct(p1["W_in"]), "l1a_gbias": ct(p1["gate_bias"][:, None]),
        "l1a_rmsw": ct(p1["rms_weight"][None, :]), "l1a_wout": ct(p1["W_out"]),
        "l1a_l1mask": C["l1mask"], "l1a_c_ident": C["ident"],
        "l1f_rhs": ct(inp["L1_ffn_rhs"]), "l1f_c_ident": C["ident"],
        "l2a_wqkvt": ct(np.asarray(p2["w_qkv"]).T), "l2a_wot": ct(np.asarray(p2["w_out"]).T),
        "l2a_gvec": ct(p2["g"][None, :]), "l2a_cosb": C["cosb"], "l2a_sinb": C["sinb"],
        "l2a_maskc": C["maskc"], "l2a_pmkT": pmkT, "l2a_pmvT": pmvT, "l2a_c_ident": C["ident"],
        "l2f_ctrl": ct(np.asarray(p2f["controller"]).reshape(1024, 64)),
        "l2f_f1": ct(np.asarray(p2f["f1"]).reshape(1024, 4096)),
        "l2f_f2": ct(np.asarray(p2f["f2"]).reshape(4096, 1024)),
        "l2f_biasT": ct(np.asarray(p2f["bias"]).reshape(4096, 1)),
        "l2f_tiecol": C["tiecol"], "l2f_c_ident": C["ident"], "l2f_sel": C["selc"],
        "l3a_wqt": ct(np.asarray(p3["Wq"]).T), "l3a_wkt": ct(np.asarray(p3["Wk"]).T),
        "l3a_wvt": ct(np.asarray(p3["Wv"]).T), "l3a_wot": ct(np.asarray(p3["Wo"]).T),
        "l3a_c_ident": C["ident"],
        "l3f_w1t": ct(np.asarray(p3f["W1"]).T), "l3f_b1": ct(p3f["b1"][None, :]),
        "l3f_w2t": ct(np.asarray(p3f["W2"]).T), "l3f_b2": ct(p3f["b2"][None, :]),
        "l3f_mask": np.asarray(maskscale[b]),
    })
    return m


_FUSED = {}


def kernel_fused(**inp):
    B = inp["L0_attn_x"].shape[0]
    if B not in _FUSED:
        _FUSED[B] = build_fused(B).finalize()
    nc = _FUSED[B]
    C = _consts()
    maskscale = _dropout_mask()
    maps = [fused_in_map(inp, C, maskscale, b) for b in range(B)]
    res = run_bass_kernel_spmd(nc, maps, core_ids=list(range(B)))
    return np.stack([np.asarray(res.results[b]["xfinal"]) for b in range(B)])


# version 05
# speedup vs baseline: 1.2406x; 1.2365x over previous
import contextlib
import os
import numpy as np

import concourse.bass as bass
import concourse.mybir as mybir
from concourse.bass_utils import run_bass_kernel_spmd
from concourse.alu_op_type import AluOpType as OP

F32 = mybir.dt.float32
BF16 = mybir.dt.bfloat16
I32 = mybir.dt.int32
AF = mybir.ActivationFunctionType
AX = mybir.AxisListType

T = 2048
D = 1024
NCORE = 8
NT = T // 128
LN2 = float(np.log(2.0))


class PB:
    def __init__(self, tag):
        self.tag = tag
        self.nc = bass.Bass("TRN2", target_bir_lowering=False)
        self.es = contextlib.ExitStack()
        self.ops = []
        self.n_sem = 16
        self.sems = [self.es.enter_context(self.nc.semaphore(f"ch{i}"))
                     for i in range(self.n_sem)]
        self._sb_names = {}
        self._cms = []
        self._finalized = False

    def _uniq(self, name):
        k = self._sb_names.get(name, 0)
        self._sb_names[name] = k + 1
        return name if k == 0 else f"{name}_{k}"

    def ccol(self, value):
        key = float(value)
        if not hasattr(self, "_ccols"):
            self._ccols = {}
        if key not in self._ccols:
            t = self.sb(f"cc_{len(self._ccols)}", (128, 1), F32)
            self.G(lambda e: e.memset(t[:, :1], key))
            self._ccols[key] = t
        return self._ccols[key][:, :1]

    def sb(self, name, shape, dt=F32):
        cm = self.nc.sbuf_tensor(self._uniq(name), list(shape), dt)
        t = cm.__enter__()
        self._cms.append((id(t), cm))
        return t

    def sbfree(self, *tiles):
        ids = {id(t) for t in tiles}
        keep = []
        for tid, cm in reversed(self._cms):
            if tid in ids:
                cm.__exit__(None, None, None)
                ids.discard(tid)
            else:
                keep.append((tid, cm))
        self._cms = list(reversed(keep))

    def ps(self, name, shape, dt=F32):
        cm = self.nc.psum_tensor(self._uniq(name), list(shape), dt)
        t = cm.__enter__()
        self._cms.append((id(t), cm))
        return t

    import contextlib as _ctx

    @_ctx.contextmanager
    def scope(self):
        start = len(self._cms)
        ccols_before = dict(getattr(self, "_ccols", {}))
        yield
        for tid, cm in reversed(self._cms[start:]):
            cm.__exit__(None, None, None)
        self._cms = self._cms[:start]
        self._ccols = ccols_before

    def din(self, name, shape, dt=F32):
        return self.nc.dram_tensor(name, list(shape), dt, kind="ExternalInput").ap()

    def dout(self, name, shape, dt=F32):
        return self.nc.dram_tensor(name, list(shape), dt, kind="ExternalOutput").ap()

    def op(self, eng, fn, inc=1, kind=None):
        if kind is None:
            kind = "c" if inc == 1 else ("g" if eng == "gpsimd" else "h")
        self.ops.append((eng, fn, inc, kind))

    def V(self, fn):
        self.op("vector", fn, 1)

    def S(self, fn):
        self.op("scalar", fn, 1)

    def TE(self, fn):
        self.op("tensor", fn, 1)

    def G(self, fn):
        self.op("gpsimd", fn, 1)

    def dma(self, dst, src, q="sync", **kw):
        self.op(q, lambda e: e.dma_start(out=dst, in_=src, **kw), 16)

    def dmaT(self, dst, src, q="scalar"):
        self.op(q, lambda e: e.dma_start(out=dst, in_=src, transpose=True), 16)

    def finalize(self):
        assert not self._finalized
        self._finalized = True
        blk = self.es.enter_context(self.nc.Block())
        ops = self.ops
        n = len(ops)
        sem_idx = []
        rot = [0, 0, 0]
        for eng, fn, inc, kind in ops:
            if kind == "g":
                sem_idx.append(9 + rot[1] % 7)
                rot[1] += 1
            elif kind == "h":
                sem_idx.append(6 + rot[2] % 3)
                rot[2] += 1
            else:
                sem_idx.append(rot[0] % 6)
                rot[0] += 1
        need_wait = [False] * n
        inc_on = [True] * n
        _EL = os.environ.get('KELIDE', '2')
        EL_C = _EL in ('1', '3')
        EL_D = _EL in ('1', '2')
        for i in range(1, n):
            pe, _, _, pk = ops[i - 1]
            ce, _, _, ck = ops[i]
            if EL_C and pk == "c" and ck == "c" and pe == ce:
                need_wait[i] = False
            elif EL_D and pk in "gh" and pk == ck and pe == ce:
                need_wait[i] = False
            else:
                need_wait[i] = True
        for i in range(n - 1):
            if EL_C and ops[i][3] == "c" and not need_wait[i + 1] and ops[i + 1][3] == "c":
                inc_on[i] = False
        sem_of = [self.sems[j] for j in sem_idx]
        tot = [0] * self.n_sem
        val_after = [0] * n
        last_inc = [-1] * n
        li = -1
        for i in range(n):
            last_inc[i] = li
            if inc_on[i]:
                tot[sem_idx[i]] += ops[i][2]
                val_after[i] = tot[sem_idx[i]]
                li = i
        by_eng = {}
        for i, (eng, fn, inc, kind) in enumerate(ops):
            by_eng.setdefault(eng, []).append(i)
        for eng, idxs in by_eng.items():
            def body(e, idxs=idxs):
                for i in idxs:
                    j = last_inc[i]
                    if need_wait[i] and j >= 0:
                        e.wait_ge(sem_of[j], val_after[j])
                    r = ops[i][1](e)
                    if inc_on[i]:
                        r.then_inc(sem_of[i], ops[i][2])
            getattr(blk, eng)(body)
        @blk.vector
        def _(e):
            for s in range(self.n_sem):
                if tot[s] > 0:
                    e.wait_ge(self.sems[s], tot[s])
        for _, cm in reversed(self._cms):
            cm.__exit__(None, None, None)
        self._cms = []
        self.es.close()
        return self.nc

    def run(self, in_maps, core_ids=None):
        nc = self.finalize()
        return run_bass_kernel_spmd(nc, in_maps,
                                    core_ids=core_ids or list(range(len(in_maps))))


def e_matmul_acc(pb, out_ps, lhsT_tiles, rhs_tiles, out_sl=None):
    K = len(lhsT_tiles)
    for k in range(K):
        lt, rt = lhsT_tiles[k], rhs_tiles[k]
        pb.TE(lambda e, lt=lt, rt=rt, s=(k == 0), p=(k == K - 1):
              e.matmul(out_ps if out_sl is None else out_ps, lt, rt,
                       start=s, stop=p))


def e_round_rne(pb, out, inp, itile):
    pb.V(lambda e: e.tensor_copy(itile, inp))
    pb.V(lambda e: e.tensor_copy(out, itile))


def make_constants():
    cons = {}
    cons["c_ident_bf16"] = np.eye(128, dtype=np.float32)
    cons["c_ones_row"] = np.ones((1, 128), np.float32)
    return cons


def host_prep(inp):
    pre = {}
    f = np.float32
    pre["x0"] = np.ascontiguousarray(inp["L0_attn_x"], dtype=f)
    pre["l1f_rhs"] = np.ascontiguousarray(inp["L1_ffn_rhs"], dtype=f)
    return pre


def prog_layernorm(pb=None, px="", d_x=None, d_xn=None, d_xnb=None):
    pb = pb or PB("ln")
    d_x = d_x if d_x is not None else pb.din(px + "x", [T, D], F32)
    d_xn = d_xn if d_xn is not None else pb.dout(px + "xn", [T, D], F32)
    d_xnb = d_xnb if d_xnb is not None else pb.dout(px + "xnb", [T, D], BF16)
    xt = pb.sb("xt", (128, D))
    sq = pb.sb("sq", (128, D))
    mcol = pb.sb("mcol", (128, 1))
    vcol = pb.sb("vcol", (128, 1))
    rcol = pb.sb("rcol", (128, 1))
    ot = pb.sb("ot", (128, D))
    ob = pb.sb("ob", (128, D), BF16)
    for i in range(NT):
        sl = slice(i * 128, (i + 1) * 128)
        pb.dma(xt[:, :], d_x[sl, :])
        pb.V(lambda e: e.reduce_sum(mcol[:, :1], xt[:, :], axis=AX.X))
        pb.V(lambda e: e.tensor_scalar(mcol[:, :1], mcol[:, :1], 1.0 / D, None, OP.mult))
        pb.V(lambda e: e.tensor_scalar(ot[:, :], xt[:, :], mcol[:, :1], None, OP.subtract))
        pb.S(lambda e: e.activation(sq[:, :], ot[:, :], AF.Square))
        pb.V(lambda e: e.reduce_sum(vcol[:, :1], sq[:, :], axis=AX.X))
        pb.V(lambda e: e.tensor_scalar(vcol[:, :1], vcol[:, :1], 1.0 / D, 1e-5, OP.mult, OP.add))
        pb.S(lambda e: e.activation(rcol[:, :1], vcol[:, :1], AF.Ln))
        pb.S(lambda e: e.activation(rcol[:, :1], rcol[:, :1], AF.Exp, scale=-0.5))
        pb.V(lambda e: e.tensor_scalar(ot[:, :], ot[:, :], rcol[:, :1], None, OP.mult))
        pb.V(lambda e: e.tensor_copy(ob[:, :], ot[:, :]))
        pb.dma(d_xn[sl, :], ot[:, :], q="gpsimd")
        pb.dma(d_xnb[sl, :], ob[:, :], q="gpsimd")
    return pb


def prog_l1_ffn(pb=None, px="", d_xn=None, d_out=None):
    pb = pb or PB("l1f")
    d_xn = d_xn if d_xn is not None else pb.din(px + "xn", [T, D], F32)
    d_rhs = pb.din(px + "rhs", [D, D], F32)
    d_out = d_out if d_out is not None else pb.dout(px + "out", [T, D], F32)

    rt = [pb.sb(f"rt{i}", (128, D)) for i in range(8)]
    for i in range(8):
        pb.dma(rt[i][:, :], d_rhs[i * 128:(i + 1) * 128, :])
    am = pb.sb("am", (128, D))
    abst = pb.sb("abst", (128, D))
    pb.S(lambda e: e.activation(am[:, :], rt[0][:, :], AF.Abs))
    for i in range(1, 8):
        pb.S(lambda e, i=i: e.activation(abst[:, :], rt[i][:, :], AF.Abs))
        pb.V(lambda e: e.tensor_tensor(am[:, :], am[:, :], abst[:, :], OP.max))
    ident = pb.sb("ident", (128, 128))
    d_id = pb.din(px + "c_ident", [128, 128], F32)
    pb.dma(ident[:, :], d_id)
    ptr = pb.ps("ptr", (128, 128))
    colmax = pb.sb("colmax", (128, 8))
    for j in range(8):
        pb.TE(lambda e, j=j: e.transpose(ptr[:, :], am[:, j * 128:(j + 1) * 128], ident[:, :]))
        pb.V(lambda e, j=j: e.tensor_reduce(colmax[:, j:j + 1], ptr[:, :], axis=AX.X, op=OP.max))
    rsc = pb.sb("rsc", (128, 8))
    rin = pb.sb("rin", (128, 8))
    pb.V(lambda e: e.tensor_scalar(rsc[:, :8], colmax[:, :8], 1e-12, 1.0 / 127.0, OP.max, OP.mult))
    pb.V(lambda e: e.reciprocal(rin[:, :8], rsc[:, :8]))
    ones1 = pb.sb("ones1", (1, 128))
    pb.G(lambda e: e.memset(ones1[:, :], 1.0))
    rinT = pb.ps("rinT", (128, 128))
    pb.TE(lambda e: e.transpose(rinT[:8, :], rin[:, :8], ident[:, :]))
    rinTs = pb.sb("rinTs", (8, 128))
    pb.V(lambda e: e.tensor_copy(rinTs[:, :], rinT[:8, :]))
    rowflat = pb.sb("rowflat", (1, D))
    for j in range(8):
        pb.dma(rowflat[0:1, j * 128:(j + 1) * 128], rinTs[j:j + 1, :], q="gpsimd")
    binst = pb.ps("binst", (128, 512))
    rinv_full = pb.sb("rinv_full", (128, D))
    for j in range(2):
        pb.TE(lambda e, j=j: e.matmul(binst[:, :], ones1[:1, :], rowflat[0:1, j * 512:(j + 1) * 512],
                                      start=True, stop=True))
        pb.V(lambda e, j=j: e.tensor_copy(rinv_full[:, j * 512:(j + 1) * 512], binst[:, :]))
    qr = [pb.sb(f"qr{i}", (128, D), BF16) for i in range(8)]
    it = pb.sb("it", (128, D), I32)
    tmp = pb.sb("tmp", (128, D))
    for i in range(8):
        pb.V(lambda e, i=i: e.tensor_tensor(tmp[:, :], rt[i][:, :], rinv_full[:, :], OP.mult))
        pb.V(lambda e: e.tensor_copy(it[:, :], tmp[:, :]))
        pb.V(lambda e, i=i: e.tensor_copy(qr[i][:, :], it[:, :]))
    rscTs = pb.sb("rscTs", (8, 128))
    rscT = pb.ps("rscT", (128, 128))
    pb.TE(lambda e: e.transpose(rscT[:8, :], rsc[:, :8], ident[:, :]))
    pb.V(lambda e: e.tensor_copy(rscTs[:, :], rscT[:8, :]))
    rowflat2 = pb.sb("rowflat2", (1, D))
    for j in range(8):
        pb.dma(rowflat2[0:1, j * 128:(j + 1) * 128], rscTs[j:j + 1, :], q="gpsimd")
    rsc_full = pb.sb("rsc_full", (128, D))
    for j in range(2):
        pb.TE(lambda e, j=j: e.matmul(binst[:, :], ones1[:1, :], rowflat2[0:1, j * 512:(j + 1) * 512],
                                      start=True, stop=True))
        pb.V(lambda e, j=j: e.tensor_copy(rsc_full[:, j * 512:(j + 1) * 512], binst[:, :]))

    xt = pb.sb("xt", (128, D))
    rowm = pb.sb("rowm", (128, 1))
    lsc = pb.sb("lsc", (128, 1))
    linv = pb.sb("linv", (128, 1))
    ql = pb.sb("ql", (128, D), BF16)
    qlT = pb.sb("qlT", (128, D), BF16)
    idb = pb.sb("idb", (128, 128), BF16)
    pb.V(lambda e: e.tensor_copy(idb[:, :], ident[:, :]))
    ptb = pb.ps("ptb", (128, 128), BF16)
    acc = pb.ps("acc", (128, 512))
    ot = pb.sb("ot", (128, D))
    for i in range(NT):
        sl = slice(i * 128, (i + 1) * 128)
        pb.dma(xt[:, :], d_xn[sl, :])
        pb.S(lambda e: e.activation(tmp[:, :], xt[:, :], AF.Abs))
        pb.V(lambda e: e.tensor_reduce(rowm[:, :1], tmp[:, :], axis=AX.X, op=OP.max))
        pb.V(lambda e: e.tensor_scalar(lsc[:, :1], rowm[:, :1], 1e-12, 1.0 / 127.0, OP.max, OP.mult))
        pb.V(lambda e: e.reciprocal(linv[:, :1], lsc[:, :1]))
        pb.V(lambda e: e.tensor_scalar(tmp[:, :], xt[:, :], linv[:, :1], None, OP.mult))
        pb.V(lambda e: e.tensor_copy(it[:, :], tmp[:, :]))
        pb.V(lambda e: e.tensor_copy(ql[:, :], it[:, :]))
        for j in range(8):
            pb.TE(lambda e, j=j: e.transpose(ptb[:, :], ql[:, j * 128:(j + 1) * 128], idb[:, :]))
            pb.V(lambda e, j=j: e.tensor_copy(qlT[:, j * 128:(j + 1) * 128], ptb[:, :]))
        for half in range(2):
            for k in range(8):
                pb.TE(lambda e, k=k, half=half, s=(k == 0), p=(k == 7):
                      e.matmul(acc[:, :], qlT[:, k * 128:(k + 1) * 128],
                               qr[k][:, half * 512:(half + 1) * 512], start=s, stop=p))
            pb.V(lambda e, half=half: e.scalar_tensor_tensor(
                ot[:, half * 512:(half + 1) * 512], acc[:, :], lsc[:, :1],
                rsc_full[:, half * 512:(half + 1) * 512], OP.mult, OP.mult))
        pb.dma(d_out[sl, :], ot[:, :], q="gpsimd")
    return pb





C_L2 = float(0.5 / np.log(2.0))


def e_fq52(pb, outb, x, sc, ebias=2.0, eclamp=-14.0, keep_m=False, relu=False, npart=128):
    B, I, INV, S, V = sc["B"], sc["I"], sc["INV"], sc["S"], sc["V"]
    pb.S(lambda e: e.activation(B, x, AF.Square))
    pb.S(lambda e: e.activation(B, B, AF.Ln))
    pb.V(lambda e: e.tensor_scalar(B, B, C_L2, 0.5, OP.mult, OP.subtract))
    pb.V(lambda e: e.tensor_copy(I, B))
    pb.V(lambda e: e.tensor_copy(B, I))
    pb.V(lambda e: e.tensor_scalar(B, B, eclamp, None, OP.max))
    bp = pb.ccol(ebias * LN2)[:npart, :1]
    bn = pb.ccol(-ebias * LN2)[:npart, :1]
    pb.S(lambda e: e.activation(INV, B, AF.Exp, scale=-LN2, bias=bp))
    pb.S(lambda e: e.activation(S, B, AF.Exp, scale=LN2, bias=bn))
    pb.V(lambda e: e.tensor_tensor(V, x, INV, OP.mult))
    pb.V(lambda e: e.tensor_copy(I, V))
    pb.V(lambda e: e.tensor_copy(V, I))
    if relu:
        pb.V(lambda e: e.tensor_scalar(V, V, 0.0, None, OP.max))
    if keep_m:
        return
    pb.V(lambda e: e.tensor_tensor(outb, V, S, OP.mult))


def e_fq52_of_grid(pb, outb, sc):
    B, I, S, V = sc["B"], sc["I"], sc["S"], sc["V"]
    pb.V(lambda e: e.tensor_scalar(B, V, 1.0 / 32.0, None, OP.mult))
    pb.V(lambda e: e.tensor_copy(I, B))
    pb.V(lambda e: e.tensor_copy(B, I))
    pb.V(lambda e: e.tensor_scalar(S, S, 32.0, None, OP.mult))
    pb.V(lambda e: e.tensor_tensor(outb, B, S, OP.mult))


def prog_l3_ffn(dbg=False, upto=99, pb=None, px="", d_xn=None, d_out=None):
    pb = pb or PB("l3f")
    d_xn = d_xn if d_xn is not None else pb.din(px + "xn", [T, D], F32)
    d_w1t = pb.din(px + "w1t", [D, 4096], F32)
    d_b1 = pb.din(px + "b1", [1, 4096], F32)
    d_w2t = pb.din(px + "w2t", [4096, D], F32)
    d_b2 = pb.din(px + "b2", [1, D], F32)
    d_mask = pb.din(px + "mask", [T, D], BF16)
    d_out = d_out if d_out is not None else pb.dout(px + "out", [T, D], F32)
    nc = pb.nc
    ikind = "ExternalOutput" if dbg else "Internal"
    d_qx = nc.dram_tensor(px + "qx", [T, D], BF16, kind=ikind).ap()
    d_qw1 = nc.dram_tensor(px + "qw1", [D, 4096], BF16, kind=ikind).ap()
    d_qw2 = nc.dram_tensor(px + "qw2", [4096, D], BF16, kind=ikind).ap()
    d_hq = nc.dram_tensor(px + "hq", [T, 4096], BF16, kind=ikind).ap()
    d_y1 = nc.dram_tensor(px + "y1dbg", [T, 4096], F32, kind=ikind).ap() if dbg else None

    FW = 2048
    xt = pb.sb("xt", (128, FW))
    scr = {k: pb.sb("scr" + k, (128, FW)) for k in ["B", "INV", "S", "V"]}
    scr["I"] = pb.sb("scrI", (128, FW), I32)
    ob = pb.sb("ob", (128, FW), BF16)

    def scrv(F):
        return {k: (v[:, :F] if k != "I" else v[:, :F]) for k, v in scr.items()}

    for v in (2.0 * LN2, -2.0 * LN2, 7.0 * LN2, -7.0 * LN2):
        pb.ccol(v)
    for i in range(NT):
        sl = slice(i * 128, (i + 1) * 128)
        pb.dma(xt[:, :D], d_xn[sl, :])
        e_fq52(pb, ob[:, :D], xt[:, :D], scrv(D))
        pb.dma(d_qx[sl, :], ob[:, :D], q="gpsimd")
    if upto < 1:
        d_zzz = pb.dout(px + "zzz", [1, 1], F32)
        pb.dma(d_zzz, ob[:1, :1], q="gpsimd")
        return pb
    for i in range(8):
        sl = slice(i * 128, (i + 1) * 128)
        for hh in range(2):
            cs = slice(hh * 2048, (hh + 1) * 2048)
            pb.dma(xt[:, :], d_w1t[sl, cs])
            e_fq52(pb, ob[:, :], xt[:, :], scrv(2048))
            pb.dma(d_qw1[sl, cs], ob[:, :], q="gpsimd")
    for i in range(32):
        sl = slice(i * 128, (i + 1) * 128)
        pb.dma(xt[:, :D], d_w2t[sl, :])
        e_fq52(pb, ob[:, :D], xt[:, :D], scrv(D))
        pb.dma(d_qw2[sl, :], ob[:, :D], q="gpsimd")
    b1r = pb.sb("b1r", (1, 4096))
    qb1 = pb.sb("qb1", (1, 4096), BF16)
    pb.dma(b1r[:1, :], d_b1)
    for hh in range(2):
        cs = slice(hh * 2048, (hh + 1) * 2048)
        sc1 = {k: v[:1, :2048] for k, v in scr.items()}
        e_fq52(pb, qb1[:1, cs], b1r[:1, cs], sc1, npart=1)
    b2r = pb.sb("b2r", (1, D))
    qb2 = pb.sb("qb2", (1, D), BF16)
    pb.dma(b2r[:1, :], d_b2)
    sc2 = {k: v[:1, :D] for k, v in scr.items()}
    e_fq52(pb, qb2[:1, :], b2r[:1, :], sc2, npart=1)
    if upto < 2:
        d_zzz = pb.dout(px + "zzz", [1, 1], F32)
        pb.dma(d_zzz, ob[:1, :1], q="gpsimd")
        return pb
    ones_b = pb.sb("ones_b", (1, 128), BF16)
    pb.G(lambda e: e.memset(ones_b[:, :], 1.0))
    acc = pb.ps("acc", (128, 512))
    hqt = pb.sb("hqt", (128, 4096), BF16)
    qxT = [pb.sb(f"qxT{k}", (128, T), BF16) for k in range(8)]
    for k in range(8):
        pb.dmaT(qxT[k][:, :], d_qx[:, k * 128:(k + 1) * 128])
    qw1s = [pb.sb(f"qw1s{k}", (128, 4096), BF16) for k in range(8)]
    for k in range(8):
        pb.dma(qw1s[k][:, :], d_qw1[k * 128:(k + 1) * 128, :])
    for i in range(NT):
        tsl = slice(i * 128, (i + 1) * 128)
        for nchunk in range(8):
            nsl = slice(nchunk * 512, (nchunk + 1) * 512)
            for k in range(8):
                pb.TE(lambda e, k=k, tsl=tsl, nsl=nsl, s=(k == 0):
                      e.matmul(acc[:, :], qxT[k][:, tsl], qw1s[k][:, nsl],
                               start=s, stop=False))
            pb.TE(lambda e, nsl=nsl: e.matmul(acc[:, :], ones_b[:1, :], qb1[:1, nsl],
                                              start=False, stop=True))
            s5 = scrv(512)
            if dbg:
                pb.V(lambda e, nsl=nsl: e.tensor_copy(scr["B"][:, :512], acc[:, :]))
                pb.dma(d_y1[tsl, nsl], scr["B"][:, :512], q="gpsimd")
            e_fq52(pb, None, acc[:, :], s5, ebias=7.0, eclamp=-30.0,
                   keep_m=True, relu=True)
            e_fq52_of_grid(pb, hqt[:, nsl], s5)
        pb.dma(d_hq[tsl, :], hqt[:, :], q="gpsimd")
    pb.sbfree(*qw1s)
    pb.sbfree(*qxT)
    hqT = [pb.sb(f"hqT{k}", (128, 1024), BF16) for k in range(32)]
    w2s = pb.sb("w2s", (128, D), BF16)
    mt = pb.sb("mt", (128, D), BF16)
    otile = pb.sb("otile", (128, D))
    acc2 = pb.ps("acc2", (128, 512))
    for half in range(2):
        hsl = slice(half * 1024, (half + 1) * 1024)
        for k in range(32):
            pb.dmaT(hqT[k][:, :], d_hq[hsl, k * 128:(k + 1) * 128])
        for it in range(8):
            i = half * 8 + it
            tsl = slice(i * 128, (i + 1) * 128)
            ttsl = slice(it * 128, (it + 1) * 128)
            for nchunk in range(2):
                nsl = slice(nchunk * 512, (nchunk + 1) * 512)
                for k in range(32):
                    pb.dma(w2s[:, :], d_qw2[k * 128:(k + 1) * 128, :])
                    pb.TE(lambda e, k=k, ttsl=ttsl, nsl=nsl, s=(k == 0):
                          e.matmul(acc2[:, :], hqT[k][:, ttsl], w2s[:, nsl],
                                   start=s, stop=False))
                pb.TE(lambda e, nsl=nsl: e.matmul(acc2[:, :], ones_b[:1, :], qb2[:1, nsl],
                                                  start=False, stop=True))
                s5 = scrv(512)
                e_fq52(pb, None, acc2[:, :], s5, ebias=7.0, eclamp=-30.0, keep_m=True)
                pb.V(lambda e, nsl=nsl, s5=s5: e.tensor_tensor(otile[:, nsl], s5["V"], s5["S"], OP.mult))
            pb.dma(mt[:, :], d_mask[tsl, :])
            pb.V(lambda e: e.tensor_tensor(otile[:, :], otile[:, :], mt[:, :], OP.mult))
            pb.dma(d_out[tsl, :], otile[:, :], q="gpsimd")
    return pb


def prog_l0_ffn(pb=None, px="", d_xnb=None, d_out=None):
    pb = pb or PB("l0f")
    d_xnb = d_xnb if d_xnb is not None else pb.din(px + "xnb", [T, D], BF16)
    d_rwt = pb.din(px + "rwt", [D, 8], F32)
    d_rb = pb.din(px + "rb", [1, 8], F32)
    d_ewt = pb.din(px + "ewt", [8, D, D], F32)
    d_eb = pb.din(px + "eb", [8, D], F32)
    d_out = d_out if d_out is not None else pb.dout(px + "out", [T, D], F32)

    xnT = [pb.sb(f"xnT{k}", (128, T), BF16) for k in range(8)]
    for k in range(8):
        pb.dmaT(xnT[k][:, :], d_xnb[:, k * 128:(k + 1) * 128])
    rwts = pb.sb("rwts", (128, 64), BF16)
    for k in range(8):
        pb.op("gpsimd", lambda e, k=k: e.dma_start(
            out=rwts[:, k * 8:(k + 1) * 8], in_=d_rwt[k * 128:(k + 1) * 128, :]), 16)
    ones1 = pb.sb("ones1", (1, 128))
    pb.G(lambda e: e.memset(ones1[:, :], 1.0))
    rbrow = pb.sb("rbrow", (1, 8))
    pb.dma(rbrow[:1, :], d_rb)
    psb = pb.ps("psb", (128, 128))
    rbb = pb.sb("rbb", (128, 8))
    pb.TE(lambda e: e.matmul(psb[:, :8], ones1[:1, :], rbrow[:1, :8], start=True, stop=True))
    pb.V(lambda e: e.tensor_copy(rbb[:, :8], psb[:, :8]))
    ebrow = pb.sb("ebrow", (8, D))
    pb.dma(ebrow[:8, :], d_eb)
    ebr0 = pb.sb("ebr0", (1, D), BF16)
    onesbf = pb.sb("onesbf", (1, 128), BF16)
    pb.G(lambda e: e.memset(onesbf[:, :], 1.0))

    accp = pb.ps("accp", (128, 512))
    lg = pb.ps("lg", (128, 8))
    swall = pb.sb("swall", (128, 128))
    ew8 = [pb.sb(f"ew8_{k}", (128, D), BF16) for k in range(8)]
    outt = [pb.sb(f"outt{i}", (128, D)) for i in range(NT)]
    lgt = pb.sb("lgt", (128, 8))
    mx = pb.sb("mx", (128, 1))
    w8 = pb.sb("w8", (128, 8))
    den = pb.sb("den", (128, 1))
    pmax = pb.sb("pmax", (128, 1))
    selw = pb.sb("selw", (128, 8))
    running = pb.sb("running", (128, 1))
    mcol = pb.sb("mcol2", (128, 1))
    ew = pb.sb("ew", (128, D), BF16)
    out_t = pb.sb("out_t", (128, D))
    for i in range(NT):
        tsl = slice(i * 128, (i + 1) * 128)
        for k in range(8):
            pb.TE(lambda e, k=k, tsl=tsl, s=(k == 0), p=(k == 7):
                  e.matmul(lg[:, :8], xnT[k][:, tsl], rwts[:, k * 8:(k + 1) * 8],
                           start=s, stop=p))
        pb.V(lambda e: e.tensor_tensor(lgt[:, :8], lg[:, :8], rbb[:, :8], OP.add))
        pb.V(lambda e: e.tensor_reduce(mx[:, :1], lgt[:, :8], axis=AX.X, op=OP.max))
        pb.V(lambda e: e.tensor_scalar(mx[:, :1], mx[:, :1], -1.0, None, OP.mult))
        pb.S(lambda e: e.activation(w8[:, :8], lgt[:, :8], AF.Exp, bias=mx[:, :1],
                                    accum_out=den[:, :1]))
        pb.V(lambda e: e.reciprocal(den[:, :1], den[:, :1]))
        pb.V(lambda e: e.tensor_scalar(w8[:, :8], w8[:, :8], den[:, :1], None, OP.mult))
        pb.V(lambda e: e.tensor_reduce(pmax[:, :1], w8[:, :8], axis=AX.X, op=OP.max))
        pb.G(lambda e: e.memset(running[:, :1], 0.0))
        for ex in range(8):
            pb.V(lambda e, ex=ex: e.tensor_scalar(mcol[:, :1], w8[:, ex:ex + 1],
                                                  pmax[:, :1], None, OP.is_ge))
            pb.V(lambda e, ex=ex: e.scalar_tensor_tensor(
                selw[:, ex:ex + 1], running[:, :1], 1.0, mcol[:, :1], OP.subtract, OP.mult))
            pb.V(lambda e, ex=ex: e.tensor_scalar(selw[:, ex:ex + 1], selw[:, ex:ex + 1],
                                                  -1.0, None, OP.mult))
            pb.V(lambda e, ex=ex: e.tensor_tensor(running[:, :1], running[:, :1],
                                                  selw[:, ex:ex + 1], OP.add))
            pb.V(lambda e, ex=ex: e.tensor_tensor(selw[:, ex:ex + 1], selw[:, ex:ex + 1],
                                                  pmax[:, :1], OP.mult))
        pb.V(lambda e, i=i: e.tensor_copy(swall[:, i * 8:(i + 1) * 8], selw[:, :8]))
    for i in range(NT):
        pb.G(lambda e, i=i: e.memset(outt[i][:, :], 0.0))
    for ex in range(8):
        pb.op("gpsimd", lambda e, ex=ex: e.dma_start(
            out=ebr0[:1, :], in_=ebrow[ex:ex + 1, :]), 16)
        for k in range(8):
            pb.op("gpsimd", lambda e, ex=ex, k=k: e.dma_start(
                out=ew8[k][:, :], in_=d_ewt[ex, k * 128:(k + 1) * 128, :]), 16)
        for i in range(NT):
            tsl = slice(i * 128, (i + 1) * 128)
            for half in range(2):
                nsl = slice(half * 512, (half + 1) * 512)
                for k in range(8):
                    pb.TE(lambda e, k=k, tsl=tsl, nsl=nsl, s=(k == 0):
                          e.matmul(accp[:, :], xnT[k][:, tsl], ew8[k][:, nsl],
                                   start=s, stop=False))
                pb.TE(lambda e, nsl=nsl: e.matmul(accp[:, :], onesbf[:1, :], ebr0[:1, nsl], start=False, stop=True))
                pb.V(lambda e, ex=ex, i=i, nsl=nsl: e.scalar_tensor_tensor(
                    outt[i][:, nsl], accp[:, :], swall[:, i * 8 + ex:i * 8 + ex + 1],
                    outt[i][:, nsl], OP.mult, OP.add))
    for i in range(NT):
        pb.dma(d_out[i * 128:(i + 1) * 128, :], outt[i][:, :], q="gpsimd")
    return pb



def prog_l2_ffn(upto=99, pb=None, px="", d_xnb=None, d_out=None):
    pb = pb or PB("l2f")
    d_xnb = d_xnb if d_xnb is not None else pb.din(px + "xnb", [T, D], BF16)
    d_ctrl = pb.din(px + "ctrl", [D, 64], F32)
    d_f1 = pb.din(px + "f1", [D, 4096], F32)
    d_f2 = pb.din(px + "f2", [4096, D], F32)
    d_biasT = pb.din(px + "biasT", [4096, 1], F32)
    d_tie = pb.din(px + "tiecol", [128, 1], F32)
    d_ident = pb.din(px + "c_ident", [128, 128], F32)
    d_out = d_out if d_out is not None else pb.dout(px + "out", [T, D], F32)
    nc = pb.nc
    d_innerM = nc.dram_tensor(px + "innerM", [4096, T], BF16, kind="Internal").ap()

    xnT = [pb.sb(f"xnT{k}", (128, T), BF16) for k in range(8)]
    for k in range(8):
        pb.dmaT(xnT[k][:, :], d_xnb[:, k * 128:(k + 1) * 128])
    ident = pb.sb("ident", (128, 128))
    pb.dma(ident[:, :], d_ident)
    tie = pb.sb("tie", (128, 1))
    pb.dma(tie[:, :], d_tie)
    ctrls = pb.sb("ctrls", (128, 8 * 64), BF16)
    for k in range(8):
        pb.op("gpsimd", lambda e, k=k: e.dma_start(
            out=ctrls[:, k * 64:(k + 1) * 64], in_=d_ctrl[k * 128:(k + 1) * 128, :]), 16)

    lgp = pb.ps("lgp", (128, 64))
    lgt = pb.sb("lgt", (128, 64))
    ptp = pb.ps("ptp", (128, 128))
    lgT = pb.sb("lgT", (64, 128))
    mx8 = pb.sb("mx8", (64, 8))
    permT = pb.sb("permT", (64, T))
    for i in range(NT):
        tsl = slice(i * 128, (i + 1) * 128)
        for k in range(8):
            pb.TE(lambda e, k=k, tsl=tsl, s=(k == 0), p=(k == 7):
                  e.matmul(lgp[:, :64], xnT[k][:, tsl], ctrls[:, k * 64:(k + 1) * 64],
                           start=s, stop=p))
        pb.V(lambda e: e.tensor_scalar(lgt[:, :64], lgp[:, :64], tie[:, :1], None, OP.add))
        pb.TE(lambda e: e.transpose(ptp[:64, :], lgt[:, :64], ident[:, :]))
        pb.V(lambda e: e.tensor_copy(lgT[:, :], ptp[:64, :128]))
        pb.V(lambda e: e.reduce_max(mx8[:, :8], lgT[:, :].rearrange("p (a b) -> p a b", a=8), axis=AX.X))
        pb.V(lambda e, tsl=tsl: e.tensor_tensor(
            permT[:, tsl].rearrange("p (a b) -> p a b", a=8),
            lgT[:, :].rearrange("p (a b) -> p a b", a=8),
            mx8[:, :8].broadcast_to([64, 8, 16]),
            OP.is_equal))

    if upto < 1:
        d_pt = pb.dout(px + "permTdbg", [64, T], F32)
        pb.dma(d_pt, permT[:, :], q="gpsimd")
        return pb
    d_sel = pb.din(px + "sel", [64, 4096], F32)
    sel64 = pb.sb("sel64", (64, 4096), BF16)
    pb.op("gpsimd", lambda e: e.dma_start(out=sel64[:, :], in_=d_sel), 16)
    permTb = pb.sb("permTb", (64, T), BF16)
    pb.V(lambda e: e.tensor_copy(permTb[:, :], permT[:, :]))
    f1t = pb.sb("f1t", (128, 4096), BF16)
    bcol = pb.sb("bcol", (128, 1))
    pin = pb.ps("pin", (128, 512))
    pbm = pb.ps("pbm", (128, 512))
    inn = pb.sb("inn", (128, T))
    pmb = pb.sb("pmb", (128, T))
    imb = pb.sb("imb", (128, T), BF16)
    for c in range(32):
        csl = slice(c * 128, (c + 1) * 128)
        for k in range(8):
            pb.op("gpsimd", lambda e, k=k, c=c: e.dma_start(
                out=f1t[:, k * 128:(k + 1) * 128],
                in_=d_f1[k * 128:(k + 1) * 128, c * 128:(c + 1) * 128]), 16)
        pb.dma(bcol[:, :], d_biasT[csl, :])
        for th in range(4):
            tfl = slice(th * 512, (th + 1) * 512)
            for k in range(8):
                pb.TE(lambda e, k=k, tfl=tfl, s=(k == 0), p=(k == 7):
                      e.matmul(pin[:, :], f1t[:, k * 128:(k + 1) * 128], xnT[k][:, tfl],
                               start=s, stop=p))
            pb.S(lambda e, tfl=tfl: e.activation(inn[:, tfl], pin[:, :], AF.Relu,
                                                 bias=bcol[:, :1]))
            pb.TE(lambda e, tfl=tfl, csl=csl: e.matmul(
                pbm[:, :], sel64[:64, csl], permTb[:64, tfl], start=True, stop=True))
            pb.V(lambda e, tfl=tfl: e.tensor_copy(pmb[:, tfl], pbm[:, :]))
        pb.V(lambda e: e.tensor_tensor(imb[:, :], inn[:, :], pmb[:, :], OP.mult))
        pb.dma(d_innerM[csl, :], imb[:, :], q="gpsimd")

    if upto < 2:
        d_zz = pb.dout(px + "zz2", [1, 1], F32)
        pb.dma(d_zz, permT[:1, :1], q="gpsimd")
        return pb
    f2t = pb.sb("f2t", (128, D), BF16)
    po = [pb.ps(f"po{h}", (128, 512)) for h in range(2)]
    ot = pb.sb("ot2", (128, D))
    imt = pb.sb("imt", (128, 128), BF16)
    for i in range(NT if upto >= 3 else 1):
        tsl = slice(i * 128, (i + 1) * 128)
        for c in range(32):
            pb.op("gpsimd", lambda e, c=c: e.dma_start(
                out=f2t[:, :], in_=d_f2[c * 128:(c + 1) * 128, :]), 16)
            pb.dma(imt[:, :], d_innerM[c * 128:(c + 1) * 128, tsl])
            for h in range(2):
                pb.TE(lambda e, c=c, h=h, s=(c == 0), p=(c == 31):
                      e.matmul(po[h][:, :], imt[:, :], f2t[:, h * 512:(h + 1) * 512],
                               start=s, stop=p))
        for h in range(2):
            pb.V(lambda e, h=h: e.tensor_copy(ot[:, h * 512:(h + 1) * 512], po[h][:, :]))
        pb.dma(d_out[tsl, :], ot[:, :], q="gpsimd")
    return pb


def prog_l2_attn(dbg=False, pb=None, px="", d_xn=None, d_out=None):
    pb = pb or PB("l2a")
    d_xn = d_xn if d_xn is not None else pb.din(px + "xn", [T, D], F32)
    d_wqkvt = pb.din(px + "wqkvt", [D, 3072], F32)
    d_wo_t = pb.din(px + "wot", [D, D], F32)
    d_g = pb.din(px + "gvec", [1, D], F32)
    d_cos = pb.din(px + "cosb", [T, D], F32)
    d_sin = pb.din(px + "sinb", [T, D], F32)
    d_mask = pb.din(px + "maskc", [512, 512], F32)
    d_pmk = pb.din(px + "pmkT", [128, 256], F32)
    d_pmv = pb.din(px + "pmvT", [16, 1024], F32)
    d_out = d_out if d_out is not None else pb.dout(px + "out", [T, D], F32)
    nc = pb.nc
    ik = "ExternalOutput" if dbg else "Internal"
    d_x2b = nc.dram_tensor(px + "x2b", [T, D], BF16, kind=ik).ap()
    d_q = nc.dram_tensor(px + "qro", [T, D], BF16, kind=ik).ap()
    d_k = nc.dram_tensor(px + "kro", [T, D], BF16, kind=ik).ap()
    d_v = nc.dram_tensor(px + "vt", [T, D], BF16, kind=ik).ap()
    d_ao = nc.dram_tensor(px + "ao", [T, D], BF16, kind=ik).ap()
    d_sc0 = nc.dram_tensor(px + "sc0", [128, 528], F32, kind=ik).ap() if dbg else None
    d_pr0 = nc.dram_tensor(px + "pr0", [128, 528], BF16, kind=ik).ap() if dbg else None

    gb = pb.sb("gb", (1, D))
    pb.dma(gb[:1, :], d_g)
    ones1 = pb.sb("ones1", (1, 128))
    pb.G(lambda e: e.memset(ones1[:, :], 1.0))
    gfull = pb.sb("gfull", (128, D))
    psb = pb.ps("psb", (128, 512))
    for j in range(2):
        pb.TE(lambda e, j=j: e.matmul(psb[:, :], ones1[:1, :], gb[:1, j * 512:(j + 1) * 512],
                                      start=True, stop=True))
        pb.V(lambda e, j=j: e.tensor_copy(gfull[:, j * 512:(j + 1) * 512], psb[:, :]))
    xt = pb.sb("xt", (128, D))
    sq = pb.sb("sq", (128, D))
    ms = pb.sb("ms", (128, 1))
    x2 = pb.sb("x2", (128, D))
    x2bt = pb.sb("x2bt", (128, D), BF16)
    for i in range(NT):
        tsl = slice(i * 128, (i + 1) * 128)
        pb.dma(xt[:, :], d_xn[tsl, :])
        pb.S(lambda e: e.activation(sq[:, :], xt[:, :], AF.Square))
        pb.V(lambda e: e.reduce_sum(ms[:, :1], sq[:, :], axis=AX.X))
        pb.V(lambda e: e.tensor_scalar(ms[:, :1], ms[:, :1], 1.0 / D, 1.1920929e-07, OP.mult, OP.add))
        pb.S(lambda e: e.activation(ms[:, :1], ms[:, :1], AF.Ln))
        pb.S(lambda e: e.activation(ms[:, :1], ms[:, :1], AF.Exp, scale=-0.5))
        pb.V(lambda e: e.tensor_scalar(x2[:, :], xt[:, :], ms[:, :1], None, OP.mult))
        pb.V(lambda e: e.tensor_tensor(x2[:, :], x2[:, :], gfull[:, :], OP.mult))
        pb.V(lambda e: e.tensor_copy(x2bt[:, :], x2[:, :]))
        pb.dma(d_x2b[tsl, :], x2bt[:, :], q="gpsimd")
    x2T = [pb.sb(f"x2T{k}", (128, T), BF16) for k in range(8)]
    for k in range(8):
        pb.dmaT(x2T[k][:, :], d_x2b[:, k * 128:(k + 1) * 128])
    wq = pb.sb("wq", (128, 512), BF16)
    pq = pb.ps("pq", (128, 512))
    qkv = pb.sb("qkv", (128, 3072))
    cost = pb.sb("cost", (128, D))
    sint = pb.sb("sint", (128, D))
    rot = pb.sb("rot", (128, D))
    ro = pb.sb("ro", (128, D))
    rob = pb.sb("rob", (128, D), BF16)
    for i in range(NT):
        tsl = slice(i * 128, (i + 1) * 128)
        for nch in range(6):
            for k in range(8):
                pb.op("gpsimd", lambda e, k=k, nch=nch: e.dma_start(
                    out=wq[:, :], in_=d_wqkvt[k * 128:(k + 1) * 128,
                                              nch * 512:(nch + 1) * 512]), 16)
                pb.TE(lambda e, k=k, tsl=tsl, s=(k == 0), p=(k == 7):
                      e.matmul(pq[:, :], x2T[k][:, tsl], wq[:, :], start=s, stop=p))
            pb.V(lambda e, nch=nch: e.tensor_copy(qkv[:, nch * 512:(nch + 1) * 512], pq[:, :]))
        pb.dma(cost[:, :], d_cos[tsl, :])
        pb.dma(sint[:, :], d_sin[tsl, :])
        for part, dst in ((0, d_q), (1, d_k)):
            qsl = slice(part * D, (part + 1) * D)
            pb.V(lambda e, qsl=qsl: e.tensor_copy(
                rot[:, 0:D:2], qkv[:, qsl][:, 1:D:2]))
            pb.V(lambda e, qsl=qsl: e.tensor_copy(
                rot[:, 1:D:2], qkv[:, qsl][:, 0:D:2]))
            pb.V(lambda e, qsl=qsl: e.tensor_tensor(ro[:, :], qkv[:, qsl], cost[:, :], OP.mult))
            pb.V(lambda e: e.tensor_tensor(rot[:, :], rot[:, :], sint[:, :], OP.mult))
            pb.V(lambda e: e.tensor_tensor(ro[:, :], ro[:, :], rot[:, :], OP.add))
            pb.V(lambda e: e.tensor_copy(rob[:, :], ro[:, :]))
            pb.dma(dst[tsl, :], rob[:, :], q="gpsimd")
        pb.V(lambda e: e.tensor_copy(rob[:, :], qkv[:, 2 * D:3 * D]))
        pb.dma(d_v[tsl, :], rob[:, :], q="gpsimd")


    pmk = pb.sb("pmk", (128, 256), BF16)
    pb.op("gpsimd", lambda e: e.dma_start(out=pmk[:, :], in_=d_pmk), 16)
    pmv = pb.sb("pmv", (16, 1024), BF16)
    pb.op("gpsimd", lambda e: e.dma_start(out=pmv[:, :], in_=d_pmv), 16)
    idb = pb.sb("idb", (128, 128), BF16)
    pb.G(lambda e: e.memset(idb[:, :], 0.0))
    pb.V(lambda e: e.iota(idb[:, :], None)) if False else None
    d_ident = pb.din(px + "c_ident", [128, 128], F32)
    identf = pb.sb("identf", (128, 128))
    pb.dma(identf[:, :], d_ident)
    pb.V(lambda e: e.tensor_copy(idb[:, :], identf[:, :]))
    qT2 = pb.sb("qT2", (128, T), BF16)
    kT2 = pb.sb("kT2", (128, T), BF16)
    vseg = [pb.sb(f"vseg{j}", (128, D), BF16) for j in range(4)]
    mskt = pb.sb("mskt", (128, 512))
    ps_k = pb.ps("ps_k", (128, 512))
    ps_pm = pb.ps("ps_pm", (128, 16))
    sct = pb.sb("sct", (128, 528))
    mxc = pb.sb("mxc", (128, 1))
    den = pb.sb("den", (128, 1))
    prb = pb.sb("prb", (128, 528), BF16)
    ptb = pb.ps("ptb", (128, 128), BF16)
    lhsT = pb.sb("lhsTt", (128, 128 * 5), BF16)
    pov = pb.ps("pov", (128, 64))
    oht = pb.sb("oht", (128, D))
    for seg in range(4):
        ssl = slice(seg * 512, (seg + 1) * 512)
        for j in range(4):
            pb.dma(vseg[j][:, :], d_v[seg * 512 + j * 128:seg * 512 + (j + 1) * 128, :])
        for hp in range(8):
            pb.dmaT(qT2[:, :], d_q[:, hp * 128:(hp + 1) * 128])
            pb.dmaT(kT2[:, :], d_k[:, hp * 128:(hp + 1) * 128])
            for hh in range(2):
                h = hp * 2 + hh
                qsl = slice(hh * 64, (hh + 1) * 64)
                for ti in range(4):
                    trow = seg * 512 + ti * 128
                    pb.TE(lambda e, qsl=qsl, trow=trow, ssl=ssl:
                          e.matmul(ps_k[:, :], qT2[qsl, trow:trow + 128],
                                   kT2[qsl, ssl], start=True, stop=True))
                    pb.TE(lambda e, qsl=qsl, trow=trow, h=h:
                          e.matmul(ps_pm[:, :], qT2[qsl, trow:trow + 128],
                                   pmk[qsl, h * 16:(h + 1) * 16], start=True, stop=True))
                    pb.dma(mskt[:, :], d_mask[ti * 128:(ti + 1) * 128, :])
                    pb.V(lambda e: e.scalar_tensor_tensor(
                        sct[:, 0:512], ps_k[:, :], 0.125, mskt[:, :], OP.mult, OP.add))
                    pb.V(lambda e: e.tensor_scalar(sct[:, 512:528], ps_pm[:, :], 0.125, None, OP.mult))
                    if dbg and seg == 0 and h == 0 and ti == 0:
                        pb.dma(d_sc0, sct[:, :528], q="gpsimd")
                    pb.V(lambda e: e.reduce_max(mxc[:, :1], sct[:, :528], axis=AX.X))
                    pb.V(lambda e: e.tensor_scalar(mxc[:, :1], mxc[:, :1], -1.0, None, OP.mult))
                    pb.S(lambda e: e.activation(prb[:, :528], sct[:, :528], AF.Exp,
                                                bias=mxc[:, :1], accum_out=den[:, :1]))
                    pb.V(lambda e: e.reciprocal(den[:, :1], den[:, :1]))
                    if dbg and seg == 0 and h == 0 and ti == 0:
                        pb.dma(d_pr0, prb[:, :528], q="gpsimd")
                    for c in range(4):
                        pb.TE(lambda e, c=c: e.transpose(
                            ptb[:, :], prb[:, c * 128:(c + 1) * 128], idb[:, :]))
                        pb.V(lambda e, c=c: e.tensor_copy(
                            lhsT[:, c * 128:(c + 1) * 128], ptb[:, :]))
                    pb.TE(lambda e: e.transpose(ptb[:16, :], prb[:, 512:528], idb[:, :]))
                    pb.V(lambda e: e.tensor_copy(lhsT[:16, 512:640], ptb[:16, :128]))
                    for c in range(4):
                        pb.TE(lambda e, c=c, h=h, s=(c == 0):
                              e.matmul(pov[:, :], lhsT[:, c * 128:(c + 1) * 128],
                                       vseg[c][:, h * 64:(h + 1) * 64], start=s, stop=False))
                    pb.TE(lambda e, h=h: e.matmul(pov[:, :], lhsT[:16, 512:640],
                                                  pmv[:16, h * 64:(h + 1) * 64],
                                                  start=False, stop=True))
                    pb.V(lambda e, h=h: e.tensor_scalar(
                        oht[:, h * 64:(h + 1) * 64], pov[:, :], den[:, :1], None, OP.mult))
                    pb.V(lambda e, h=h: e.tensor_copy(rob[:, h * 64:(h + 1) * 64],
                                                      oht[:, h * 64:(h + 1) * 64]))
                    pb.op("gpsimd", lambda e, h=h, trow=trow: e.dma_start(
                        out=d_ao[trow:trow + 128, h * 64:(h + 1) * 64],
                        in_=rob[:, h * 64:(h + 1) * 64]), 16)
    aoT = [pb.sb(f"aoT{k}", (128, T), BF16) for k in range(8)]
    for k in range(8):
        pb.dmaT(aoT[k][:, :], d_ao[:, k * 128:(k + 1) * 128])
    wo = pb.sb("wo", (128, D), BF16)
    ot = pb.sb("otl2", (128, D))
    po = pb.ps("po_l2", (128, 512))
    for i in range(NT):
        tsl = slice(i * 128, (i + 1) * 128)
        for half in range(2):
            nsl = slice(half * 512, (half + 1) * 512)
            for k in range(8):
                pb.op("gpsimd", lambda e, k=k: e.dma_start(
                    out=wo[:, :], in_=d_wo_t[k * 128:(k + 1) * 128, :]), 16)
                pb.TE(lambda e, k=k, tsl=tsl, nsl=nsl, s=(k == 0), p=(k == 7):
                      e.matmul(po[:, :], aoT[k][:, tsl], wo[:, nsl], start=s, stop=p))
            pb.V(lambda e, nsl=nsl: e.tensor_copy(ot[:, nsl], po[:, :]))
        pb.dma(d_out[tsl, :], ot[:, :], q="gpsimd")
    return pb


def prog_l1_attn(dbg=False, pb=None, px="", d_xn=None, d_out=None):
    pb = pb or PB("l1a")
    d_xn = d_xn if d_xn is not None else pb.din(px + "xn", [T, D], F32)
    d_win = pb.din(px + "win", [D, 3328], F32)
    d_gb = pb.din(px + "gbias", [D, 1], F32)
    d_rmsw = pb.din(px + "rmsw", [1, D], F32)
    d_wout = pb.din(px + "wout", [D, D], F32)
    d_mask = pb.din(px + "l1mask", [16 * 128, 32], F32)
    d_ident = pb.din(px + "c_ident", [128, 128], F32)
    d_out = d_out if d_out is not None else pb.dout(px + "out", [T, D], F32)
    nc = pb.nc
    ik = "ExternalOutput" if dbg else "Internal"
    d_n1b = nc.dram_tensor(px + "n1b", [T, D], BF16, kind=ik).ap()
    d_z = nc.dram_tensor(px + "zsig", [D, T], BF16, kind=ik).ap()
    d_xkx = nc.dram_tensor(px + "xkx", [D, T], F32, kind=ik).ap() if dbg else None
    d_aoT = nc.dram_tensor(px + "aoT", [D, T], BF16, kind=ik).ap()

    xt2 = [pb.sb(f"l1x{j}", (128, T), BF16) for j in range(8)]
    gt = [pb.sb(f"l1g{j}", (128, T), BF16) for j in range(8)]
    om = [pb.sb(f"l1om{j}", (128, T), BF16) for j in range(8)]
    qk0 = pb.sb("l1q", (128, T), BF16)
    kk0 = pb.sb("l1k", (128, T), BF16)
    ident = pb.sb("identl1", (128, 128))
    pb.dma(ident[:, :], d_ident)
    idb = pb.sb("idbl1", (128, 128), BF16)
    pb.V(lambda e: e.tensor_copy(idb[:, :], ident[:, :]))
    ones1 = pb.sb("ones1l1", (1, 128))
    pb.G(lambda e: e.memset(ones1[:, :], 1.0))
    onesb = pb.sb("onesbl1", (1, 128), BF16)
    pb.G(lambda e: e.memset(onesb[:, :], 1.0))
    onecol = pb.sb("onecol", (64, 1), BF16)
    pb.G(lambda e: e.memset(onecol[:, :], 1.0))

    rw = pb.sb("rwl1", (1, D))
    pb.dma(rw[:1, :], d_rmsw)
    rwf = pb.sb("rwfl1", (128, D))
    psb = pb.ps("psbl1", (128, 512))
    for j in range(2):
        pb.TE(lambda e, j=j: e.matmul(psb[:, :], ones1[:1, :], rw[:1, j * 512:(j + 1) * 512],
                                      start=True, stop=True))
        pb.V(lambda e, j=j: e.tensor_copy(rwf[:, j * 512:(j + 1) * 512], psb[:, :]))
    xt = pb.sb("xtl1", (128, D))
    sq = pb.sb("sql1", (128, D))
    ms = pb.sb("msl1", (128, 1))
    n1 = pb.sb("n1l1", (128, D))
    n1b = pb.sb("n1bl1", (128, D), BF16)
    for i in range(NT):
        tsl = slice(i * 128, (i + 1) * 128)
        pb.dma(xt[:, :], d_xn[tsl, :])
        pb.S(lambda e: e.activation(sq[:, :], xt[:, :], AF.Square))
        pb.V(lambda e: e.reduce_sum(ms[:, :1], sq[:, :], axis=AX.X))
        pb.V(lambda e: e.tensor_scalar(ms[:, :1], ms[:, :1], 1.0 / D, 1e-6, OP.mult, OP.add))
        pb.S(lambda e: e.activation(ms[:, :1], ms[:, :1], AF.Ln))
        pb.S(lambda e: e.activation(ms[:, :1], ms[:, :1], AF.Exp, scale=-0.5))
        pb.V(lambda e: e.tensor_scalar(n1[:, :], xt[:, :], ms[:, :1], None, OP.mult))
        pb.V(lambda e: e.tensor_tensor(n1[:, :], n1[:, :], rwf[:, :], OP.mult))
        pb.V(lambda e: e.tensor_copy(n1b[:, :], n1[:, :]))
        pb.dma(d_n1b[tsl, :], n1b[:, :], q="gpsimd")

    wt = pb.sb("wtl1", (128, 128), BF16)
    gbc = pb.sb("gbc", (128, 1))
    pp = pb.ps("ppl1", (128, 512))
    ztile = pb.sb("ztileb", (128, T), BF16)
    n1T = [pb.sb(f"n1T{k}", (128, T), BF16) for k in range(8)]
    for k in range(8):
        pb.dmaT(n1T[k][:, :], d_n1b[:, k * 128:(k + 1) * 128])
    for c in range(26):
        if 16 <= c < 24:
            pb.dma(gbc[:, :], d_gb[(c - 16) * 128:(c - 15) * 128, :])
        for tc in range(4):
            tfl = slice(tc * 512, (tc + 1) * 512)
            for k in range(8):
                pb.op("gpsimd", lambda e, k=k, c=c: e.dma_start(
                    out=wt[:, :], in_=d_win[k * 128:(k + 1) * 128, c * 128:(c + 1) * 128]), 16)
                pb.TE(lambda e, k=k, tfl=tfl, s=(k == 0), p=(k == 7):
                      e.matmul(pp[:, :], wt[:, :], n1T[k][:, tfl], start=s, stop=p))
            if c < 8:
                pb.S(lambda e, tfl=tfl: e.activation(ztile[:, tfl], pp[:, :], AF.Sigmoid))
            elif c < 16:
                pb.V(lambda e, c=c, tfl=tfl: e.tensor_copy(xt2[c - 8][:, tfl], pp[:, :]))
            elif c < 24:
                pb.S(lambda e, c=c, tfl=tfl: e.activation(gt[c - 16][:, tfl], pp[:, :],
                                                          AF.Sigmoid, bias=gbc[:, :1]))
                pb.V(lambda e, c=c, tfl=tfl: e.tensor_scalar(
                    om[c - 16][:, tfl], gt[c - 16][:, tfl], -1.0, 1.0, OP.mult, OP.add))
            elif c == 24:
                pb.V(lambda e, tfl=tfl: e.tensor_copy(qk0[:, tfl], pp[:, :]))
            else:
                pb.V(lambda e, tfl=tfl: e.tensor_copy(kk0[:, tfl], pp[:, :]))
        if c < 8:
            pb.dma(d_z[c * 128:(c + 1) * 128, :], ztile[:, :], q="gpsimd")
    pb.sbfree(*n1T)

    at = pb.sb("atl1", (128, T), BF16)
    vt = pb.sb("vtl1", (128, T), BF16)
    sco = pb.sb("scol1", (128, T))
    xkx, xkk, aot = xt2, om, gt
    for j in range(8):
        pb.V(lambda e, j=j: e.tensor_copy(at[:, :], gt[j][:, :]))
        pb.G(lambda e: e.memset(at[:, 0:T:64], 0.0))
        pb.V(lambda e, j=j: e.tensor_tensor(vt[:, :], om[j][:, :], xt2[j][:, :], OP.mult))
        pb.V(lambda e: e.tensor_tensor_scan(sco[:, :], at[:, :], vt[:, :], 0.0, OP.mult, OP.add))
        pb.V(lambda e, j=j: e.tensor_copy(xkx[j][:, :], sco[:, :]))
        pb.V(lambda e, j=j: e.tensor_tensor(vt[:, :], om[j][:, :], kk0[:, :], OP.mult))
        pb.V(lambda e: e.tensor_tensor_scan(sco[:, :], at[:, :], vt[:, :], 0.0, OP.mult, OP.add))
        pb.V(lambda e, j=j: e.tensor_copy(xkk[j][:, :], sco[:, :]))
        if dbg and j == 0:
            pb.dma(d_xkx[0:128, :], xkx[0][:, :], q="gpsimd")
    chk = [pb.sb(f"chk{j}", (128, 32), BF16) for j in range(8)]
    chx = [pb.sb(f"chx{j}", (128, 32), BF16) for j in range(8)]
    for j in range(8):
        pb.V(lambda e, j=j: e.tensor_copy(chk[j][:, :], xkk[j][:, 0:T:64]))
        pb.V(lambda e, j=j: e.tensor_copy(chx[j][:, :], xkx[j][:, 0:T:64]))
    ptc = pb.ps("ptcl1", (128, 64), BF16)
    chxT = [pb.sb(f"chxT{h}", (32, 64), BF16) for h in range(16)]
    chxh = pb.sb("chxh", (64, 32), BF16)
    for h in range(16):
        pb.V(lambda e, h=h: e.tensor_copy(chxh[:, :], chx[h // 2][(h % 2) * 64:(h % 2) * 64 + 64, :]))
        pb.TE(lambda e: e.transpose(ptc[:32, :64], chxh[:, :], idb[:64, :64]))
        pb.V(lambda e, h=h: e.tensor_copy(chxT[h][:, :], ptc[:32, :64]))

    msk = pb.sb("mskl1", (128, 32))
    pgenA = pb.ps("pgenA", (128, 128))
    psc = pgenA[:, 0:32]
    plse = pgenA[:, 32:33]
    prodb = pb.sb("prodb", (64, 128), BF16)
    sct = pb.sb("sctl1", (128, 32))
    lse = pb.sb("lsel1", (128, 1))
    mx = pb.sb("mxl1", (128, 1))
    wst = pb.sb("wstl1", (128, 32), BF16)
    iw = pb.sb("iwl1", (128, 1))
    den = pb.sb("denl1", (128, 1))
    pwT = pb.ps("pwTl1", (128, 128), BF16)
    wT = pb.sb("wTl1", (32, 128), BF16)
    pot = pb.ps("potl1", (64, 128))
    piw = pb.ps("piwl1", (64, 128))
    o1 = pb.sb("o1l1", (64, 128))
    iwTp = pgenA
    iwT = pb.sb("iwTl1", (1, 128))
    denT = pb.sb("denTl1", (1, 128))
    for h in range(16):
        j = h // 2
        rsl = slice((h % 2) * 64, (h % 2) * 64 + 64)
        for i in range(NT):
            tsl = slice(i * 128, (i + 1) * 128)
            pb.TE(lambda e, j=j, rsl=rsl, tsl=tsl: e.matmul(
                psc, qk0[rsl, tsl], chk[j][rsl, :], start=True, stop=True))
            pb.dma(msk[:, :], d_mask[i * 128:(i + 1) * 128, :])
            pb.V(lambda e: e.scalar_tensor_tensor(sct[:, :], psc, 0.125,
                                                  msk[:, :], OP.mult, OP.add))
            pb.V(lambda e, j=j, rsl=rsl, tsl=tsl: e.tensor_tensor(
                prodb[:, :], qk0[rsl, tsl], xkk[j][rsl, tsl], OP.mult))
            pb.TE(lambda e: e.matmul(plse, prodb[:, :], onecol[:, :1],
                                     start=True, stop=True))
            pb.V(lambda e: e.tensor_scalar(lse[:, :1], plse, 0.125, None, OP.mult))
            pb.V(lambda e: e.reduce_max(mx[:, :1], sct[:, :], axis=AX.X))
            pb.V(lambda e: e.tensor_tensor(mx[:, :1], mx[:, :1], lse[:, :1], OP.max))
            pb.V(lambda e: e.tensor_scalar(mx[:, :1], mx[:, :1], -1.0, None, OP.mult))
            pb.S(lambda e: e.activation(wst[:, :], sct[:, :], AF.Exp, bias=mx[:, :1],
                                        accum_out=den[:, :1]))
            pb.S(lambda e: e.activation(iw[:, :1], lse[:, :1], AF.Exp, bias=mx[:, :1]))
            pb.V(lambda e: e.tensor_tensor(den[:, :1], den[:, :1], iw[:, :1], OP.add))
            pb.TE(lambda e: e.transpose(pwT[:32, :], wst[:, :], idb[:, :]))
            pb.V(lambda e: e.tensor_copy(wT[:, :], pwT[:32, :128]))
            pb.TE(lambda e, h=h: e.matmul(pot[:, :], chxT[h][:, :], wT[:, :],
                                          start=True, stop=True))
            pb.V(lambda e: e.tensor_copy(o1[:, :], pot[:, :]))
            pb.TE(lambda e: e.transpose(iwTp[:1, :], iw[:, :1], ident[:, :]))
            pb.V(lambda e: e.tensor_copy(iwT[:1, :], iwTp[:1, :128]))
            pb.V(lambda e: e.reciprocal(denT[:1, :] if False else den[:, :1], den[:, :1])) if False else None
            pb.TE(lambda e: e.matmul(piw[:, :], ones1[:1, :64], iwT[:1, :],
                                     start=True, stop=True))
            pb.V(lambda e, j=j, rsl=rsl, tsl=tsl: e.scalar_tensor_tensor(
                o1[:, :], piw[:, :], 1.0, xkx[j][rsl, tsl], OP.mult, OP.mult))
            pb.V(lambda e: e.tensor_tensor(o1[:, :], pot[:, :], o1[:, :], OP.add))
            pb.V(lambda e: e.reciprocal(den[:, :1], den[:, :1]))
            pb.TE(lambda e: e.transpose(iwTp[:1, :], den[:, :1], ident[:, :]))
            pb.V(lambda e: e.tensor_copy(denT[:1, :], iwTp[:1, :128]))
            pb.TE(lambda e: e.matmul(piw[:, :], ones1[:1, :64], denT[:1, :],
                                     start=True, stop=True))
            pb.V(lambda e: e.scalar_tensor_tensor(o1[:, :], piw[:, :], 1.0, o1[:, :],
                                                  OP.mult, OP.mult))
            pb.V(lambda e, j=j, rsl=rsl, tsl=tsl: e.tensor_copy(aot[j][rsl, tsl], o1[:, :]))
    for j in range(8):
        pb.op("gpsimd", lambda e, j=j: e.dma_start(out=ztile[:, :], in_=d_z[j * 128:(j + 1) * 128, :]), 16)
        pb.V(lambda e, j=j: e.tensor_tensor(aot[j][:, :], aot[j][:, :], ztile[:, :], OP.mult))
        pb.dma(d_aoT[j * 128:(j + 1) * 128, :], aot[j][:, :], q="gpsimd")
    wo = pb.sb("wol1", (128, D), BF16)
    po = pb.ps("pol1", (128, 512))
    ot = pb.sb("otl1", (128, D))
    for i in range(NT):
        tsl = slice(i * 128, (i + 1) * 128)
        for half in range(2):
            nsl = slice(half * 512, (half + 1) * 512)
            for k in range(8):
                pb.op("gpsimd", lambda e, k=k: e.dma_start(
                    out=wo[:, :], in_=d_wout[k * 128:(k + 1) * 128, :]), 16)
                pb.TE(lambda e, k=k, tsl=tsl, nsl=nsl, s=(k == 0), p=(k == 7):
                      e.matmul(po[:, :], aot[k][:, tsl], wo[:, nsl], start=s, stop=p))
            pb.V(lambda e, nsl=nsl: e.tensor_copy(ot[:, nsl], po[:, :]))
        pb.dma(xt[:, :], d_xn[tsl, :])
        pb.V(lambda e: e.tensor_tensor(ot[:, :], ot[:, :], xt[:, :], OP.add))
        pb.dma(d_out[tsl, :], ot[:, :], q="gpsimd")
    return pb


def e_global_scalar_max(pb, dstcol, srctile, F, scr, ident, ones1, pgen):
    pb.V(lambda e: e.reduce_max(scr, srctile, axis=AX.X))
    pb.TE(lambda e: e.transpose(pgen[:1, :], scr, ident))
    pb.V(lambda e: e.tensor_copy(dstcol, pgen[:1, 0:1])) if False else None
    pb.V(lambda e: e.reduce_max(pgen[:1, 0:1] if False else scr[:1, :1], pgen[:1, :128].rearrange("p f -> p f"), axis=AX.X))
    pb.TE(lambda e: e.matmul(pgen[:, 0:1], ones1[:1, :], scr[:1, :1], start=True, stop=True))
    pb.V(lambda e: e.tensor_copy(dstcol, pgen[:, 0:1]))


def prog_l3_attn(ncore=NCORE, dbg=False, pb=None, px="", d_xn=None, d_out=None):
    pb = pb or PB("l3a")
    d_xn = d_xn if d_xn is not None else pb.din(px + "xn", [T, D], F32)
    d_wt = {w: pb.din(px + f"w{w}t", [D, D], F32) for w in "qkvo"}
    d_ident = pb.din(px + "c_ident", [128, 128], F32)
    d_out = d_out if d_out is not None else pb.dout(px + "out", [T, D], F32)
    nc = pb.nc
    ik = "ExternalOutput" if dbg else "Internal"
    d_qx = nc.dram_tensor(px + "qx", [T, D], BF16, kind=ik).ap()
    d_qwt = {w: nc.dram_tensor(px + f"qw{w}", [D, D], BF16, kind="Internal").ap() for w in "qkvo"}
    d_y = {w: nc.dram_tensor(px + f"y{w}", [T, D], F32, kind=ik).ap() for w in "qkv"}
    d_q3 = {w: nc.dram_tensor(px + f"q3{w}", [T, D], BF16, kind=ik).ap() for w in "qkv"}
    d_sc = nc.dram_tensor(px + "scl3", [16 * T, T], F32, kind="Internal").ap()
    d_w6 = nc.dram_tensor(px + "w6l3", [16 * T, T], F32, kind="Internal").ap()
    d_q6 = nc.dram_tensor(px + "q6l3", [16 * T, T], BF16, kind="Internal").ap()
    d_yo = nc.dram_tensor(px + "yol3", [T, D], F32, kind=ik).ap()
    d_at = nc.dram_tensor(px + "atl3", [T, D], BF16, kind="Internal").ap()
    d_yf = nc.dram_tensor(px + "yfl3", [T, D], F32, kind="Internal").ap()
    cc_in = nc.dram_tensor(px + "cc_in", [128, 8], F32, kind="Internal")
    cc_out = nc.dram_tensor(px + "cc_out", [128, 8], F32, kind="Internal")

    ident = pb.sb("identl3", (128, 128))
    pb.dma(ident[:, :], d_ident)
    ones1 = pb.sb("ones1l3", (1, 128))
    pb.G(lambda e: e.memset(ones1[:, :], 1.0))
    pgen = pb.ps("pgenl3", (128, 128))
    scr1 = pb.sb("scr1l3", (128, 1))
    stats = pb.sb("stats", (128, 8))
    pb.G(lambda e: e.memset(stats[:, :], 0.0))
    xt = pb.sb("xtl3", (128, D))
    ab = pb.sb("abl3", (128, 2 * D))
    itl = pb.sb("itl3", (128, D), I32)
    vb = pb.sb("vbl3", (128, D))
    ob = pb.sb("obl3", (128, D), BF16)
    run = pb.sb("runl3", (128, 1))

    def allreduce(cols, op=OP.max):
        pb.dma(cc_in.ap(), stats[:, :8], q="gpsimd")
        pb.op("gpsimd", lambda e: e.collective_compute(
            "AllReduce", op, replica_groups=[list(range(ncore))],
            ins=[cc_in.ap().opt()], outs=[cc_out.ap().opt()]), 1, kind="x")
        pb.op("gpsimd", lambda e: e.dma_start(out=stats[:, :8], in_=cc_out.ap()), 16)

    def fqscale(dst, src):
        pb.V(lambda e: e.tensor_scalar(dst, src, 1.0 / 127.0, 1e-8, OP.mult, OP.max))

    swcol = pb.sb("swcol", (128, 4))
    winv = pb.sb("winv", (128, 1))
    for wi, w in enumerate("qkvo"):
        pb.G(lambda e: e.memset(run[:, :1], 0.0))
        for i in range(8):
            pb.dma(xt[:, :], d_wt[w][i * 128:(i + 1) * 128, :])
            pb.S(lambda e: e.activation(ab[:, :D], xt[:, :], AF.Abs))
            pb.V(lambda e: e.reduce_max(scr1[:, :1], ab[:, :D], axis=AX.X))
            pb.V(lambda e: e.tensor_tensor(run[:, :1], run[:, :1], scr1[:, :1], OP.max))
        pb.TE(lambda e: e.transpose(pgen[:1, :], run[:, :1], ident[:, :]))
        pb.V(lambda e: e.tensor_copy(ab[:1, :128], pgen[:1, :128]))
        pb.V(lambda e: e.reduce_max(scr1[:1, :1], ab[:1, :128], axis=AX.X))
        pb.TE(lambda e: e.matmul(pgen[:, 0:1], ones1[:1, :], scr1[:1, :1], start=True, stop=True))
        pb.V(lambda e, wi=wi: e.tensor_scalar(swcol[:, wi:wi + 1], pgen[:, 0:1],
                                              1.0 / 127.0, 1e-8, OP.mult, OP.max))
        pb.V(lambda e, wi=wi: e.reciprocal(winv[:, :1], swcol[:, wi:wi + 1]))
        for i in range(8):
            pb.dma(xt[:, :], d_wt[w][i * 128:(i + 1) * 128, :])
            pb.V(lambda e: e.tensor_scalar(vb[:, :], xt[:, :], winv[:, :1], None, OP.mult))
            pb.V(lambda e: e.tensor_copy(itl[:, :], vb[:, :]))
            pb.V(lambda e: e.tensor_copy(ob[:, :], itl[:, :]))
            pb.dma(d_qwt[w][i * 128:(i + 1) * 128, :], ob[:, :], q="gpsimd")

    pb.G(lambda e: e.memset(run[:, :1], 0.0))
    for i in range(NT):
        pb.dma(xt[:, :], d_xn[i * 128:(i + 1) * 128, :])
        pb.S(lambda e: e.activation(ab[:, :D], xt[:, :], AF.Abs))
        pb.V(lambda e: e.reduce_max(scr1[:, :1], ab[:, :D], axis=AX.X))
        pb.V(lambda e: e.tensor_tensor(run[:, :1], run[:, :1], scr1[:, :1], OP.max))
    pb.TE(lambda e: e.transpose(pgen[:1, :], run[:, :1], ident[:, :]))
    pb.V(lambda e: e.tensor_copy(ab[:1, :128], pgen[:1, :128]))
    pb.V(lambda e: e.reduce_max(scr1[:1, :1], ab[:1, :128], axis=AX.X))
    pb.TE(lambda e: e.matmul(pgen[:, 0:1], ones1[:1, :], scr1[:1, :1], start=True, stop=True))
    pb.V(lambda e: e.tensor_copy(stats[:, 0:1], pgen[:, 0:1]))
    allreduce(1)
    sx = pb.sb("sxl3", (128, 1))
    sxi = pb.sb("sxil3", (128, 1))
    fqscale(sx[:, :1], stats[:, 0:1])
    pb.V(lambda e: e.reciprocal(sxi[:, :1], sx[:, :1]))
    for i in range(NT):
        pb.dma(xt[:, :], d_xn[i * 128:(i + 1) * 128, :])
        pb.V(lambda e: e.tensor_scalar(vb[:, :], xt[:, :], sxi[:, :1], None, OP.mult))
        pb.V(lambda e: e.tensor_copy(itl[:, :], vb[:, :]))
        pb.V(lambda e: e.tensor_copy(ob[:, :], itl[:, :]))
        pb.dma(d_qx[i * 128:(i + 1) * 128, :], ob[:, :], q="gpsimd")

    wq = pb.sb("wql3", (128, D), BF16)
    pq = pb.ps("pql3", (128, 512))
    sxw = pb.sb("sxwl3", (128, 1))
    ymax = pb.sb("ymaxl3", (128, 3))
    s2c = pb.sb("s2c", (128, 3))
    s3c = pb.sb("s3c", (128, 3))
    r23 = pb.sb("r23", (128, 3))
    i1 = pb.sb("i1l3", (128, 1), I32)
    t1c = pb.sb("t1c", (128, 1))
    s2i = pb.sb("s2i", (128, 3))
    qxT = [pb.sb(f"qxTl3{k}", (128, T), BF16) for k in range(8)]
    for k in range(8):
        pb.dmaT(qxT[k][:, :], d_qx[:, k * 128:(k + 1) * 128])
    for wi, w in enumerate("qkv"):
        pb.V(lambda e, wi=wi: e.tensor_tensor(sxw[:, :1], sx[:, :1], swcol[:, wi:wi + 1], OP.mult))
        pb.G(lambda e: e.memset(run[:, :1], 0.0))
        for i in range(NT):
            tsl = slice(i * 128, (i + 1) * 128)
            for half in range(2):
                nsl = slice(half * 512, (half + 1) * 512)
                for k in range(8):
                    pb.op("gpsimd", lambda e, k=k, w=w: e.dma_start(
                        out=wq[:, :], in_=d_qwt[w][k * 128:(k + 1) * 128, :]), 16)
                    pb.TE(lambda e, k=k, tsl=tsl, nsl=nsl, s=(k == 0), p=(k == 7):
                          e.matmul(pq[:, :], qxT[k][:, tsl], wq[:, nsl], start=s, stop=p))
                pb.V(lambda e, nsl=nsl: e.tensor_scalar(vb[:, nsl], pq[:, :], sxw[:, :1], None, OP.mult))
            pb.S(lambda e: e.activation(ab[:, :D], vb[:, :], AF.Abs))
            pb.V(lambda e: e.reduce_max(scr1[:, :1], ab[:, :D], axis=AX.X))
            pb.V(lambda e: e.tensor_tensor(run[:, :1], run[:, :1], scr1[:, :1], OP.max))
            pb.dma(d_y[w][tsl, :], vb[:, :], q="gpsimd")
        pb.TE(lambda e: e.transpose(pgen[:1, :], run[:, :1], ident[:, :]))
        pb.V(lambda e: e.tensor_copy(ab[:1, :128], pgen[:1, :128]))
        pb.V(lambda e: e.reduce_max(scr1[:1, :1], ab[:1, :128], axis=AX.X))
        pb.TE(lambda e: e.matmul(pgen[:, 0:1], ones1[:1, :], scr1[:1, :1], start=True, stop=True))
        pb.V(lambda e, wi=wi: e.tensor_copy(stats[:, wi:wi + 1], pgen[:, 0:1]))
    allreduce(3)

    for wi in range(3):
        fqscale(s2c[:, wi:wi + 1], stats[:, wi:wi + 1])
        pb.V(lambda e, wi=wi: e.reciprocal(t1c[:, :1], s2c[:, wi:wi + 1]))
        pb.V(lambda e, wi=wi: e.tensor_tensor(t1c[:, :1], stats[:, wi:wi + 1], t1c[:, :1], OP.mult))
        pb.V(lambda e: e.tensor_copy(i1[:, :1], t1c[:, :1]))
        pb.V(lambda e: e.tensor_copy(t1c[:, :1], i1[:, :1]))
        pb.V(lambda e, wi=wi: e.tensor_tensor(t1c[:, :1], t1c[:, :1], s2c[:, wi:wi + 1], OP.mult))
        fqscale(s3c[:, wi:wi + 1], t1c[:, :1])
        pb.V(lambda e, wi=wi: e.reciprocal(r23[:, wi:wi + 1], s3c[:, wi:wi + 1]))
        pb.V(lambda e, wi=wi: e.tensor_tensor(r23[:, wi:wi + 1], s2c[:, wi:wi + 1],
                                              r23[:, wi:wi + 1], OP.mult))
    for wi in range(3):
        pb.V(lambda e, wi=wi: e.reciprocal(s2i[:, wi:wi + 1], s2c[:, wi:wi + 1]))
    for wi, w in enumerate("qkv"):
        for i in range(NT):
            tsl = slice(i * 128, (i + 1) * 128)
            pb.dma(xt[:, :], d_y[w][tsl, :])
            pb.V(lambda e, wi=wi: e.tensor_scalar(vb[:, :], xt[:, :], s2i[:, wi:wi + 1], None, OP.mult))
            pb.V(lambda e: e.tensor_copy(itl[:, :], vb[:, :]))
            pb.V(lambda e: e.tensor_copy(vb[:, :], itl[:, :]))
            pb.V(lambda e, wi=wi: e.tensor_scalar(vb[:, :], vb[:, :], r23[:, wi:wi + 1], None, OP.mult))
            pb.V(lambda e: e.tensor_copy(itl[:, :], vb[:, :]))
            pb.V(lambda e: e.tensor_copy(ob[:, :], itl[:, :]))
            pb.dma(d_q3[w][tsl, :], ob[:, :], q="gpsimd")
    pb.sbfree(*qxT)

    qT2 = pb.sb("qT2l3", (128, T), BF16)
    kT2 = pb.sb("kT2l3", (128, T), BF16)
    sct = pb.sb("sctl3", (128, T))
    rowmx = pb.sb("rowmx", (128, 256))
    denst = pb.sb("denst", (128, 256))
    psck = pb.ps("psckl3", (128, 512))
    pb.G(lambda e: e.memset(run[:, :1], 0.0))
    for hp in range(8):
        pb.dmaT(qT2[:, :], d_q3["q"][:, hp * 128:(hp + 1) * 128])
        pb.dmaT(kT2[:, :], d_q3["k"][:, hp * 128:(hp + 1) * 128])
        for hh in range(2):
            h = hp * 2 + hh
            rsl = slice(hh * 64, hh * 64 + 64)
            for i in range(NT):
                tsl = slice(i * 128, (i + 1) * 128)
                for sc4 in range(4):
                    ssl = slice(sc4 * 512, (sc4 + 1) * 512)
                    pb.TE(lambda e, rsl=rsl, tsl=tsl, ssl=ssl: e.matmul(
                        psck[:, :], qT2[rsl, tsl], kT2[rsl, ssl], start=True, stop=True))
                    pb.V(lambda e, ssl=ssl: e.tensor_copy(sct[:, ssl], psck[:, :]))
                pb.V(lambda e: e.reduce_max(scr1[:, :1], sct[:, :], axis=AX.X))
                pb.V(lambda e, h=h, i=i: e.tensor_copy(rowmx[:, h * 16 + i:h * 16 + i + 1], scr1[:, :1]))
                pb.S(lambda e: e.activation(ab[:1, :1], scr1[:1, :1], AF.Abs)) if False else None
                pb.S(lambda e: e.activation(ab[:, :1024], sct[:, :1024], AF.Abs))
                pb.S(lambda e: e.activation(ab[:, 1024:2048], sct[:, 1024:2048], AF.Abs))
                pb.V(lambda e: e.reduce_max(t1c[:, :1], ab[:, :T], axis=AX.X))
                pb.V(lambda e: e.tensor_tensor(run[:, :1], run[:, :1], t1c[:, :1], OP.max))
                pb.dma(d_sc[h * T + i * 128:h * T + (i + 1) * 128, :], sct[:, :], q="gpsimd")
    pb.TE(lambda e: e.transpose(pgen[:1, :], run[:, :1], ident[:, :]))
    pb.V(lambda e: e.tensor_copy(ab[:1, :128], pgen[:1, :128]))
    pb.V(lambda e: e.reduce_max(scr1[:1, :1], ab[:1, :128], axis=AX.X))
    pb.TE(lambda e: e.matmul(pgen[:, 0:1], ones1[:1, :], scr1[:1, :1], start=True, stop=True))
    pb.V(lambda e: e.tensor_copy(stats[:, 0:1], pgen[:, 0:1]))
    pb.V(lambda e: e.tensor_tensor(stats[:, 0:1], stats[:, 0:1], s3c[:, 0:1], OP.mult))
    pb.V(lambda e: e.tensor_tensor(stats[:, 0:1], stats[:, 0:1], s3c[:, 1:2], OP.mult))
    allreduce(1)
    s5 = pb.sb("s5l3", (128, 1))
    gam = pb.sb("gaml3", (128, 1))
    alp = pb.sb("alpl3", (128, 1))
    fqscale(s5[:, :1], stats[:, 0:1])
    pb.V(lambda e: e.reciprocal(gam[:, :1], s5[:, :1]))
    pb.V(lambda e: e.tensor_tensor(gam[:, :1], gam[:, :1], s3c[:, 0:1], OP.mult))
    pb.V(lambda e: e.tensor_tensor(gam[:, :1], gam[:, :1], s3c[:, 1:2], OP.mult))
    pb.V(lambda e: e.tensor_scalar(alp[:, :1], s5[:, :1], 0.125, None, OP.mult))

    q5mx = pb.sb("q5mx", (128, 1))
    negb = pb.sb("negbl3", (128, 1))
    wtile = pb.sb("wtilel3", (128, T))
    runmin = pb.sb("runminl3", (128, 1))
    pb.G(lambda e: e.memset(runmin[:, :1], 1e30))
    for h in range(16):
        for i in range(NT):
            rr = slice(h * T + i * 128, h * T + (i + 1) * 128)
            pb.dma(sct[:, :], d_sc[rr, :])
            pb.V(lambda e: e.tensor_scalar(sct[:, :], sct[:, :], gam[:, :1], None, OP.mult))
            pb.V(lambda e: e.tensor_copy(itl[:, :], sct[:, :1024]))
            pb.V(lambda e: e.tensor_copy(sct[:, :1024], itl[:, :]))
            pb.V(lambda e: e.tensor_copy(itl[:, :], sct[:, 1024:2048]))
            pb.V(lambda e: e.tensor_copy(sct[:, 1024:2048], itl[:, :]))
            pb.V(lambda e, h=h, i=i: e.tensor_tensor(q5mx[:, :1], rowmx[:, h * 16 + i:h * 16 + i + 1], gam[:, :1], OP.mult))
            pb.V(lambda e: e.tensor_copy(i1[:, :1], q5mx[:, :1]))
            pb.V(lambda e: e.tensor_copy(q5mx[:, :1], i1[:, :1]))
            pb.V(lambda e: e.tensor_tensor(negb[:, :1], q5mx[:, :1], alp[:, :1], OP.mult))
            pb.V(lambda e: e.tensor_scalar(negb[:, :1], negb[:, :1], -1.0, None, OP.mult))
            pb.S(lambda e: e.activation(wtile[:, :], sct[:, :], AF.Exp, bias=negb[:, :1],
                                        scale=alp[:, :1], accum_out=scr1[:, :1]))
            pb.V(lambda e, h=h, i=i: e.tensor_copy(denst[:, h * 16 + i:h * 16 + i + 1], scr1[:, :1]))
            pb.V(lambda e: e.tensor_tensor(runmin[:, :1], runmin[:, :1], scr1[:, :1], OP.min))
            pb.dma(d_w6[rr, :], wtile[:, :], q="gpsimd")
    pb.V(lambda e: e.tensor_scalar(runmin[:, :1], runmin[:, :1], -1.0, None, OP.mult))
    pb.TE(lambda e: e.transpose(pgen[:1, :], runmin[:, :1], ident[:, :]))
    pb.V(lambda e: e.tensor_copy(ab[:1, :128], pgen[:1, :128]))
    pb.V(lambda e: e.reduce_max(scr1[:1, :1], ab[:1, :128], axis=AX.X))
    pb.TE(lambda e: e.matmul(pgen[:, 0:1], ones1[:1, :], scr1[:1, :1], start=True, stop=True))
    pb.V(lambda e: e.tensor_copy(stats[:, 0:1], pgen[:, 0:1]))
    allreduce(1)
    s6 = pb.sb("s6l3", (128, 1))
    pb.V(lambda e: e.tensor_scalar(t1c[:, :1], stats[:, 0:1], -1.0, None, OP.mult))
    pb.V(lambda e: e.reciprocal(t1c[:, :1], t1c[:, :1]))
    fqscale(s6[:, :1], t1c[:, :1])

    q3vt = [pb.sb(f"q3vt{j}", (128, D), BF16) for j in range(NT)]
    for j in range(NT):
        pb.dma(q3vt[j][:, :], d_q3["v"][j * 128:(j + 1) * 128, :])
    beta = pb.sb("betal3", (128, 1))
    for h in range(16):
        for i in range(NT):
            rr = slice(h * T + i * 128, h * T + (i + 1) * 128)
            pb.dma(sct[:, :], d_w6[rr, :])
            pb.V(lambda e, h=h, i=i: e.tensor_tensor(beta[:, :1], denst[:, h * 16 + i:h * 16 + i + 1], s6[:, :1], OP.mult))
            pb.V(lambda e: e.reciprocal(beta[:, :1], beta[:, :1]))
            pb.V(lambda e: e.tensor_scalar(sct[:, :], sct[:, :], beta[:, :1], None, OP.mult))
            pb.V(lambda e: e.tensor_copy(itl[:, :], sct[:, :1024]))
            pb.V(lambda e: e.tensor_copy(sct[:, :1024], itl[:, :]))
            pb.V(lambda e: e.tensor_copy(itl[:, :], sct[:, 1024:2048]))
            pb.V(lambda e: e.tensor_copy(sct[:, 1024:2048], itl[:, :]))
            pb.V(lambda e: e.tensor_copy(wtile[:, :], sct[:, :]))
            pb.dma(d_q6[rr, :], wtile[:, :], q="gpsimd")
    s6v = pb.sb("s6vl3", (128, 1))
    pb.V(lambda e: e.tensor_tensor(s6v[:, :1], s6[:, :1], s3c[:, 2:3], OP.mult))
    q6T = pb.sb("q6Tl3", (128, T), BF16)
    pov = pb.ps("povl3", (128, 64))
    yot = pb.sb("yotl3", (128, D))
    pb.G(lambda e: e.memset(run[:, :1], 0.0))
    for i in range(NT):
        tsl = slice(i * 128, (i + 1) * 128)
        for h in range(16):
            for c in range(NT):
                if i == 0 or True:
                    pass
            pass
        pass
    for h in range(16):
        for c in range(NT):
            pb.dmaT(q6T[:, :] if False else q6T[:, :],
                    d_w6[0:1, 0:128]) if False else None
        pass
    q6hT = [pb.sb(f"q6hT{c}", (128, T), BF16) for c in range(NT)]
    ymaxc = [pb.sb(f"ymaxc{i}", (128, 1)) for i in range(NT)]
    for i in range(NT):
        pb.G(lambda e, i=i: e.memset(ymaxc[i][:, :1], 0.0))
    for h in range(16):
        for c in range(NT):
            pb.op("scalar", lambda e, h=h, c=c: e.dma_start(
                out=q6hT[c][:, :],
                in_=d_q6[h * T:(h + 1) * T, c * 128:(c + 1) * 128],
                transpose=True), 16)
        for i in range(NT):
            tsl = slice(i * 128, (i + 1) * 128)
            for c in range(NT):
                pb.TE(lambda e, c=c, h=h, tsl=tsl, s=(c == 0), p=(c == NT - 1):
                      e.matmul(pov[:, :], q6hT[c][:, tsl], q3vt[c][:, h * 64:(h + 1) * 64],
                               start=s, stop=p))
            pb.V(lambda e: e.tensor_scalar(yot[:, 0:64], pov[:, :], s6v[:, :1], None, OP.mult))
            pb.S(lambda e: e.activation(ab[:, 0:64], yot[:, 0:64], AF.Abs))
            pb.V(lambda e: e.reduce_max(scr1[:, :1], ab[:, 0:64], axis=AX.X))
            pb.V(lambda e, i=i: e.tensor_tensor(ymaxc[i][:, :1], ymaxc[i][:, :1], scr1[:, :1], OP.max))
            pb.op("gpsimd", lambda e, h=h, tsl=tsl: e.dma_start(
                out=d_yo[tsl, h * 64:(h + 1) * 64], in_=yot[:, 0:64]), 16)
    for i in range(NT):
        pb.V(lambda e, i=i: e.tensor_tensor(run[:, :1], run[:, :1], ymaxc[i][:, :1], OP.max))
    pb.sbfree(*(list(reversed(ymaxc))))
    pb.sbfree(*(list(reversed(q6hT))))
    pb.TE(lambda e: e.transpose(pgen[:1, :], run[:, :1], ident[:, :]))
    pb.V(lambda e: e.tensor_copy(ab[:1, :128], pgen[:1, :128]))
    pb.V(lambda e: e.reduce_max(scr1[:1, :1], ab[:1, :128], axis=AX.X))
    pb.TE(lambda e: e.matmul(pgen[:, 0:1], ones1[:1, :], scr1[:1, :1], start=True, stop=True))
    pb.V(lambda e: e.tensor_copy(stats[:, 0:1], pgen[:, 0:1]))
    allreduce(1)

    s7 = pb.sb("s7l3", (128, 1))
    s7i = pb.sb("s7il3", (128, 1))
    r78 = pb.sb("r78l3", (128, 1))
    fqscale(s7[:, :1], stats[:, 0:1])
    pb.V(lambda e: e.reciprocal(s7i[:, :1], s7[:, :1]))
    pb.V(lambda e: e.tensor_tensor(t1c[:, :1], stats[:, 0:1], s7i[:, :1], OP.mult))
    pb.V(lambda e: e.tensor_copy(i1[:, :1], t1c[:, :1]))
    pb.V(lambda e: e.tensor_copy(t1c[:, :1], i1[:, :1]))
    pb.V(lambda e: e.tensor_tensor(t1c[:, :1], t1c[:, :1], s7[:, :1], OP.mult))
    fqscale(t1c[:, :1], t1c[:, :1])
    pb.V(lambda e: e.reciprocal(r78[:, :1], t1c[:, :1]))
    pb.V(lambda e: e.tensor_tensor(r78[:, :1], s7[:, :1], r78[:, :1], OP.mult))
    s8w = pb.sb("s8wl3", (128, 1))
    pb.V(lambda e: e.tensor_tensor(s8w[:, :1], t1c[:, :1], swcol[:, 3:4], OP.mult))
    for i in range(NT):
        tsl = slice(i * 128, (i + 1) * 128)
        pb.dma(xt[:, :], d_yo[tsl, :])
        pb.V(lambda e: e.tensor_scalar(vb[:, :], xt[:, :], s7i[:, :1], None, OP.mult))
        pb.V(lambda e: e.tensor_copy(itl[:, :], vb[:, :]))
        pb.V(lambda e: e.tensor_copy(vb[:, :], itl[:, :]))
        pb.V(lambda e: e.tensor_scalar(vb[:, :], vb[:, :], r78[:, :1], None, OP.mult))
        pb.V(lambda e: e.tensor_copy(itl[:, :], vb[:, :]))
        pb.V(lambda e: e.tensor_copy(ob[:, :], itl[:, :]))
        pb.dma(d_at[tsl, :], ob[:, :], q="gpsimd")
    atT = [pb.sb(f"atTl3{k}", (128, T), BF16) for k in range(8)]
    for k in range(8):
        pb.dmaT(atT[k][:, :], d_at[:, k * 128:(k + 1) * 128])
    pb.G(lambda e: e.memset(run[:, :1], 0.0))
    for i in range(NT):
        tsl = slice(i * 128, (i + 1) * 128)
        for half in range(2):
            nsl = slice(half * 512, (half + 1) * 512)
            for k in range(8):
                pb.op("gpsimd", lambda e, k=k: e.dma_start(
                    out=wq[:, :], in_=d_qwt["o"][k * 128:(k + 1) * 128, :]), 16)
                pb.TE(lambda e, k=k, tsl=tsl, nsl=nsl, s=(k == 0), p=(k == 7):
                      e.matmul(pq[:, :], atT[k][:, tsl], wq[:, nsl], start=s, stop=p))
            pb.V(lambda e, nsl=nsl: e.tensor_scalar(vb[:, nsl], pq[:, :], s8w[:, :1], None, OP.mult))
        pb.S(lambda e: e.activation(ab[:, :D], vb[:, :], AF.Abs))
        pb.V(lambda e: e.reduce_max(scr1[:, :1], ab[:, :D], axis=AX.X))
        pb.V(lambda e: e.tensor_tensor(run[:, :1], run[:, :1], scr1[:, :1], OP.max))
        pb.dma(d_yf[tsl, :], vb[:, :], q="gpsimd")
    pb.TE(lambda e: e.transpose(pgen[:1, :], run[:, :1], ident[:, :]))
    pb.V(lambda e: e.tensor_copy(ab[:1, :128], pgen[:1, :128]))
    pb.V(lambda e: e.reduce_max(scr1[:1, :1], ab[:1, :128], axis=AX.X))
    pb.TE(lambda e: e.matmul(pgen[:, 0:1], ones1[:1, :], scr1[:1, :1], start=True, stop=True))
    pb.V(lambda e: e.tensor_copy(stats[:, 0:1], pgen[:, 0:1]))
    allreduce(1)
    sf = pb.sb("sfl3", (128, 1))
    sfi = pb.sb("sfil3", (128, 1))
    fqscale(sf[:, :1], stats[:, 0:1])
    pb.V(lambda e: e.reciprocal(sfi[:, :1], sf[:, :1]))
    for i in range(NT):
        tsl = slice(i * 128, (i + 1) * 128)
        pb.dma(xt[:, :], d_yf[tsl, :])
        pb.V(lambda e: e.tensor_scalar(vb[:, :], xt[:, :], sfi[:, :1], None, OP.mult))
        pb.V(lambda e: e.tensor_copy(itl[:, :], vb[:, :]))
        pb.V(lambda e: e.tensor_copy(vb[:, :], itl[:, :]))
        pb.V(lambda e: e.tensor_scalar(vb[:, :], vb[:, :], sf[:, :1], None, OP.mult))
        pb.dma(d_out[tsl, :], vb[:, :], q="gpsimd")
    return pb


def prog_l0_attn(dbg=False, pb=None, px="", d_xn=None, d_out=None):
    E = 2048
    pb = pb or PB("l0a")
    d_xn = d_xn if d_xn is not None else pb.din(px + "xn", [T, D], F32)
    d_inw = pb.din(px + "inw", [D, 2 * E], F32)
    d_convw = pb.din(px + "convw", [E, 4], F32)
    d_convb = pb.din(px + "convb", [E, 1], F32)
    d_dtw = pb.din(px + "dtw", [E, E], F32)
    d_dtb = pb.din(px + "dtb", [E, 1], F32)
    d_bw = pb.din(px + "bw", [E, 64], F32)
    d_cw = pb.din(px + "cw", [E, 64], F32)
    d_alog = pb.din(px + "alog", [1, 64], F32)
    d_outw = pb.din(px + "outw", [E, D], F32)
    d_gatew = pb.din(px + "gatew", [D, E], F32)
    d_upw = pb.din(px + "upw", [D, E], F32)
    d_downw = pb.din(px + "downw", [E, D], F32)
    d_inormw = pb.din(px + "inormw", [1, D], F32)
    d_fnormw = pb.din(px + "fnormw", [1, D], F32)
    d_out = d_out if d_out is not None else pb.dout(px + "out", [T, D], F32)
    nc = pb.nc
    ik = "ExternalOutput" if dbg else "Internal"
    d_n1b = nc.dram_tensor(px + "n1b0", [T, D], BF16, kind=ik).ap()
    d_xz = nc.dram_tensor(px + "xz0", [E, T], F32, kind=ik).ap()
    d_zs = nc.dram_tensor(px + "zs0", [E, T], BF16, kind=ik).ap()
    d_xc = nc.dram_tensor(px + "xc0", [E, T], BF16, kind=ik).ap()
    d_dt = nc.dram_tensor(px + "dt0", [E, T], BF16, kind=ik).ap()
    d_u = nc.dram_tensor(px + "u0", [E, T], BF16, kind=ik).ap()
    d_y = nc.dram_tensor(px + "y0", [E, T], BF16, kind=ik).ap()
    d_xo = nc.dram_tensor(px + "xo0", [T, D], F32, kind=ik).ap()
    d_n2b = nc.dram_tensor(px + "n2b0", [T, D], BF16, kind=ik).ap()
    d_h = nc.dram_tensor(px + "h0", [E, T], BF16, kind=ik).ap()

    ones1 = pb.sb("ones1l0", (1, 128))
    pb.G(lambda e: e.memset(ones1[:, :], 1.0))
    pbig = pb.ps("pbigl0", (128, 2048))
    psA = pb.ps("psAl0", (128, 512))
    psB = pb.ps("psBl0", (128, 512))
    psb = psA
    rw = pb.sb("rwl0", (1, D))
    pb.dma(rw[:1, :], d_inormw)
    rwf = pb.sb("rwfl0", (128, D))
    for j in range(2):
        pb.TE(lambda e, j=j: e.matmul(psb[:, :], ones1[:1, :], rw[:1, j * 512:(j + 1) * 512],
                                      start=True, stop=True))
        pb.V(lambda e, j=j: e.tensor_copy(rwf[:, j * 512:(j + 1) * 512], psb[:, :]))
    xt = pb.sb("xtl0", (128, D))
    sq = pb.sb("sql0", (128, D))
    ms = pb.sb("msl0", (128, 1))
    n1 = pb.sb("n1l0", (128, D))
    n1b = pb.sb("n1bl0", (128, D), BF16)
    for i in range(NT):
        tsl = slice(i * 128, (i + 1) * 128)
        pb.dma(xt[:, :], d_xn[tsl, :])
        pb.S(lambda e: e.activation(sq[:, :], xt[:, :], AF.Square))
        pb.V(lambda e: e.reduce_sum(ms[:, :1], sq[:, :], axis=AX.X))
        pb.V(lambda e: e.tensor_scalar(ms[:, :1], ms[:, :1], 1.0 / D, 1e-5, OP.mult, OP.add))
        pb.S(lambda e: e.activation(ms[:, :1], ms[:, :1], AF.Ln))
        pb.S(lambda e: e.activation(ms[:, :1], ms[:, :1], AF.Exp, scale=-0.5))
        pb.V(lambda e: e.tensor_scalar(n1[:, :], xt[:, :], ms[:, :1], None, OP.mult))
        pb.V(lambda e: e.tensor_tensor(n1[:, :], n1[:, :], rwf[:, :], OP.mult))
        pb.V(lambda e: e.tensor_copy(n1b[:, :], n1[:, :]))
        pb.dma(d_n1b[tsl, :], n1b[:, :], q="gpsimd")
    wt = pb.sb("wtl0", (128, 128), BF16)
    stile = pb.sb("stileL0", (128, 512))
    sbt = pb.sb("sbtL0", (128, T), BF16)
    n1T = [pb.sb(f"n1T0{k}", (128, T), BF16) for k in range(8)]
    for k in range(8):
        pb.dmaT(n1T[k][:, :], d_n1b[:, k * 128:(k + 1) * 128])
    for c in range(32):
        for tc in range(4):
            tfl = slice(tc * 512, (tc + 1) * 512)
            for k in range(8):
                pb.op("gpsimd", lambda e, k=k, c=c: e.dma_start(
                    out=wt[:, :], in_=d_inw[k * 128:(k + 1) * 128, c * 128:(c + 1) * 128]), 16)
                pb.TE(lambda e, k=k, tfl=tfl, s=(k == 0), p=(k == 7):
                      e.matmul(psb[:, :], wt[:, :], n1T[k][:, tfl], start=s, stop=p))
            if c < 16:
                pb.V(lambda e: e.tensor_copy(stile[:, :], psb[:, :]))
                pb.dma(d_xz[c * 128:(c + 1) * 128, tfl], stile[:, :], q="gpsimd")
            else:
                pb.S(lambda e, tfl=tfl: e.activation(sbt[:, tfl], psb[:, :], AF.Silu))
        if c >= 16:
            pb.dma(d_zs[(c - 16) * 128:(c - 15) * 128, :], sbt[:, :], q="gpsimd")
    pb.sbfree(*n1T)
    BT = pb.sb("BTl0", (64, T), BF16)
    CT = pb.sb("CTl0", (64, T), BF16)
    arow = pb.sb("arowl0", (1, 64))
    afull = pb.sb("afulll0", (128, 64))
    browk = pb.sb("browk", (1, T), BF16)
    crowk = pb.sb("crowk", (1, T), BF16)
    onesbl0 = pb.sb("onesbl0", (1, 128), BF16)
    pb.G(lambda e: e.memset(onesbl0[:, :], 1.0))
    xck = pb.sb("xckl0", (128, T), BF16)
    xzp = pb.sb("xzp", (128, T + 3))
    wcv = pb.sb("wcv", (128, 4))
    cbc = pb.sb("cbc", (128, 1))
    acc = pb.sb("accl0", (128, T))
    for j in range(16):
        esl = slice(j * 128, (j + 1) * 128)
        pb.G(lambda e: e.memset(xzp[:, 0:1], 0.0))
        pb.G(lambda e: e.memset(xzp[:, T + 1:T + 3], 0.0))
        pb.dma(xzp[:, 1:T + 1], d_xz[esl, :])
        pb.dma(wcv[:, :], d_convw[esl, :])
        pb.dma(cbc[:, :], d_convb[esl, :])
        pb.V(lambda e: e.tensor_scalar(acc[:, :], xzp[:, 0:T], wcv[:, 0:1], None, OP.mult))
        for kk in range(1, 4):
            pb.V(lambda e, kk=kk: e.scalar_tensor_tensor(
                acc[:, :], xzp[:, kk:kk + T], wcv[:, kk:kk + 1], acc[:, :], OP.mult, OP.add))
        pb.V(lambda e: e.tensor_scalar(acc[:, :], acc[:, :], cbc[:, :1], None, OP.add))
        pb.S(lambda e: e.activation(acc[:, :], acc[:, :], AF.Silu))
        pb.V(lambda e: e.tensor_copy(sbt[:, :], acc[:, :]))
        pb.dma(d_xc[esl, :], sbt[:, :], q="gpsimd")
    pd = [pbig[:, t_ * 512:(t_ + 1) * 512] for t_ in range(4)]
    dtb = pb.sb("dtbl0", (128, 1))
    dtt = pb.sb("dttl0", (128, T))
    au = pb.sb("aul0", (128, T))
    for j in range(16):
        esl = slice(j * 128, (j + 1) * 128)
        for k in range(16):
            pb.op("gpsimd", lambda e, k=k, j=j: e.dma_start(
                out=wt[:, :], in_=d_dtw[k * 128:(k + 1) * 128, j * 128:(j + 1) * 128]), 16)
            pb.op("gpsimd", lambda e, k=k: e.dma_start(
                out=xck[:, :], in_=d_xc[k * 128:(k + 1) * 128, :]), 16)
            for tc in range(4):
                pb.TE(lambda e, k=k, tc=tc, s=(k == 0), p=(k == 15):
                      e.matmul(pd[tc], wt[:, :], xck[:, tc * 512:(tc + 1) * 512],
                               start=s, stop=p))
        pb.dma(dtb[:, :], d_dtb[esl, :])
        for tc in range(4):
            pb.V(lambda e, tc=tc: e.tensor_scalar(dtt[:, tc * 512:(tc + 1) * 512],
                                                  pd[tc], dtb[:, :1], None, OP.add))
        pb.S(lambda e: e.activation(au[:, :], dtt[:, :], AF.Abs))
        pb.S(lambda e: e.activation(au[:, :], au[:, :], AF.Exp, scale=-1.0))
        pb.S(lambda e: e.activation(au[:, :], au[:, :], AF.Ln, bias=1.0))
        pb.V(lambda e: e.tensor_scalar(dtt[:, :], dtt[:, :], 0.0, None, OP.max))
        pb.V(lambda e: e.tensor_tensor(dtt[:, :], dtt[:, :], au[:, :], OP.add))
        pb.V(lambda e: e.tensor_scalar(dtt[:, :], dtt[:, :], 0.001, 0.1, OP.max, OP.min))
        pb.V(lambda e: e.tensor_copy(sbt[:, :], dtt[:, :]))
        pb.dma(d_dt[esl, :], sbt[:, :], q="gpsimd")
        pb.op("gpsimd", lambda e, esl=esl: e.dma_start(out=xck[:, :], in_=d_xc[esl, :]), 16)
        pb.V(lambda e: e.tensor_tensor(dtt[:, :], dtt[:, :], xck[:, :], OP.mult))
        pb.V(lambda e: e.tensor_copy(sbt[:, :], dtt[:, :]))
        pb.dma(d_u[esl, :], sbt[:, :], q="gpsimd")
    bwt = pb.sb("bwtl0", (128, 128), BF16)
    for (dst, wsrc) in ((BT, d_bw), (CT, d_cw)):
        for tc in range(4):
            for k in range(16):
                pb.op("gpsimd", lambda e, k=k, wsrc=wsrc: e.dma_start(
                    out=bwt[:, 0:64], in_=wsrc[k * 128:(k + 1) * 128, :]), 16)
                pb.op("gpsimd", lambda e, k=k: e.dma_start(
                    out=xck[:, :], in_=d_xc[k * 128:(k + 1) * 128, :]), 16)
                pb.TE(lambda e, k=k, tc=tc, s=(k == 0), p=(k == 15):
                      e.matmul(psb[:64, :], bwt[:, 0:64], xck[:, tc * 512:(tc + 1) * 512],
                               start=s, stop=p))
            pb.V(lambda e, dst=dst, tc=tc: e.tensor_copy(dst[:, tc * 512:(tc + 1) * 512], psb[:64, :]))
    pb.dma(arow[:1, :], d_alog)
    pb.S(lambda e: e.activation(arow[:1, :], arow[:1, :], AF.Exp))
    pb.V(lambda e: e.tensor_scalar(arow[:1, :], arow[:1, :], -1.0, None, OP.mult))
    pab = psA[:, 0:64]
    pb.TE(lambda e: e.matmul(pab, ones1[:1, :], arow[:1, :], start=True, stop=True))
    pb.V(lambda e: e.tensor_copy(afull[:, :], pab))
    pb.sbfree(bwt, au, dtt, dtb, acc, cbc, wcv, xzp)
    pbb = pbig
    Bb = pb.sb("Bbl0", (128, T), BF16)
    Cb = pb.sb("Cbl0", (128, T), BF16)
    dth = [pb.sb(f"dth{j}", (128, T), BF16) for j in range(8)]
    uh = [pb.sb(f"uh{j}", (128, T), BF16) for j in range(8)]
    ya = [pb.sb(f"ya{j}", (128, T)) for j in range(8)]
    sco = pb.sb("scol0", (128, T))
    bt2 = pb.sb("bt2l0", (128, T))
    for half in range(2):
        for j in range(8):
            je = half * 8 + j
            pb.op("gpsimd", lambda e, j=j, je=je: e.dma_start(
                out=dth[j][:, :], in_=d_dt[je * 128:(je + 1) * 128, :]), 16)
            pb.op("gpsimd", lambda e, j=j, je=je: e.dma_start(
                out=uh[j][:, :], in_=d_u[je * 128:(je + 1) * 128, :]), 16)
            pb.G(lambda e, j=j: e.memset(ya[j][:, :], 0.0))
        for n in range(64):
            pb.op("gpsimd", lambda e, n=n: e.dma_start(out=browk[:1, :], in_=BT[n:n + 1, :]), 16)
            pb.op("gpsimd", lambda e, n=n: e.dma_start(out=crowk[:1, :], in_=CT[n:n + 1, :]), 16)
            for tc in range(4):
                pb.TE(lambda e, tc=tc: e.matmul(pbb[:, tc * 512:(tc + 1) * 512], onesbl0[:1, :],
                                                browk[:1, tc * 512:(tc + 1) * 512],
                                                start=True, stop=True))
            pb.V(lambda e: e.tensor_copy(Bb[:, :], pbb[:, :]))
            for tc in range(4):
                pb.TE(lambda e, tc=tc: e.matmul(pbb[:, tc * 512:(tc + 1) * 512], onesbl0[:1, :],
                                                crowk[:1, tc * 512:(tc + 1) * 512],
                                                start=True, stop=True))
            pb.V(lambda e: e.tensor_copy(Cb[:, :], pbb[:, :]))
            for j in range(8):
                pb.V(lambda e, j=j, n=n: e.tensor_scalar(
                    sco[:, :], dth[j][:, :], afull[:, n:n + 1], None, OP.mult))
                pb.S(lambda e: e.activation(sco[:, :], sco[:, :], AF.Exp))
                pb.V(lambda e, j=j: e.tensor_tensor(bt2[:, :], uh[j][:, :], Bb[:, :], OP.mult))
                pb.V(lambda e: e.tensor_tensor_scan(bt2[:, :], sco[:, :], bt2[:, :], 0.0,
                                                    OP.mult, OP.add))
                pb.V(lambda e: e.tensor_tensor(bt2[:, :], bt2[:, :], Cb[:, :], OP.mult))
                pb.V(lambda e, j=j: e.tensor_tensor(ya[j][:, :], ya[j][:, :], bt2[:, :], OP.add))
        for j in range(8):
            je = half * 8 + j
            pb.op("gpsimd", lambda e, je=je: e.dma_start(
                out=xck[:, :], in_=d_zs[je * 128:(je + 1) * 128, :]), 16)
            pb.V(lambda e, j=j: e.tensor_tensor(ya[j][:, :], ya[j][:, :], xck[:, :], OP.mult))
            pb.V(lambda e, j=j: e.tensor_copy(sbt[:, :], ya[j][:, :]))
            pb.dma(d_y[je * 128:(je + 1) * 128, :], sbt[:, :], q="gpsimd")
    pb.sbfree(bt2, sco)
    pb.sbfree(*(list(reversed(ya)) + list(reversed(uh)) + list(reversed(dth))))
    pb.sbfree(Cb, Bb)
    yk = pb.sb("ykl0", (128, 128), BF16)
    wo = pb.sb("wol0", (128, D), BF16)
    po2 = psA
    ot = pb.sb("otl0", (128, D))
    rw2 = pb.sb("rw2l0", (1, D))
    pb.dma(rw2[:1, :], d_fnormw)
    rwf2 = pb.sb("rwf2l0", (128, D))
    for j in range(2):
        pb.TE(lambda e, j=j: e.matmul(psb[:, :], ones1[:1, :], rw2[:1, j * 512:(j + 1) * 512],
                                      start=True, stop=True))
        pb.V(lambda e, j=j: e.tensor_copy(rwf2[:, j * 512:(j + 1) * 512], psb[:, :]))
    for i in range(NT):
        tsl = slice(i * 128, (i + 1) * 128)
        for half in range(2):
            nsl = slice(half * 512, (half + 1) * 512)
            for k in range(16):
                pb.op("gpsimd", lambda e, k=k, tsl=tsl: e.dma_start(
                    out=yk[:, :], in_=d_y[k * 128:(k + 1) * 128, tsl]), 16)
                pb.op("gpsimd", lambda e, k=k: e.dma_start(
                    out=wo[:, :], in_=d_outw[k * 128:(k + 1) * 128, :]), 16)
                pb.TE(lambda e, k=k, nsl=nsl, s=(k == 0), p=(k == 15):
                      e.matmul(po2[:, :], yk[:, :], wo[:, nsl], start=s, stop=p))
            pb.V(lambda e, nsl=nsl: e.tensor_copy(ot[:, nsl], po2[:, :]))
        pb.dma(xt[:, :], d_xn[tsl, :])
        pb.V(lambda e: e.tensor_tensor(ot[:, :], ot[:, :], xt[:, :], OP.add))
        pb.dma(d_xo[tsl, :], ot[:, :], q="gpsimd")
        pb.S(lambda e: e.activation(sq[:, :], ot[:, :], AF.Square))
        pb.V(lambda e: e.reduce_sum(ms[:, :1], sq[:, :], axis=AX.X))
        pb.V(lambda e: e.tensor_scalar(ms[:, :1], ms[:, :1], 1.0 / D, 1e-5, OP.mult, OP.add))
        pb.S(lambda e: e.activation(ms[:, :1], ms[:, :1], AF.Ln))
        pb.S(lambda e: e.activation(ms[:, :1], ms[:, :1], AF.Exp, scale=-0.5))
        pb.V(lambda e: e.tensor_scalar(n1[:, :], ot[:, :], ms[:, :1], None, OP.mult))
        pb.V(lambda e: e.tensor_tensor(n1[:, :], n1[:, :], rwf2[:, :], OP.mult))
        pb.V(lambda e: e.tensor_copy(n1b[:, :], n1[:, :]))
        pb.dma(d_n2b[tsl, :], n1b[:, :], q="gpsimd")
    n2T = [pb.sb(f"n2T0{k}", (128, T), BF16) for k in range(8)]
    for k in range(8):
        pb.dmaT(n2T[k][:, :], d_n2b[:, k * 128:(k + 1) * 128])
    pg = [psA, psB]
    gs = pb.sb("gsl0", (128, T))
    for j in range(16):
        esl = slice(j * 128, (j + 1) * 128)
        for th in range(2):
            tfl = slice(th * 1024, (th + 1) * 1024)
            for k in range(8):
                pb.op("gpsimd", lambda e, k=k, j=j: e.dma_start(
                    out=wt[:, :], in_=d_gatew[k * 128:(k + 1) * 128, j * 128:(j + 1) * 128]), 16)
                for t2 in range(2):
                    pb.TE(lambda e, k=k, th=th, t2=t2, s=(k == 0), p=(k == 7):
                          e.matmul(pg[t2][:, :], wt[:, :],
                                   n1T[0][:, 0:512] if False else n2T[k][:, th * 1024 + t2 * 512:th * 1024 + (t2 + 1) * 512],
                                   start=s, stop=p))
            for t2 in range(2):
                pb.S(lambda e, th=th, t2=t2: e.activation(
                    gs[:, th * 1024 + t2 * 512:th * 1024 + (t2 + 1) * 512], pg[t2][:, :], AF.Silu))
        for th in range(2):
            for k in range(8):
                pb.op("gpsimd", lambda e, k=k, j=j: e.dma_start(
                    out=wt[:, :], in_=d_upw[k * 128:(k + 1) * 128, j * 128:(j + 1) * 128]), 16)
                for t2 in range(2):
                    pb.TE(lambda e, k=k, th=th, t2=t2, s=(k == 0), p=(k == 7):
                          e.matmul(pg[t2][:, :], wt[:, :],
                                   n2T[k][:, th * 1024 + t2 * 512:th * 1024 + (t2 + 1) * 512],
                                   start=s, stop=p))
            for t2 in range(2):
                pb.V(lambda e, th=th, t2=t2: e.scalar_tensor_tensor(
                    sbt[:, th * 1024 + t2 * 512:th * 1024 + (t2 + 1) * 512],
                    pg[t2][:, :], 1.0,
                    gs[:, th * 1024 + t2 * 512:th * 1024 + (t2 + 1) * 512], OP.mult, OP.mult))
        pb.dma(d_h[esl, :], sbt[:, :], q="gpsimd")
    for i in range(NT):
        tsl = slice(i * 128, (i + 1) * 128)
        for half in range(2):
            nsl = slice(half * 512, (half + 1) * 512)
            for k in range(16):
                pb.op("gpsimd", lambda e, k=k, tsl=tsl: e.dma_start(
                    out=yk[:, :], in_=d_h[k * 128:(k + 1) * 128, tsl]), 16)
                pb.op("gpsimd", lambda e, k=k: e.dma_start(
                    out=wo[:, :], in_=d_downw[k * 128:(k + 1) * 128, :]), 16)
                pb.TE(lambda e, k=k, nsl=nsl, s=(k == 0), p=(k == 15):
                      e.matmul(po2[:, :], yk[:, :], wo[:, nsl], start=s, stop=p))
            pb.V(lambda e, nsl=nsl: e.tensor_copy(ot[:, nsl], po2[:, :]))
        pb.dma(xt[:, :], d_xo[tsl, :])
        pb.V(lambda e: e.tensor_tensor(ot[:, :], ot[:, :], xt[:, :], OP.add))
        pb.dma(d_out[tsl, :], ot[:, :], q="gpsimd")
    return pb


def prog_combine(pb=None, px="", d_x=None, d_a=None, d_f=None, d_o=None):
    pb = pb or PB("comb")
    d_x = d_x if d_x is not None else pb.din(px + "x", [T, D], F32)
    d_a = d_a if d_a is not None else pb.din(px + "a", [T, D], F32)
    d_f = d_f if d_f is not None else pb.din(px + "f", [T, D], F32)
    d_o = d_o if d_o is not None else pb.dout(px + "out", [T, D], F32)
    t1 = pb.sb("t1c2", (128, D))
    t2 = pb.sb("t2c2", (128, D))
    t3 = pb.sb("t3c2", (128, D))
    for i in range(NT):
        tsl = slice(i * 128, (i + 1) * 128)
        pb.dma(t1[:, :], d_x[tsl, :])
        pb.dma(t2[:, :], d_a[tsl, :])
        pb.dma(t3[:, :], d_f[tsl, :])
        pb.V(lambda e: e.tensor_tensor(t1[:, :], t1[:, :], t2[:, :], OP.add))
        pb.V(lambda e: e.tensor_tensor(t1[:, :], t1[:, :], t3[:, :], OP.add))
        pb.dma(d_o[tsl, :], t1[:, :], q="gpsimd")
    return pb


_PROGS = {}


def _get_prog(name, builder):
    if name not in _PROGS:
        pb = builder()
        _PROGS[name] = pb.finalize()
    return _PROGS[name]


def _consts():
    c = {}
    c["ident"] = np.eye(128, dtype=np.float32)
    tt_ = np.arange(T)[:, None] // 64
    cc_ = np.arange(32)[None, :]
    c["l1mask"] = np.where(tt_ > cc_, 0.0, -1e30).astype(np.float32)
    dh = 64
    inv = 1.0 / 10000.0 ** (np.arange(0, dh, 2) / dh)
    fr = np.arange(T)[:, None] * inv[None, :]
    fr = np.repeat(fr, 2, axis=-1)
    cos = np.cos(fr).astype(np.float32)
    sin = np.sin(fr).astype(np.float32)
    sgn = np.tile(np.array([-1.0, 1.0], np.float32), dh // 2)
    c["cosb"] = np.tile(cos, (1, 16))
    c["sinb"] = np.tile((sin * sgn).astype(np.float32), (1, 16))
    ii = np.arange(512)[:, None]
    jj = np.arange(512)[None, :]
    c["maskc"] = np.where(jj <= ii, 0.0, -1e30).astype(np.float32)
    c["tiecol"] = np.tile(np.linspace(0, 1e-6, 16).astype(np.float32), 8)[:, None].copy()
    selc = np.zeros((64, 4096), np.float32)
    for p in range(4096):
        selc[p // 64, p] = 1.0
    c["selc"] = selc
    return c


def _dropout_mask():
    import jax
    with jax.default_device(jax.devices("cpu")[0]):
        keep = np.asarray(jax.random.bernoulli(jax.random.key(42), 0.8, (8, T, D)))
    import ml_dtypes
    return (keep.astype(np.float32) * 1.25).astype(ml_dtypes.bfloat16)


def kernel(**inp):
    return kernel_fused(**inp)


def kernel_multilaunch(**inp):
    f = np.float32
    ct = lambda a: np.ascontiguousarray(a, dtype=f)
    C = _consts()
    B = inp["L0_attn_x"].shape[0]
    x = [ct(inp["L0_attn_x"][b]) for b in range(B)]
    cores = list(range(B))

    def run(name, builder, maps):
        nc = _get_prog(name, builder)
        return run_bass_kernel_spmd(nc, maps, core_ids=cores).results

    maskscale = _dropout_mask()

    p0 = {k[8:]: inp[k] for k in inp if k.startswith("L0_attn_") and k != "L0_attn_x"}
    p0f = {k[7:]: inp[k] for k in inp if k.startswith("L0_ffn_")}
    p1 = {k[8:]: inp[k] for k in inp if k.startswith("L1_attn_")}
    p2 = {k[8:]: inp[k] for k in inp if k.startswith("L2_attn_")}
    p2f = {k[7:]: inp[k] for k in inp if k.startswith("L2_ffn_")}
    p3 = {k[8:]: inp[k] for k in inp if k.startswith("L3_attn_")}
    p3f = {k[7:]: inp[k] for k in inp if k.startswith("L3_ffn_")}

    for li in range(4):
        r = run("ln", prog_layernorm, [{"x": x[b]} for b in range(B)])
        xn = [np.asarray(r[b]["xn"]) for b in range(B)]
        xnb = [np.asarray(r[b]["xnb"]) for b in range(B)]
        if li == 0:
            ra = run("l0a", prog_l0_attn, [{
                "xn": xn[b], "inw": ct(p0["in_proj_W"]),
                "convw": ct(p0["conv_W"][:, 0, :].T), "convb": ct(p0["conv_b"][:, None]),
                "dtw": ct(p0["dt_W"]), "dtb": ct(p0["dt_b"][:, None]),
                "bw": ct(p0["B_W"]), "cw": ct(p0["C_W"]), "alog": ct(p0["A_log"][None, :]),
                "outw": ct(p0["out_proj_W"]), "gatew": ct(p0["gate_W"]),
                "upw": ct(p0["up_W"]), "downw": ct(p0["down_W"]),
                "inormw": ct(p0["input_norm_w"][None, :]),
                "fnormw": ct(p0["ffn_norm_w"][None, :])} for b in range(B)])
            rf = run("l0f", prog_l0_ffn, [{
                "xnb": xnb[b], "rwt": ct(p0f["router_w"].T),
                "rb": ct(p0f["router_b"][None, :]),
                "ewt": ct(p0f["expert_w"].transpose(0, 2, 1)),
                "eb": ct(p0f["expert_b"])} for b in range(B)])
        elif li == 1:
            ra = run("l1a", prog_l1_attn, [{
                "xn": xn[b], "win": ct(p1["W_in"]), "gbias": ct(p1["gate_bias"][:, None]),
                "rmsw": ct(p1["rms_weight"][None, :]), "wout": ct(p1["W_out"]),
                "l1mask": C["l1mask"], "c_ident": C["ident"]} for b in range(B)])
            rf = run("l1f", prog_l1_ffn, [{
                "xn": xn[b], "rhs": ct(inp["L1_ffn_rhs"]),
                "c_ident": C["ident"]} for b in range(B)])
        elif li == 2:
            pmkT = ct(np.concatenate([np.asarray(p2["pm"])[0].transpose(2, 0, 1).reshape(64, 256)] * 2, axis=0))
            pmvT = ct(np.asarray(p2["pm"])[1].transpose(1, 0, 2).reshape(16, 1024))
            ra = run("l2a", prog_l2_attn, [{
                "xn": xn[b], "wqkvt": ct(np.asarray(p2["w_qkv"]).T),
                "wot": ct(np.asarray(p2["w_out"]).T), "gvec": ct(p2["g"][None, :]),
                "cosb": C["cosb"], "sinb": C["sinb"], "maskc": C["maskc"],
                "pmkT": pmkT, "pmvT": pmvT, "c_ident": C["ident"]} for b in range(B)])
            rf = run("l2f", prog_l2_ffn, [{
                "xnb": xnb[b], "ctrl": ct(np.asarray(p2f["controller"]).reshape(1024, 64)),
                "f1": ct(np.asarray(p2f["f1"]).reshape(1024, 4096)),
                "f2": ct(np.asarray(p2f["f2"]).reshape(4096, 1024)),
                "biasT": ct(np.asarray(p2f["bias"]).reshape(4096, 1)),
                "tiecol": C["tiecol"], "c_ident": C["ident"],
                "sel": C["selc"]} for b in range(B)])
        else:
            ra = run("l3a", lambda: prog_l3_attn(ncore=B), [{
                "xn": xn[b], "wqt": ct(np.asarray(p3["Wq"]).T), "wkt": ct(np.asarray(p3["Wk"]).T),
                "wvt": ct(np.asarray(p3["Wv"]).T), "wot": ct(np.asarray(p3["Wo"]).T),
                "c_ident": C["ident"]} for b in range(B)])
            rf = run("l3f", prog_l3_ffn, [{
                "xn": xn[b], "w1t": ct(np.asarray(p3f["W1"]).T),
                "b1": ct(p3f["b1"][None, :]), "w2t": ct(np.asarray(p3f["W2"]).T),
                "b2": ct(p3f["b2"][None, :]), "mask": np.asarray(maskscale[b])} for b in range(B)])
        rc = run("comb", prog_combine, [{
            "x": x[b], "a": np.asarray(ra[b]["out"]), "f": np.asarray(rf[b]["out"])}
            for b in range(B)])
        x = [np.asarray(rc[b]["out"]) for b in range(B)]
    return np.stack(x)


def build_fused(B=NCORE):
    pb = PB("mega")
    nc = pb.nc
    d_x0 = pb.din("x0", [T, D], F32)
    cur = d_x0
    d_final = pb.dout("xfinal", [T, D], F32)
    for li in range(4):
        d_xn = nc.dram_tensor(f"xn{li}", [T, D], F32, kind="Internal").ap()
        d_xnb = nc.dram_tensor(f"xnb{li}", [T, D], BF16, kind="Internal").ap()
        with pb.scope():
            prog_layernorm(pb=pb, px=f"ln{li}_", d_x=cur, d_xn=d_xn, d_xnb=d_xnb)
        d_a = nc.dram_tensor(f"abr{li}", [T, D], F32, kind="Internal").ap()
        d_f = nc.dram_tensor(f"fbr{li}", [T, D], F32, kind="Internal").ap()
        if li == 0:
            with pb.scope():
                prog_l0_attn(pb=pb, px="l0a_", d_xn=d_xn, d_out=d_a)
            with pb.scope():
                prog_l0_ffn(pb=pb, px="l0f_", d_xnb=d_xnb, d_out=d_f)
        elif li == 1:
            with pb.scope():
                prog_l1_attn(pb=pb, px="l1a_", d_xn=d_xn, d_out=d_a)
            with pb.scope():
                prog_l1_ffn(pb=pb, px="l1f_", d_xn=d_xn, d_out=d_f)
        elif li == 2:
            with pb.scope():
                prog_l2_attn(pb=pb, px="l2a_", d_xn=d_xn, d_out=d_a)
            with pb.scope():
                prog_l2_ffn(pb=pb, px="l2f_", d_xnb=d_xnb, d_out=d_f)
        else:
            with pb.scope():
                prog_l3_attn(ncore=B, pb=pb, px="l3a_", d_xn=d_xn, d_out=d_a)
            with pb.scope():
                prog_l3_ffn(pb=pb, px="l3f_", d_xn=d_xn, d_out=d_f)
        d_nx = d_final if li == 3 else nc.dram_tensor(f"xres{li + 1}", [T, D], F32, kind="Internal").ap()
        with pb.scope():
            prog_combine(pb=pb, px=f"cb{li}_", d_x=cur, d_a=d_a, d_f=d_f, d_o=d_nx)
        cur = d_nx
    return pb


def fused_in_map(inp, C, maskscale, b):
    f = np.float32
    ct = lambda a: np.ascontiguousarray(a, dtype=f)
    p0 = {k[8:]: inp[k] for k in inp if k.startswith("L0_attn_") and k != "L0_attn_x"}
    p0f = {k[7:]: inp[k] for k in inp if k.startswith("L0_ffn_")}
    p1 = {k[8:]: inp[k] for k in inp if k.startswith("L1_attn_")}
    p2 = {k[8:]: inp[k] for k in inp if k.startswith("L2_attn_")}
    p2f = {k[7:]: inp[k] for k in inp if k.startswith("L2_ffn_")}
    p3 = {k[8:]: inp[k] for k in inp if k.startswith("L3_attn_")}
    p3f = {k[7:]: inp[k] for k in inp if k.startswith("L3_ffn_")}
    pmkT = ct(np.concatenate([np.asarray(p2["pm"])[0].transpose(2, 0, 1).reshape(64, 256)] * 2, axis=0))
    pmvT = ct(np.asarray(p2["pm"])[1].transpose(1, 0, 2).reshape(16, 1024))
    m = {"x0": ct(inp["L0_attn_x"][b])}
    m.update({
        "l0a_inw": ct(p0["in_proj_W"]), "l0a_convw": ct(p0["conv_W"][:, 0, :].T),
        "l0a_convb": ct(p0["conv_b"][:, None]), "l0a_dtw": ct(p0["dt_W"]),
        "l0a_dtb": ct(p0["dt_b"][:, None]), "l0a_bw": ct(p0["B_W"]), "l0a_cw": ct(p0["C_W"]),
        "l0a_alog": ct(p0["A_log"][None, :]), "l0a_outw": ct(p0["out_proj_W"]),
        "l0a_gatew": ct(p0["gate_W"]), "l0a_upw": ct(p0["up_W"]), "l0a_downw": ct(p0["down_W"]),
        "l0a_inormw": ct(p0["input_norm_w"][None, :]), "l0a_fnormw": ct(p0["ffn_norm_w"][None, :]),
        "l0f_rwt": ct(np.asarray(p0f["router_w"]).T), "l0f_rb": ct(p0f["router_b"][None, :]),
        "l0f_ewt": ct(np.asarray(p0f["expert_w"]).transpose(0, 2, 1)), "l0f_eb": ct(p0f["expert_b"]),
        "l1a_win": ct(p1["W_in"]), "l1a_gbias": ct(p1["gate_bias"][:, None]),
        "l1a_rmsw": ct(p1["rms_weight"][None, :]), "l1a_wout": ct(p1["W_out"]),
        "l1a_l1mask": C["l1mask"], "l1a_c_ident": C["ident"],
        "l1f_rhs": ct(inp["L1_ffn_rhs"]), "l1f_c_ident": C["ident"],
        "l2a_wqkvt": ct(np.asarray(p2["w_qkv"]).T), "l2a_wot": ct(np.asarray(p2["w_out"]).T),
        "l2a_gvec": ct(p2["g"][None, :]), "l2a_cosb": C["cosb"], "l2a_sinb": C["sinb"],
        "l2a_maskc": C["maskc"], "l2a_pmkT": pmkT, "l2a_pmvT": pmvT, "l2a_c_ident": C["ident"],
        "l2f_ctrl": ct(np.asarray(p2f["controller"]).reshape(1024, 64)),
        "l2f_f1": ct(np.asarray(p2f["f1"]).reshape(1024, 4096)),
        "l2f_f2": ct(np.asarray(p2f["f2"]).reshape(4096, 1024)),
        "l2f_biasT": ct(np.asarray(p2f["bias"]).reshape(4096, 1)),
        "l2f_tiecol": C["tiecol"], "l2f_c_ident": C["ident"], "l2f_sel": C["selc"],
        "l3a_wqt": ct(np.asarray(p3["Wq"]).T), "l3a_wkt": ct(np.asarray(p3["Wk"]).T),
        "l3a_wvt": ct(np.asarray(p3["Wv"]).T), "l3a_wot": ct(np.asarray(p3["Wo"]).T),
        "l3a_c_ident": C["ident"],
        "l3f_w1t": ct(np.asarray(p3f["W1"]).T), "l3f_b1": ct(p3f["b1"][None, :]),
        "l3f_w2t": ct(np.asarray(p3f["W2"]).T), "l3f_b2": ct(p3f["b2"][None, :]),
        "l3f_mask": np.asarray(maskscale[b]),
    })
    return m


_FUSED = {}


def kernel_fused(**inp):
    B = inp["L0_attn_x"].shape[0]
    if B not in _FUSED:
        _FUSED[B] = build_fused(B).finalize()
    nc = _FUSED[B]
    C = _consts()
    maskscale = _dropout_mask()
    maps = [fused_in_map(inp, C, maskscale, b) for b in range(B)]
    res = run_bass_kernel_spmd(nc, maps, core_ids=list(range(B)))
    return np.stack([np.asarray(res.results[b]["xfinal"]) for b in range(B)])


# version 07 (final)
# speedup vs baseline: 1.3914x; 1.1216x over previous
import contextlib
import os
import numpy as np

import concourse.bass as bass
import concourse.mybir as mybir
from concourse.bass_utils import run_bass_kernel_spmd
from concourse.alu_op_type import AluOpType as OP

F32 = mybir.dt.float32
BF16 = mybir.dt.bfloat16
I32 = mybir.dt.int32
AF = mybir.ActivationFunctionType
AX = mybir.AxisListType

T = 2048
D = 1024
NCORE = 8
NT = T // 128
LN2 = float(np.log(2.0))


class PB:
    def __init__(self, tag):
        self.tag = tag
        self.nc = bass.Bass("TRN2", target_bir_lowering=False)
        self.es = contextlib.ExitStack()
        self.ops = []
        self.n_sem = 16
        self.sems = [self.es.enter_context(self.nc.semaphore(f"ch{i}"))
                     for i in range(self.n_sem)]
        self._sb_names = {}
        self._cms = []
        self._finalized = False

    def _uniq(self, name):
        k = self._sb_names.get(name, 0)
        self._sb_names[name] = k + 1
        return name if k == 0 else f"{name}_{k}"

    def ccol(self, value):
        key = float(value)
        if not hasattr(self, "_ccols"):
            self._ccols = {}
        if key not in self._ccols:
            t = self.sb(f"cc_{len(self._ccols)}", (128, 1), F32)
            self.G(lambda e: e.memset(t[:, :1], key))
            self._ccols[key] = t
        return self._ccols[key][:, :1]

    def sb(self, name, shape, dt=F32):
        cm = self.nc.sbuf_tensor(self._uniq(name), list(shape), dt)
        t = cm.__enter__()
        self._cms.append((id(t), cm))
        return t

    def sbfree(self, *tiles):
        ids = {id(t) for t in tiles}
        keep = []
        for tid, cm in reversed(self._cms):
            if tid in ids:
                cm.__exit__(None, None, None)
                ids.discard(tid)
            else:
                keep.append((tid, cm))
        self._cms = list(reversed(keep))

    def ps(self, name, shape, dt=F32):
        cm = self.nc.psum_tensor(self._uniq(name), list(shape), dt)
        t = cm.__enter__()
        self._cms.append((id(t), cm))
        return t

    import contextlib as _ctx

    @_ctx.contextmanager
    def scope(self):
        start = len(self._cms)
        ccols_before = dict(getattr(self, "_ccols", {}))
        yield
        for tid, cm in reversed(self._cms[start:]):
            cm.__exit__(None, None, None)
        self._cms = self._cms[:start]
        self._ccols = ccols_before

    def din(self, name, shape, dt=F32):
        return self.nc.dram_tensor(name, list(shape), dt, kind="ExternalInput").ap()

    def dout(self, name, shape, dt=F32):
        return self.nc.dram_tensor(name, list(shape), dt, kind="ExternalOutput").ap()

    def op(self, eng, fn, inc=1, kind=None):
        if kind is None:
            kind = "c" if inc == 1 else ("g" if eng == "gpsimd" else "h")
        self.ops.append((eng, fn, inc, kind))

    def V(self, fn):
        self.op("vector", fn, 1)

    def S(self, fn):
        self.op("scalar", fn, 1)

    def TE(self, fn):
        self.op("tensor", fn, 1)

    def G(self, fn):
        self.op("gpsimd", fn, 1)

    def dma(self, dst, src, q="sync", **kw):
        self.op(q, lambda e: e.dma_start(out=dst, in_=src, **kw), 16)

    def dmaT(self, dst, src, q="scalar"):
        self.op(q, lambda e: e.dma_start(out=dst, in_=src, transpose=True), 16)

    def finalize(self):
        assert not self._finalized
        self._finalized = True
        blk = self.es.enter_context(self.nc.Block())
        ops = self.ops
        n = len(ops)
        sem_idx = []
        rot = [0, 0, 0]
        for eng, fn, inc, kind in ops:
            if kind == "g":
                sem_idx.append(9 + rot[1] % 7)
                rot[1] += 1
            elif kind == "h":
                sem_idx.append(6 + rot[2] % 3)
                rot[2] += 1
            else:
                sem_idx.append(rot[0] % 6)
                rot[0] += 1
        need_wait = [False] * n
        inc_on = [True] * n
        _EL = os.environ.get('KELIDE', '2')
        EL_C = _EL in ('1', '3')
        EL_D = _EL in ('1', '2')
        for i in range(1, n):
            pe, _, _, pk = ops[i - 1]
            ce, _, _, ck = ops[i]
            if EL_C and pk == "c" and ck == "c" and pe == ce:
                need_wait[i] = False
            elif EL_D and pk in "gh" and pk == ck and pe == ce:
                need_wait[i] = False
            else:
                need_wait[i] = True
        for i in range(n - 1):
            if EL_C and ops[i][3] == "c" and not need_wait[i + 1] and ops[i + 1][3] == "c":
                inc_on[i] = False
        sem_of = [self.sems[j] for j in sem_idx]
        tot = [0] * self.n_sem
        val_after = [0] * n
        last_inc = [-1] * n
        li = -1
        for i in range(n):
            last_inc[i] = li
            if inc_on[i]:
                tot[sem_idx[i]] += ops[i][2]
                val_after[i] = tot[sem_idx[i]]
                li = i
        by_eng = {}
        for i, (eng, fn, inc, kind) in enumerate(ops):
            by_eng.setdefault(eng, []).append(i)
        for eng, idxs in by_eng.items():
            def body(e, idxs=idxs):
                for i in idxs:
                    j = last_inc[i]
                    if need_wait[i] and j >= 0:
                        e.wait_ge(sem_of[j], val_after[j])
                    r = ops[i][1](e)
                    if inc_on[i]:
                        r.then_inc(sem_of[i], ops[i][2])
            getattr(blk, eng)(body)
        @blk.vector
        def _(e):
            for s in range(self.n_sem):
                if tot[s] > 0:
                    e.wait_ge(self.sems[s], tot[s])
        for _, cm in reversed(self._cms):
            cm.__exit__(None, None, None)
        self._cms = []
        self.es.close()
        return self.nc

    def run(self, in_maps, core_ids=None):
        nc = self.finalize()
        return run_bass_kernel_spmd(nc, in_maps,
                                    core_ids=core_ids or list(range(len(in_maps))))


def e_matmul_acc(pb, out_ps, lhsT_tiles, rhs_tiles, out_sl=None):
    K = len(lhsT_tiles)
    for k in range(K):
        lt, rt = lhsT_tiles[k], rhs_tiles[k]
        pb.TE(lambda e, lt=lt, rt=rt, s=(k == 0), p=(k == K - 1):
              e.matmul(out_ps if out_sl is None else out_ps, lt, rt,
                       start=s, stop=p))


def e_round_rne(pb, out, inp, itile):
    pb.V(lambda e: e.tensor_copy(itile, inp))
    pb.V(lambda e: e.tensor_copy(out, itile))


def make_constants():
    cons = {}
    cons["c_ident_bf16"] = np.eye(128, dtype=np.float32)
    cons["c_ones_row"] = np.ones((1, 128), np.float32)
    return cons


def host_prep(inp):
    pre = {}
    f = np.float32
    pre["x0"] = np.ascontiguousarray(inp["L0_attn_x"], dtype=f)
    pre["l1f_rhs"] = np.ascontiguousarray(inp["L1_ffn_rhs"], dtype=f)
    return pre


def prog_layernorm(pb=None, px="", d_x=None, d_xn=None, d_xnb=None):
    pb = pb or PB("ln")
    d_x = d_x if d_x is not None else pb.din(px + "x", [T, D], F32)
    d_xn = d_xn if d_xn is not None else pb.dout(px + "xn", [T, D], F32)
    d_xnb = d_xnb if d_xnb is not None else pb.dout(px + "xnb", [T, D], BF16)
    xt = pb.sb("xt", (128, D))
    sq = pb.sb("sq", (128, D))
    mcol = pb.sb("mcol", (128, 1))
    vcol = pb.sb("vcol", (128, 1))
    rcol = pb.sb("rcol", (128, 1))
    ot = pb.sb("ot", (128, D))
    ob = pb.sb("ob", (128, D), BF16)
    for i in range(NT):
        sl = slice(i * 128, (i + 1) * 128)
        pb.dma(xt[:, :], d_x[sl, :])
        pb.V(lambda e: e.reduce_sum(mcol[:, :1], xt[:, :], axis=AX.X))
        pb.V(lambda e: e.tensor_scalar(mcol[:, :1], mcol[:, :1], 1.0 / D, None, OP.mult))
        pb.V(lambda e: e.tensor_scalar(ot[:, :], xt[:, :], mcol[:, :1], None, OP.subtract))
        pb.S(lambda e: e.activation(sq[:, :], ot[:, :], AF.Square))
        pb.V(lambda e: e.reduce_sum(vcol[:, :1], sq[:, :], axis=AX.X))
        pb.V(lambda e: e.tensor_scalar(vcol[:, :1], vcol[:, :1], 1.0 / D, 1e-5, OP.mult, OP.add))
        pb.S(lambda e: e.activation(rcol[:, :1], vcol[:, :1], AF.Ln))
        pb.S(lambda e: e.activation(rcol[:, :1], rcol[:, :1], AF.Exp, scale=-0.5))
        pb.V(lambda e: e.tensor_scalar(ot[:, :], ot[:, :], rcol[:, :1], None, OP.mult))
        pb.V(lambda e: e.tensor_copy(ob[:, :], ot[:, :]))
        pb.dma(d_xn[sl, :], ot[:, :], q="gpsimd")
        pb.dma(d_xnb[sl, :], ob[:, :], q="gpsimd")
    return pb


def prog_l1_ffn(pb=None, px="", d_xn=None, d_out=None):
    pb = pb or PB("l1f")
    d_xn = d_xn if d_xn is not None else pb.din(px + "xn", [T, D], F32)
    d_rhs = pb.din(px + "rhs", [D, D], F32)
    d_out = d_out if d_out is not None else pb.dout(px + "out", [T, D], F32)

    rt = [pb.sb(f"rt{i}", (128, D)) for i in range(8)]
    for i in range(8):
        pb.dma(rt[i][:, :], d_rhs[i * 128:(i + 1) * 128, :])
    am = pb.sb("am", (128, D))
    abst = pb.sb("abst", (128, D))
    pb.S(lambda e: e.activation(am[:, :], rt[0][:, :], AF.Abs))
    for i in range(1, 8):
        pb.S(lambda e, i=i: e.activation(abst[:, :], rt[i][:, :], AF.Abs))
        pb.V(lambda e: e.tensor_tensor(am[:, :], am[:, :], abst[:, :], OP.max))
    ident = pb.sb("ident", (128, 128))
    d_id = pb.din(px + "c_ident", [128, 128], F32)
    pb.dma(ident[:, :], d_id)
    ptr = pb.ps("ptr", (128, 128))
    colmax = pb.sb("colmax", (128, 8))
    for j in range(8):
        pb.TE(lambda e, j=j: e.transpose(ptr[:, :], am[:, j * 128:(j + 1) * 128], ident[:, :]))
        pb.V(lambda e, j=j: e.tensor_reduce(colmax[:, j:j + 1], ptr[:, :], axis=AX.X, op=OP.max))
    rsc = pb.sb("rsc", (128, 8))
    rin = pb.sb("rin", (128, 8))
    pb.V(lambda e: e.tensor_scalar(rsc[:, :8], colmax[:, :8], 1e-12, 1.0 / 127.0, OP.max, OP.mult))
    pb.V(lambda e: e.reciprocal(rin[:, :8], rsc[:, :8]))
    ones1 = pb.sb("ones1", (1, 128))
    pb.G(lambda e: e.memset(ones1[:, :], 1.0))
    rinT = pb.ps("rinT", (128, 128))
    pb.TE(lambda e: e.transpose(rinT[:8, :], rin[:, :8], ident[:, :]))
    rinTs = pb.sb("rinTs", (8, 128))
    pb.V(lambda e: e.tensor_copy(rinTs[:, :], rinT[:8, :]))
    rowflat = pb.sb("rowflat", (1, D))
    for j in range(8):
        pb.dma(rowflat[0:1, j * 128:(j + 1) * 128], rinTs[j:j + 1, :], q="gpsimd")
    binst = pb.ps("binst", (128, 512))
    rinv_full = pb.sb("rinv_full", (128, D))
    for j in range(2):
        pb.TE(lambda e, j=j: e.matmul(binst[:, :], ones1[:1, :], rowflat[0:1, j * 512:(j + 1) * 512],
                                      start=True, stop=True))
        pb.V(lambda e, j=j: e.tensor_copy(rinv_full[:, j * 512:(j + 1) * 512], binst[:, :]))
    qr = [pb.sb(f"qr{i}", (128, D), BF16) for i in range(8)]
    it = pb.sb("it", (128, D), I32)
    tmp = pb.sb("tmp", (128, D))
    for i in range(8):
        pb.V(lambda e, i=i: e.tensor_tensor(tmp[:, :], rt[i][:, :], rinv_full[:, :], OP.mult))
        pb.V(lambda e: e.tensor_copy(it[:, :], tmp[:, :]))
        pb.V(lambda e, i=i: e.tensor_copy(qr[i][:, :], it[:, :]))
    rscTs = pb.sb("rscTs", (8, 128))
    rscT = pb.ps("rscT", (128, 128))
    pb.TE(lambda e: e.transpose(rscT[:8, :], rsc[:, :8], ident[:, :]))
    pb.V(lambda e: e.tensor_copy(rscTs[:, :], rscT[:8, :]))
    rowflat2 = pb.sb("rowflat2", (1, D))
    for j in range(8):
        pb.dma(rowflat2[0:1, j * 128:(j + 1) * 128], rscTs[j:j + 1, :], q="gpsimd")
    rsc_full = pb.sb("rsc_full", (128, D))
    for j in range(2):
        pb.TE(lambda e, j=j: e.matmul(binst[:, :], ones1[:1, :], rowflat2[0:1, j * 512:(j + 1) * 512],
                                      start=True, stop=True))
        pb.V(lambda e, j=j: e.tensor_copy(rsc_full[:, j * 512:(j + 1) * 512], binst[:, :]))

    xt = pb.sb("xt", (128, D))
    rowm = pb.sb("rowm", (128, 1))
    lsc = pb.sb("lsc", (128, 1))
    linv = pb.sb("linv", (128, 1))
    ql = pb.sb("ql", (128, D), BF16)
    qlT = pb.sb("qlT", (128, D), BF16)
    idb = pb.sb("idb", (128, 128), BF16)
    pb.V(lambda e: e.tensor_copy(idb[:, :], ident[:, :]))
    ptb = pb.ps("ptb", (128, 128), BF16)
    acc = pb.ps("acc", (128, 512))
    ot = pb.sb("ot", (128, D))
    for i in range(NT):
        sl = slice(i * 128, (i + 1) * 128)
        pb.dma(xt[:, :], d_xn[sl, :])
        pb.S(lambda e: e.activation(tmp[:, :], xt[:, :], AF.Abs))
        pb.V(lambda e: e.tensor_reduce(rowm[:, :1], tmp[:, :], axis=AX.X, op=OP.max))
        pb.V(lambda e: e.tensor_scalar(lsc[:, :1], rowm[:, :1], 1e-12, 1.0 / 127.0, OP.max, OP.mult))
        pb.V(lambda e: e.reciprocal(linv[:, :1], lsc[:, :1]))
        pb.V(lambda e: e.tensor_scalar(tmp[:, :], xt[:, :], linv[:, :1], None, OP.mult))
        pb.V(lambda e: e.tensor_copy(it[:, :], tmp[:, :]))
        pb.V(lambda e: e.tensor_copy(ql[:, :], it[:, :]))
        for j in range(8):
            pb.TE(lambda e, j=j: e.transpose(ptb[:, :], ql[:, j * 128:(j + 1) * 128], idb[:, :]))
            pb.V(lambda e, j=j: e.tensor_copy(qlT[:, j * 128:(j + 1) * 128], ptb[:, :]))
        for half in range(2):
            for k in range(8):
                pb.TE(lambda e, k=k, half=half, s=(k == 0), p=(k == 7):
                      e.matmul(acc[:, :], qlT[:, k * 128:(k + 1) * 128],
                               qr[k][:, half * 512:(half + 1) * 512], start=s, stop=p))
            pb.V(lambda e, half=half: e.scalar_tensor_tensor(
                ot[:, half * 512:(half + 1) * 512], acc[:, :], lsc[:, :1],
                rsc_full[:, half * 512:(half + 1) * 512], OP.mult, OP.mult))
        pb.dma(d_out[sl, :], ot[:, :], q="gpsimd")
    return pb





C_L2 = float(0.5 / np.log(2.0))


def e_fq52(pb, outb, x, sc, ebias=2.0, eclamp=-14.0, keep_m=False, relu=False, npart=128):
    B, I, INV, S, V = sc["B"], sc["I"], sc["INV"], sc["S"], sc["V"]
    pb.S(lambda e: e.activation(B, x, AF.Square))
    pb.S(lambda e: e.activation(B, B, AF.Ln))
    pb.V(lambda e: e.tensor_scalar(B, B, C_L2, 0.5, OP.mult, OP.subtract))
    pb.V(lambda e: e.tensor_copy(I, B))
    pb.V(lambda e: e.tensor_copy(B, I))
    pb.V(lambda e: e.tensor_scalar(B, B, eclamp, None, OP.max))
    bp = pb.ccol(ebias * LN2)[:npart, :1]
    bn = pb.ccol(-ebias * LN2)[:npart, :1]
    pb.S(lambda e: e.activation(INV, B, AF.Exp, scale=-LN2, bias=bp))
    pb.S(lambda e: e.activation(S, B, AF.Exp, scale=LN2, bias=bn))
    pb.V(lambda e: e.tensor_tensor(V, x, INV, OP.mult))
    pb.V(lambda e: e.tensor_copy(I, V))
    pb.V(lambda e: e.tensor_copy(V, I))
    if relu:
        pb.V(lambda e: e.tensor_scalar(V, V, 0.0, None, OP.max))
    if keep_m:
        return
    pb.V(lambda e: e.tensor_tensor(outb, V, S, OP.mult))


def e_fq52_of_grid(pb, outb, sc):
    B, I, S, V = sc["B"], sc["I"], sc["S"], sc["V"]
    pb.V(lambda e: e.tensor_scalar(B, V, 1.0 / 32.0, None, OP.mult))
    pb.V(lambda e: e.tensor_copy(I, B))
    pb.V(lambda e: e.tensor_copy(B, I))
    pb.V(lambda e: e.tensor_scalar(S, S, 32.0, None, OP.mult))
    pb.V(lambda e: e.tensor_tensor(outb, B, S, OP.mult))


def prog_l3_ffn(dbg=False, upto=99, pb=None, px="", d_xn=None, d_out=None):
    pb = pb or PB("l3f")
    d_xn = d_xn if d_xn is not None else pb.din(px + "xn", [T, D], F32)
    d_w1t = pb.din(px + "w1t", [D, 4096], F32)
    d_b1 = pb.din(px + "b1", [1, 4096], F32)
    d_w2t = pb.din(px + "w2t", [4096, D], F32)
    d_b2 = pb.din(px + "b2", [1, D], F32)
    d_mask = pb.din(px + "mask", [T, D], BF16)
    d_out = d_out if d_out is not None else pb.dout(px + "out", [T, D], F32)
    nc = pb.nc
    ikind = "ExternalOutput" if dbg else "Internal"
    d_qx = nc.dram_tensor(px + "qx", [T, D], BF16, kind=ikind).ap()
    d_qw1 = nc.dram_tensor(px + "qw1", [D, 4096], BF16, kind=ikind).ap()
    d_qw2 = nc.dram_tensor(px + "qw2", [4096, D], BF16, kind=ikind).ap()
    d_hq = nc.dram_tensor(px + "hq", [T, 4096], BF16, kind=ikind).ap()
    d_y1 = nc.dram_tensor(px + "y1dbg", [T, 4096], F32, kind=ikind).ap() if dbg else None

    FW = 2048
    xt = pb.sb("xt", (128, FW))
    scr = {k: pb.sb("scr" + k, (128, FW)) for k in ["B", "INV", "S", "V"]}
    scr["I"] = pb.sb("scrI", (128, FW), I32)
    ob = pb.sb("ob", (128, FW), BF16)

    def scrv(F):
        return {k: (v[:, :F] if k != "I" else v[:, :F]) for k, v in scr.items()}

    for v in (2.0 * LN2, -2.0 * LN2, 7.0 * LN2, -7.0 * LN2):
        pb.ccol(v)
    for i in range(NT):
        sl = slice(i * 128, (i + 1) * 128)
        pb.dma(xt[:, :D], d_xn[sl, :])
        e_fq52(pb, ob[:, :D], xt[:, :D], scrv(D))
        pb.dma(d_qx[sl, :], ob[:, :D], q="gpsimd")
    if upto < 1:
        d_zzz = pb.dout(px + "zzz", [1, 1], F32)
        pb.dma(d_zzz, ob[:1, :1], q="gpsimd")
        return pb
    for i in range(8):
        sl = slice(i * 128, (i + 1) * 128)
        for hh in range(2):
            cs = slice(hh * 2048, (hh + 1) * 2048)
            pb.dma(xt[:, :], d_w1t[sl, cs])
            e_fq52(pb, ob[:, :], xt[:, :], scrv(2048))
            pb.dma(d_qw1[sl, cs], ob[:, :], q="gpsimd")
    for i in range(32):
        sl = slice(i * 128, (i + 1) * 128)
        pb.dma(xt[:, :D], d_w2t[sl, :])
        e_fq52(pb, ob[:, :D], xt[:, :D], scrv(D))
        pb.dma(d_qw2[sl, :], ob[:, :D], q="gpsimd")
    b1r = pb.sb("b1r", (1, 4096))
    qb1 = pb.sb("qb1", (1, 4096), BF16)
    pb.dma(b1r[:1, :], d_b1)
    for hh in range(2):
        cs = slice(hh * 2048, (hh + 1) * 2048)
        sc1 = {k: v[:1, :2048] for k, v in scr.items()}
        e_fq52(pb, qb1[:1, cs], b1r[:1, cs], sc1, npart=1)
    b2r = pb.sb("b2r", (1, D))
    qb2 = pb.sb("qb2", (1, D), BF16)
    pb.dma(b2r[:1, :], d_b2)
    sc2 = {k: v[:1, :D] for k, v in scr.items()}
    e_fq52(pb, qb2[:1, :], b2r[:1, :], sc2, npart=1)
    if upto < 2:
        d_zzz = pb.dout(px + "zzz", [1, 1], F32)
        pb.dma(d_zzz, ob[:1, :1], q="gpsimd")
        return pb
    ones_b = pb.sb("ones_b", (1, 128), BF16)
    pb.G(lambda e: e.memset(ones_b[:, :], 1.0))
    acc = pb.ps("acc", (128, 512))
    hqt = pb.sb("hqt", (128, 4096), BF16)
    qxT = [pb.sb(f"qxT{k}", (128, T), BF16) for k in range(8)]
    for k in range(8):
        pb.dmaT(qxT[k][:, :], d_qx[:, k * 128:(k + 1) * 128])
    qw1s = [pb.sb(f"qw1s{k}", (128, 4096), BF16) for k in range(8)]
    for k in range(8):
        pb.dma(qw1s[k][:, :], d_qw1[k * 128:(k + 1) * 128, :])
    for i in range(NT):
        tsl = slice(i * 128, (i + 1) * 128)
        for nchunk in range(8):
            nsl = slice(nchunk * 512, (nchunk + 1) * 512)
            for k in range(8):
                pb.TE(lambda e, k=k, tsl=tsl, nsl=nsl, s=(k == 0):
                      e.matmul(acc[:, :], qxT[k][:, tsl], qw1s[k][:, nsl],
                               start=s, stop=False))
            pb.TE(lambda e, nsl=nsl: e.matmul(acc[:, :], ones_b[:1, :], qb1[:1, nsl],
                                              start=False, stop=True))
            s5 = scrv(512)
            if dbg:
                pb.V(lambda e, nsl=nsl: e.tensor_copy(scr["B"][:, :512], acc[:, :]))
                pb.dma(d_y1[tsl, nsl], scr["B"][:, :512], q="gpsimd")
            e_fq52(pb, None, acc[:, :], s5, ebias=7.0, eclamp=-30.0,
                   keep_m=True, relu=True)
            e_fq52_of_grid(pb, hqt[:, nsl], s5)
        pb.dma(d_hq[tsl, :], hqt[:, :], q="gpsimd")
    pb.sbfree(*qw1s)
    pb.sbfree(*qxT)
    hqT = [pb.sb(f"hqT{k}", (128, 1024), BF16) for k in range(32)]
    w2s = pb.sb("w2s", (128, D), BF16)
    mt = pb.sb("mt", (128, D), BF16)
    otile = pb.sb("otile", (128, D))
    acc2 = pb.ps("acc2", (128, 512))
    for half in range(2):
        hsl = slice(half * 1024, (half + 1) * 1024)
        for k in range(32):
            pb.dmaT(hqT[k][:, :], d_hq[hsl, k * 128:(k + 1) * 128])
        if half == 0:
            w2r = [pb.sb(f"w2r{k}", (128, D), BF16) for k in range(20)]
            for k in range(20):
                pb.dma(w2r[k][:, :], d_qw2[k * 128:(k + 1) * 128, :])
        for it in range(8):
            i = half * 8 + it
            tsl = slice(i * 128, (i + 1) * 128)
            ttsl = slice(it * 128, (it + 1) * 128)
            for nchunk in range(2):
                nsl = slice(nchunk * 512, (nchunk + 1) * 512)
                for k in range(32):
                    if k >= 20:
                        pb.dma(w2s[:, :], d_qw2[k * 128:(k + 1) * 128, :])
                    pb.TE(lambda e, k=k, ttsl=ttsl, nsl=nsl, s=(k == 0):
                          e.matmul(acc2[:, :], hqT[k][:, ttsl],
                                   (w2r[k] if k < 20 else w2s)[:, nsl],
                                   start=s, stop=False))
                pb.TE(lambda e, nsl=nsl: e.matmul(acc2[:, :], ones_b[:1, :], qb2[:1, nsl],
                                                  start=False, stop=True))
                s5 = scrv(512)
                e_fq52(pb, None, acc2[:, :], s5, ebias=7.0, eclamp=-30.0, keep_m=True)
                pb.V(lambda e, nsl=nsl, s5=s5: e.tensor_tensor(otile[:, nsl], s5["V"], s5["S"], OP.mult))
            pb.dma(mt[:, :], d_mask[tsl, :])
            pb.V(lambda e: e.tensor_tensor(otile[:, :], otile[:, :], mt[:, :], OP.mult))
            pb.dma(d_out[tsl, :], otile[:, :], q="gpsimd")
    return pb


def prog_l0_ffn(pb=None, px="", d_xnb=None, d_out=None):
    pb = pb or PB("l0f")
    d_xnb = d_xnb if d_xnb is not None else pb.din(px + "xnb", [T, D], BF16)
    d_rwt = pb.din(px + "rwt", [D, 8], F32)
    d_rb = pb.din(px + "rb", [1, 8], F32)
    d_ewt = pb.din(px + "ewt", [8, D, D], F32)
    d_eb = pb.din(px + "eb", [8, D], F32)
    d_out = d_out if d_out is not None else pb.dout(px + "out", [T, D], F32)

    xnT = [pb.sb(f"xnT{k}", (128, T), BF16) for k in range(8)]
    for k in range(8):
        pb.dmaT(xnT[k][:, :], d_xnb[:, k * 128:(k + 1) * 128])
    rwts = pb.sb("rwts", (128, 64), BF16)
    for k in range(8):
        pb.op("gpsimd", lambda e, k=k: e.dma_start(
            out=rwts[:, k * 8:(k + 1) * 8], in_=d_rwt[k * 128:(k + 1) * 128, :]), 16)
    ones1 = pb.sb("ones1", (1, 128))
    pb.G(lambda e: e.memset(ones1[:, :], 1.0))
    rbrow = pb.sb("rbrow", (1, 8))
    pb.dma(rbrow[:1, :], d_rb)
    psb = pb.ps("psb", (128, 128))
    rbb = pb.sb("rbb", (128, 8))
    pb.TE(lambda e: e.matmul(psb[:, :8], ones1[:1, :], rbrow[:1, :8], start=True, stop=True))
    pb.V(lambda e: e.tensor_copy(rbb[:, :8], psb[:, :8]))
    ebrow = pb.sb("ebrow", (8, D))
    pb.dma(ebrow[:8, :], d_eb)
    ebr0 = pb.sb("ebr0", (1, D), BF16)
    onesbf = pb.sb("onesbf", (1, 128), BF16)
    pb.G(lambda e: e.memset(onesbf[:, :], 1.0))

    accp = pb.ps("accp", (128, 512))
    lg = pb.ps("lg", (128, 8))
    swall = pb.sb("swall", (128, 128))
    ew8 = [pb.sb(f"ew8_{k}", (128, D), BF16) for k in range(8)]
    outt = [pb.sb(f"outt{i}", (128, D)) for i in range(NT)]
    lgt = pb.sb("lgt", (128, 8))
    mx = pb.sb("mx", (128, 1))
    w8 = pb.sb("w8", (128, 8))
    den = pb.sb("den", (128, 1))
    pmax = pb.sb("pmax", (128, 1))
    selw = pb.sb("selw", (128, 8))
    running = pb.sb("running", (128, 1))
    mcol = pb.sb("mcol2", (128, 1))
    ew = pb.sb("ew", (128, D), BF16)
    out_t = pb.sb("out_t", (128, D))
    for i in range(NT):
        tsl = slice(i * 128, (i + 1) * 128)
        for k in range(8):
            pb.TE(lambda e, k=k, tsl=tsl, s=(k == 0), p=(k == 7):
                  e.matmul(lg[:, :8], xnT[k][:, tsl], rwts[:, k * 8:(k + 1) * 8],
                           start=s, stop=p))
        pb.V(lambda e: e.tensor_tensor(lgt[:, :8], lg[:, :8], rbb[:, :8], OP.add))
        pb.V(lambda e: e.tensor_reduce(mx[:, :1], lgt[:, :8], axis=AX.X, op=OP.max))
        pb.V(lambda e: e.tensor_scalar(mx[:, :1], mx[:, :1], -1.0, None, OP.mult))
        pb.S(lambda e: e.activation(w8[:, :8], lgt[:, :8], AF.Exp, bias=mx[:, :1],
                                    accum_out=den[:, :1]))
        pb.V(lambda e: e.reciprocal(den[:, :1], den[:, :1]))
        pb.V(lambda e: e.tensor_scalar(w8[:, :8], w8[:, :8], den[:, :1], None, OP.mult))
        pb.V(lambda e: e.tensor_reduce(pmax[:, :1], w8[:, :8], axis=AX.X, op=OP.max))
        pb.G(lambda e: e.memset(running[:, :1], 0.0))
        for ex in range(8):
            pb.V(lambda e, ex=ex: e.tensor_scalar(mcol[:, :1], w8[:, ex:ex + 1],
                                                  pmax[:, :1], None, OP.is_ge))
            pb.V(lambda e, ex=ex: e.scalar_tensor_tensor(
                selw[:, ex:ex + 1], running[:, :1], 1.0, mcol[:, :1], OP.subtract, OP.mult))
            pb.V(lambda e, ex=ex: e.tensor_scalar(selw[:, ex:ex + 1], selw[:, ex:ex + 1],
                                                  -1.0, None, OP.mult))
            pb.V(lambda e, ex=ex: e.tensor_tensor(running[:, :1], running[:, :1],
                                                  selw[:, ex:ex + 1], OP.add))
            pb.V(lambda e, ex=ex: e.tensor_tensor(selw[:, ex:ex + 1], selw[:, ex:ex + 1],
                                                  pmax[:, :1], OP.mult))
        pb.V(lambda e, i=i: e.tensor_copy(swall[:, i * 8:(i + 1) * 8], selw[:, :8]))
    for i in range(NT):
        pb.G(lambda e, i=i: e.memset(outt[i][:, :], 0.0))
    for ex in range(8):
        pb.op("gpsimd", lambda e, ex=ex: e.dma_start(
            out=ebr0[:1, :], in_=ebrow[ex:ex + 1, :]), 16)
        for k in range(8):
            pb.op("gpsimd", lambda e, ex=ex, k=k: e.dma_start(
                out=ew8[k][:, :], in_=d_ewt[ex, k * 128:(k + 1) * 128, :]), 16)
        for i in range(NT):
            tsl = slice(i * 128, (i + 1) * 128)
            for half in range(2):
                nsl = slice(half * 512, (half + 1) * 512)
                for k in range(8):
                    pb.TE(lambda e, k=k, tsl=tsl, nsl=nsl, s=(k == 0):
                          e.matmul(accp[:, :], xnT[k][:, tsl], ew8[k][:, nsl],
                                   start=s, stop=False))
                pb.TE(lambda e, nsl=nsl: e.matmul(accp[:, :], onesbf[:1, :], ebr0[:1, nsl], start=False, stop=True))
                pb.V(lambda e, ex=ex, i=i, nsl=nsl: e.scalar_tensor_tensor(
                    outt[i][:, nsl], accp[:, :], swall[:, i * 8 + ex:i * 8 + ex + 1],
                    outt[i][:, nsl], OP.mult, OP.add))
    for i in range(NT):
        pb.dma(d_out[i * 128:(i + 1) * 128, :], outt[i][:, :], q="gpsimd")
    return pb



def prog_l2_ffn(upto=99, pb=None, px="", d_xnb=None, d_out=None):
    pb = pb or PB("l2f")
    d_xnb = d_xnb if d_xnb is not None else pb.din(px + "xnb", [T, D], BF16)
    d_ctrl = pb.din(px + "ctrl", [D, 64], F32)
    d_f1 = pb.din(px + "f1", [D, 4096], F32)
    d_f2 = pb.din(px + "f2", [4096, D], F32)
    d_biasT = pb.din(px + "biasT", [4096, 1], F32)
    d_tie = pb.din(px + "tiecol", [128, 1], F32)
    d_ident = pb.din(px + "c_ident", [128, 128], F32)
    d_out = d_out if d_out is not None else pb.dout(px + "out", [T, D], F32)
    nc = pb.nc
    d_innerM = nc.dram_tensor(px + "innerM", [4096, T], BF16, kind="Internal").ap()

    xnT = [pb.sb(f"xnT{k}", (128, T), BF16) for k in range(8)]
    for k in range(8):
        pb.dmaT(xnT[k][:, :], d_xnb[:, k * 128:(k + 1) * 128])
    ident = pb.sb("ident", (128, 128))
    pb.dma(ident[:, :], d_ident)
    tie = pb.sb("tie", (128, 1))
    pb.dma(tie[:, :], d_tie)
    ctrls = pb.sb("ctrls", (128, 8 * 64), BF16)
    for k in range(8):
        pb.op("gpsimd", lambda e, k=k: e.dma_start(
            out=ctrls[:, k * 64:(k + 1) * 64], in_=d_ctrl[k * 128:(k + 1) * 128, :]), 16)

    lgp = pb.ps("lgp", (128, 64))
    lgt = pb.sb("lgt", (128, 64))
    ptp = pb.ps("ptp", (128, 128))
    lgT = pb.sb("lgT", (64, 128))
    mx8 = pb.sb("mx8", (64, 8))
    permT = pb.sb("permT", (64, T))
    for i in range(NT):
        tsl = slice(i * 128, (i + 1) * 128)
        for k in range(8):
            pb.TE(lambda e, k=k, tsl=tsl, s=(k == 0), p=(k == 7):
                  e.matmul(lgp[:, :64], xnT[k][:, tsl], ctrls[:, k * 64:(k + 1) * 64],
                           start=s, stop=p))
        pb.V(lambda e: e.tensor_scalar(lgt[:, :64], lgp[:, :64], tie[:, :1], None, OP.add))
        pb.TE(lambda e: e.transpose(ptp[:64, :], lgt[:, :64], ident[:, :]))
        pb.V(lambda e: e.tensor_copy(lgT[:, :], ptp[:64, :128]))
        pb.V(lambda e: e.reduce_max(mx8[:, :8], lgT[:, :].rearrange("p (a b) -> p a b", a=8), axis=AX.X))
        pb.V(lambda e, tsl=tsl: e.tensor_tensor(
            permT[:, tsl].rearrange("p (a b) -> p a b", a=8),
            lgT[:, :].rearrange("p (a b) -> p a b", a=8),
            mx8[:, :8].broadcast_to([64, 8, 16]),
            OP.is_equal))

    if upto < 1:
        d_pt = pb.dout(px + "permTdbg", [64, T], F32)
        pb.dma(d_pt, permT[:, :], q="gpsimd")
        return pb
    d_sel = pb.din(px + "sel", [64, 4096], F32)
    sel64 = pb.sb("sel64", (64, 4096), BF16)
    pb.op("gpsimd", lambda e: e.dma_start(out=sel64[:, :], in_=d_sel), 16)
    permTb = pb.sb("permTb", (64, T), BF16)
    pb.V(lambda e: e.tensor_copy(permTb[:, :], permT[:, :]))
    f1t = pb.sb("f1t", (128, 4096), BF16)
    bcol = pb.sb("bcol", (128, 1))
    pin = pb.ps("pin", (128, 512))
    pbm = pb.ps("pbm", (128, 512))
    inn = pb.sb("inn", (128, T))
    pmb = pb.sb("pmb", (128, T))
    imb = pb.sb("imb", (128, T), BF16)
    for c in range(32):
        csl = slice(c * 128, (c + 1) * 128)
        for k in range(8):
            pb.op("gpsimd", lambda e, k=k, c=c: e.dma_start(
                out=f1t[:, k * 128:(k + 1) * 128],
                in_=d_f1[k * 128:(k + 1) * 128, c * 128:(c + 1) * 128]), 16)
        pb.dma(bcol[:, :], d_biasT[csl, :])
        for th in range(4):
            tfl = slice(th * 512, (th + 1) * 512)
            for k in range(8):
                pb.TE(lambda e, k=k, tfl=tfl, s=(k == 0), p=(k == 7):
                      e.matmul(pin[:, :], f1t[:, k * 128:(k + 1) * 128], xnT[k][:, tfl],
                               start=s, stop=p))
            pb.S(lambda e, tfl=tfl: e.activation(inn[:, tfl], pin[:, :], AF.Relu,
                                                 bias=bcol[:, :1]))
            pb.TE(lambda e, tfl=tfl, csl=csl: e.matmul(
                pbm[:, :], sel64[:64, csl], permTb[:64, tfl], start=True, stop=True))
            pb.V(lambda e, tfl=tfl: e.tensor_copy(pmb[:, tfl], pbm[:, :]))
        pb.V(lambda e: e.tensor_tensor(imb[:, :], inn[:, :], pmb[:, :], OP.mult))
        pb.dma(d_innerM[csl, :], imb[:, :], q="gpsimd")

    if upto < 2:
        d_zz = pb.dout(px + "zz2", [1, 1], F32)
        pb.dma(d_zz, permT[:1, :1], q="gpsimd")
        return pb
    f2t = pb.sb("f2t", (128, D), BF16)
    po = [pb.ps(f"po{h}", (128, 512)) for h in range(2)]
    ot = pb.sb("ot2", (128, D))
    imt = pb.sb("imt", (128, 128), BF16)
    for i in range(NT if upto >= 3 else 1):
        tsl = slice(i * 128, (i + 1) * 128)
        for c in range(32):
            pb.op("gpsimd", lambda e, c=c: e.dma_start(
                out=f2t[:, :], in_=d_f2[c * 128:(c + 1) * 128, :]), 16)
            pb.dma(imt[:, :], d_innerM[c * 128:(c + 1) * 128, tsl])
            for h in range(2):
                pb.TE(lambda e, c=c, h=h, s=(c == 0), p=(c == 31):
                      e.matmul(po[h][:, :], imt[:, :], f2t[:, h * 512:(h + 1) * 512],
                               start=s, stop=p))
        for h in range(2):
            pb.V(lambda e, h=h: e.tensor_copy(ot[:, h * 512:(h + 1) * 512], po[h][:, :]))
        pb.dma(d_out[tsl, :], ot[:, :], q="gpsimd")
    return pb


def prog_l2_attn(dbg=False, pb=None, px="", d_xn=None, d_out=None):
    pb = pb or PB("l2a")
    d_xn = d_xn if d_xn is not None else pb.din(px + "xn", [T, D], F32)
    d_wqkvt = pb.din(px + "wqkvt", [D, 3072], F32)
    d_wo_t = pb.din(px + "wot", [D, D], F32)
    d_g = pb.din(px + "gvec", [1, D], F32)
    d_cos = pb.din(px + "cosb", [T, D], F32)
    d_sin = pb.din(px + "sinb", [T, D], F32)
    d_mask = pb.din(px + "maskc", [512, 512], F32)
    d_pmk = pb.din(px + "pmkT", [128, 256], F32)
    d_pmv = pb.din(px + "pmvT", [16, 1024], F32)
    d_out = d_out if d_out is not None else pb.dout(px + "out", [T, D], F32)
    nc = pb.nc
    ik = "ExternalOutput" if dbg else "Internal"
    d_x2b = nc.dram_tensor(px + "x2b", [T, D], BF16, kind=ik).ap()
    d_q = nc.dram_tensor(px + "qro", [T, D], BF16, kind=ik).ap()
    d_k = nc.dram_tensor(px + "kro", [T, D], BF16, kind=ik).ap()
    d_v = nc.dram_tensor(px + "vt", [T, D], BF16, kind=ik).ap()
    d_ao = nc.dram_tensor(px + "ao", [T, D], BF16, kind=ik).ap()
    d_sc0 = nc.dram_tensor(px + "sc0", [128, 528], F32, kind=ik).ap() if dbg else None
    d_pr0 = nc.dram_tensor(px + "pr0", [128, 528], BF16, kind=ik).ap() if dbg else None

    gb = pb.sb("gb", (1, D))
    pb.dma(gb[:1, :], d_g)
    ones1 = pb.sb("ones1", (1, 128))
    pb.G(lambda e: e.memset(ones1[:, :], 1.0))
    gfull = pb.sb("gfull", (128, D))
    psb = pb.ps("psb", (128, 512))
    for j in range(2):
        pb.TE(lambda e, j=j: e.matmul(psb[:, :], ones1[:1, :], gb[:1, j * 512:(j + 1) * 512],
                                      start=True, stop=True))
        pb.V(lambda e, j=j: e.tensor_copy(gfull[:, j * 512:(j + 1) * 512], psb[:, :]))
    xt = pb.sb("xt", (128, D))
    sq = pb.sb("sq", (128, D))
    ms = pb.sb("ms", (128, 1))
    x2 = pb.sb("x2", (128, D))
    x2bt = pb.sb("x2bt", (128, D), BF16)
    for i in range(NT):
        tsl = slice(i * 128, (i + 1) * 128)
        pb.dma(xt[:, :], d_xn[tsl, :])
        pb.S(lambda e: e.activation(sq[:, :], xt[:, :], AF.Square))
        pb.V(lambda e: e.reduce_sum(ms[:, :1], sq[:, :], axis=AX.X))
        pb.V(lambda e: e.tensor_scalar(ms[:, :1], ms[:, :1], 1.0 / D, 1.1920929e-07, OP.mult, OP.add))
        pb.S(lambda e: e.activation(ms[:, :1], ms[:, :1], AF.Ln))
        pb.S(lambda e: e.activation(ms[:, :1], ms[:, :1], AF.Exp, scale=-0.5))
        pb.V(lambda e: e.tensor_scalar(x2[:, :], xt[:, :], ms[:, :1], None, OP.mult))
        pb.V(lambda e: e.tensor_tensor(x2[:, :], x2[:, :], gfull[:, :], OP.mult))
        pb.V(lambda e: e.tensor_copy(x2bt[:, :], x2[:, :]))
        pb.dma(d_x2b[tsl, :], x2bt[:, :], q="gpsimd")
    x2T = [pb.sb(f"x2T{k}", (128, T), BF16) for k in range(8)]
    for k in range(8):
        pb.dmaT(x2T[k][:, :], d_x2b[:, k * 128:(k + 1) * 128])
    wq = pb.sb("wq", (128, 512), BF16)
    wqr = [pb.sb(f"wqr{j}", (128, 512), BF16) for j in range(48)]
    for nch in range(6):
        for k in range(8):
            pb.op("gpsimd", lambda e, k=k, nch=nch: e.dma_start(
                out=wqr[nch * 8 + k][:, :], in_=d_wqkvt[k * 128:(k + 1) * 128,
                                                        nch * 512:(nch + 1) * 512]), 16)
    pq = pb.ps("pq", (128, 512))
    qkv = pb.sb("qkv", (128, 3072))
    cost = pb.sb("cost", (128, D))
    sint = pb.sb("sint", (128, D))
    rot = pb.sb("rot", (128, D))
    ro = pb.sb("ro", (128, D))
    rob = pb.sb("rob", (128, D), BF16)
    for i in range(NT):
        tsl = slice(i * 128, (i + 1) * 128)
        for nch in range(6):
            for k in range(8):
                pb.TE(lambda e, k=k, nch=nch, tsl=tsl, s=(k == 0), p=(k == 7):
                      e.matmul(pq[:, :], x2T[k][:, tsl], wqr[nch * 8 + k][:, :],
                               start=s, stop=p))
            pb.V(lambda e, nch=nch: e.tensor_copy(qkv[:, nch * 512:(nch + 1) * 512], pq[:, :]))
        pb.dma(cost[:, :], d_cos[tsl, :])
        pb.dma(sint[:, :], d_sin[tsl, :])
        for part, dst in ((0, d_q), (1, d_k)):
            qsl = slice(part * D, (part + 1) * D)
            pb.V(lambda e, qsl=qsl: e.tensor_copy(
                rot[:, 0:D:2], qkv[:, qsl][:, 1:D:2]))
            pb.V(lambda e, qsl=qsl: e.tensor_copy(
                rot[:, 1:D:2], qkv[:, qsl][:, 0:D:2]))
            pb.V(lambda e, qsl=qsl: e.tensor_tensor(ro[:, :], qkv[:, qsl], cost[:, :], OP.mult))
            pb.V(lambda e: e.tensor_tensor(rot[:, :], rot[:, :], sint[:, :], OP.mult))
            pb.V(lambda e: e.tensor_tensor(ro[:, :], ro[:, :], rot[:, :], OP.add))
            pb.V(lambda e: e.tensor_copy(rob[:, :], ro[:, :]))
            pb.dma(dst[tsl, :], rob[:, :], q="gpsimd")
        pb.V(lambda e: e.tensor_copy(rob[:, :], qkv[:, 2 * D:3 * D]))
        pb.dma(d_v[tsl, :], rob[:, :], q="gpsimd")


    pmk = pb.sb("pmk", (128, 256), BF16)
    pb.op("gpsimd", lambda e: e.dma_start(out=pmk[:, :], in_=d_pmk), 16)
    pmv = pb.sb("pmv", (16, 1024), BF16)
    pb.op("gpsimd", lambda e: e.dma_start(out=pmv[:, :], in_=d_pmv), 16)
    idb = pb.sb("idb", (128, 128), BF16)
    pb.G(lambda e: e.memset(idb[:, :], 0.0))
    pb.V(lambda e: e.iota(idb[:, :], None)) if False else None
    d_ident = pb.din(px + "c_ident", [128, 128], F32)
    identf = pb.sb("identf", (128, 128))
    pb.dma(identf[:, :], d_ident)
    pb.V(lambda e: e.tensor_copy(idb[:, :], identf[:, :]))
    qT2 = pb.sb("qT2", (128, T), BF16)
    kT2 = pb.sb("kT2", (128, T), BF16)
    vseg = [pb.sb(f"vseg{j}", (128, D), BF16) for j in range(4)]
    mskt = pb.sb("mskt", (128, 512))
    ps_k = pb.ps("ps_k", (128, 512))
    ps_pm = pb.ps("ps_pm", (128, 16))
    sct = pb.sb("sct", (128, 528))
    mxc = pb.sb("mxc", (128, 1))
    den = pb.sb("den", (128, 1))
    prb = pb.sb("prb", (128, 528), BF16)
    ptb = pb.ps("ptb", (128, 128), BF16)
    lhsT = pb.sb("lhsTt", (128, 128 * 5), BF16)
    pov = pb.ps("pov", (128, 64))
    oht = pb.sb("oht", (128, D))
    for seg in range(4):
        ssl = slice(seg * 512, (seg + 1) * 512)
        for j in range(4):
            pb.dma(vseg[j][:, :], d_v[seg * 512 + j * 128:seg * 512 + (j + 1) * 128, :])
        for hp in range(8):
            pb.dmaT(qT2[:, :], d_q[:, hp * 128:(hp + 1) * 128])
            pb.dmaT(kT2[:, :], d_k[:, hp * 128:(hp + 1) * 128])
            for hh in range(2):
                h = hp * 2 + hh
                qsl = slice(hh * 64, (hh + 1) * 64)
                for ti in range(4):
                    trow = seg * 512 + ti * 128
                    pb.TE(lambda e, qsl=qsl, trow=trow, ssl=ssl:
                          e.matmul(ps_k[:, :], qT2[qsl, trow:trow + 128],
                                   kT2[qsl, ssl], start=True, stop=True))
                    pb.TE(lambda e, qsl=qsl, trow=trow, h=h:
                          e.matmul(ps_pm[:, :], qT2[qsl, trow:trow + 128],
                                   pmk[qsl, h * 16:(h + 1) * 16], start=True, stop=True))
                    pb.dma(mskt[:, :], d_mask[ti * 128:(ti + 1) * 128, :])
                    pb.V(lambda e: e.scalar_tensor_tensor(
                        sct[:, 0:512], ps_k[:, :], 0.125, mskt[:, :], OP.mult, OP.add))
                    pb.V(lambda e: e.tensor_scalar(sct[:, 512:528], ps_pm[:, :], 0.125, None, OP.mult))
                    if dbg and seg == 0 and h == 0 and ti == 0:
                        pb.dma(d_sc0, sct[:, :528], q="gpsimd")
                    pb.V(lambda e: e.reduce_max(mxc[:, :1], sct[:, :528], axis=AX.X))
                    pb.V(lambda e: e.tensor_scalar(mxc[:, :1], mxc[:, :1], -1.0, None, OP.mult))
                    pb.S(lambda e: e.activation(prb[:, :528], sct[:, :528], AF.Exp,
                                                bias=mxc[:, :1], accum_out=den[:, :1]))
                    pb.V(lambda e: e.reciprocal(den[:, :1], den[:, :1]))
                    if dbg and seg == 0 and h == 0 and ti == 0:
                        pb.dma(d_pr0, prb[:, :528], q="gpsimd")
                    for c in range(4):
                        pb.TE(lambda e, c=c: e.transpose(
                            ptb[:, :], prb[:, c * 128:(c + 1) * 128], idb[:, :]))
                        pb.V(lambda e, c=c: e.tensor_copy(
                            lhsT[:, c * 128:(c + 1) * 128], ptb[:, :]))
                    pb.TE(lambda e: e.transpose(ptb[:16, :], prb[:, 512:528], idb[:, :]))
                    pb.V(lambda e: e.tensor_copy(lhsT[:16, 512:640], ptb[:16, :128]))
                    for c in range(4):
                        pb.TE(lambda e, c=c, h=h, s=(c == 0):
                              e.matmul(pov[:, :], lhsT[:, c * 128:(c + 1) * 128],
                                       vseg[c][:, h * 64:(h + 1) * 64], start=s, stop=False))
                    pb.TE(lambda e, h=h: e.matmul(pov[:, :], lhsT[:16, 512:640],
                                                  pmv[:16, h * 64:(h + 1) * 64],
                                                  start=False, stop=True))
                    pb.V(lambda e, h=h: e.tensor_scalar(
                        oht[:, h * 64:(h + 1) * 64], pov[:, :], den[:, :1], None, OP.mult))
                    pb.V(lambda e, h=h: e.tensor_copy(rob[:, h * 64:(h + 1) * 64],
                                                      oht[:, h * 64:(h + 1) * 64]))
                    pb.op("gpsimd", lambda e, h=h, trow=trow: e.dma_start(
                        out=d_ao[trow:trow + 128, h * 64:(h + 1) * 64],
                        in_=rob[:, h * 64:(h + 1) * 64]), 16)
    aoT = [pb.sb(f"aoT{k}", (128, T), BF16) for k in range(8)]
    for k in range(8):
        pb.dmaT(aoT[k][:, :], d_ao[:, k * 128:(k + 1) * 128])
    wo = pb.sb("wo", (128, D), BF16)
    ot = pb.sb("otl2", (128, D))
    po = pb.ps("po_l2", (128, 512))
    for i in range(NT):
        tsl = slice(i * 128, (i + 1) * 128)
        for half in range(2):
            nsl = slice(half * 512, (half + 1) * 512)
            for k in range(8):
                pb.op("gpsimd", lambda e, k=k: e.dma_start(
                    out=wo[:, :], in_=d_wo_t[k * 128:(k + 1) * 128, :]), 16)
                pb.TE(lambda e, k=k, tsl=tsl, nsl=nsl, s=(k == 0), p=(k == 7):
                      e.matmul(po[:, :], aoT[k][:, tsl], wo[:, nsl], start=s, stop=p))
            pb.V(lambda e, nsl=nsl: e.tensor_copy(ot[:, nsl], po[:, :]))
        pb.dma(d_out[tsl, :], ot[:, :], q="gpsimd")
    return pb


def prog_l1_attn(dbg=False, pb=None, px="", d_xn=None, d_out=None):
    pb = pb or PB("l1a")
    d_xn = d_xn if d_xn is not None else pb.din(px + "xn", [T, D], F32)
    d_win = pb.din(px + "win", [D, 3328], F32)
    d_gb = pb.din(px + "gbias", [D, 1], F32)
    d_rmsw = pb.din(px + "rmsw", [1, D], F32)
    d_wout = pb.din(px + "wout", [D, D], F32)
    d_mask = pb.din(px + "l1mask", [16 * 128, 32], F32)
    d_ident = pb.din(px + "c_ident", [128, 128], F32)
    d_out = d_out if d_out is not None else pb.dout(px + "out", [T, D], F32)
    nc = pb.nc
    ik = "ExternalOutput" if dbg else "Internal"
    d_n1b = nc.dram_tensor(px + "n1b", [T, D], BF16, kind=ik).ap()
    d_z = nc.dram_tensor(px + "zsig", [D, T], BF16, kind=ik).ap()
    d_xkx = nc.dram_tensor(px + "xkx", [D, T], F32, kind=ik).ap() if dbg else None
    d_aoT = nc.dram_tensor(px + "aoT", [D, T], BF16, kind=ik).ap()

    xt2 = [pb.sb(f"l1x{j}", (128, T), BF16) for j in range(8)]
    gt = [pb.sb(f"l1g{j}", (128, T), BF16) for j in range(8)]
    om = [pb.sb(f"l1om{j}", (128, T), BF16) for j in range(8)]
    qk0 = pb.sb("l1q", (128, T), BF16)
    kk0 = pb.sb("l1k", (128, T), BF16)
    ident = pb.sb("identl1", (128, 128))
    pb.dma(ident[:, :], d_ident)
    idb = pb.sb("idbl1", (128, 128), BF16)
    pb.V(lambda e: e.tensor_copy(idb[:, :], ident[:, :]))
    ones1 = pb.sb("ones1l1", (1, 128))
    pb.G(lambda e: e.memset(ones1[:, :], 1.0))
    onesb = pb.sb("onesbl1", (1, 128), BF16)
    pb.G(lambda e: e.memset(onesb[:, :], 1.0))
    onecol = pb.sb("onecol", (64, 1), BF16)
    pb.G(lambda e: e.memset(onecol[:, :], 1.0))

    rw = pb.sb("rwl1", (1, D))
    pb.dma(rw[:1, :], d_rmsw)
    rwf = pb.sb("rwfl1", (128, D))
    psb = pb.ps("psbl1", (128, 512))
    for j in range(2):
        pb.TE(lambda e, j=j: e.matmul(psb[:, :], ones1[:1, :], rw[:1, j * 512:(j + 1) * 512],
                                      start=True, stop=True))
        pb.V(lambda e, j=j: e.tensor_copy(rwf[:, j * 512:(j + 1) * 512], psb[:, :]))
    xt = pb.sb("xtl1", (128, D))
    sq = pb.sb("sql1", (128, D))
    ms = pb.sb("msl1", (128, 1))
    n1 = pb.sb("n1l1", (128, D))
    n1b = pb.sb("n1bl1", (128, D), BF16)
    for i in range(NT):
        tsl = slice(i * 128, (i + 1) * 128)
        pb.dma(xt[:, :], d_xn[tsl, :])
        pb.S(lambda e: e.activation(sq[:, :], xt[:, :], AF.Square))
        pb.V(lambda e: e.reduce_sum(ms[:, :1], sq[:, :], axis=AX.X))
        pb.V(lambda e: e.tensor_scalar(ms[:, :1], ms[:, :1], 1.0 / D, 1e-6, OP.mult, OP.add))
        pb.S(lambda e: e.activation(ms[:, :1], ms[:, :1], AF.Ln))
        pb.S(lambda e: e.activation(ms[:, :1], ms[:, :1], AF.Exp, scale=-0.5))
        pb.V(lambda e: e.tensor_scalar(n1[:, :], xt[:, :], ms[:, :1], None, OP.mult))
        pb.V(lambda e: e.tensor_tensor(n1[:, :], n1[:, :], rwf[:, :], OP.mult))
        pb.V(lambda e: e.tensor_copy(n1b[:, :], n1[:, :]))
        pb.dma(d_n1b[tsl, :], n1b[:, :], q="gpsimd")

    wt = pb.sb("wtl1", (128, 128), BF16)
    wt8l1 = [pb.sb(f"wt8l1_{k}", (128, 128), BF16) for k in range(8)]
    gbc = pb.sb("gbc", (128, 1))
    pp = pb.ps("ppl1", (128, 512))
    ztile = pb.sb("ztileb", (128, T), BF16)
    n1T = [pb.sb(f"n1T{k}", (128, T), BF16) for k in range(8)]
    for k in range(8):
        pb.dmaT(n1T[k][:, :], d_n1b[:, k * 128:(k + 1) * 128])
    for c in range(26):
        if 16 <= c < 24:
            pb.dma(gbc[:, :], d_gb[(c - 16) * 128:(c - 15) * 128, :])
        for k in range(8):
            pb.op("gpsimd", lambda e, k=k, c=c: e.dma_start(
                out=wt8l1[k][:, :], in_=d_win[k * 128:(k + 1) * 128, c * 128:(c + 1) * 128]), 16)
        for tc in range(4):
            tfl = slice(tc * 512, (tc + 1) * 512)
            for k in range(8):
                pb.TE(lambda e, k=k, tfl=tfl, s=(k == 0), p=(k == 7):
                      e.matmul(pp[:, :], wt8l1[k][:, :], n1T[k][:, tfl], start=s, stop=p))
            if c < 8:
                pb.S(lambda e, tfl=tfl: e.activation(ztile[:, tfl], pp[:, :], AF.Sigmoid))
            elif c < 16:
                pb.V(lambda e, c=c, tfl=tfl: e.tensor_copy(xt2[c - 8][:, tfl], pp[:, :]))
            elif c < 24:
                pb.S(lambda e, c=c, tfl=tfl: e.activation(gt[c - 16][:, tfl], pp[:, :],
                                                          AF.Sigmoid, bias=gbc[:, :1]))
                pb.V(lambda e, c=c, tfl=tfl: e.tensor_scalar(
                    om[c - 16][:, tfl], gt[c - 16][:, tfl], -1.0, 1.0, OP.mult, OP.add))
            elif c == 24:
                pb.V(lambda e, tfl=tfl: e.tensor_copy(qk0[:, tfl], pp[:, :]))
            else:
                pb.V(lambda e, tfl=tfl: e.tensor_copy(kk0[:, tfl], pp[:, :]))
        if c < 8:
            pb.dma(d_z[c * 128:(c + 1) * 128, :], ztile[:, :], q="gpsimd")
    pb.sbfree(*n1T)

    at = pb.sb("atl1", (128, T), BF16)
    vt = pb.sb("vtl1", (128, T), BF16)
    sco = pb.sb("scol1", (128, T))
    xkx, xkk, aot = xt2, om, gt
    for j in range(8):
        pb.V(lambda e, j=j: e.tensor_copy(at[:, :], gt[j][:, :]))
        pb.G(lambda e: e.memset(at[:, 0:T:64], 0.0))
        pb.V(lambda e, j=j: e.tensor_tensor(vt[:, :], om[j][:, :], xt2[j][:, :], OP.mult))
        pb.V(lambda e: e.tensor_tensor_scan(sco[:, :], at[:, :], vt[:, :], 0.0, OP.mult, OP.add))
        pb.V(lambda e, j=j: e.tensor_copy(xkx[j][:, :], sco[:, :]))
        pb.V(lambda e, j=j: e.tensor_tensor(vt[:, :], om[j][:, :], kk0[:, :], OP.mult))
        pb.V(lambda e: e.tensor_tensor_scan(sco[:, :], at[:, :], vt[:, :], 0.0, OP.mult, OP.add))
        pb.V(lambda e, j=j: e.tensor_copy(xkk[j][:, :], sco[:, :]))
        if dbg and j == 0:
            pb.dma(d_xkx[0:128, :], xkx[0][:, :], q="gpsimd")
    chk = [pb.sb(f"chk{j}", (128, 32), BF16) for j in range(8)]
    chx = [pb.sb(f"chx{j}", (128, 32), BF16) for j in range(8)]
    for j in range(8):
        pb.V(lambda e, j=j: e.tensor_copy(chk[j][:, :], xkk[j][:, 0:T:64]))
        pb.V(lambda e, j=j: e.tensor_copy(chx[j][:, :], xkx[j][:, 0:T:64]))
    ptc = pb.ps("ptcl1", (128, 64), BF16)
    chxT = [pb.sb(f"chxT{h}", (32, 64), BF16) for h in range(16)]
    chxh = pb.sb("chxh", (64, 32), BF16)
    for h in range(16):
        pb.V(lambda e, h=h: e.tensor_copy(chxh[:, :], chx[h // 2][(h % 2) * 64:(h % 2) * 64 + 64, :]))
        pb.TE(lambda e: e.transpose(ptc[:32, :64], chxh[:, :], idb[:64, :64]))
        pb.V(lambda e, h=h: e.tensor_copy(chxT[h][:, :], ptc[:32, :64]))

    msk = pb.sb("mskl1", (128, 32))
    pgenA = pb.ps("pgenA", (128, 128))
    psc = pgenA[:, 0:32]
    plse = pgenA[:, 32:33]
    prodb = pb.sb("prodb", (64, 128), BF16)
    sct = pb.sb("sctl1", (128, 32))
    lse = pb.sb("lsel1", (128, 1))
    mx = pb.sb("mxl1", (128, 1))
    wst = pb.sb("wstl1", (128, 32), BF16)
    iw = pb.sb("iwl1", (128, 1))
    den = pb.sb("denl1", (128, 1))
    pwT = pb.ps("pwTl1", (128, 128), BF16)
    wT = pb.sb("wTl1", (32, 128), BF16)
    pot = pb.ps("potl1", (64, 128))
    piw = pb.ps("piwl1", (64, 128))
    o1 = pb.sb("o1l1", (64, 128))
    iwTp = pgenA
    iwT = pb.sb("iwTl1", (1, 128))
    denT = pb.sb("denTl1", (1, 128))
    for h in range(16):
        j = h // 2
        rsl = slice((h % 2) * 64, (h % 2) * 64 + 64)
        for i in range(NT):
            tsl = slice(i * 128, (i + 1) * 128)
            pb.TE(lambda e, j=j, rsl=rsl, tsl=tsl: e.matmul(
                psc, qk0[rsl, tsl], chk[j][rsl, :], start=True, stop=True))
            pb.dma(msk[:, :], d_mask[i * 128:(i + 1) * 128, :])
            pb.V(lambda e: e.scalar_tensor_tensor(sct[:, :], psc, 0.125,
                                                  msk[:, :], OP.mult, OP.add))
            pb.V(lambda e, j=j, rsl=rsl, tsl=tsl: e.tensor_tensor(
                prodb[:, :], qk0[rsl, tsl], xkk[j][rsl, tsl], OP.mult))
            pb.TE(lambda e: e.matmul(plse, prodb[:, :], onecol[:, :1],
                                     start=True, stop=True))
            pb.V(lambda e: e.tensor_scalar(lse[:, :1], plse, 0.125, None, OP.mult))
            pb.V(lambda e: e.reduce_max(mx[:, :1], sct[:, :], axis=AX.X))
            pb.V(lambda e: e.tensor_tensor(mx[:, :1], mx[:, :1], lse[:, :1], OP.max))
            pb.V(lambda e: e.tensor_scalar(mx[:, :1], mx[:, :1], -1.0, None, OP.mult))
            pb.S(lambda e: e.activation(wst[:, :], sct[:, :], AF.Exp, bias=mx[:, :1],
                                        accum_out=den[:, :1]))
            pb.S(lambda e: e.activation(iw[:, :1], lse[:, :1], AF.Exp, bias=mx[:, :1]))
            pb.V(lambda e: e.tensor_tensor(den[:, :1], den[:, :1], iw[:, :1], OP.add))
            pb.TE(lambda e: e.transpose(pwT[:32, :], wst[:, :], idb[:, :]))
            pb.V(lambda e: e.tensor_copy(wT[:, :], pwT[:32, :128]))
            pb.TE(lambda e, h=h: e.matmul(pot[:, :], chxT[h][:, :], wT[:, :],
                                          start=True, stop=True))
            pb.V(lambda e: e.tensor_copy(o1[:, :], pot[:, :]))
            pb.TE(lambda e: e.transpose(iwTp[:1, :], iw[:, :1], ident[:, :]))
            pb.V(lambda e: e.tensor_copy(iwT[:1, :], iwTp[:1, :128]))
            pb.V(lambda e: e.reciprocal(denT[:1, :] if False else den[:, :1], den[:, :1])) if False else None
            pb.TE(lambda e: e.matmul(piw[:, :], ones1[:1, :64], iwT[:1, :],
                                     start=True, stop=True))
            pb.V(lambda e, j=j, rsl=rsl, tsl=tsl: e.scalar_tensor_tensor(
                o1[:, :], piw[:, :], 1.0, xkx[j][rsl, tsl], OP.mult, OP.mult))
            pb.V(lambda e: e.tensor_tensor(o1[:, :], pot[:, :], o1[:, :], OP.add))
            pb.V(lambda e: e.reciprocal(den[:, :1], den[:, :1]))
            pb.TE(lambda e: e.transpose(iwTp[:1, :], den[:, :1], ident[:, :]))
            pb.V(lambda e: e.tensor_copy(denT[:1, :], iwTp[:1, :128]))
            pb.TE(lambda e: e.matmul(piw[:, :], ones1[:1, :64], denT[:1, :],
                                     start=True, stop=True))
            pb.V(lambda e: e.scalar_tensor_tensor(o1[:, :], piw[:, :], 1.0, o1[:, :],
                                                  OP.mult, OP.mult))
            pb.V(lambda e, j=j, rsl=rsl, tsl=tsl: e.tensor_copy(aot[j][rsl, tsl], o1[:, :]))
    for j in range(8):
        pb.op("gpsimd", lambda e, j=j: e.dma_start(out=ztile[:, :], in_=d_z[j * 128:(j + 1) * 128, :]), 16)
        pb.V(lambda e, j=j: e.tensor_tensor(aot[j][:, :], aot[j][:, :], ztile[:, :], OP.mult))
        pb.dma(d_aoT[j * 128:(j + 1) * 128, :], aot[j][:, :], q="gpsimd")
    wo = pb.sb("wol1", (128, D), BF16)
    po = pb.ps("pol1", (128, 512))
    ot = pb.sb("otl1", (128, D))
    for i in range(NT):
        tsl = slice(i * 128, (i + 1) * 128)
        for half in range(2):
            nsl = slice(half * 512, (half + 1) * 512)
            for k in range(8):
                pb.op("gpsimd", lambda e, k=k: e.dma_start(
                    out=wo[:, :], in_=d_wout[k * 128:(k + 1) * 128, :]), 16)
                pb.TE(lambda e, k=k, tsl=tsl, nsl=nsl, s=(k == 0), p=(k == 7):
                      e.matmul(po[:, :], aot[k][:, tsl], wo[:, nsl], start=s, stop=p))
            pb.V(lambda e, nsl=nsl: e.tensor_copy(ot[:, nsl], po[:, :]))
        pb.dma(xt[:, :], d_xn[tsl, :])
        pb.V(lambda e: e.tensor_tensor(ot[:, :], ot[:, :], xt[:, :], OP.add))
        pb.dma(d_out[tsl, :], ot[:, :], q="gpsimd")
    return pb


def e_global_scalar_max(pb, dstcol, srctile, F, scr, ident, ones1, pgen):
    pb.V(lambda e: e.reduce_max(scr, srctile, axis=AX.X))
    pb.TE(lambda e: e.transpose(pgen[:1, :], scr, ident))
    pb.V(lambda e: e.tensor_copy(dstcol, pgen[:1, 0:1])) if False else None
    pb.V(lambda e: e.reduce_max(pgen[:1, 0:1] if False else scr[:1, :1], pgen[:1, :128].rearrange("p f -> p f"), axis=AX.X))
    pb.TE(lambda e: e.matmul(pgen[:, 0:1], ones1[:1, :], scr[:1, :1], start=True, stop=True))
    pb.V(lambda e: e.tensor_copy(dstcol, pgen[:, 0:1]))


def prog_l3_attn(ncore=NCORE, dbg=False, pb=None, px="", d_xn=None, d_out=None):
    pb = pb or PB("l3a")
    d_xn = d_xn if d_xn is not None else pb.din(px + "xn", [T, D], F32)
    d_wt = {w: pb.din(px + f"w{w}t", [D, D], F32) for w in "qkvo"}
    d_ident = pb.din(px + "c_ident", [128, 128], F32)
    d_out = d_out if d_out is not None else pb.dout(px + "out", [T, D], F32)
    nc = pb.nc
    ik = "ExternalOutput" if dbg else "Internal"
    d_qx = nc.dram_tensor(px + "qx", [T, D], BF16, kind=ik).ap()
    d_qwt = {w: nc.dram_tensor(px + f"qw{w}", [D, D], BF16, kind="Internal").ap() for w in "qkvo"}
    d_y = {w: nc.dram_tensor(px + f"y{w}", [T, D], F32, kind=ik).ap() for w in "qkv"}
    d_q3 = {w: nc.dram_tensor(px + f"q3{w}", [T, D], BF16, kind=ik).ap() for w in "qkv"}
    d_sc = nc.dram_tensor(px + "scl3", [16 * T, T], F32, kind="Internal").ap()
    d_w6 = nc.dram_tensor(px + "w6l3", [16 * T, T], F32, kind="Internal").ap()
    d_q6 = nc.dram_tensor(px + "q6l3", [16 * T, T], BF16, kind="Internal").ap()
    d_yo = nc.dram_tensor(px + "yol3", [T, D], F32, kind=ik).ap()
    d_at = nc.dram_tensor(px + "atl3", [T, D], BF16, kind="Internal").ap()
    d_yf = nc.dram_tensor(px + "yfl3", [T, D], F32, kind="Internal").ap()
    cc_in = nc.dram_tensor(px + "cc_in", [128, 8], F32, kind="Internal")
    cc_out = nc.dram_tensor(px + "cc_out", [128, 8], F32, kind="Internal")

    ident = pb.sb("identl3", (128, 128))
    pb.dma(ident[:, :], d_ident)
    ones1 = pb.sb("ones1l3", (1, 128))
    pb.G(lambda e: e.memset(ones1[:, :], 1.0))
    pgen = pb.ps("pgenl3", (128, 128))
    scr1 = pb.sb("scr1l3", (128, 1))
    stats = pb.sb("stats", (128, 8))
    pb.G(lambda e: e.memset(stats[:, :], 0.0))
    xt = pb.sb("xtl3", (128, D))
    ab = pb.sb("abl3", (128, 2 * D))
    itl = pb.sb("itl3", (128, D), I32)
    vb = pb.sb("vbl3", (128, D))
    ob = pb.sb("obl3", (128, D), BF16)
    run = pb.sb("runl3", (128, 1))

    def allreduce(cols, op=OP.max):
        pb.dma(cc_in.ap(), stats[:, :8], q="gpsimd")
        pb.op("gpsimd", lambda e: e.collective_compute(
            "AllReduce", op, replica_groups=[list(range(ncore))],
            ins=[cc_in.ap().opt()], outs=[cc_out.ap().opt()]), 1, kind="x")
        pb.op("gpsimd", lambda e: e.dma_start(out=stats[:, :8], in_=cc_out.ap()), 16)

    def fqscale(dst, src):
        pb.V(lambda e: e.tensor_scalar(dst, src, 1.0 / 127.0, 1e-8, OP.mult, OP.max))

    swcol = pb.sb("swcol", (128, 4))
    winv = pb.sb("winv", (128, 1))
    for wi, w in enumerate("qkvo"):
        pb.G(lambda e: e.memset(run[:, :1], 0.0))
        for i in range(8):
            pb.dma(xt[:, :], d_wt[w][i * 128:(i + 1) * 128, :])
            pb.S(lambda e: e.activation(ab[:, :D], xt[:, :], AF.Abs))
            pb.V(lambda e: e.reduce_max(scr1[:, :1], ab[:, :D], axis=AX.X))
            pb.V(lambda e: e.tensor_tensor(run[:, :1], run[:, :1], scr1[:, :1], OP.max))
        pb.TE(lambda e: e.transpose(pgen[:1, :], run[:, :1], ident[:, :]))
        pb.V(lambda e: e.tensor_copy(ab[:1, :128], pgen[:1, :128]))
        pb.V(lambda e: e.reduce_max(scr1[:1, :1], ab[:1, :128], axis=AX.X))
        pb.TE(lambda e: e.matmul(pgen[:, 0:1], ones1[:1, :], scr1[:1, :1], start=True, stop=True))
        pb.V(lambda e, wi=wi: e.tensor_scalar(swcol[:, wi:wi + 1], pgen[:, 0:1],
                                              1.0 / 127.0, 1e-8, OP.mult, OP.max))
        pb.V(lambda e, wi=wi: e.reciprocal(winv[:, :1], swcol[:, wi:wi + 1]))
        for i in range(8):
            pb.dma(xt[:, :], d_wt[w][i * 128:(i + 1) * 128, :])
            pb.V(lambda e: e.tensor_scalar(vb[:, :], xt[:, :], winv[:, :1], None, OP.mult))
            pb.V(lambda e: e.tensor_copy(itl[:, :], vb[:, :]))
            pb.V(lambda e: e.tensor_copy(ob[:, :], itl[:, :]))
            pb.dma(d_qwt[w][i * 128:(i + 1) * 128, :], ob[:, :], q="gpsimd")

    pb.G(lambda e: e.memset(run[:, :1], 0.0))
    for i in range(NT):
        pb.dma(xt[:, :], d_xn[i * 128:(i + 1) * 128, :])
        pb.S(lambda e: e.activation(ab[:, :D], xt[:, :], AF.Abs))
        pb.V(lambda e: e.reduce_max(scr1[:, :1], ab[:, :D], axis=AX.X))
        pb.V(lambda e: e.tensor_tensor(run[:, :1], run[:, :1], scr1[:, :1], OP.max))
    pb.TE(lambda e: e.transpose(pgen[:1, :], run[:, :1], ident[:, :]))
    pb.V(lambda e: e.tensor_copy(ab[:1, :128], pgen[:1, :128]))
    pb.V(lambda e: e.reduce_max(scr1[:1, :1], ab[:1, :128], axis=AX.X))
    pb.TE(lambda e: e.matmul(pgen[:, 0:1], ones1[:1, :], scr1[:1, :1], start=True, stop=True))
    pb.V(lambda e: e.tensor_copy(stats[:, 0:1], pgen[:, 0:1]))
    allreduce(1)
    sx = pb.sb("sxl3", (128, 1))
    sxi = pb.sb("sxil3", (128, 1))
    fqscale(sx[:, :1], stats[:, 0:1])
    pb.V(lambda e: e.reciprocal(sxi[:, :1], sx[:, :1]))
    for i in range(NT):
        pb.dma(xt[:, :], d_xn[i * 128:(i + 1) * 128, :])
        pb.V(lambda e: e.tensor_scalar(vb[:, :], xt[:, :], sxi[:, :1], None, OP.mult))
        pb.V(lambda e: e.tensor_copy(itl[:, :], vb[:, :]))
        pb.V(lambda e: e.tensor_copy(ob[:, :], itl[:, :]))
        pb.dma(d_qx[i * 128:(i + 1) * 128, :], ob[:, :], q="gpsimd")

    wq = pb.sb("wql3", (128, D), BF16)
    pq = pb.ps("pql3", (128, 512))
    sxw = pb.sb("sxwl3", (128, 1))
    ymax = pb.sb("ymaxl3", (128, 3))
    s2c = pb.sb("s2c", (128, 3))
    s3c = pb.sb("s3c", (128, 3))
    r23 = pb.sb("r23", (128, 3))
    i1 = pb.sb("i1l3", (128, 1), I32)
    t1c = pb.sb("t1c", (128, 1))
    s2i = pb.sb("s2i", (128, 3))
    qxT = [pb.sb(f"qxTl3{k}", (128, T), BF16) for k in range(8)]
    for k in range(8):
        pb.dmaT(qxT[k][:, :], d_qx[:, k * 128:(k + 1) * 128])
    for wi, w in enumerate("qkv"):
        pb.V(lambda e, wi=wi: e.tensor_tensor(sxw[:, :1], sx[:, :1], swcol[:, wi:wi + 1], OP.mult))
        pb.G(lambda e: e.memset(run[:, :1], 0.0))
        for i in range(NT):
            tsl = slice(i * 128, (i + 1) * 128)
            for half in range(2):
                nsl = slice(half * 512, (half + 1) * 512)
                for k in range(8):
                    pb.op("gpsimd", lambda e, k=k, w=w: e.dma_start(
                        out=wq[:, :], in_=d_qwt[w][k * 128:(k + 1) * 128, :]), 16)
                    pb.TE(lambda e, k=k, tsl=tsl, nsl=nsl, s=(k == 0), p=(k == 7):
                          e.matmul(pq[:, :], qxT[k][:, tsl], wq[:, nsl], start=s, stop=p))
                pb.V(lambda e, nsl=nsl: e.tensor_scalar(vb[:, nsl], pq[:, :], sxw[:, :1], None, OP.mult))
            pb.S(lambda e: e.activation(ab[:, :D], vb[:, :], AF.Abs))
            pb.V(lambda e: e.reduce_max(scr1[:, :1], ab[:, :D], axis=AX.X))
            pb.V(lambda e: e.tensor_tensor(run[:, :1], run[:, :1], scr1[:, :1], OP.max))
            pb.dma(d_y[w][tsl, :], vb[:, :], q="gpsimd")
        pb.TE(lambda e: e.transpose(pgen[:1, :], run[:, :1], ident[:, :]))
        pb.V(lambda e: e.tensor_copy(ab[:1, :128], pgen[:1, :128]))
        pb.V(lambda e: e.reduce_max(scr1[:1, :1], ab[:1, :128], axis=AX.X))
        pb.TE(lambda e: e.matmul(pgen[:, 0:1], ones1[:1, :], scr1[:1, :1], start=True, stop=True))
        pb.V(lambda e, wi=wi: e.tensor_copy(stats[:, wi:wi + 1], pgen[:, 0:1]))
    allreduce(3)

    for wi in range(3):
        fqscale(s2c[:, wi:wi + 1], stats[:, wi:wi + 1])
        pb.V(lambda e, wi=wi: e.reciprocal(t1c[:, :1], s2c[:, wi:wi + 1]))
        pb.V(lambda e, wi=wi: e.tensor_tensor(t1c[:, :1], stats[:, wi:wi + 1], t1c[:, :1], OP.mult))
        pb.V(lambda e: e.tensor_copy(i1[:, :1], t1c[:, :1]))
        pb.V(lambda e: e.tensor_copy(t1c[:, :1], i1[:, :1]))
        pb.V(lambda e, wi=wi: e.tensor_tensor(t1c[:, :1], t1c[:, :1], s2c[:, wi:wi + 1], OP.mult))
        fqscale(s3c[:, wi:wi + 1], t1c[:, :1])
        pb.V(lambda e, wi=wi: e.reciprocal(r23[:, wi:wi + 1], s3c[:, wi:wi + 1]))
        pb.V(lambda e, wi=wi: e.tensor_tensor(r23[:, wi:wi + 1], s2c[:, wi:wi + 1],
                                              r23[:, wi:wi + 1], OP.mult))
    for wi in range(3):
        pb.V(lambda e, wi=wi: e.reciprocal(s2i[:, wi:wi + 1], s2c[:, wi:wi + 1]))
    for wi, w in enumerate("qkv"):
        for i in range(NT):
            tsl = slice(i * 128, (i + 1) * 128)
            pb.dma(xt[:, :], d_y[w][tsl, :])
            pb.V(lambda e, wi=wi: e.tensor_scalar(vb[:, :], xt[:, :], s2i[:, wi:wi + 1], None, OP.mult))
            pb.V(lambda e: e.tensor_copy(itl[:, :], vb[:, :]))
            pb.V(lambda e: e.tensor_copy(vb[:, :], itl[:, :]))
            pb.V(lambda e, wi=wi: e.tensor_scalar(vb[:, :], vb[:, :], r23[:, wi:wi + 1], None, OP.mult))
            pb.V(lambda e: e.tensor_copy(itl[:, :], vb[:, :]))
            pb.V(lambda e: e.tensor_copy(ob[:, :], itl[:, :]))
            pb.dma(d_q3[w][tsl, :], ob[:, :], q="gpsimd")
    pb.sbfree(*qxT)

    qT2 = pb.sb("qT2l3", (128, T), BF16)
    kT2 = pb.sb("kT2l3", (128, T), BF16)
    sct = pb.sb("sctl3", (128, T))
    rowmx = pb.sb("rowmx", (128, 256))
    denst = pb.sb("denst", (128, 256))
    psck = pb.ps("psckl3", (128, 512))
    pb.G(lambda e: e.memset(run[:, :1], 0.0))
    for hp in range(8):
        pb.dmaT(qT2[:, :], d_q3["q"][:, hp * 128:(hp + 1) * 128])
        pb.dmaT(kT2[:, :], d_q3["k"][:, hp * 128:(hp + 1) * 128])
        for hh in range(2):
            h = hp * 2 + hh
            rsl = slice(hh * 64, hh * 64 + 64)
            for i in range(NT):
                tsl = slice(i * 128, (i + 1) * 128)
                for sc4 in range(4):
                    ssl = slice(sc4 * 512, (sc4 + 1) * 512)
                    pb.TE(lambda e, rsl=rsl, tsl=tsl, ssl=ssl: e.matmul(
                        psck[:, :], qT2[rsl, tsl], kT2[rsl, ssl], start=True, stop=True))
                    pb.V(lambda e, ssl=ssl: e.tensor_copy(sct[:, ssl], psck[:, :]))
                pb.V(lambda e: e.reduce_max(scr1[:, :1], sct[:, :], axis=AX.X))
                pb.V(lambda e, h=h, i=i: e.tensor_copy(rowmx[:, h * 16 + i:h * 16 + i + 1], scr1[:, :1]))
                pb.S(lambda e: e.activation(ab[:1, :1], scr1[:1, :1], AF.Abs)) if False else None
                pb.S(lambda e: e.activation(ab[:, :1024], sct[:, :1024], AF.Abs))
                pb.S(lambda e: e.activation(ab[:, 1024:2048], sct[:, 1024:2048], AF.Abs))
                pb.V(lambda e: e.reduce_max(t1c[:, :1], ab[:, :T], axis=AX.X))
                pb.V(lambda e: e.tensor_tensor(run[:, :1], run[:, :1], t1c[:, :1], OP.max))
                pb.dma(d_sc[h * T + i * 128:h * T + (i + 1) * 128, :], sct[:, :], q="gpsimd")
    pb.TE(lambda e: e.transpose(pgen[:1, :], run[:, :1], ident[:, :]))
    pb.V(lambda e: e.tensor_copy(ab[:1, :128], pgen[:1, :128]))
    pb.V(lambda e: e.reduce_max(scr1[:1, :1], ab[:1, :128], axis=AX.X))
    pb.TE(lambda e: e.matmul(pgen[:, 0:1], ones1[:1, :], scr1[:1, :1], start=True, stop=True))
    pb.V(lambda e: e.tensor_copy(stats[:, 0:1], pgen[:, 0:1]))
    pb.V(lambda e: e.tensor_tensor(stats[:, 0:1], stats[:, 0:1], s3c[:, 0:1], OP.mult))
    pb.V(lambda e: e.tensor_tensor(stats[:, 0:1], stats[:, 0:1], s3c[:, 1:2], OP.mult))
    allreduce(1)
    s5 = pb.sb("s5l3", (128, 1))
    gam = pb.sb("gaml3", (128, 1))
    alp = pb.sb("alpl3", (128, 1))
    fqscale(s5[:, :1], stats[:, 0:1])
    pb.V(lambda e: e.reciprocal(gam[:, :1], s5[:, :1]))
    pb.V(lambda e: e.tensor_tensor(gam[:, :1], gam[:, :1], s3c[:, 0:1], OP.mult))
    pb.V(lambda e: e.tensor_tensor(gam[:, :1], gam[:, :1], s3c[:, 1:2], OP.mult))
    pb.V(lambda e: e.tensor_scalar(alp[:, :1], s5[:, :1], 0.125, None, OP.mult))

    q5mx = pb.sb("q5mx", (128, 1))
    negb = pb.sb("negbl3", (128, 1))
    wtile = pb.sb("wtilel3", (128, T))
    runmin = pb.sb("runminl3", (128, 1))
    pb.G(lambda e: e.memset(runmin[:, :1], 1e30))
    for h in range(16):
        for i in range(NT):
            rr = slice(h * T + i * 128, h * T + (i + 1) * 128)
            pb.dma(sct[:, :], d_sc[rr, :])
            pb.V(lambda e: e.tensor_scalar(sct[:, :], sct[:, :], gam[:, :1], None, OP.mult))
            pb.V(lambda e: e.tensor_copy(itl[:, :], sct[:, :1024]))
            pb.V(lambda e: e.tensor_copy(sct[:, :1024], itl[:, :]))
            pb.V(lambda e: e.tensor_copy(itl[:, :], sct[:, 1024:2048]))
            pb.V(lambda e: e.tensor_copy(sct[:, 1024:2048], itl[:, :]))
            pb.V(lambda e, h=h, i=i: e.tensor_tensor(q5mx[:, :1], rowmx[:, h * 16 + i:h * 16 + i + 1], gam[:, :1], OP.mult))
            pb.V(lambda e: e.tensor_copy(i1[:, :1], q5mx[:, :1]))
            pb.V(lambda e: e.tensor_copy(q5mx[:, :1], i1[:, :1]))
            pb.V(lambda e: e.tensor_tensor(negb[:, :1], q5mx[:, :1], alp[:, :1], OP.mult))
            pb.V(lambda e: e.tensor_scalar(negb[:, :1], negb[:, :1], -1.0, None, OP.mult))
            pb.S(lambda e: e.activation(wtile[:, :], sct[:, :], AF.Exp, bias=negb[:, :1],
                                        scale=alp[:, :1], accum_out=scr1[:, :1]))
            pb.V(lambda e, h=h, i=i: e.tensor_copy(denst[:, h * 16 + i:h * 16 + i + 1], scr1[:, :1]))
            pb.V(lambda e: e.tensor_tensor(runmin[:, :1], runmin[:, :1], scr1[:, :1], OP.min))
            pb.dma(d_w6[rr, :], wtile[:, :], q="gpsimd")
    pb.V(lambda e: e.tensor_scalar(runmin[:, :1], runmin[:, :1], -1.0, None, OP.mult))
    pb.TE(lambda e: e.transpose(pgen[:1, :], runmin[:, :1], ident[:, :]))
    pb.V(lambda e: e.tensor_copy(ab[:1, :128], pgen[:1, :128]))
    pb.V(lambda e: e.reduce_max(scr1[:1, :1], ab[:1, :128], axis=AX.X))
    pb.TE(lambda e: e.matmul(pgen[:, 0:1], ones1[:1, :], scr1[:1, :1], start=True, stop=True))
    pb.V(lambda e: e.tensor_copy(stats[:, 0:1], pgen[:, 0:1]))
    allreduce(1)
    s6 = pb.sb("s6l3", (128, 1))
    pb.V(lambda e: e.tensor_scalar(t1c[:, :1], stats[:, 0:1], -1.0, None, OP.mult))
    pb.V(lambda e: e.reciprocal(t1c[:, :1], t1c[:, :1]))
    fqscale(s6[:, :1], t1c[:, :1])

    q3vt = [pb.sb(f"q3vt{j}", (128, D), BF16) for j in range(NT)]
    for j in range(NT):
        pb.dma(q3vt[j][:, :], d_q3["v"][j * 128:(j + 1) * 128, :])
    beta = pb.sb("betal3", (128, 1))
    for h in range(16):
        for i in range(NT):
            rr = slice(h * T + i * 128, h * T + (i + 1) * 128)
            pb.dma(sct[:, :], d_w6[rr, :])
            pb.V(lambda e, h=h, i=i: e.tensor_tensor(beta[:, :1], denst[:, h * 16 + i:h * 16 + i + 1], s6[:, :1], OP.mult))
            pb.V(lambda e: e.reciprocal(beta[:, :1], beta[:, :1]))
            pb.V(lambda e: e.tensor_scalar(sct[:, :], sct[:, :], beta[:, :1], None, OP.mult))
            pb.V(lambda e: e.tensor_copy(itl[:, :], sct[:, :1024]))
            pb.V(lambda e: e.tensor_copy(sct[:, :1024], itl[:, :]))
            pb.V(lambda e: e.tensor_copy(itl[:, :], sct[:, 1024:2048]))
            pb.V(lambda e: e.tensor_copy(sct[:, 1024:2048], itl[:, :]))
            pb.V(lambda e: e.tensor_copy(wtile[:, :], sct[:, :]))
            pb.dma(d_q6[rr, :], wtile[:, :], q="gpsimd")
    s6v = pb.sb("s6vl3", (128, 1))
    pb.V(lambda e: e.tensor_tensor(s6v[:, :1], s6[:, :1], s3c[:, 2:3], OP.mult))
    q6T = pb.sb("q6Tl3", (128, T), BF16)
    pov = pb.ps("povl3", (128, 64))
    yot = pb.sb("yotl3", (128, D))
    pb.G(lambda e: e.memset(run[:, :1], 0.0))
    for i in range(NT):
        tsl = slice(i * 128, (i + 1) * 128)
        for h in range(16):
            for c in range(NT):
                if i == 0 or True:
                    pass
            pass
        pass
    for h in range(16):
        for c in range(NT):
            pb.dmaT(q6T[:, :] if False else q6T[:, :],
                    d_w6[0:1, 0:128]) if False else None
        pass
    q6hT = [pb.sb(f"q6hT{c}", (128, T), BF16) for c in range(NT)]
    ymaxc = [pb.sb(f"ymaxc{i}", (128, 1)) for i in range(NT)]
    for i in range(NT):
        pb.G(lambda e, i=i: e.memset(ymaxc[i][:, :1], 0.0))
    for h in range(16):
        for c in range(NT):
            pb.op("scalar", lambda e, h=h, c=c: e.dma_start(
                out=q6hT[c][:, :],
                in_=d_q6[h * T:(h + 1) * T, c * 128:(c + 1) * 128],
                transpose=True), 16)
        for i in range(NT):
            tsl = slice(i * 128, (i + 1) * 128)
            for c in range(NT):
                pb.TE(lambda e, c=c, h=h, tsl=tsl, s=(c == 0), p=(c == NT - 1):
                      e.matmul(pov[:, :], q6hT[c][:, tsl], q3vt[c][:, h * 64:(h + 1) * 64],
                               start=s, stop=p))
            pb.V(lambda e: e.tensor_scalar(yot[:, 0:64], pov[:, :], s6v[:, :1], None, OP.mult))
            pb.S(lambda e: e.activation(ab[:, 0:64], yot[:, 0:64], AF.Abs))
            pb.V(lambda e: e.reduce_max(scr1[:, :1], ab[:, 0:64], axis=AX.X))
            pb.V(lambda e, i=i: e.tensor_tensor(ymaxc[i][:, :1], ymaxc[i][:, :1], scr1[:, :1], OP.max))
            pb.op("gpsimd", lambda e, h=h, tsl=tsl: e.dma_start(
                out=d_yo[tsl, h * 64:(h + 1) * 64], in_=yot[:, 0:64]), 16)
    for i in range(NT):
        pb.V(lambda e, i=i: e.tensor_tensor(run[:, :1], run[:, :1], ymaxc[i][:, :1], OP.max))
    pb.sbfree(*(list(reversed(ymaxc))))
    pb.sbfree(*(list(reversed(q6hT))))
    pb.TE(lambda e: e.transpose(pgen[:1, :], run[:, :1], ident[:, :]))
    pb.V(lambda e: e.tensor_copy(ab[:1, :128], pgen[:1, :128]))
    pb.V(lambda e: e.reduce_max(scr1[:1, :1], ab[:1, :128], axis=AX.X))
    pb.TE(lambda e: e.matmul(pgen[:, 0:1], ones1[:1, :], scr1[:1, :1], start=True, stop=True))
    pb.V(lambda e: e.tensor_copy(stats[:, 0:1], pgen[:, 0:1]))
    allreduce(1)

    s7 = pb.sb("s7l3", (128, 1))
    s7i = pb.sb("s7il3", (128, 1))
    r78 = pb.sb("r78l3", (128, 1))
    fqscale(s7[:, :1], stats[:, 0:1])
    pb.V(lambda e: e.reciprocal(s7i[:, :1], s7[:, :1]))
    pb.V(lambda e: e.tensor_tensor(t1c[:, :1], stats[:, 0:1], s7i[:, :1], OP.mult))
    pb.V(lambda e: e.tensor_copy(i1[:, :1], t1c[:, :1]))
    pb.V(lambda e: e.tensor_copy(t1c[:, :1], i1[:, :1]))
    pb.V(lambda e: e.tensor_tensor(t1c[:, :1], t1c[:, :1], s7[:, :1], OP.mult))
    fqscale(t1c[:, :1], t1c[:, :1])
    pb.V(lambda e: e.reciprocal(r78[:, :1], t1c[:, :1]))
    pb.V(lambda e: e.tensor_tensor(r78[:, :1], s7[:, :1], r78[:, :1], OP.mult))
    s8w = pb.sb("s8wl3", (128, 1))
    pb.V(lambda e: e.tensor_tensor(s8w[:, :1], t1c[:, :1], swcol[:, 3:4], OP.mult))
    for i in range(NT):
        tsl = slice(i * 128, (i + 1) * 128)
        pb.dma(xt[:, :], d_yo[tsl, :])
        pb.V(lambda e: e.tensor_scalar(vb[:, :], xt[:, :], s7i[:, :1], None, OP.mult))
        pb.V(lambda e: e.tensor_copy(itl[:, :], vb[:, :]))
        pb.V(lambda e: e.tensor_copy(vb[:, :], itl[:, :]))
        pb.V(lambda e: e.tensor_scalar(vb[:, :], vb[:, :], r78[:, :1], None, OP.mult))
        pb.V(lambda e: e.tensor_copy(itl[:, :], vb[:, :]))
        pb.V(lambda e: e.tensor_copy(ob[:, :], itl[:, :]))
        pb.dma(d_at[tsl, :], ob[:, :], q="gpsimd")
    atT = [pb.sb(f"atTl3{k}", (128, T), BF16) for k in range(8)]
    for k in range(8):
        pb.dmaT(atT[k][:, :], d_at[:, k * 128:(k + 1) * 128])
    pb.G(lambda e: e.memset(run[:, :1], 0.0))
    for i in range(NT):
        tsl = slice(i * 128, (i + 1) * 128)
        for half in range(2):
            nsl = slice(half * 512, (half + 1) * 512)
            for k in range(8):
                pb.op("gpsimd", lambda e, k=k: e.dma_start(
                    out=wq[:, :], in_=d_qwt["o"][k * 128:(k + 1) * 128, :]), 16)
                pb.TE(lambda e, k=k, tsl=tsl, nsl=nsl, s=(k == 0), p=(k == 7):
                      e.matmul(pq[:, :], atT[k][:, tsl], wq[:, nsl], start=s, stop=p))
            pb.V(lambda e, nsl=nsl: e.tensor_scalar(vb[:, nsl], pq[:, :], s8w[:, :1], None, OP.mult))
        pb.S(lambda e: e.activation(ab[:, :D], vb[:, :], AF.Abs))
        pb.V(lambda e: e.reduce_max(scr1[:, :1], ab[:, :D], axis=AX.X))
        pb.V(lambda e: e.tensor_tensor(run[:, :1], run[:, :1], scr1[:, :1], OP.max))
        pb.dma(d_yf[tsl, :], vb[:, :], q="gpsimd")
    pb.TE(lambda e: e.transpose(pgen[:1, :], run[:, :1], ident[:, :]))
    pb.V(lambda e: e.tensor_copy(ab[:1, :128], pgen[:1, :128]))
    pb.V(lambda e: e.reduce_max(scr1[:1, :1], ab[:1, :128], axis=AX.X))
    pb.TE(lambda e: e.matmul(pgen[:, 0:1], ones1[:1, :], scr1[:1, :1], start=True, stop=True))
    pb.V(lambda e: e.tensor_copy(stats[:, 0:1], pgen[:, 0:1]))
    allreduce(1)
    sf = pb.sb("sfl3", (128, 1))
    sfi = pb.sb("sfil3", (128, 1))
    fqscale(sf[:, :1], stats[:, 0:1])
    pb.V(lambda e: e.reciprocal(sfi[:, :1], sf[:, :1]))
    for i in range(NT):
        tsl = slice(i * 128, (i + 1) * 128)
        pb.dma(xt[:, :], d_yf[tsl, :])
        pb.V(lambda e: e.tensor_scalar(vb[:, :], xt[:, :], sfi[:, :1], None, OP.mult))
        pb.V(lambda e: e.tensor_copy(itl[:, :], vb[:, :]))
        pb.V(lambda e: e.tensor_copy(vb[:, :], itl[:, :]))
        pb.V(lambda e: e.tensor_scalar(vb[:, :], vb[:, :], sf[:, :1], None, OP.mult))
        pb.dma(d_out[tsl, :], vb[:, :], q="gpsimd")
    return pb


def prog_l0_attn(dbg=False, pb=None, px="", d_xn=None, d_out=None):
    E = 2048
    pb = pb or PB("l0a")
    d_xn = d_xn if d_xn is not None else pb.din(px + "xn", [T, D], F32)
    d_inw = pb.din(px + "inw", [D, 2 * E], F32)
    d_convw = pb.din(px + "convw", [E, 4], F32)
    d_convb = pb.din(px + "convb", [E, 1], F32)
    d_dtw = pb.din(px + "dtw", [E, E], F32)
    d_dtb = pb.din(px + "dtb", [E, 1], F32)
    d_bw = pb.din(px + "bw", [E, 64], F32)
    d_cw = pb.din(px + "cw", [E, 64], F32)
    d_alog = pb.din(px + "alog", [1, 64], F32)
    d_outw = pb.din(px + "outw", [E, D], F32)
    d_gatew = pb.din(px + "gatew", [D, E], F32)
    d_upw = pb.din(px + "upw", [D, E], F32)
    d_downw = pb.din(px + "downw", [E, D], F32)
    d_inormw = pb.din(px + "inormw", [1, D], F32)
    d_fnormw = pb.din(px + "fnormw", [1, D], F32)
    d_out = d_out if d_out is not None else pb.dout(px + "out", [T, D], F32)
    nc = pb.nc
    ik = "ExternalOutput" if dbg else "Internal"
    d_n1b = nc.dram_tensor(px + "n1b0", [T, D], BF16, kind=ik).ap()
    d_xz = nc.dram_tensor(px + "xz0", [E, T], F32, kind=ik).ap()
    d_zs = nc.dram_tensor(px + "zs0", [E, T], BF16, kind=ik).ap()
    d_xc = nc.dram_tensor(px + "xc0", [E, T], BF16, kind=ik).ap()
    d_dt = nc.dram_tensor(px + "dt0", [E, T], BF16, kind=ik).ap()
    d_u = nc.dram_tensor(px + "u0", [E, T], BF16, kind=ik).ap()
    d_y = nc.dram_tensor(px + "y0", [E, T], BF16, kind=ik).ap()
    d_xo = nc.dram_tensor(px + "xo0", [T, D], F32, kind=ik).ap()
    d_n2b = nc.dram_tensor(px + "n2b0", [T, D], BF16, kind=ik).ap()
    d_h = nc.dram_tensor(px + "h0", [E, T], BF16, kind=ik).ap()

    ones1 = pb.sb("ones1l0", (1, 128))
    pb.G(lambda e: e.memset(ones1[:, :], 1.0))
    pbig = pb.ps("pbigl0", (128, 2048))
    psA = pb.ps("psAl0", (128, 512))
    psB = pb.ps("psBl0", (128, 512))
    psb = psA
    rw = pb.sb("rwl0", (1, D))
    pb.dma(rw[:1, :], d_inormw)
    rwf = pb.sb("rwfl0", (128, D))
    for j in range(2):
        pb.TE(lambda e, j=j: e.matmul(psb[:, :], ones1[:1, :], rw[:1, j * 512:(j + 1) * 512],
                                      start=True, stop=True))
        pb.V(lambda e, j=j: e.tensor_copy(rwf[:, j * 512:(j + 1) * 512], psb[:, :]))
    xt = pb.sb("xtl0", (128, D))
    sq = pb.sb("sql0", (128, D))
    ms = pb.sb("msl0", (128, 1))
    n1 = pb.sb("n1l0", (128, D))
    n1b = pb.sb("n1bl0", (128, D), BF16)
    for i in range(NT):
        tsl = slice(i * 128, (i + 1) * 128)
        pb.dma(xt[:, :], d_xn[tsl, :])
        pb.S(lambda e: e.activation(sq[:, :], xt[:, :], AF.Square))
        pb.V(lambda e: e.reduce_sum(ms[:, :1], sq[:, :], axis=AX.X))
        pb.V(lambda e: e.tensor_scalar(ms[:, :1], ms[:, :1], 1.0 / D, 1e-5, OP.mult, OP.add))
        pb.S(lambda e: e.activation(ms[:, :1], ms[:, :1], AF.Ln))
        pb.S(lambda e: e.activation(ms[:, :1], ms[:, :1], AF.Exp, scale=-0.5))
        pb.V(lambda e: e.tensor_scalar(n1[:, :], xt[:, :], ms[:, :1], None, OP.mult))
        pb.V(lambda e: e.tensor_tensor(n1[:, :], n1[:, :], rwf[:, :], OP.mult))
        pb.V(lambda e: e.tensor_copy(n1b[:, :], n1[:, :]))
        pb.dma(d_n1b[tsl, :], n1b[:, :], q="gpsimd")
    wt = pb.sb("wtl0", (128, 128), BF16)
    stile = pb.sb("stileL0", (128, 512))
    sbt = pb.sb("sbtL0", (128, T), BF16)
    wt8 = [pb.sb(f"wt8_{k}", (128, 128), BF16) for k in range(8)]
    n1T = [pb.sb(f"n1T0{k}", (128, T), BF16) for k in range(8)]
    for k in range(8):
        pb.dmaT(n1T[k][:, :], d_n1b[:, k * 128:(k + 1) * 128])
    for c in range(32):
        for k in range(8):
            pb.op("gpsimd", lambda e, k=k, c=c: e.dma_start(
                out=wt8[k][:, :], in_=d_inw[k * 128:(k + 1) * 128, c * 128:(c + 1) * 128]), 16)
        for tc in range(4):
            tfl = slice(tc * 512, (tc + 1) * 512)
            for k in range(8):
                pb.TE(lambda e, k=k, tfl=tfl, s=(k == 0), p=(k == 7):
                      e.matmul(psb[:, :], wt8[k][:, :], n1T[k][:, tfl], start=s, stop=p))
            if c < 16:
                pb.V(lambda e: e.tensor_copy(stile[:, :], psb[:, :]))
                pb.dma(d_xz[c * 128:(c + 1) * 128, tfl], stile[:, :], q="gpsimd")
            else:
                pb.S(lambda e, tfl=tfl: e.activation(sbt[:, tfl], psb[:, :], AF.Silu))
        if c >= 16:
            pb.dma(d_zs[(c - 16) * 128:(c - 15) * 128, :], sbt[:, :], q="gpsimd")
    pb.sbfree(*n1T)
    BT = pb.sb("BTl0", (64, T), BF16)
    CT = pb.sb("CTl0", (64, T), BF16)
    arow = pb.sb("arowl0", (1, 64))
    afull = pb.sb("afulll0", (128, 64))
    browk = pb.sb("browk", (1, T), BF16)
    crowk = pb.sb("crowk", (1, T), BF16)
    onesbl0 = pb.sb("onesbl0", (1, 128), BF16)
    pb.G(lambda e: e.memset(onesbl0[:, :], 1.0))
    xck = pb.sb("xckl0", (128, T), BF16)
    xzp = pb.sb("xzp", (128, T + 3))
    wcv = pb.sb("wcv", (128, 4))
    cbc = pb.sb("cbc", (128, 1))
    acc = pb.sb("accl0", (128, T))
    for j in range(16):
        esl = slice(j * 128, (j + 1) * 128)
        pb.G(lambda e: e.memset(xzp[:, 0:1], 0.0))
        pb.G(lambda e: e.memset(xzp[:, T + 1:T + 3], 0.0))
        pb.dma(xzp[:, 1:T + 1], d_xz[esl, :])
        pb.dma(wcv[:, :], d_convw[esl, :])
        pb.dma(cbc[:, :], d_convb[esl, :])
        pb.V(lambda e: e.tensor_scalar(acc[:, :], xzp[:, 0:T], wcv[:, 0:1], None, OP.mult))
        for kk in range(1, 4):
            pb.V(lambda e, kk=kk: e.scalar_tensor_tensor(
                acc[:, :], xzp[:, kk:kk + T], wcv[:, kk:kk + 1], acc[:, :], OP.mult, OP.add))
        pb.V(lambda e: e.tensor_scalar(acc[:, :], acc[:, :], cbc[:, :1], None, OP.add))
        pb.S(lambda e: e.activation(acc[:, :], acc[:, :], AF.Silu))
        pb.V(lambda e: e.tensor_copy(sbt[:, :], acc[:, :]))
        pb.dma(d_xc[esl, :], sbt[:, :], q="gpsimd")
    pd = [pbig[:, t_ * 512:(t_ + 1) * 512] for t_ in range(4)]
    dtb = pb.sb("dtbl0", (128, 1))
    dtt = pb.sb("dttl0", (128, T))
    au = pb.sb("aul0", (128, T))
    for j in range(16):
        esl = slice(j * 128, (j + 1) * 128)
        for k in range(16):
            pb.op("gpsimd", lambda e, k=k, j=j: e.dma_start(
                out=wt[:, :], in_=d_dtw[k * 128:(k + 1) * 128, j * 128:(j + 1) * 128]), 16)
            pb.op("gpsimd", lambda e, k=k: e.dma_start(
                out=xck[:, :], in_=d_xc[k * 128:(k + 1) * 128, :]), 16)
            for tc in range(4):
                pb.TE(lambda e, k=k, tc=tc, s=(k == 0), p=(k == 15):
                      e.matmul(pd[tc], wt[:, :], xck[:, tc * 512:(tc + 1) * 512],
                               start=s, stop=p))
        pb.dma(dtb[:, :], d_dtb[esl, :])
        for tc in range(4):
            pb.V(lambda e, tc=tc: e.tensor_scalar(dtt[:, tc * 512:(tc + 1) * 512],
                                                  pd[tc], dtb[:, :1], None, OP.add))
        pb.S(lambda e: e.activation(au[:, :], dtt[:, :], AF.Abs))
        pb.S(lambda e: e.activation(au[:, :], au[:, :], AF.Exp, scale=-1.0))
        pb.S(lambda e: e.activation(au[:, :], au[:, :], AF.Ln, bias=1.0))
        pb.V(lambda e: e.tensor_scalar(dtt[:, :], dtt[:, :], 0.0, None, OP.max))
        pb.V(lambda e: e.tensor_tensor(dtt[:, :], dtt[:, :], au[:, :], OP.add))
        pb.V(lambda e: e.tensor_scalar(dtt[:, :], dtt[:, :], 0.001, 0.1, OP.max, OP.min))
        pb.V(lambda e: e.tensor_copy(sbt[:, :], dtt[:, :]))
        pb.dma(d_dt[esl, :], sbt[:, :], q="gpsimd")
        pb.op("gpsimd", lambda e, esl=esl: e.dma_start(out=xck[:, :], in_=d_xc[esl, :]), 16)
        pb.V(lambda e: e.tensor_tensor(dtt[:, :], dtt[:, :], xck[:, :], OP.mult))
        pb.V(lambda e: e.tensor_copy(sbt[:, :], dtt[:, :]))
        pb.dma(d_u[esl, :], sbt[:, :], q="gpsimd")
    bwt = pb.sb("bwtl0", (128, 128), BF16)
    for (dst, wsrc) in ((BT, d_bw), (CT, d_cw)):
        for tc in range(4):
            for k in range(16):
                pb.op("gpsimd", lambda e, k=k, wsrc=wsrc: e.dma_start(
                    out=bwt[:, 0:64], in_=wsrc[k * 128:(k + 1) * 128, :]), 16)
                pb.op("gpsimd", lambda e, k=k: e.dma_start(
                    out=xck[:, :], in_=d_xc[k * 128:(k + 1) * 128, :]), 16)
                pb.TE(lambda e, k=k, tc=tc, s=(k == 0), p=(k == 15):
                      e.matmul(psb[:64, :], bwt[:, 0:64], xck[:, tc * 512:(tc + 1) * 512],
                               start=s, stop=p))
            pb.V(lambda e, dst=dst, tc=tc: e.tensor_copy(dst[:, tc * 512:(tc + 1) * 512], psb[:64, :]))
    pb.dma(arow[:1, :], d_alog)
    pb.S(lambda e: e.activation(arow[:1, :], arow[:1, :], AF.Exp))
    pb.V(lambda e: e.tensor_scalar(arow[:1, :], arow[:1, :], -1.0, None, OP.mult))
    pab = psA[:, 0:64]
    pb.TE(lambda e: e.matmul(pab, ones1[:1, :], arow[:1, :], start=True, stop=True))
    pb.V(lambda e: e.tensor_copy(afull[:, :], pab))
    pb.sbfree(bwt, au, dtt, dtb, acc, cbc, wcv, xzp)
    pbb = pbig
    Bb = pb.sb("Bbl0", (128, T), BF16)
    Cb = pb.sb("Cbl0", (128, T), BF16)
    dth = [pb.sb(f"dth{j}", (128, T), BF16) for j in range(8)]
    uh = [pb.sb(f"uh{j}", (128, T), BF16) for j in range(8)]
    ya = [pb.sb(f"ya{j}", (128, T)) for j in range(8)]
    sco = pb.sb("scol0", (128, T))
    bt2 = pb.sb("bt2l0", (128, T))
    for half in range(2):
        for j in range(8):
            je = half * 8 + j
            pb.op("gpsimd", lambda e, j=j, je=je: e.dma_start(
                out=dth[j][:, :], in_=d_dt[je * 128:(je + 1) * 128, :]), 16)
            pb.op("gpsimd", lambda e, j=j, je=je: e.dma_start(
                out=uh[j][:, :], in_=d_u[je * 128:(je + 1) * 128, :]), 16)
            pb.G(lambda e, j=j: e.memset(ya[j][:, :], 0.0))
        for n in range(64):
            pb.op("gpsimd", lambda e, n=n: e.dma_start(out=browk[:1, :], in_=BT[n:n + 1, :]), 16)
            pb.op("gpsimd", lambda e, n=n: e.dma_start(out=crowk[:1, :], in_=CT[n:n + 1, :]), 16)
            for tc in range(4):
                pb.TE(lambda e, tc=tc: e.matmul(pbb[:, tc * 512:(tc + 1) * 512], onesbl0[:1, :],
                                                browk[:1, tc * 512:(tc + 1) * 512],
                                                start=True, stop=True))
            pb.V(lambda e: e.tensor_copy(Bb[:, :], pbb[:, :]))
            for tc in range(4):
                pb.TE(lambda e, tc=tc: e.matmul(pbb[:, tc * 512:(tc + 1) * 512], onesbl0[:1, :],
                                                crowk[:1, tc * 512:(tc + 1) * 512],
                                                start=True, stop=True))
            pb.V(lambda e: e.tensor_copy(Cb[:, :], pbb[:, :]))
            for j in range(8):
                pb.V(lambda e, j=j, n=n: e.tensor_scalar(
                    sco[:, :], dth[j][:, :], afull[:, n:n + 1], None, OP.mult))
                pb.S(lambda e: e.activation(sco[:, :], sco[:, :], AF.Exp))
                pb.V(lambda e, j=j: e.tensor_tensor(bt2[:, :], uh[j][:, :], Bb[:, :], OP.mult))
                pb.V(lambda e: e.tensor_tensor_scan(bt2[:, :], sco[:, :], bt2[:, :], 0.0,
                                                    OP.mult, OP.add))
                pb.V(lambda e: e.tensor_tensor(bt2[:, :], bt2[:, :], Cb[:, :], OP.mult))
                pb.V(lambda e, j=j: e.tensor_tensor(ya[j][:, :], ya[j][:, :], bt2[:, :], OP.add))
        for j in range(8):
            je = half * 8 + j
            pb.op("gpsimd", lambda e, je=je: e.dma_start(
                out=xck[:, :], in_=d_zs[je * 128:(je + 1) * 128, :]), 16)
            pb.V(lambda e, j=j: e.tensor_tensor(ya[j][:, :], ya[j][:, :], xck[:, :], OP.mult))
            pb.V(lambda e, j=j: e.tensor_copy(sbt[:, :], ya[j][:, :]))
            pb.dma(d_y[je * 128:(je + 1) * 128, :], sbt[:, :], q="gpsimd")
    pb.sbfree(bt2, sco)
    pb.sbfree(*(list(reversed(ya)) + list(reversed(uh)) + list(reversed(dth))))
    pb.sbfree(Cb, Bb)
    yk = pb.sb("ykl0", (128, 128), BF16)
    wo = pb.sb("wol0", (128, D), BF16)
    po2 = psA
    ot = pb.sb("otl0", (128, D))
    rw2 = pb.sb("rw2l0", (1, D))
    pb.dma(rw2[:1, :], d_fnormw)
    rwf2 = pb.sb("rwf2l0", (128, D))
    for j in range(2):
        pb.TE(lambda e, j=j: e.matmul(psb[:, :], ones1[:1, :], rw2[:1, j * 512:(j + 1) * 512],
                                      start=True, stop=True))
        pb.V(lambda e, j=j: e.tensor_copy(rwf2[:, j * 512:(j + 1) * 512], psb[:, :]))
    for i in range(NT):
        tsl = slice(i * 128, (i + 1) * 128)
        for half in range(2):
            nsl = slice(half * 512, (half + 1) * 512)
            for k in range(16):
                pb.op("gpsimd", lambda e, k=k, tsl=tsl: e.dma_start(
                    out=yk[:, :], in_=d_y[k * 128:(k + 1) * 128, tsl]), 16)
                pb.op("gpsimd", lambda e, k=k: e.dma_start(
                    out=wo[:, :], in_=d_outw[k * 128:(k + 1) * 128, :]), 16)
                pb.TE(lambda e, k=k, nsl=nsl, s=(k == 0), p=(k == 15):
                      e.matmul(po2[:, :], yk[:, :], wo[:, nsl], start=s, stop=p))
            pb.V(lambda e, nsl=nsl: e.tensor_copy(ot[:, nsl], po2[:, :]))
        pb.dma(xt[:, :], d_xn[tsl, :])
        pb.V(lambda e: e.tensor_tensor(ot[:, :], ot[:, :], xt[:, :], OP.add))
        pb.dma(d_xo[tsl, :], ot[:, :], q="gpsimd")
        pb.S(lambda e: e.activation(sq[:, :], ot[:, :], AF.Square))
        pb.V(lambda e: e.reduce_sum(ms[:, :1], sq[:, :], axis=AX.X))
        pb.V(lambda e: e.tensor_scalar(ms[:, :1], ms[:, :1], 1.0 / D, 1e-5, OP.mult, OP.add))
        pb.S(lambda e: e.activation(ms[:, :1], ms[:, :1], AF.Ln))
        pb.S(lambda e: e.activation(ms[:, :1], ms[:, :1], AF.Exp, scale=-0.5))
        pb.V(lambda e: e.tensor_scalar(n1[:, :], ot[:, :], ms[:, :1], None, OP.mult))
        pb.V(lambda e: e.tensor_tensor(n1[:, :], n1[:, :], rwf2[:, :], OP.mult))
        pb.V(lambda e: e.tensor_copy(n1b[:, :], n1[:, :]))
        pb.dma(d_n2b[tsl, :], n1b[:, :], q="gpsimd")
    n2T = [pb.sb(f"n2T0{k}", (128, T), BF16) for k in range(8)]
    for k in range(8):
        pb.dmaT(n2T[k][:, :], d_n2b[:, k * 128:(k + 1) * 128])
    pg = [psA, psB]
    gs = pb.sb("gsl0", (128, T))
    for j in range(16):
        esl = slice(j * 128, (j + 1) * 128)
        for k in range(8):
            pb.op("gpsimd", lambda e, k=k, j=j: e.dma_start(
                out=wt8[k][:, :], in_=d_gatew[k * 128:(k + 1) * 128, j * 128:(j + 1) * 128]), 16)
        for th in range(2):
            tfl = slice(th * 1024, (th + 1) * 1024)
            for k in range(8):
                for t2 in range(2):
                    pb.TE(lambda e, k=k, th=th, t2=t2, s=(k == 0), p=(k == 7):
                          e.matmul(pg[t2][:, :], wt8[k][:, :],
                                   n2T[k][:, th * 1024 + t2 * 512:th * 1024 + (t2 + 1) * 512],
                                   start=s, stop=p))
            for t2 in range(2):
                pb.S(lambda e, th=th, t2=t2: e.activation(
                    gs[:, th * 1024 + t2 * 512:th * 1024 + (t2 + 1) * 512], pg[t2][:, :], AF.Silu))
        for k in range(8):
            pb.op("gpsimd", lambda e, k=k, j=j: e.dma_start(
                out=wt8[k][:, :], in_=d_upw[k * 128:(k + 1) * 128, j * 128:(j + 1) * 128]), 16)
        for th in range(2):
            for k in range(8):
                for t2 in range(2):
                    pb.TE(lambda e, k=k, th=th, t2=t2, s=(k == 0), p=(k == 7):
                          e.matmul(pg[t2][:, :], wt8[k][:, :],
                                   n2T[k][:, th * 1024 + t2 * 512:th * 1024 + (t2 + 1) * 512],
                                   start=s, stop=p))
            for t2 in range(2):
                pb.V(lambda e, th=th, t2=t2: e.scalar_tensor_tensor(
                    sbt[:, th * 1024 + t2 * 512:th * 1024 + (t2 + 1) * 512],
                    pg[t2][:, :], 1.0,
                    gs[:, th * 1024 + t2 * 512:th * 1024 + (t2 + 1) * 512], OP.mult, OP.mult))
        pb.dma(d_h[esl, :], sbt[:, :], q="gpsimd")
    for i in range(NT):
        tsl = slice(i * 128, (i + 1) * 128)
        for half in range(2):
            nsl = slice(half * 512, (half + 1) * 512)
            for k in range(16):
                pb.op("gpsimd", lambda e, k=k, tsl=tsl: e.dma_start(
                    out=yk[:, :], in_=d_h[k * 128:(k + 1) * 128, tsl]), 16)
                pb.op("gpsimd", lambda e, k=k: e.dma_start(
                    out=wo[:, :], in_=d_downw[k * 128:(k + 1) * 128, :]), 16)
                pb.TE(lambda e, k=k, nsl=nsl, s=(k == 0), p=(k == 15):
                      e.matmul(po2[:, :], yk[:, :], wo[:, nsl], start=s, stop=p))
            pb.V(lambda e, nsl=nsl: e.tensor_copy(ot[:, nsl], po2[:, :]))
        pb.dma(xt[:, :], d_xo[tsl, :])
        pb.V(lambda e: e.tensor_tensor(ot[:, :], ot[:, :], xt[:, :], OP.add))
        pb.dma(d_out[tsl, :], ot[:, :], q="gpsimd")
    return pb


def prog_combine(pb=None, px="", d_x=None, d_a=None, d_f=None, d_o=None):
    pb = pb or PB("comb")
    d_x = d_x if d_x is not None else pb.din(px + "x", [T, D], F32)
    d_a = d_a if d_a is not None else pb.din(px + "a", [T, D], F32)
    d_f = d_f if d_f is not None else pb.din(px + "f", [T, D], F32)
    d_o = d_o if d_o is not None else pb.dout(px + "out", [T, D], F32)
    t1 = pb.sb("t1c2", (128, D))
    t2 = pb.sb("t2c2", (128, D))
    t3 = pb.sb("t3c2", (128, D))
    for i in range(NT):
        tsl = slice(i * 128, (i + 1) * 128)
        pb.dma(t1[:, :], d_x[tsl, :])
        pb.dma(t2[:, :], d_a[tsl, :])
        pb.dma(t3[:, :], d_f[tsl, :])
        pb.V(lambda e: e.tensor_tensor(t1[:, :], t1[:, :], t2[:, :], OP.add))
        pb.V(lambda e: e.tensor_tensor(t1[:, :], t1[:, :], t3[:, :], OP.add))
        pb.dma(d_o[tsl, :], t1[:, :], q="gpsimd")
    return pb


_PROGS = {}


def _get_prog(name, builder):
    if name not in _PROGS:
        pb = builder()
        _PROGS[name] = pb.finalize()
    return _PROGS[name]


def _consts():
    c = {}
    c["ident"] = np.eye(128, dtype=np.float32)
    tt_ = np.arange(T)[:, None] // 64
    cc_ = np.arange(32)[None, :]
    c["l1mask"] = np.where(tt_ > cc_, 0.0, -1e30).astype(np.float32)
    dh = 64
    inv = 1.0 / 10000.0 ** (np.arange(0, dh, 2) / dh)
    fr = np.arange(T)[:, None] * inv[None, :]
    fr = np.repeat(fr, 2, axis=-1)
    cos = np.cos(fr).astype(np.float32)
    sin = np.sin(fr).astype(np.float32)
    sgn = np.tile(np.array([-1.0, 1.0], np.float32), dh // 2)
    c["cosb"] = np.tile(cos, (1, 16))
    c["sinb"] = np.tile((sin * sgn).astype(np.float32), (1, 16))
    ii = np.arange(512)[:, None]
    jj = np.arange(512)[None, :]
    c["maskc"] = np.where(jj <= ii, 0.0, -1e30).astype(np.float32)
    c["tiecol"] = np.tile(np.linspace(0, 1e-6, 16).astype(np.float32), 8)[:, None].copy()
    selc = np.zeros((64, 4096), np.float32)
    for p in range(4096):
        selc[p // 64, p] = 1.0
    c["selc"] = selc
    return c


def _dropout_mask():
    import jax
    with jax.default_device(jax.devices("cpu")[0]):
        keep = np.asarray(jax.random.bernoulli(jax.random.key(42), 0.8, (8, T, D)))
    import ml_dtypes
    return (keep.astype(np.float32) * 1.25).astype(ml_dtypes.bfloat16)


def kernel(**inp):
    return kernel_fused(**inp)


def kernel_multilaunch(**inp):
    f = np.float32
    ct = lambda a: np.ascontiguousarray(a, dtype=f)
    C = _consts()
    B = inp["L0_attn_x"].shape[0]
    x = [ct(inp["L0_attn_x"][b]) for b in range(B)]
    cores = list(range(B))

    def run(name, builder, maps):
        nc = _get_prog(name, builder)
        return run_bass_kernel_spmd(nc, maps, core_ids=cores).results

    maskscale = _dropout_mask()

    p0 = {k[8:]: inp[k] for k in inp if k.startswith("L0_attn_") and k != "L0_attn_x"}
    p0f = {k[7:]: inp[k] for k in inp if k.startswith("L0_ffn_")}
    p1 = {k[8:]: inp[k] for k in inp if k.startswith("L1_attn_")}
    p2 = {k[8:]: inp[k] for k in inp if k.startswith("L2_attn_")}
    p2f = {k[7:]: inp[k] for k in inp if k.startswith("L2_ffn_")}
    p3 = {k[8:]: inp[k] for k in inp if k.startswith("L3_attn_")}
    p3f = {k[7:]: inp[k] for k in inp if k.startswith("L3_ffn_")}

    for li in range(4):
        r = run("ln", prog_layernorm, [{"x": x[b]} for b in range(B)])
        xn = [np.asarray(r[b]["xn"]) for b in range(B)]
        xnb = [np.asarray(r[b]["xnb"]) for b in range(B)]
        if li == 0:
            ra = run("l0a", prog_l0_attn, [{
                "xn": xn[b], "inw": ct(p0["in_proj_W"]),
                "convw": ct(p0["conv_W"][:, 0, :].T), "convb": ct(p0["conv_b"][:, None]),
                "dtw": ct(p0["dt_W"]), "dtb": ct(p0["dt_b"][:, None]),
                "bw": ct(p0["B_W"]), "cw": ct(p0["C_W"]), "alog": ct(p0["A_log"][None, :]),
                "outw": ct(p0["out_proj_W"]), "gatew": ct(p0["gate_W"]),
                "upw": ct(p0["up_W"]), "downw": ct(p0["down_W"]),
                "inormw": ct(p0["input_norm_w"][None, :]),
                "fnormw": ct(p0["ffn_norm_w"][None, :])} for b in range(B)])
            rf = run("l0f", prog_l0_ffn, [{
                "xnb": xnb[b], "rwt": ct(p0f["router_w"].T),
                "rb": ct(p0f["router_b"][None, :]),
                "ewt": ct(p0f["expert_w"].transpose(0, 2, 1)),
                "eb": ct(p0f["expert_b"])} for b in range(B)])
        elif li == 1:
            ra = run("l1a", prog_l1_attn, [{
                "xn": xn[b], "win": ct(p1["W_in"]), "gbias": ct(p1["gate_bias"][:, None]),
                "rmsw": ct(p1["rms_weight"][None, :]), "wout": ct(p1["W_out"]),
                "l1mask": C["l1mask"], "c_ident": C["ident"]} for b in range(B)])
            rf = run("l1f", prog_l1_ffn, [{
                "xn": xn[b], "rhs": ct(inp["L1_ffn_rhs"]),
                "c_ident": C["ident"]} for b in range(B)])
        elif li == 2:
            pmkT = ct(np.concatenate([np.asarray(p2["pm"])[0].transpose(2, 0, 1).reshape(64, 256)] * 2, axis=0))
            pmvT = ct(np.asarray(p2["pm"])[1].transpose(1, 0, 2).reshape(16, 1024))
            ra = run("l2a", prog_l2_attn, [{
                "xn": xn[b], "wqkvt": ct(np.asarray(p2["w_qkv"]).T),
                "wot": ct(np.asarray(p2["w_out"]).T), "gvec": ct(p2["g"][None, :]),
                "cosb": C["cosb"], "sinb": C["sinb"], "maskc": C["maskc"],
                "pmkT": pmkT, "pmvT": pmvT, "c_ident": C["ident"]} for b in range(B)])
            rf = run("l2f", prog_l2_ffn, [{
                "xnb": xnb[b], "ctrl": ct(np.asarray(p2f["controller"]).reshape(1024, 64)),
                "f1": ct(np.asarray(p2f["f1"]).reshape(1024, 4096)),
                "f2": ct(np.asarray(p2f["f2"]).reshape(4096, 1024)),
                "biasT": ct(np.asarray(p2f["bias"]).reshape(4096, 1)),
                "tiecol": C["tiecol"], "c_ident": C["ident"],
                "sel": C["selc"]} for b in range(B)])
        else:
            ra = run("l3a", lambda: prog_l3_attn(ncore=B), [{
                "xn": xn[b], "wqt": ct(np.asarray(p3["Wq"]).T), "wkt": ct(np.asarray(p3["Wk"]).T),
                "wvt": ct(np.asarray(p3["Wv"]).T), "wot": ct(np.asarray(p3["Wo"]).T),
                "c_ident": C["ident"]} for b in range(B)])
            rf = run("l3f", prog_l3_ffn, [{
                "xn": xn[b], "w1t": ct(np.asarray(p3f["W1"]).T),
                "b1": ct(p3f["b1"][None, :]), "w2t": ct(np.asarray(p3f["W2"]).T),
                "b2": ct(p3f["b2"][None, :]), "mask": np.asarray(maskscale[b])} for b in range(B)])
        rc = run("comb", prog_combine, [{
            "x": x[b], "a": np.asarray(ra[b]["out"]), "f": np.asarray(rf[b]["out"])}
            for b in range(B)])
        x = [np.asarray(rc[b]["out"]) for b in range(B)]
    return np.stack(x)


def build_fused(B=NCORE):
    pb = PB("mega")
    nc = pb.nc
    d_x0 = pb.din("x0", [T, D], F32)
    cur = d_x0
    d_final = pb.dout("xfinal", [T, D], F32)
    for li in range(4):
        d_xn = nc.dram_tensor(f"xn{li}", [T, D], F32, kind="Internal").ap()
        d_xnb = nc.dram_tensor(f"xnb{li}", [T, D], BF16, kind="Internal").ap()
        with pb.scope():
            prog_layernorm(pb=pb, px=f"ln{li}_", d_x=cur, d_xn=d_xn, d_xnb=d_xnb)
        d_a = nc.dram_tensor(f"abr{li}", [T, D], F32, kind="Internal").ap()
        d_f = nc.dram_tensor(f"fbr{li}", [T, D], F32, kind="Internal").ap()
        if li == 0:
            with pb.scope():
                prog_l0_attn(pb=pb, px="l0a_", d_xn=d_xn, d_out=d_a)
            with pb.scope():
                prog_l0_ffn(pb=pb, px="l0f_", d_xnb=d_xnb, d_out=d_f)
        elif li == 1:
            with pb.scope():
                prog_l1_attn(pb=pb, px="l1a_", d_xn=d_xn, d_out=d_a)
            with pb.scope():
                prog_l1_ffn(pb=pb, px="l1f_", d_xn=d_xn, d_out=d_f)
        elif li == 2:
            with pb.scope():
                prog_l2_attn(pb=pb, px="l2a_", d_xn=d_xn, d_out=d_a)
            with pb.scope():
                prog_l2_ffn(pb=pb, px="l2f_", d_xnb=d_xnb, d_out=d_f)
        else:
            with pb.scope():
                prog_l3_attn(ncore=B, pb=pb, px="l3a_", d_xn=d_xn, d_out=d_a)
            with pb.scope():
                prog_l3_ffn(pb=pb, px="l3f_", d_xn=d_xn, d_out=d_f)
        d_nx = d_final if li == 3 else nc.dram_tensor(f"xres{li + 1}", [T, D], F32, kind="Internal").ap()
        with pb.scope():
            prog_combine(pb=pb, px=f"cb{li}_", d_x=cur, d_a=d_a, d_f=d_f, d_o=d_nx)
        cur = d_nx
    return pb


def fused_in_map(inp, C, maskscale, b):
    f = np.float32
    ct = lambda a: np.ascontiguousarray(a, dtype=f)
    p0 = {k[8:]: inp[k] for k in inp if k.startswith("L0_attn_") and k != "L0_attn_x"}
    p0f = {k[7:]: inp[k] for k in inp if k.startswith("L0_ffn_")}
    p1 = {k[8:]: inp[k] for k in inp if k.startswith("L1_attn_")}
    p2 = {k[8:]: inp[k] for k in inp if k.startswith("L2_attn_")}
    p2f = {k[7:]: inp[k] for k in inp if k.startswith("L2_ffn_")}
    p3 = {k[8:]: inp[k] for k in inp if k.startswith("L3_attn_")}
    p3f = {k[7:]: inp[k] for k in inp if k.startswith("L3_ffn_")}
    pmkT = ct(np.concatenate([np.asarray(p2["pm"])[0].transpose(2, 0, 1).reshape(64, 256)] * 2, axis=0))
    pmvT = ct(np.asarray(p2["pm"])[1].transpose(1, 0, 2).reshape(16, 1024))
    m = {"x0": ct(inp["L0_attn_x"][b])}
    m.update({
        "l0a_inw": ct(p0["in_proj_W"]), "l0a_convw": ct(p0["conv_W"][:, 0, :].T),
        "l0a_convb": ct(p0["conv_b"][:, None]), "l0a_dtw": ct(p0["dt_W"]),
        "l0a_dtb": ct(p0["dt_b"][:, None]), "l0a_bw": ct(p0["B_W"]), "l0a_cw": ct(p0["C_W"]),
        "l0a_alog": ct(p0["A_log"][None, :]), "l0a_outw": ct(p0["out_proj_W"]),
        "l0a_gatew": ct(p0["gate_W"]), "l0a_upw": ct(p0["up_W"]), "l0a_downw": ct(p0["down_W"]),
        "l0a_inormw": ct(p0["input_norm_w"][None, :]), "l0a_fnormw": ct(p0["ffn_norm_w"][None, :]),
        "l0f_rwt": ct(np.asarray(p0f["router_w"]).T), "l0f_rb": ct(p0f["router_b"][None, :]),
        "l0f_ewt": ct(np.asarray(p0f["expert_w"]).transpose(0, 2, 1)), "l0f_eb": ct(p0f["expert_b"]),
        "l1a_win": ct(p1["W_in"]), "l1a_gbias": ct(p1["gate_bias"][:, None]),
        "l1a_rmsw": ct(p1["rms_weight"][None, :]), "l1a_wout": ct(p1["W_out"]),
        "l1a_l1mask": C["l1mask"], "l1a_c_ident": C["ident"],
        "l1f_rhs": ct(inp["L1_ffn_rhs"]), "l1f_c_ident": C["ident"],
        "l2a_wqkvt": ct(np.asarray(p2["w_qkv"]).T), "l2a_wot": ct(np.asarray(p2["w_out"]).T),
        "l2a_gvec": ct(p2["g"][None, :]), "l2a_cosb": C["cosb"], "l2a_sinb": C["sinb"],
        "l2a_maskc": C["maskc"], "l2a_pmkT": pmkT, "l2a_pmvT": pmvT, "l2a_c_ident": C["ident"],
        "l2f_ctrl": ct(np.asarray(p2f["controller"]).reshape(1024, 64)),
        "l2f_f1": ct(np.asarray(p2f["f1"]).reshape(1024, 4096)),
        "l2f_f2": ct(np.asarray(p2f["f2"]).reshape(4096, 1024)),
        "l2f_biasT": ct(np.asarray(p2f["bias"]).reshape(4096, 1)),
        "l2f_tiecol": C["tiecol"], "l2f_c_ident": C["ident"], "l2f_sel": C["selc"],
        "l3a_wqt": ct(np.asarray(p3["Wq"]).T), "l3a_wkt": ct(np.asarray(p3["Wk"]).T),
        "l3a_wvt": ct(np.asarray(p3["Wv"]).T), "l3a_wot": ct(np.asarray(p3["Wo"]).T),
        "l3a_c_ident": C["ident"],
        "l3f_w1t": ct(np.asarray(p3f["W1"]).T), "l3f_b1": ct(p3f["b1"][None, :]),
        "l3f_w2t": ct(np.asarray(p3f["W2"]).T), "l3f_b2": ct(p3f["b2"][None, :]),
        "l3f_mask": np.asarray(maskscale[b]),
    })
    return m


_FUSED = {}


def kernel_fused(**inp):
    B = inp["L0_attn_x"].shape[0]
    if B not in _FUSED:
        _FUSED[B] = build_fused(B).finalize()
    nc = _FUSED[B]
    C = _consts()
    maskscale = _dropout_mask()
    maps = [fused_in_map(inp, C, maskscale, b) for b in range(B)]
    res = run_bass_kernel_spmd(nc, maps, core_ids=list(range(B)))
    return np.stack([np.asarray(res.results[b]["xfinal"]) for b in range(B)])
